# version 16
# baseline (speedup 1.0000x reference)
import numpy as np
import ml_dtypes
from contextlib import ExitStack
import concourse.bass as bass
import concourse.mybir as mybir
from concourse.bass_utils import run_bass_kernel_spmd

F32 = mybir.dt.float32
BF16 = mybir.dt.bfloat16
AF = mybir.ActivationFunctionType
ALU = mybir.AluOpType
AX = mybir.AxisListType

B, S, D = 2, 8192, 1024
DI, NST, RDT, KC, DF = 2048, 16, 64, 4, 4096
DSL = DI // 4
TQ = S // 4
TC = 512
NCH = S // TC
ARG = 1
NAR = NCH // ARG
NSC = 4
NSLOT = 2
ENGS = ("sync", "vector", "scalar", "gpsimd", "tensor")
GROUPS = [[0, 1, 2, 3], [4, 5, 6, 7]]


class Sem:
    def __init__(self, nc, es, name):
        self.h = es.enter_context(nc.semaphore(name))
        self.n = 0


class Prog:
    def __init__(self):
        self.plan = {e: [] for e in ENGS}
    def emit(self, eng, fn):
        self.plan[eng].append(("op", fn))
    def wait(self, eng, sem, val):
        if val > 0:
            self.plan[eng].append(("wait", sem.h, val))
    def inc(self, eng, sem, k=1):
        sem.n += k
        self.plan[eng].append(("inc", sem.h, k))
        return sem.n
    def run(self, nc):
        def runner(name):
            def go(e):
                last = None
                for item in self.plan[name]:
                    if item[0] == "op":
                        last = item[1](e)
                    elif item[0] == "wait":
                        e.wait_ge(item[1], item[2])
                    else:
                        assert last is not None, f"inc with no prior op on {name}"
                        last.then_inc(item[1], item[2])
                        last = None
            return go
        with nc.Block() as block:
            block.sync(runner("sync"))
            block.vector(runner("vector"))
            block.scalar(runner("scalar"))
            block.gpsimd(runner("gpsimd"))
            block.tensor(runner("tensor"))


def build(debug=(), detect_races=True):
    DO_CACHE = True
    DO_PH2 = True
    DO_WT = True
    FFN_STOP = 4

    nc = bass.Bass("TRN2", target_bir_lowering=False, detect_race_conditions=detect_races)
    es = ExitStack()
    E = lambda t: es.enter_context(t)

    x_q = nc.dram_tensor("x_q", [TQ, D], F32, kind="ExternalInput").ap()
    w_in = nc.dram_tensor("w_in", [D, 2 * DSL], F32, kind="ExternalInput").ap()
    conv_w = nc.dram_tensor("conv_w", [128, 4, KC], F32, kind="ExternalInput").ap()
    conv_b = nc.dram_tensor("conv_b", [128, 4], F32, kind="ExternalInput").ap()
    xproj_w = nc.dram_tensor("xproj_w", [DSL, 96], F32, kind="ExternalInput").ap()
    dtw = nc.dram_tensor("dtw", [RDT, DSL], F32, kind="ExternalInput").ap()
    dtb = nc.dram_tensor("dtb", [128, 4], F32, kind="ExternalInput").ap()
    dtlnw = nc.dram_tensor("dtlnw", [RDT, 1], F32, kind="ExternalInput").ap()
    blnw = nc.dram_tensor("blnw", [1, NST], F32, kind="ExternalInput").ap()
    clnw = nc.dram_tensor("clnw", [1, NST], F32, kind="ExternalInput").ap()
    alog = nc.dram_tensor("alog", [128, 4, NST], F32, kind="ExternalInput").ap()
    dskip = nc.dram_tensor("dskip", [128, 4], F32, kind="ExternalInput").ap()
    wout = nc.dram_tensor("wout", [DSL, D], F32, kind="ExternalInput").ap()
    inlnw = nc.dram_tensor("inlnw", [1, D], F32, kind="ExternalInput").ap()
    ident_bf = nc.dram_tensor("ident_bf", [128, 128], BF16, kind="ExternalInput").ap()
    ident_f = nc.dram_tensor("ident_f", [128, 128], F32, kind="ExternalInput").ap()
    gate_w = nc.dram_tensor("gate_w", [D, DF], F32, kind="ExternalInput").ap()
    up_w = nc.dram_tensor("up_w", [D, DF], F32, kind="ExternalInput").ap()
    down_w = nc.dram_tensor("down_w", [DF, D], F32, kind="ExternalInput").ap()
    pmw = nc.dram_tensor("pmw", [1, D], F32, kind="ExternalInput").ap()
    w_r = nc.dram_tensor("w_r", [D, D], F32, kind="ExternalInput").ap()
    w_k = nc.dram_tensor("w_k", [D, D], F32, kind="ExternalInput").ap()
    w_v = nc.dram_tensor("w_v", [D, D], F32, kind="ExternalInput").ap()
    mu_r = nc.dram_tensor("mu_r", [128, 8], F32, kind="ExternalInput").ap()
    mu_k = nc.dram_tensor("mu_k", [128, 8], F32, kind="ExternalInput").ap()
    selp = nc.dram_tensor("selp", [16, 4], BF16, kind="ExternalInput").ap()

    out_q = nc.dram_tensor("out_q", [TQ, D], F32, kind="ExternalOutput").ap()

    hT_my = nc.dram_tensor("hT_my", [D, TQ], BF16).ap()
    hT_all = nc.dram_tensor("hT_all", [4 * D, TQ], BF16).ap()
    xp_part = [nc.dram_tensor(f"xp_part{j}", [96, ARG * TC], F32).ap() for j in range(NAR)]
    xp_red = [nc.dram_tensor(f"xp_red{j}", [96, ARG * TC], F32).ap() for j in range(NAR)]
    op_part = nc.dram_tensor("op_part", [S, D], F32).ap()
    gate_bf = nc.dram_tensor("gate_bf", [D, DF], BF16).ap()
    up_bf = nc.dram_tensor("up_bf", [D, DF], BF16).ap()
    down_bf = nc.dram_tensor("down_bf", [DF, D], BF16).ap()
    wrT_bf = nc.dram_tensor("wrT_bf", [D, D], BF16).ap()
    wkT_bf = nc.dram_tensor("wkT_bf", [D, D], BF16).ap()
    wvT_bf = nc.dram_tensor("wvT_bf", [D, D], BF16).ap()
    bnd_my = [nc.dram_tensor(f"bnd_my{i}", [4, D], BF16).ap() for i in range(2)]
    bnd_all = [nc.dram_tensor(f"bnd_all{i}", [16, D], BF16).ap() for i in range(2)]
    xh_my = nc.dram_tensor("xh_my", [4, 128, 8, 512], BF16).ap()
    xh_all = nc.dram_tensor("xh_all", [16, 128, 8, 512], BF16).ap()
    xv_my = nc.dram_tensor("xv_my", [TQ, D], BF16).ap()
    xv_all = nc.dram_tensor("xv_all", [S, D], BF16).ap()
    rs1_out = nc.dram_tensor("rs1_out", [TQ, D], F32).ap()

    ident_sb = E(nc.sbuf_tensor("ident_sb", [128, 128], BF16))
    ones_sb = E(nc.sbuf_tensor("ones_sb", [128, 128], F32))
    cst_sb = E(nc.sbuf_tensor("cst_sb", [128, 2], F32))
    onesbf_sb = E(nc.sbuf_tensor("onesbf_sb", [128, 1], BF16))
    es1 = ExitStack()
    E1 = lambda t: es1.enter_context(t)
    w_in_bf = E1(nc.sbuf_tensor("w_in_bf", [128, 8, 2 * DSL], BF16))
    xproj_sb = E1(nc.sbuf_tensor("xproj_sb", [128, 4, 96], F32))
    dtw_sb = E1(nc.sbuf_tensor("dtw_sb", [RDT, DSL], F32))
    wout_bf = E1(nc.sbuf_tensor("wout_bf", [128, 4, D], BF16))
    convw_sb = E1(nc.sbuf_tensor("convw_sb", [128, 4, KC], F32))
    convb_sb = E1(nc.sbuf_tensor("convb_sb", [128, 4], F32))
    dtb_sb = E1(nc.sbuf_tensor("dtb_sb", [128, 4], F32))
    dtlnw_sb = E1(nc.sbuf_tensor("dtlnw_sb", [RDT, 1], F32))
    clnw_eff = E1(nc.sbuf_tensor("clnw_eff", [128, NST], F32))
    blnw_rep = E1(nc.sbuf_tensor("blnw_rep", [128, NST], F32))
    A_sb = E1(nc.sbuf_tensor("A_sb", [128, 4, NST], F32))
    dskip_sb = E1(nc.sbuf_tensor("dskip_sb", [128, 4], F32))

    hTin = [E1(nc.sbuf_tensor(f"hTin{i}", [128, 8, TC], BF16)) for i in range(2)]
    xc_sl = [E1(nc.sbuf_tensor(f"xc{i}", [128, 4, TC + 3], F32)) for i in range(NSLOT)]
    sz_sl = [E1(nc.sbuf_tensor(f"sz{i}", [128, 4, TC], BF16)) for i in range(NSLOT)]
    xpev = E1(nc.sbuf_tensor("xpev", [80, TC], F32))
    xpev_c = E1(nc.sbuf_tensor("xpev_c", [16, TC], F32))
    xpd_in = E1(nc.sbuf_tensor("xpd_in", [RDT, TC], F32))
    xpb_in = E1(nc.sbuf_tensor("xpb_in", [NST, TC], F32))
    xpc_in = E1(nc.sbuf_tensor("xpc_in", [NST, TC], F32))
    sel_sb = E1(nc.sbuf_tensor("sel_sb", [128, 128], F32))
    rstdd = E1(nc.sbuf_tensor("rstdd", [1, TC], F32))
    rstdb = E1(nc.sbuf_tensor("rstdb", [1, TC], F32))
    rstdc = E1(nc.sbuf_tensor("rstdc", [1, TC], F32))
    dtrms = E1(nc.sbuf_tensor("dtrms", [RDT, TC], F32))
    dt_sb = E1(nc.sbuf_tensor("dt_sb", [128, 4, TC], F32))
    u_sb = E1(nc.sbuf_tensor("u_sb", [128, TC], F32))
    up_sb = E1(nc.sbuf_tensor("up_sb", [128, 4, TC], F32))
    state = E1(nc.sbuf_tensor("state", [128, 4, NST], F32))
    yred = E1(nc.sbuf_tensor("yred", [128, TC], F32))
    t1_sb = E1(nc.sbuf_tensor("t1_sb", [128, TC], F32))
    cfin = E1(nc.sbuf_tensor("cfin", [128, TC], F32))
    yg_bf = E1(nc.sbuf_tensor("yg_bf", [128, 4, TC], BF16))
    oo_sb = E1(nc.sbuf_tensor("oo_sb", [128, D], F32))

    es0 = ExitStack()
    E0 = lambda t: es0.enter_context(t)
    wstage = E0(nc.sbuf_tensor("wstage", [128, 2 * DSL], F32))
    wln_rep = E0(nc.sbuf_tensor("wln_rep", [128, D], F32))
    xt_sb = [E0(nc.sbuf_tensor(f"xt{i}", [128, D], F32)) for i in range(2)]
    sq_scr = E0(nc.sbuf_tensor("sq_scr", [128, D], F32))
    ln1_t = E0(nc.sbuf_tensor("ln1_t", [128, D], F32))
    h2_bf = E0(nc.sbuf_tensor("h2_bf", [128, D], BF16))
    hTcol = [E0(nc.sbuf_tensor(f"hTcol{i}", [128, 8, 512], BF16)) for i in range(2)]
    st0 = E0(nc.sbuf_tensor("st0", [128, 8], F32))
    wst_bf = [E0(nc.sbuf_tensor(f"wst_bf{i}", [128, D], BF16)) for i in range(2)]
    wst_f = [E0(nc.sbuf_tensor(f"wst_f{i}", [128, D], F32)) for i in range(2)]
    wrTcol = E0(nc.sbuf_tensor("wrTcol", [128, 8, 128], BF16))

    esp = ExitStack()
    p_tr = [esp.enter_context(nc.psum_tensor(f"p_tr{i}", [128, 128], BF16)) for i in range(2)]
    p_wt = [esp.enter_context(nc.psum_tensor(f"p_wt{i}", [128, 128], BF16)) for i in range(2)]

    sW = Sem(nc, es, "sW")
    sWC = Sem(nc, es, "sWC")
    sX = [Sem(nc, es, "sX0"), Sem(nc, es, "sX1")]
    sH = [Sem(nc, es, "sH0"), Sem(nc, es, "sH1")]
    sBC = Sem(nc, es, "sBC")
    sBR = Sem(nc, es, "sBR")
    sSt = Sem(nc, es, "sSt")
    sHT = Sem(nc, es, "sHT")
    sDBG = Sem(nc, es, "sDBG")
    sXP = [Sem(nc, es, "sXP0"), Sem(nc, es, "sXP1")]
    sOP = [Sem(nc, es, "sOP0"), Sem(nc, es, "sOP1")]
    sCC = Sem(nc, es, "sCC")
    sAS = Sem(nc, es, "sAS")
    sVS = Sem(nc, es, "sVS")
    sGB = Sem(nc, es, "sGB")
    sG2 = Sem(nc, es, "sG2")
    sG3 = Sem(nc, es, "sG3")
    sRS = Sem(nc, es, "sRS")
    sRS2 = Sem(nc, es, "sRS2")
    sWL = [Sem(nc, es, "sWL0"), Sem(nc, es, "sWL1")]
    sGBp = [Sem(nc, es, "sGBp0"), Sem(nc, es, "sGBp1")]
    sPC = Sem(nc, es, "sPC")
    sWT = Sem(nc, es, "sWT")
    sWE = Sem(nc, es, "sWE")
    sT = Sem(nc, es, "sT")
    sA = Sem(nc, es, "sA")
    sV = Sem(nc, es, "sV")
    sP = Sem(nc, es, "sP")
    sQGA = [Sem(nc, es, f"sQGA{i}") for i in range(8)]
    sQG3 = Sem(nc, es, "sQG3")

    P = Prog()

    P.emit("vector", lambda e: e.memset(cst_sb[:, 0:1], 1e-5))
    P.emit("vector", lambda e: e.memset(cst_sb[:, 1:2], 1e-6))
    P.inc("vector", sV)
    loads = [
        (convw_sb, conv_w), (convb_sb, conv_b), (dtb_sb, dtb),
        (dtlnw_sb, dtlnw), (A_sb, alog), (dskip_sb, dskip),
        (ident_sb, ident_bf), (xproj_sb, xproj_w.rearrange("(a p) m -> p a m", p=128)),
        (dtw_sb, dtw),
    ]
    for dst, src in loads:
        sW.n += 16
        P.emit("sync", lambda e, d=dst, s=src: e.dma_start(out=d[:], in_=s).then_inc(sW.h, 16))
    for dst, src, n in ((blnw_rep, blnw, NST), (clnw_eff, clnw, NST), (wln_rep, inlnw, D)):
        sW.n += 16
        P.emit("sync", lambda e, d=dst, s=src, nn=n: e.dma_start(
            out=d[:], in_=s.broadcast_to([128, nn])).then_inc(sW.h, 16))
    w_loaded = sW.n

    cast_jobs = []
    for kk in range(8):
        cast_jobs.append((w_in_bf[:, kk, :], w_in[kk * 128:(kk + 1) * 128, :], 2 * DSL))
    for kk in range(4):
        cast_jobs.append((wout_bf[:, kk, :], wout[kk * 128:(kk + 1) * 128, :], D))
    for dst, src, width in cast_jobs:
        P.wait("sync", sP, sP.n)
        sWC.n += 16
        P.emit("sync", lambda e, s=src, w=width: e.dma_start(
            out=wstage[:, 0:w], in_=s).then_inc(sWC.h, 16))
        P.wait("gpsimd", sWC, sWC.n)
        P.emit("gpsimd", lambda e, d=dst, w=width: e.tensor_copy(out=d, in_=wstage[:, 0:w]))
        P.inc("gpsimd", sP)
    P.wait("gpsimd", sW, w_loaded)
    P.emit("gpsimd", lambda e: e.memset(ones_sb[:], 1.0))
    P.emit("gpsimd", lambda e: e.memset(onesbf_sb[:], 1.0))
    P.emit("gpsimd", lambda e: e.memset(sel_sb[:], 1.0))
    P.inc("gpsimd", sP); P.wait("gpsimd", sP, sP.n)
    P.emit("gpsimd", lambda e: e.memset(sel_sb[64:65, :], 0.0))
    w_casts = P.inc("gpsimd", sP)

    cache_jobs = []
    for wsrc, wdst, RT, CC in ((gate_w, gate_bf, 8, DF), (up_w, up_bf, 8, DF),
                               (down_w, down_bf, 32, D)):
        ncc = CC // D
        for rt in range(RT):
            for cc in range(max(1, ncc)):
                c0_, c1_ = cc * D, min(CC, (cc + 1) * D)
                cache_jobs.append((wsrc[rt * 128:(rt + 1) * 128, c0_:c1_],
                                   wdst[rt * 128:(rt + 1) * 128, c0_:c1_], None, None))
    if DO_WT:
        for wsrc, wdst in ((w_r, wrT_bf), (w_k, wkT_bf), (w_v, wvT_bf)):
            for rt in range(8):
                cache_jobs.append((wsrc[rt * 128:(rt + 1) * 128, :], wdst, "T", rt))
    cache_state = dict(jw=0, casts=[0, 0], stores=[0, 0], wt_ev={}, wt_slot=0)

    def emit_cache_jobs(n):
        if not DO_CACHE:
            return
        for _ in range(n):
            jw = cache_state["jw"]
            if jw >= len(cache_jobs):
                return
            cache_state["jw"] = jw + 1
            srcs, dsts, kind, rt = cache_jobs[jw]
            p = jw % 2
            P.wait("scalar", sPC, cache_state["casts"][p])
            sWL[p].n += 16
            P.emit("scalar", lambda e, s=srcs, pp=p, sm=sWL[p]: e.dma_start(
                out=wst_f[pp][:, 0:s.shape[1]], in_=s).then_inc(sm.h, 16))
            P.wait("gpsimd", sWL[p], sWL[p].n)
            P.wait("gpsimd", sGBp[p], cache_state["stores"][p])
            P.emit("gpsimd", lambda e, pp=p, w=srcs.shape[1]: e.tensor_copy(
                out=wst_bf[pp][:, 0:w], in_=wst_f[pp][:, 0:w]))
            cache_state["casts"][p] = P.inc("gpsimd", sPC)
            if kind is None:
                P.wait("sync", sPC, cache_state["casts"][p])
                sGBp[p].n += 16
                sGB.n += 16
                P.emit("sync", lambda e, d=dsts, pp=p, w=srcs.shape[1], sm=sGBp[p]: e.dma_start(
                    out=d, in_=wst_bf[pp][:, 0:w]).then_inc(sm.h, 16))
                P.emit("sync", lambda e: e.nop()) if False else None
                cache_state["stores"][p] = sGBp[p].n
            else:
                P.wait("tensor", sPC, cache_state["casts"][p])
                for mm in range(8):
                    slot = cache_state["wt_slot"]
                    cache_state["wt_slot"] = (slot + 1) % 2
                    P.wait("tensor", sWE, cache_state["wt_ev"].get(slot, 0))
                    P.emit("tensor", lambda e, pp=p, m_=mm, sl=slot: e.transpose(
                        p_wt[sl][:], wst_bf[pp][:, m_ * 128:(m_ + 1) * 128], ident_sb[:]))
                    vt_w = P.inc("tensor", sWT)
                    P.wait("scalar", sWT, vt_w)
                    P.emit("scalar", lambda e, m_=mm, sl=slot: e.activation(
                        out=wrTcol[:, m_, :], in_=p_wt[sl][:], func=AF.Copy))
                    cache_state["wt_ev"][slot] = P.inc("scalar", sWE)
                P.wait("sync", sWE, sWE.n)
                sGBp[p].n += 16
                P.emit("sync", lambda e, d=dsts, r=rt, sm=sGBp[p]: e.dma_start(
                    out=d.rearrange("(m p) c -> p m c", p=128)[:, :, r * 128:(r + 1) * 128],
                    in_=wrTcol[:]).then_inc(sm.h, 16))
                cache_state["stores"][p] = sGBp[p].n
                P.wait("scalar", sGBp[p], sGBp[p].n)

    P.wait("vector", sW, w_loaded)
    P.emit("vector", lambda e: e.tensor_mul(out=clnw_eff[:], in0=clnw_eff[:], in1=blnw_rep[:]))
    P.inc("vector", sV)
    P.wait("scalar", sW, w_loaded)
    P.emit("scalar", lambda e: e.activation(out=A_sb[:], in_=A_sb[:], func=AF.Exp))
    w_a_exp = P.inc("scalar", sA)
    P.wait("vector", sA, w_a_exp)
    P.emit("vector", lambda e: e.tensor_scalar_mul(out=A_sb[:], in0=A_sb[:], scalar1=-1.0))
    w_small_v = P.inc("vector", sV)

    ntile = TQ // 128
    ph0_a_done = {}
    ph0_t_done = {}
    ph0_st = {}
    vv_st_prev = 0
    ev_parity = [0, 0]
    for i in range(ntile):
        buf = xt_sb[i % 2]
        if i >= 2:
            P.wait("sync", sA, ph0_a_done[i - 2])
        sxp = sX[i % 2]
        sxp.n += 16
        P.emit("sync", lambda e, b=buf, i_=i, s=sxp: e.dma_start(
            out=b[:], in_=x_q[i_ * 128:(i_ + 1) * 128, :]).then_inc(s.h, 16))
        v_x = sxp.n
        P.wait("vector", sxp, v_x)
        P.emit("vector", lambda e, b=buf: e.reduce_sum(out=st0[:, 0:1], in_=b[:], axis=AX.X))
        P.inc("vector", sVS); P.wait("vector", sVS, sVS.n)
        P.wait("scalar", sxp, v_x)
        P.wait("scalar", sV, vv_st_prev)
        P.emit("scalar", lambda e, b=buf: e.activation(
            out=sq_scr[:], in_=b[:], func=AF.Square, accum_out=st0[:, 1:2]))
        va_sq = P.inc("scalar", sA)
        P.wait("vector", sA, va_sq)
        P.emit("vector", lambda e: e.tensor_scalar_mul(out=st0[:, 2:3], in0=st0[:, 0:1], scalar1=1.0 / D))
        P.emit("vector", lambda e: e.tensor_scalar_mul(out=st0[:, 3:4], in0=st0[:, 1:2], scalar1=1.0 / D))
        P.inc("vector", sVS); P.wait("vector", sVS, sVS.n)
        P.emit("vector", lambda e: e.tensor_mul(out=st0[:, 4:5], in0=st0[:, 2:3], in1=st0[:, 2:3]))
        P.inc("vector", sVS); P.wait("vector", sVS, sVS.n)
        P.emit("vector", lambda e: e.tensor_sub(out=st0[:, 3:4], in0=st0[:, 3:4], in1=st0[:, 4:5]))
        vv_st = P.inc("vector", sV)
        vv_st_prev = vv_st
        P.wait("scalar", sV, vv_st)
        P.emit("scalar", lambda e: e.activation(out=st0[:, 5:6], in_=st0[:, 3:4], func=AF.Ln, bias=cst_sb[:, 0:1]))
        P.inc("scalar", sAS); P.wait("scalar", sAS, sAS.n)
        P.emit("scalar", lambda e: e.activation(out=st0[:, 5:6], in_=st0[:, 5:6], func=AF.Exp, scale=-0.5))
        va_r = P.inc("scalar", sA)
        P.wait("scalar", sA, va_r)
        P.wait("vector", sA, va_r)
        P.emit("vector", lambda e: e.tensor_mul(out=st0[:, 6:7], in0=st0[:, 2:3], in1=st0[:, 5:6]))
        P.inc("vector", sVS); P.wait("vector", sVS, sVS.n)
        P.emit("vector", lambda e: e.tensor_scalar_mul(out=st0[:, 6:7], in0=st0[:, 6:7], scalar1=-1.0))
        vv_nmr = P.inc("vector", sV)
        P.wait("scalar", sV, vv_nmr)
        P.emit("scalar", lambda e, b=buf: e.activation(
            out=ln1_t[:], in_=b[:], func=AF.Identity, scale=st0[:, 5:6], bias=st0[:, 6:7]))
        P.emit("scalar", lambda e: e.activation(
            out=sq_scr[:], in_=ln1_t[:], func=AF.Square, accum_out=st0[:, 7:8]))
        P.inc("scalar", sAS); P.wait("scalar", sAS, sAS.n)
        P.emit("scalar", lambda e: e.activation(out=st0[:, 7:8], in_=st0[:, 7:8], func=AF.Ln, scale=1.0 / D, bias=cst_sb[:, 1:2]))
        P.inc("scalar", sAS); P.wait("scalar", sAS, sAS.n)
        P.emit("scalar", lambda e: e.activation(out=st0[:, 7:8], in_=st0[:, 7:8], func=AF.Exp, scale=-0.5))
        va_ln = P.inc("scalar", sA)
        P.wait("scalar", sA, va_ln)
        ph0_a_done[i] = va_ln
        P.wait("vector", sA, va_ln)
        if i >= 1:
            P.wait("vector", sT, ph0_t_done[i - 1])
        P.emit("vector", lambda e: e.scalar_tensor_tensor(
            out=h2_bf[:], in0=ln1_t[:], scalar=st0[:, 7:8], in1=wln_rep[:],
            op0=ALU.mult, op1=ALU.mult))
        vv_h2 = P.inc("vector", sV)
        col = hTcol[(i // 4) % 2]
        io = (i % 4) * 128
        P.wait("tensor", sV, vv_h2)
        if i % 4 == 0 and i >= 8:
            P.wait("scalar", sHT, ph0_st[i // 4 - 2])
        for a in range(8):
            pt = p_tr[a % 2]
            P.wait("tensor", sA, ev_parity[a % 2])
            P.emit("tensor", lambda e, a_=a, p=pt: e.transpose(
                p[:], h2_bf[:, a_ * 128:(a_ + 1) * 128], ident_sb[:]))
            vt = P.inc("tensor", sT)
            P.wait("scalar", sT, vt)
            P.emit("scalar", lambda e, a_=a, p=pt, c=col, o=io: e.activation(
                out=c[:, a_, o:o + 128], in_=p[:], func=AF.Copy))
            ev_parity[a % 2] = P.inc("scalar", sA)
        ph0_t_done[i] = sT.n
        if i % 4 == 3:
            P.wait("sync", sA, sA.n)
            sHT.n += 16
            P.emit("sync", lambda e, c=col, i_=i: e.dma_start(
                out=hT_my.rearrange("(a p) l -> p a l", p=128)[:, :, (i_ - 3) * 128:(i_ + 1) * 128],
                in_=c[:]).then_inc(sHT.h, 16))
            ph0_st[i // 4] = sHT.n
        emit_cache_jobs(8)
    emit_cache_jobs(10**9)
    P.wait("gpsimd", sHT, sHT.n)
    sCC.n += 1
    P.emit("gpsimd", lambda e: e.collective_compute(
        "AllGather", ALU.bypass, replica_groups=GROUPS,
        ins=[hT_my.opt()], outs=[hT_all.opt()]).then_inc(sCC.h, 1))
    v_ag1 = sCC.n

    es0.close()
    esp.close()
    esS = ExitStack()
    ES = lambda t: esS.enter_context(t)
    crep_sl = [ES(nc.sbuf_tensor(f"crep{i}", [128, 4, TC], F32)) for i in range(2)]
    dA_b = ES(nc.sbuf_tensor("dA", [128, 4, TC], F32))
    bb_b = ES(nc.sbuf_tensor("bb", [128, 4, TC], F32))
    H_sl = [ES(nc.sbuf_tensor(f"H{i}", [128, 4, TC], F32)) for i in range(2)]
    G_b = ES(nc.sbuf_tensor("G", [128, 4, TC], F32))
    yacc4 = ES(nc.sbuf_tensor("yacc4", [128, 4, TC], F32))
    bsrcB = ES(nc.sbuf_tensor("bsrcB", [128, 4, TC], F32))
    bsrcC = ES(nc.sbuf_tensor("bsrcC", [128, 4, TC], F32))
    rcrep_sb = ES(nc.sbuf_tensor("rcrep", [128, TC], F32))
    p_q = [ES(nc.psum_tensor(f"p_q{i}", [128, TC], F32)) for i in range(8)]

    a_t_done = {}
    a_ev_last = {}
    a_act_done = {}
    a_xpst = {}
    a_xpev_done = {}
    a_xp_dma = {}
    ar_done = {}
    b_done_v = {}
    last_b_done = [0]
    op_dma = {}
    op_ev = {}
    b_prev_dtr = [0]
    op_st_last = [0, 0]
    bsrc_rd = [0]
    scanK_vv = {}
    scanK_h1 = {}
    scanK_vp = {}
    scanK_vp2 = {}
    scanK_vy = {}
    crep_ready = {}
    gmul_last = {}
    bank_ev = {6: 0, 7: 0}
    cc_wait = [v_ag1]

    REPL_BASE = [0, 32, 64, 65]

    def repl_mm(e, srcT, g_, jj_, bank):
        if jj_ < 3:
            b0 = REPL_BASE[jj_]
            return e.matmul(p_q[bank][:], ones_sb[b0:b0 + 1, 0:128],
                            srcT[b0:b0 + 1, g_, :], start=True, stop=True)
        return e.matmul(p_q[bank][:], sel_sb[64:66, 0:128],
                        srcT[64:66, g_, :], start=True, stop=True)

    P.emit("vector", lambda e: e.memset(state[:], 0.0))
    P.inc("vector", sV)

    def a_side(k):
        sc, q = k // 4, k % 4
        lc = sc * TC
        hbuf = hTin[k % 2]
        slot = xc_sl[k % NSLOT]
        zslot = sz_sl[k % NSLOT]
        P.wait("sync", sCC, v_ag1)
        if k >= 2:
            P.wait("sync", sT, a_t_done[k - 2])
        shp_ = sH[k % 2]
        shp_.n += 16
        P.emit("sync", lambda e, b=hbuf, q_=q, l=lc, s=shp_: e.dma_start(
            out=b[:],
            in_=hT_all[q_ * D:(q_ + 1) * D, l:l + TC].rearrange("(kk p) t -> p kk t", p=128)
        ).then_inc(s.h, 16))
        v_h = shp_.n
        P.wait("tensor", shp_, v_h)
        P.wait("tensor", sP, w_casts)
        if k == 0:
            P.wait("tensor", sWT, sWT.n)
            P.wait("tensor", sWE, sWE.n)
            P.wait("tensor", sPC, sPC.n)
            P.wait("tensor", sGBp[0], sGBp[0].n)
            P.wait("tensor", sGBp[1], sGBp[1].n)
        if k >= 1:
            P.wait("tensor", sA, a_ev_last[k - 1])
        if k >= NSLOT:
            P.wait("scalar", sV, b_done_v[k - NSLOT])
        ev_after = []
        for m in range(8):
            pp = p_q[m % 2]
            if m >= 2:
                P.wait("tensor", sA, ev_after[m - 2])
            for kk in range(8):
                P.emit("tensor", lambda e, m_=m, kk_=kk, p=pp, hb=hbuf: e.matmul(
                    p[:], w_in_bf[:, kk_, m_ * 128:(m_ + 1) * 128], hb[:, kk_, :],
                    start=(kk_ == 0), stop=(kk_ == 7)))
            vt = P.inc("tensor", sT)
            P.wait("scalar", sT, vt)
            if m < 4:
                P.emit("scalar", lambda e, m_=m, p=pp, s=slot: e.activation(
                    out=s[:, m_, 3:3 + TC], in_=p[:], func=AF.Copy))
            else:
                P.emit("scalar", lambda e, m_=m, p=pp, zs=zslot: e.activation(
                    out=zs[:, m_ - 4, :], in_=p[:], func=AF.Silu))
            ev_after.append(P.inc("scalar", sA))
        a_t_done[k] = sT.n
        a_ev_last[k] = ev_after[7]
        nslot = xc_sl[(k + 1) % NSLOT]
        P.wait("vector", sA, ev_after[3])
        if k == 0:
            P.emit("vector", lambda e, s=slot: e.memset(s[:, :, 0:3], 0.0))
            P.inc("vector", sVS); P.wait("vector", sVS, sVS.n)
        if k + 1 < NCH:
            P.emit("vector", lambda e, s=slot, ns=nslot: e.tensor_copy(
                out=ns[:, :, 0:3], in_=s[:, :, TC:TC + 3]))
            P.inc("vector", sV)
        silu_done = {}
        for a in range(4):
            fin = yred if a % 2 == 0 else cfin
            if a >= 2:
                P.wait("vector", sA, silu_done[a - 2])
            P.emit("vector", lambda e, a_=a, s=slot: e.tensor_scalar(
                out=t1_sb[:], in0=s[:, a_, 0:TC], scalar1=convw_sb[:, a_, 0:1],
                scalar2=None, op0=ALU.mult))
            P.emit("vector", lambda e, a_=a, s=slot, f=fin: e.scalar_tensor_tensor(
                out=f[:], in0=s[:, a_, 1:1 + TC], scalar=convw_sb[:, a_, 1:2],
                in1=t1_sb[:], op0=ALU.mult, op1=ALU.add))
            P.emit("vector", lambda e, a_=a, s=slot, f=fin: e.scalar_tensor_tensor(
                out=t1_sb[:], in0=s[:, a_, 2:2 + TC], scalar=convw_sb[:, a_, 2:3],
                in1=f[:], op0=ALU.mult, op1=ALU.add))
            P.emit("vector", lambda e, a_=a, s=slot, f=fin: e.scalar_tensor_tensor(
                out=f[:], in0=s[:, a_, 3:3 + TC], scalar=convw_sb[:, a_, 3:4],
                in1=t1_sb[:], op0=ALU.mult, op1=ALU.add))
            vv_c = P.inc("vector", sV)
            P.wait("scalar", sV, vv_c)
            P.emit("scalar", lambda e, a_=a, s=slot, f=fin: e.activation(
                out=s[:, a_, 3:3 + TC], in_=f[:], func=AF.Silu, bias=convb_sb[:, a_:a_ + 1]))
            silu_done[a] = P.inc("scalar", sA)
        a_act_done[k] = sA.n
        P.wait("tensor", sA, a_act_done[k])
        P.wait("tensor", sV, last_b_done[0])
        for pp_sl, c0_, c1_ in ((p_q[2][0:64, :], 0, RDT), (p_q[3][0:16, :], RDT, RDT + NST), (p_q[3][32:48, :], RDT + NST, 96)):
            for kk in range(4):
                P.emit("tensor", lambda e, p=pp_sl, kk_=kk, cc0=c0_, cc1=c1_, s=slot: e.matmul(
                    p, xproj_sb[:, kk_, cc0:cc1], s[:, kk_, 3:3 + TC],
                    start=(kk_ == 0), stop=(kk_ == 3)))
        vt_xp = P.inc("tensor", sT)
        P.wait("scalar", sT, vt_xp)
        if k >= 1:
            P.wait("scalar", sXP[a_xpst[k - 1][0]], a_xpst[k - 1][1])
        P.emit("scalar", lambda e: e.activation(out=xpev[0:RDT, :], in_=p_q[2][0:64, :], func=AF.Copy))
        P.emit("scalar", lambda e: e.activation(out=xpev[RDT:RDT + NST, :], in_=p_q[3][0:16, :], func=AF.Copy))
        P.emit("scalar", lambda e: e.activation(out=xpev_c[:], in_=p_q[3][32:48, :], func=AF.Copy))
        va_xpev = P.inc("scalar", sA)
        a_xpev_done[k] = va_xpev
        P.wait("sync", sA, va_xpev)
        j_ar = k // ARG
        sxp_ = sXP[j_ar % 2]
        if j_ar >= 2 and k % ARG == 0:
            P.wait("sync", sCC, ar_done[j_ar - 2])
        sxp_.n += 32
        P.emit("sync", lambda e, j=j_ar, q_=k % ARG, s=sxp_: e.dma_start(
            out=xp_part[j][0:80, q_ * TC:(q_ + 1) * TC], in_=xpev[:]).then_inc(s.h, 16))
        P.emit("sync", lambda e, j=j_ar, q_=k % ARG, s=sxp_: e.dma_start(
            out=xp_part[j][80:96, q_ * TC:(q_ + 1) * TC], in_=xpev_c[:]).then_inc(s.h, 16))
        a_xpst[k] = (j_ar % 2, sxp_.n)
        if k % ARG == ARG - 1:
            a_xp_dma[j_ar] = sxp_.n

    def all_reduce_xp(j):
        P.wait("gpsimd", sXP[j % 2], a_xp_dma[j])
        P.wait("gpsimd", sCC, cc_wait[0])
        sCC.n += 1
        P.emit("gpsimd", lambda e, j_=j: e.collective_compute(
            "AllReduce", ALU.add, replica_groups=GROUPS,
            ins=[xp_part[j_].opt()], outs=[xp_red[j_].opt()]).then_inc(sCC.h, 1))
        ar_done[j] = sCC.n
        cc_wait[0] = sCC.n

    def b_side(k):
        sc = k // 4
        j_ar, qa = k // ARG, k % ARG
        slot = xc_sl[k % NSLOT]
        zslot = sz_sl[k % NSLOT]
        c0, c1 = qa * TC, (qa + 1) * TC
        if k == 0:
            for _eng in ("scalar", "vector", "gpsimd"):
                P.wait(_eng, sWE, sWE.n)
                P.wait(_eng, sGBp[0], sGBp[0].n)
                P.wait(_eng, sGBp[1], sGBp[1].n)
        P.wait("scalar", sCC, ar_done[j_ar])
        P.wait("scalar", sV, b_prev_dtr[0])
        for dst, r0, r1 in ((xpd_in, 0, RDT), (xpb_in, RDT, RDT + NST), (xpc_in, RDT + NST, 96)):
            sBC.n += 16
            P.emit("scalar", lambda e, d=dst, a=r0, b_=r1, j=j_ar: e.dma_start(
                out=d[:], in_=xp_red[j][a:b_, c0:c1]).then_inc(sBC.h, 16))
        v_ld = sBC.n
        P.wait("scalar", sT, bsrc_rd[0])
        for dst, r0 in ((bsrcB, RDT), (bsrcC, RDT + NST)):
            for jj in range(4):
                b0 = REPL_BASE[jj]
                sBR.n += 16
                P.emit("scalar", lambda e, d=dst, rr=r0, jj_=jj, b0_=b0, j=j_ar: e.dma_start(
                    out=d[b0_:b0_ + 1, :, :],
                    in_=xp_red[j][rr:rr + NST, c0:c1].rearrange(
                        "(g jj) t -> jj g t", jj=4)[jj_:jj_ + 1]).then_inc(sBR.h, 16))
        v_bsrc = sBR.n
        first_stat = [True]
        for src_t, rstd_t, nrm in ((xpd_in, rstdd, RDT), (xpb_in, rstdb, NST), (xpc_in, rstdc, NST)):
            P.wait("scalar", sBC, v_ld)
            P.wait("scalar", sT, sT.n)
            P.emit("scalar", lambda e, s=src_t, n=nrm: e.activation(
                out=dtrms[0:n, :], in_=s[:], func=AF.Square))
            va = P.inc("scalar", sA)
            P.wait("tensor", sA, va)
            if first_stat[0]:
                first_stat[0] = False
                P.wait("tensor", sA, op_ev.get(k - 1, 0))
            P.emit("tensor", lambda e, n=nrm: e.matmul(
                p_q[6][0:1, :], ones_sb[0:n, 0:1], dtrms[0:n, :], start=True, stop=True))
            vt = P.inc("tensor", sT)
            P.wait("scalar", sT, vt)
            P.emit("scalar", lambda e, d=rstd_t, n=nrm: e.activation(
                out=d[:], in_=p_q[6][0:1, :], func=AF.Ln, scale=1.0 / n, bias=cst_sb[0:1, 1:2]))
            P.inc("scalar", sAS); P.wait("scalar", sAS, sAS.n)
            P.emit("scalar", lambda e, d=rstd_t: e.activation(
                out=d[:], in_=d[:], func=AF.Exp, scale=-0.5))
            P.inc("scalar", sA)
        va_rs = sA.n
        P.wait("tensor", sA, va_rs)
        P.wait("tensor", sV, b_done_v.get(k - 1, 0))
        P.emit("tensor", lambda e: e.matmul(p_q[5][0:RDT, :], ones_sb[0:1, 0:RDT], rstdd[:], start=True, stop=True))
        vt2 = P.inc("tensor", sT)
        P.wait("vector", sT, vt2)
        P.emit("vector", lambda e: e.scalar_tensor_tensor(
            out=dtrms[:], in0=xpd_in[:], scalar=dtlnw_sb[:], in1=p_q[5][0:RDT, :],
            op0=ALU.mult, op1=ALU.mult))
        vv_dtr = P.inc("vector", sV)
        b_prev_dtr[0] = vv_dtr
        P.wait("tensor", sV, vv_dtr)
        va_es = {}
        for m in range(4):
            dtb_bank = p_q[7 - (m % 2)]
            if m >= 2:
                P.wait("tensor", sA, va_es[m - 2])
            P.emit("tensor", lambda e, m_=m, p=dtb_bank: e.matmul(
                p[:], dtw_sb[:, m_ * 128:(m_ + 1) * 128], dtrms[:], start=True, stop=True))
            vt3 = P.inc("tensor", sT)
            P.wait("scalar", sT, vt3)
            P.emit("scalar", lambda e, m_=m, p=dtb_bank: e.activation(
                out=u_sb[:], in_=p[:], func=AF.Exp, bias=dtb_sb[:, m_:m_ + 1]))
            va_es[m] = P.inc("scalar", sA)
            P.emit("scalar", lambda e, m_=m: e.activation(
                out=dt_sb[:, m_, :], in_=u_sb[:], func=AF.Ln, bias=1.0))
        va_dt = P.inc("scalar", sA)
        P.wait("tensor", sA, va_dt)
        P.wait("tensor", sV, sV.n)
        P.emit("tensor", lambda e: e.matmul(p_q[5][:], ones_sb[0:1, 0:128], rstdb[:], start=True, stop=True))
        vt4 = P.inc("tensor", sT)
        P.wait("vector", sA, va_dt)
        P.wait("vector", sT, vt4)
        P.wait("vector", sA, a_act_done[k])
        for a in range(4):
            P.emit("vector", lambda e, a_=a, s=slot: e.tensor_mul(
                out=u_sb[:], in0=dt_sb[:, a_, :], in1=s[:, a_, 3:3 + TC]))
            P.emit("vector", lambda e, a_=a: e.tensor_mul(
                out=up_sb[:, a_, :], in0=u_sb[:], in1=p_q[5][:]))
        vv_up = P.inc("vector", sV)
        P.wait("tensor", sV, vv_up)
        P.emit("tensor", lambda e: e.matmul(p_q[5][:], ones_sb[0:1, 0:128], rstdc[:], start=True, stop=True))
        vt_c = P.inc("tensor", sT)
        P.wait("scalar", sT, vt_c)
        P.emit("scalar", lambda e: e.activation(out=rcrep_sb[:], in_=p_q[5][:], func=AF.Copy))
        va_rc = P.inc("scalar", sA)

        vB = {}
        vbmul_last = {}

        def repl_C(g):
            fg = k * 4 + g
            if fg >= 2:
                P.wait("scalar", sP, gmul_last.get(fg - 2, 0))
            for jj in range(4):
                bank = 6 + (jj % 2)
                P.wait("tensor", sA, bank_ev[bank])
                P.emit("tensor", lambda e, g_=g, jj_=jj, b_=bank: repl_mm(e, bsrcC, g_, jj_, b_))
                vt_r = P.inc("tensor", sT)
                P.wait("scalar", sT, vt_r)
                P.emit("scalar", lambda e, g_=g, jj_=jj, b_=bank: e.activation(
                    out=crep_sl[g_ % 2][:, jj_, :], in_=p_q[b_][:], func=AF.Copy))
                bank_ev[bank] = P.inc("scalar", sA)
            crep_ready[fg] = bank_ev[7]

        def repl_B(g):
            if g >= 1:
                P.wait("tensor", sV, vbmul_last[g - 1])
            else:
                P.wait("tensor", sV, b_done_v.get(k - 1, 0))
                P.wait("tensor", sBR, v_bsrc)
                P.wait("tensor", sA, va_rc)
                P.wait("tensor", sA, a_xpev_done[(k // 2) * 2 + 1])
            for jj in range(4):
                P.emit("tensor", lambda e, g_=g, jj_=jj: repl_mm(e, bsrcB, g_, jj_, 2 + jj_))
            vB[g] = P.inc("tensor", sT)

        def emit_reduce(g, a):
            K = k * 16 + g * 4 + a
            P.wait("vector", sP, scanK_vp2[K])
            if g == 0:
                P.emit("vector", lambda e, a_=a: e.tensor_add(
                    out=yacc4[:, a_, :], in0=G_b[:, 0, :], in1=G_b[:, 2, :]))
            else:
                P.emit("vector", lambda e, a_=a: e.tensor_add(
                    out=yacc4[:, a_, :], in0=yacc4[:, a_, :], in1=G_b[:, 0, :]))
                P.emit("vector", lambda e, a_=a: e.tensor_add(
                    out=yacc4[:, a_, :], in0=yacc4[:, a_, :], in1=G_b[:, 2, :]))
            scanK_vy[K] = P.inc("vector", sV)

        P.wait("tensor", sBR, v_bsrc)
        P.wait("tensor", sA, va_dt)
        repl_C(0)
        repl_B(0)
        pend = [None]
        for g in range(4):
            for a in range(4):
                K = k * 16 + g * 4 + a
                dAb = dA_b
                P.wait("scalar", sV, scanK_h1.get(K - 1, 0))
                if K == 0:
                    P.wait("scalar", sV, w_small_v)
                for j in range(2):
                    P.emit("scalar", lambda e, a_=a, g_=g, j_=j, d=dAb: e.activation(
                        out=d[:, j_, :], in_=dt_sb[:, a_, :], func=AF.Exp,
                        scale=A_sb[:, a_, g_ * 4 + j_:g_ * 4 + j_ + 1]))
                P.wait("scalar", sV, scanK_vv.get(K - 1, 0))
                for j in range(2, 4):
                    P.emit("scalar", lambda e, a_=a, g_=g, j_=j, d=dAb: e.activation(
                        out=d[:, j_, :], in_=dt_sb[:, a_, :], func=AF.Exp,
                        scale=A_sb[:, a_, g_ * 4 + j_:g_ * 4 + j_ + 1]))
                va_x = P.inc("scalar", sA)
                P.wait("vector", sT, vB[g])
                for j in range(4):
                    P.emit("vector", lambda e, j_=j, g_=g, a_=a: e.scalar_tensor_tensor(
                        out=bb_b[:, j_, :], in0=p_q[2 + j_][:],
                        scalar=clnw_eff[:, g_ * 4 + j_:g_ * 4 + j_ + 1],
                        in1=up_sb[:, a_, :], op0=ALU.mult, op1=ALU.mult))
                vbm = P.inc("vector", sV)
                if a == 3:
                    vbmul_last[g] = vbm
                Hb = H_sl[K % 2]
                P.wait("vector", sA, va_x)
                P.wait("vector", sP, scanK_vp.get(K - 2, 0))
                for j in range(4):
                    P.emit("vector", lambda e, a_=a, g_=g, j_=j, d=dAb, h=Hb: e.tensor_tensor_scan(
                        out=h[:, j_, :], data0=d[:, j_, :], data1=bb_b[:, j_, :],
                        initial=state[:, a_, g_ * 4 + j_:g_ * 4 + j_ + 1],
                        op0=ALU.mult, op1=ALU.add))
                    if j == 1:
                        scanK_h1[K] = P.inc("vector", sV)
                P.inc("vector", sVS); P.wait("vector", sVS, sVS.n)
                P.emit("vector", lambda e, a_=a, g_=g, h=Hb: e.tensor_copy(
                    out=state[:, a_, g_ * 4:g_ * 4 + 4], in_=h[:, :, TC - 1]))
                scanK_vv[K] = P.inc("vector", sV)
                if pend[0] is not None:
                    emit_reduce(*pend[0])
                pend[0] = (g, a)
                P.wait("gpsimd", sV, scanK_vv[K])
                if a == 0:
                    P.wait("gpsimd", sA, crep_ready[k * 4 + g])
                P.wait("gpsimd", sV, scanK_vy.get(K - 1, 0))
                P.emit("gpsimd", lambda e, g_=g, h=Hb: e.tensor_mul(
                    out=G_b[:], in0=crep_sl[g_ % 2][:], in1=h[:]))
                scanK_vp[K] = P.inc("gpsimd", sP)
                P.emit("gpsimd", lambda e: e.tensor_add(
                    out=G_b[:, 0, :], in0=G_b[:, 0, :], in1=G_b[:, 1, :]))
                P.emit("gpsimd", lambda e: e.tensor_add(
                    out=G_b[:, 2, :], in0=G_b[:, 2, :], in1=G_b[:, 3, :]))
                scanK_vp2[K] = P.inc("gpsimd", sP)
                gmul_last[k * 4 + g] = scanK_vp2[K]
                if a == 3 and g < 3:
                    repl_C(g + 1)
                    repl_B(g + 1)
        emit_reduce(*pend[0])
        bsrc_rd[0] = vB[3]
        P.wait("vector", sA, va_rc)
        for a in range(4):
            P.emit("vector", lambda e, a_=a: e.tensor_mul(
                out=t1_sb[:], in0=yacc4[:, a_, :], in1=rcrep_sb[:]))
            P.emit("vector", lambda e, a_=a, zs=zslot: e.tensor_copy(
                out=yred[:], in_=zs[:, a_, :]))
            P.emit("vector", lambda e, a_=a, s=slot: e.scalar_tensor_tensor(
                out=t1_sb[:], in0=s[:, a_, 3:3 + TC], scalar=dskip_sb[:, a_:a_ + 1],
                in1=t1_sb[:], op0=ALU.mult, op1=ALU.add))
            P.emit("vector", lambda e, a_=a: e.tensor_mul(
                out=yg_bf[:, a_, :], in0=t1_sb[:], in1=yred[:]))
            P.inc("vector", sV)
        b_done_v[k] = sV.n
        last_b_done[0] = b_done_v[k]
        P.wait("tensor", sV, b_done_v[k])
        if k >= 1:
            P.wait("tensor", sA, op_ev.get(k - 1, 0))
        P.wait("tensor", sA, bank_ev[6])
        P.wait("tensor", sA, bank_ev[7])
        op_ev_h = {}
        sc_par = sc % 2
        if k % 4 == 0 and sc >= 2:
            P.wait("sync", sCC, rs_done[sc - 2])
        for tt in range(4):
            P.wait("scalar", sOP[op_st_last[0]], op_st_last[1])
            for h in range(2):
                po = p_q[6 + h]
                if tt >= 1:
                    P.wait("tensor", sA, op_ev_h[(tt - 1, h)])
                for a in range(4):
                    P.emit("tensor", lambda e, tt_=tt, h_=h, a_=a, p=po: e.matmul(
                        p[:], yg_bf[:, a_, tt_ * 128:(tt_ + 1) * 128],
                        wout_bf[:, a_, h_ * 512:(h_ + 1) * 512],
                        start=(a_ == 0), stop=(a_ == 3)))
                vt_o = P.inc("tensor", sT)
                P.wait("scalar", sT, vt_o)
                P.emit("scalar", lambda e, h_=h, p=po: e.activation(
                    out=oo_sb[:, h_ * 512:(h_ + 1) * 512], in_=p[:], func=AF.Copy))
                op_ev_h[(tt, h)] = P.inc("scalar", sA)
            va_o = op_ev_h[(tt, 1)]
            P.wait("sync", sA, va_o)
            sOP[sc_par].n += 16
            P.emit("sync", lambda e, k_=k, tt_=tt, s=sOP[sc_par]: e.dma_start(
                out=op_part[k_ * TC + tt_ * 128:k_ * TC + (tt_ + 1) * 128, :],
                in_=oo_sb[:]).then_inc(s.h, 16))
            op_st_last[0] = sc_par
            op_st_last[1] = sOP[sc_par].n
        op_ev[k] = sA.n
        bank_ev[6] = op_ev[k]
        bank_ev[7] = op_ev[k]
        op_dma[k // 4] = sOP[sc_par].n

    def rs1(sc):
        P.wait("gpsimd", sOP[sc % 2], op_dma[sc])
        P.wait("gpsimd", sCC, cc_wait[0])
        sCC.n += 1
        P.emit("gpsimd", lambda e, s=sc: e.collective_compute(
            "ReduceScatter", ALU.add, replica_groups=GROUPS,
            ins=[op_part[s * 4 * TC:(s + 1) * 4 * TC, :].opt()],
            outs=[rs1_out[s * 512:(s + 1) * 512, :].opt()]).then_inc(sCC.h, 1))
        cc_wait[0] = sCC.n
        return sCC.n

    rs_done = {}
    for jp in range(NCH // 2):
        k0 = 2 * jp
        a_side(k0)
        all_reduce_xp(k0)
        a_side(k0 + 1)
        all_reduce_xp(k0 + 1)
        b_side(k0)
        b_side(k0 + 1)
        if (k0 + 1) % 4 == 3:
            rs_done[(k0 + 1) // 4] = rs1((k0 + 1) // 4)


    esS.close()
    es1.close()
    for _e in ENGS:
        P.wait(_e, sCC, rs_done[NSC - 1])
        P.wait(_e, sT, sT.n)
        P.wait(_e, sA, sA.n)
        P.wait(_e, sV, sV.n)
        P.wait(_e, sP, sP.n)
    va_sgl = {}
    vv_ff = {}
    t_d = {}
    vv_dev = 0
    es2a = ExitStack()
    stream_fm = es2a.enter_context(nc.sbuf_tensor("stream_fm", [128, 8, TQ], F32))
    identf_sb = es2a.enter_context(nc.sbuf_tensor("identf_sb", [128, 128], F32))
    es2 = ExitStack()
    E2 = lambda t: es2.enter_context(t)
    x_tm = E2(nc.sbuf_tensor("x_tm", [128, 4, D], F32))
    mo_tm = E2(nc.sbuf_tensor("mo_tm", [128, 4, D], F32))
    ln1b = E2(nc.sbuf_tensor("ln1b", [128, D], F32))
    sq2 = E2(nc.sbuf_tensor("sq2", [128, D], F32))
    st2 = E2(nc.sbuf_tensor("st2", [128, 8], F32))
    pmw_rep = E2(nc.sbuf_tensor("pmw_rep", [128, D], F32))
    hn_tm = E2(nc.sbuf_tensor("hn_tm", [128, 4, D], BF16))
    hn_fm = E2(nc.sbuf_tensor("hn_fm", [128, 8, TC], BF16))
    ff_sb = E2(nc.sbuf_tensor("ff_sb", [128, 32, TC], BF16))
    gw_t = [E2(nc.sbuf_tensor(f"gw{i}", [128, 8, 512], BF16)) for i in range(2)]
    uw_t = [E2(nc.sbuf_tensor(f"uw{i}", [128, 8, 512], BF16)) for i in range(2)]
    dw_t = [E2(nc.sbuf_tensor(f"dw{i}", [128, TC], BF16)) for i in range(2)]
    sgl = E2(nc.sbuf_tensor("sgl", [128, TC], F32))
    p2_tr = [E2(nc.psum_tensor(f"p2_tr{i}", [128, 128], BF16)) for i in range(2)]
    p2_g = E2(nc.psum_tensor("p2_g", [128, TC], F32))
    p2_u = E2(nc.psum_tensor("p2_u", [128, TC], F32))
    p2_d = [E2(nc.psum_tensor(f"p2_d{i}", [128, TC], F32)) for i in range(4)]

    sW.n += 48
    P.emit("sync", lambda e: e.dma_start(out=identf_sb[:], in_=ident_f).then_inc(sW.h, 16))
    P.emit("sync", lambda e: e.dma_start(out=pmw_rep[:], in_=pmw.broadcast_to([128, D])).then_inc(sW.h, 16))
    P.emit("sync", lambda e: e.dma_start(out=ident_sb[:], in_=ident_bf).then_inc(sW.h, 16))
    v_w2 = sW.n

    def phase2(sc):
        nonlocal vv_dev
        L = sc * 512
        ev_tr = p2_evtr
        va_hn = {}
        for tt in range(4):
            sld = sRS if tt % 2 == 0 else sRS2
            if tt == 0:
                P.wait("sync", sA, p2_trans_prev[0])
            sld.n += 32
            P.emit("sync", lambda e, t_=tt, s=sld: e.dma_start(
                out=x_tm[:, t_, :], in_=x_q[L + t_ * 128:L + (t_ + 1) * 128, :]).then_inc(s.h, 16))
            P.emit("sync", lambda e, t_=tt, s=sld: e.dma_start(
                out=mo_tm[:, t_, :], in_=rs1_out[L + t_ * 128:L + (t_ + 1) * 128, :]).then_inc(s.h, 16))
            v_ld = sld.n
            P.wait("vector", sld, v_ld)
            if tt == 0:
                P.wait("vector", sA, p2_trans_prev[0])
            P.emit("vector", lambda e, t_=tt: e.reduce_sum(out=st2[:, 0:1], in_=x_tm[:, t_, :], axis=AX.X))
            P.inc("vector", sVS); P.wait("vector", sVS, sVS.n)
            P.wait("scalar", sld, v_ld)
            P.wait("scalar", sV, p2_hn_prev[0])
            P.emit("scalar", lambda e, t_=tt: e.activation(
                out=sq2[:], in_=x_tm[:, t_, :], func=AF.Square, accum_out=st2[:, 1:2]))
            va_sq = P.inc("scalar", sA)
            P.wait("vector", sA, va_sq)
            P.emit("vector", lambda e: e.tensor_scalar_mul(out=st2[:, 2:3], in0=st2[:, 0:1], scalar1=1.0 / D))
            P.emit("vector", lambda e: e.tensor_scalar_mul(out=st2[:, 3:4], in0=st2[:, 1:2], scalar1=1.0 / D))
            P.inc("vector", sVS); P.wait("vector", sVS, sVS.n)
            P.emit("vector", lambda e: e.tensor_mul(out=st2[:, 4:5], in0=st2[:, 2:3], in1=st2[:, 2:3]))
            P.inc("vector", sVS); P.wait("vector", sVS, sVS.n)
            P.emit("vector", lambda e: e.tensor_sub(out=st2[:, 3:4], in0=st2[:, 3:4], in1=st2[:, 4:5]))
            vv_st = P.inc("vector", sV)
            P.wait("scalar", sV, vv_st)
            P.emit("scalar", lambda e: e.activation(out=st2[:, 5:6], in_=st2[:, 3:4], func=AF.Ln, bias=cst_sb[:, 0:1]))
            P.inc("scalar", sAS); P.wait("scalar", sAS, sAS.n)
            P.emit("scalar", lambda e: e.activation(out=st2[:, 5:6], in_=st2[:, 5:6], func=AF.Exp, scale=-0.5))
            va_r = P.inc("scalar", sA)
            P.wait("scalar", sA, va_r)
            P.wait("vector", sA, va_r)
            P.emit("vector", lambda e: e.tensor_mul(out=st2[:, 6:7], in0=st2[:, 2:3], in1=st2[:, 5:6]))
            P.inc("vector", sVS); P.wait("vector", sVS, sVS.n)
            P.emit("vector", lambda e: e.tensor_scalar_mul(out=st2[:, 6:7], in0=st2[:, 6:7], scalar1=-1.0))
            vv_nm = P.inc("vector", sV)
            P.wait("scalar", sV, vv_nm)
            P.emit("scalar", lambda e, t_=tt: e.activation(
                out=ln1b[:], in_=x_tm[:, t_, :], func=AF.Identity, scale=st2[:, 5:6], bias=st2[:, 6:7]))
            va_ln1 = P.inc("scalar", sA)
            P.wait("vector", sA, va_ln1)
            P.emit("vector", lambda e, t_=tt: e.tensor_add(out=mo_tm[:, t_, :], in0=ln1b[:], in1=mo_tm[:, t_, :]))
            P.inc("vector", sVS); P.wait("vector", sVS, sVS.n)
            P.emit("vector", lambda e, t_=tt: e.tensor_add(out=x_tm[:, t_, :], in0=x_tm[:, t_, :], in1=mo_tm[:, t_, :]))
            vv_s1 = P.inc("vector", sV)
            P.wait("scalar", sV, vv_s1)
            P.emit("scalar", lambda e, t_=tt: e.activation(
                out=sq2[:], in_=mo_tm[:, t_, :], func=AF.Square, accum_out=st2[:, 7:8]))
            P.inc("scalar", sAS); P.wait("scalar", sAS, sAS.n)
            P.emit("scalar", lambda e: e.activation(out=st2[:, 7:8], in_=st2[:, 7:8], func=AF.Ln, scale=1.0 / D, bias=cst_sb[:, 1:2]))
            P.inc("scalar", sAS); P.wait("scalar", sAS, sAS.n)
            P.emit("scalar", lambda e: e.activation(out=st2[:, 7:8], in_=st2[:, 7:8], func=AF.Exp, scale=-0.5))
            va_r3 = P.inc("scalar", sA)
            P.wait("vector", sA, va_r3)
            P.emit("vector", lambda e, t_=tt: e.scalar_tensor_tensor(
                out=hn_tm[:, t_, :], in0=mo_tm[:, t_, :], scalar=st2[:, 7:8], in1=pmw_rep[:],
                op0=ALU.mult, op1=ALU.mult))
            va_hn[tt] = P.inc("vector", sV)
            p2_hn_prev[0] = va_hn[tt]
        P.wait("tensor", sW, v_w2)
        P.wait("tensor", sV, va_hn[3])
        P.wait("tensor", sT, t_gu.get((sc - 1, 31), 0))
        slot = 0
        first_hn = [True]
        for tt in range(4):
            for kk in range(8):
                P.wait("tensor", sA, ev_tr.get(slot, 0))
                P.emit("tensor", lambda e, t_=tt, k_=kk, sl=slot: e.transpose(
                    p2_tr[sl][:], hn_tm[:, t_, k_ * 128:(k_ + 1) * 128], ident_sb[:]))
                vt = P.inc("tensor", sT)
                P.wait("scalar", sT, vt)
                if first_hn[0]:
                    first_hn[0] = False
                    P.wait("scalar", sT, t_gu.get((sc - 1, 31), 0))
                P.emit("scalar", lambda e, t_=tt, k_=kk, sl=slot: e.activation(
                    out=hn_fm[:, k_, t_ * 128:(t_ + 1) * 128], in_=p2_tr[sl][:], func=AF.Copy))
                ev_tr[slot] = P.inc("scalar", sA)
                slot = (slot + 1) % 2
        slotf = 4
        P.wait("tensor", sV, vv_dev)
        for tt in range(4):
            for kk in range(8):
                P.wait("tensor", sA, ev_tr.get(slotf, 0))
                P.emit("tensor", lambda e, t_=tt, k_=kk, sl=slotf - 4: e.transpose(
                    p2_d[sl][:, 0:128], x_tm[:, t_, k_ * 128:(k_ + 1) * 128], identf_sb[:]))
                vt = P.inc("tensor", sT)
                P.wait("scalar", sT, vt)
                P.emit("scalar", lambda e, t_=tt, k_=kk, sl=slotf - 4: e.activation(
                    out=stream_fm[:, k_, L + t_ * 128:L + (t_ + 1) * 128], in_=p2_d[sl][:, 0:128], func=AF.Copy))
                ev_tr[slotf] = P.inc("scalar", sA)
                slotf = 4 + ((slotf - 4 + 1) % 2)
        va_trans = sA.n
        p2_trans_prev[0] = va_trans
        for f in range(32):
            f4, fo = f // 4, (f % 4) * 128
            pf = (f // 4) % 2
            sgw = sG2 if pf == 0 else sG3
            if f % 4 == 0:
                if f4 >= 2:
                    P.wait("sync", sT, t_gu[(sc, (f4 - 2) * 4 + 3)])
                elif sc >= 1:
                    P.wait("sync", sT, t_gu[(sc - 1, (6 + f4) * 4 + 3)])
                P.wait("sync", sGBp[0], sGBp[0].n)
                P.wait("sync", sGBp[1], sGBp[1].n)
                sgw.n += 32
                P.emit("sync", lambda e, f_=f4, p=pf, s=sgw: e.dma_start(
                    out=gw_t[p][:], in_=gate_bf[:, f_ * 512:(f_ + 1) * 512].rearrange("(kk p) m -> p kk m", p=128)).then_inc(s.h, 16))
                P.emit("sync", lambda e, f_=f4, p=pf, s=sgw: e.dma_start(
                    out=uw_t[p][:], in_=up_bf[:, f_ * 512:(f_ + 1) * 512].rearrange("(kk p) m -> p kk m", p=128)).then_inc(s.h, 16))
                gld[(sc, f4)] = sgw.n
            P.wait("tensor", sgw, gld[(sc, f4)])
            P.wait("tensor", sA, va_trans)
            if f >= 1:
                P.wait("tensor", sV, vv_ff[f - 1])
                P.wait("tensor", sA, va_sgl[f - 1])
            elif sc >= 1:
                P.wait("tensor", sV, vv_ff.get(31, 0))
                P.wait("tensor", sA, va_sgl.get(31, 0))
            for kk in range(8):
                P.emit("tensor", lambda e, k_=kk, p=pf, o=fo: e.matmul(
                    p2_g[:], gw_t[p][:, k_, o:o + 128], hn_fm[:, k_, :], start=(k_ == 0), stop=(k_ == 7)))
            for kk in range(8):
                P.emit("tensor", lambda e, k_=kk, p=pf, o=fo: e.matmul(
                    p2_u[:], uw_t[p][:, k_, o:o + 128], hn_fm[:, k_, :], start=(k_ == 0), stop=(k_ == 7)))
            vt_gu = P.inc("tensor", sT)
            t_gu[(sc, f)] = vt_gu
            P.wait("scalar", sT, vt_gu)
            P.emit("scalar", lambda e: e.activation(out=sgl[:], in_=p2_g[:], func=AF.Silu))
            va_sgl[f] = P.inc("scalar", sA)
            P.wait("vector", sA, va_sgl[f])
            if f == 0:
                P.wait("vector", sT, t_d.get((1, 31), 0))
            P.emit("vector", lambda e, f_=f: e.tensor_mul(out=ff_sb[:, f_, :], in0=sgl[:], in1=p2_u[:]))
            vv_ff[f] = P.inc("vector", sV)
        ev_d = sA.n
        for ch in range(2):
            for f in range(32):
                pf = f % 2
                sgw = sG2 if pf == 0 else sG3
                if not (ch == 0 and f < 2):
                    P.wait("sync", sT, t_d.get((ch, f - 2), t_d.get((ch - 1, 30 + f), sT.n if False else t_gu[31] if f < 2 else t_d[(ch, f - 2)])) ) if False else None
                if ch > 0 or f >= 2:
                    prev = (ch, f - 2) if f >= 2 else (ch - 1, 30 + f)
                    P.wait("sync", sT, t_d[prev])
                elif sc >= 1:
                    P.wait("sync", sT, t_d.get((1, 30 + f), 0))
                sgw.n += 16
                P.emit("sync", lambda e, f_=f, c_=ch, p=pf, s=sgw: e.dma_start(
                    out=dw_t[p][:], in_=down_bf[f_ * 128:(f_ + 1) * 128, c_ * 512:(c_ + 1) * 512]).then_inc(s.h, 16))
                P.wait("tensor", sgw, sgw.n)
                P.wait("tensor", sV, vv_ff[31])
                if f == 0:
                    P.wait("tensor", sV, vv_dev)
                for ct in range(4):
                    P.emit("tensor", lambda e, c2=ct, p=pf, f_=f: e.matmul(
                        p2_d[c2][:], dw_t[p][:, c2 * 128:(c2 + 1) * 128], ff_sb[:, f_, :],
                        start=(f_ == 0), stop=(f_ == 31)))
                t_d[(ch, f)] = P.inc("tensor", sT)
            P.wait("vector", sT, t_d[(ch, 31)])
            for ct in range(4):
                P.emit("vector", lambda e, c_=ch, c2=ct: e.tensor_add(
                    out=stream_fm[:, c_ * 4 + c2, L:L + 512],
                    in0=stream_fm[:, c_ * 4 + c2, L:L + 512], in1=p2_d[c2][:]))
            vv_dev = P.inc("vector", sV)
        return sA.n, sV.n

    p2_evtr = {}
    p2_trans_prev = [0]
    p2_hn_prev = [0]
    gld = {}
    t_gu = {}
    p2_done = []
    for sc in range(NSC):
        p2_done.append(phase2(sc))


    def ffn_phase(which, dbg_out=None):
        W_ = which
        """which in (0, 1): first or second application."""
        es3 = ExitStack()
        E3 = lambda t: es3.enter_context(t)
        m_sb = E3(nc.sbuf_tensor(f"f_m{W_}", [1, TQ], F32))
        r_std = E3(nc.sbuf_tensor(f"f_rstd{W_}", [1, TQ], F32))
        mr_sb = E3(nc.sbuf_tensor(f"f_mr{W_}", [1, 512], F32))
        stat_ev = E3(nc.sbuf_tensor(f"f_stat{W_}", [1, 512], F32))
        ln2_fm = E3(nc.sbuf_tensor(f"f_ln2{W_}", [128, 8, 512], F32))
        d_fm = E3(nc.sbuf_tensor(f"f_d{W_}", [128, 8, 512], F32))
        xr_bf = E3(nc.sbuf_tensor(f"f_xr{W_}", [128, 8, 512], BF16))
        xk_bf = E3(nc.sbuf_tensor(f"f_xk{W_}", [128, 8, 512], BF16))
        r_sb = E3(nc.sbuf_tensor(f"f_r{W_}", [128, 8, 512], BF16))
        k2_bf = E3(nc.sbuf_tensor(f"f_k2{W_}", [128, 8, 512], BF16))
        kr_sb = E3(nc.sbuf_tensor(f"f_kr{W_}", [128, 512], F32))
        mur_sb = E3(nc.sbuf_tensor(f"f_mur{W_}", [128, 8], F32))
        muk_sb = E3(nc.sbuf_tensor(f"f_muk{W_}", [128, 8], F32))
        sel_sb = E3(nc.sbuf_tensor(f"f_sel{W_}", [16, 4], BF16))
        bndrow = E3(nc.sbuf_tensor(f"f_bndrow{W_}", [8, 128], BF16))
        bcol_fm = E3(nc.sbuf_tensor(f"f_bcol{W_}", [128, 8], BF16))
        rep_ev = E3(nc.sbuf_tensor(f"f_repev{W_}", [128, 2], F32))
        bnd_sb2 = E3(nc.sbuf_tensor(f"f_bnd2{W_}", [16, D], BF16))
        prev_sb = E3(nc.sbuf_tensor(f"f_prev{W_}", [128, D], BF16))
        prevf_sb = E3(nc.sbuf_tensor(f"f_prevf{W_}", [128, 8, 4], F32))
        t2_s = E3(nc.sbuf_tensor(f"f_t2{W_}", [128, 512], F32))
        wr_a = E3(nc.sbuf_tensor(f"f_wr{W_}", [128, 8, D], BF16))
        wk_a = E3(nc.sbuf_tensor(f"f_wk{W_}", [128, 8, D], BF16))
        wv_a = E3(nc.sbuf_tensor(f"f_wv{W_}", [128, 8, D], BF16))
        p3_s = E3(nc.psum_tensor(f"p3_s{W_}", [1, 512], F32))
        p3_rep = [E3(nc.psum_tensor(f"p3_rep{W_}_{i}", [128, 512], F32)) for i in range(2)]
        p3_r = E3(nc.psum_tensor(f"p3_r{W_}", [128, 512], F32))
        p3_k = E3(nc.psum_tensor(f"p3_k{W_}", [128, 512], F32))
        p3_v = E3(nc.psum_tensor(f"p3_v{W_}", [128, 512], F32))
        p3_t = E3(nc.psum_tensor(f"p3_t{W_}", [128, 128], BF16))

        sW.n += 48
        P.emit("sync", lambda e: e.dma_start(out=mur_sb[:], in_=mu_r).then_inc(sW.h, 16))
        P.emit("sync", lambda e: e.dma_start(out=muk_sb[:], in_=mu_k).then_inc(sW.h, 16))
        P.emit("sync", lambda e: e.dma_start(out=sel_sb[:], in_=selp).then_inc(sW.h, 16))
        v_l3 = sW.n
        P.wait("sync", sGBp[0], sGBp[0].n)
        P.wait("sync", sGBp[1], sGBp[1].n)
        sG2.n += 48
        for wsrc, wdst in ((wrT_bf, wr_a), (wkT_bf, wk_a), (wvT_bf, wv_a)):
            P.emit("sync", lambda e, s=wsrc, d=wdst: e.dma_start(
                out=d[:], in_=s.rearrange("(kk p) c -> p kk c", p=128)).then_inc(sG2.h, 16))
        w3_ld = [sG2.n]
        P.wait("vector", sW, v_l3)
        P.emit("vector", lambda e: e.tensor_scalar(
            out=mur_sb[:], in0=mur_sb[:], scalar1=1.0, scalar2=None, op0=ALU.mult))
        P.inc("vector", sVS); P.wait("vector", sVS, sVS.n)

        bnd_st = 0
        bcol_ev = [0]
        for sc in range(NSC):
            L = sc * 512
            P.wait("scalar", sV, sV.n)
            P.emit("scalar", lambda e, L_=L: e.activation(
                out=d_fm[:], in_=stream_fm[:, :, L_:L_ + 512], func=AF.Square))
            va_sq3 = P.inc("scalar", sA)
            P.wait("tensor", sA, va_sq3)
            P.wait("tensor", sW, v_l3)
            for kk in range(8):
                P.emit("tensor", lambda e, k_=kk, L_=L: e.matmul(
                    p3_s[:], ones_sb[0:128, 0:1], stream_fm[:, k_, L_:L_ + 512],
                    start=(k_ == 0), stop=(k_ == 7)))
            vt_s1 = P.inc("tensor", sT)
            P.wait("scalar", sT, vt_s1)
            P.emit("scalar", lambda e, L_=L: e.activation(out=m_sb[0:1, L_:L_ + 512], in_=p3_s[:], func=AF.Copy))
            va_s1 = P.inc("scalar", sA)
            P.wait("tensor", sA, va_s1)
            for kk in range(8):
                P.emit("tensor", lambda e, k_=kk: e.matmul(
                    p3_s[:], ones_sb[0:128, 0:1], d_fm[:, k_, :],
                    start=(k_ == 0), stop=(k_ == 7)))
            vt_s2 = P.inc("tensor", sT)
            P.wait("scalar", sT, vt_s2)
            P.emit("scalar", lambda e: e.activation(out=stat_ev[0:1, :], in_=p3_s[:], func=AF.Copy))
            va_s2 = P.inc("scalar", sA)
            P.wait("vector", sA, va_s2)
            P.emit("vector", lambda e, L_=L: e.tensor_scalar_mul(
                out=m_sb[0:1, L_:L_ + 512], in0=m_sb[0:1, L_:L_ + 512], scalar1=1.0 / D))
            P.inc("vector", sVS); P.wait("vector", sVS, sVS.n)
            P.emit("vector", lambda e, L_=L: e.tensor_mul(
                out=mr_sb[0:1, :], in0=m_sb[0:1, L_:L_ + 512], in1=m_sb[0:1, L_:L_ + 512]))
            P.inc("vector", sVS); P.wait("vector", sVS, sVS.n)
            P.emit("vector", lambda e: e.tensor_scalar_mul(
                out=stat_ev[0:1, :], in0=stat_ev[0:1, :], scalar1=1.0 / D))
            P.inc("vector", sVS); P.wait("vector", sVS, sVS.n)
            P.emit("vector", lambda e: e.tensor_sub(
                out=stat_ev[0:1, :], in0=stat_ev[0:1, :], in1=mr_sb[0:1, :]))
            vv_var = P.inc("vector", sV)
            P.wait("scalar", sV, vv_var)
            P.emit("scalar", lambda e, L_=L: e.activation(
                out=r_std[0:1, L_:L_ + 512], in_=stat_ev[0:1, :], func=AF.Ln, bias=cst_sb[0:1, 0:1]))
            P.inc("scalar", sAS); P.wait("scalar", sAS, sAS.n)
            P.emit("scalar", lambda e, L_=L: e.activation(
                out=r_std[0:1, L_:L_ + 512], in_=r_std[0:1, L_:L_ + 512], func=AF.Exp, scale=-0.5))
            va_rstd = P.inc("scalar", sA)
            P.wait("tensor", sA, va_rstd)
            P.emit("tensor", lambda e, L_=L: e.matmul(
                p3_rep[0][:, 0:1], ones_sb[0:1, 0:128], r_std[0:1, L_ + 511:L_ + 512], start=True, stop=True))
            P.emit("tensor", lambda e, L_=L: e.matmul(
                p3_rep[1][:, 0:1], ones_sb[0:1, 0:128], m_sb[0:1, L_ + 511:L_ + 512], start=True, stop=True))
            vt_rep = P.inc("tensor", sT)
            P.wait("scalar", sT, vt_rep)
            P.emit("scalar", lambda e: e.activation(out=rep_ev[:, 0:1], in_=p3_rep[0][:, 0:1], func=AF.Copy))
            P.emit("scalar", lambda e: e.activation(out=rep_ev[:, 1:2], in_=p3_rep[1][:, 0:1], func=AF.Copy))
            va_repev = P.inc("scalar", sA)
            P.wait("vector", sA, va_repev)
            P.emit("vector", lambda e: e.tensor_mul(out=rep_ev[:, 1:2], in0=rep_ev[:, 1:2], in1=rep_ev[:, 0:1]))
            P.inc("vector", sVS); P.wait("vector", sVS, sVS.n)
            P.emit("vector", lambda e, L_=L: e.scalar_tensor_tensor(
                out=bcol_fm[:], in0=stream_fm[:, :, L_ + 511], scalar=rep_ev[:, 0:1],
                in1=rep_ev[:, 1:2].broadcast_to([128, 8]), op0=ALU.mult, op1=ALU.subtract))
            vv_bcol = P.inc("vector", sV)
            P.wait("tensor", sV, vv_bcol)
            P.wait("tensor", sA, bcol_ev[0])
            P.emit("tensor", lambda e: e.transpose(p3_t[0:8, 0:128], bcol_fm[:], ident_sb[:]))
            vt_b = P.inc("tensor", sT)
            P.wait("scalar", sT, vt_b)
            P.emit("scalar", lambda e: e.activation(out=bndrow[:], in_=p3_t[0:8, 0:128], func=AF.Copy))
            va_bnd = P.inc("scalar", sA)
            bcol_ev[0] = va_bnd
            P.wait("sync", sA, va_bnd)
            sDBG.n += 16
            P.emit("sync", lambda e, s=sc, w=which: e.dma_start(
                out=bnd_my[w][s, :].rearrange("(a c) -> a c", a=8), in_=bndrow[:]).then_inc(sDBG.h, 16))
        bnd_st = sDBG.n
        P.wait("gpsimd", sDBG, bnd_st)
        sCC.n += 1
        P.emit("gpsimd", lambda e, w=which: e.collective_compute(
            "AllGather", ALU.bypass, replica_groups=GROUPS,
            ins=[bnd_my[w].opt()], outs=[bnd_all[w].opt()]).then_inc(sCC.h, 1))
        v_agb = sCC.n
        P.wait("gpsimd", sCC, v_agb)
        P.wait("sync", sCC, v_agb)
        sW.n += 16
        P.emit("sync", lambda e, w=which: e.dma_start(out=bnd_sb2[:], in_=bnd_all[w]).then_inc(sW.h, 16))
        v_bl = sW.n
        P.wait("tensor", sW, v_bl)
        for hh in range(2):
            P.emit("tensor", lambda e, h_=hh: e.matmul(
                p3_rep[h_][0:4, :], sel_sb[:], bnd_sb2[:, h_ * 512:(h_ + 1) * 512], start=True, stop=True))
        vt_sel = P.inc("tensor", sT)
        P.wait("scalar", sT, vt_sel)
        P.emit("scalar", lambda e: e.activation(out=prev_sb[0:4, 0:512], in_=p3_rep[0][0:4, :], func=AF.Copy))
        P.emit("scalar", lambda e: e.activation(out=prev_sb[0:4, 512:1024], in_=p3_rep[1][0:4, :], func=AF.Copy))
        va_prev = P.inc("scalar", sA)
        P.wait("tensor", sA, va_prev)
        ev_pt = 0
        for kk in range(8):
            P.wait("tensor", sA, ev_pt)
            P.emit("tensor", lambda e, k_=kk: e.transpose(
                p3_t[:], prev_sb[:, k_ * 128:(k_ + 1) * 128], ident_sb[:]))
            vt_p = P.inc("tensor", sT)
            P.wait("scalar", sT, vt_p)
            P.emit("scalar", lambda e, k_=kk: e.activation(
                out=prevf_sb[:, k_, :], in_=p3_t[:, 0:4], func=AF.Copy))
            ev_pt = P.inc("scalar", sA)
        va_prevf = sA.n

        def ffn_block(sc):
            L = sc * 512
            P.wait("vector", sA, va_prevf)
            P.emit("vector", lambda e, L_=L: e.tensor_mul(
                out=mr_sb[0:1, :], in0=m_sb[0:1, L_:L_ + 512], in1=r_std[0:1, L_:L_ + 512]))
            vv_mr = P.inc("vector", sV)
            P.wait("tensor", sV, vv_mr)
            P.emit("tensor", lambda e, L_=L: e.matmul(
                p3_rep[0][:], ones_sb[0:1, 0:128], r_std[0:1, L_:L_ + 512], start=True, stop=True))
            P.emit("tensor", lambda e: e.matmul(
                p3_rep[1][:], ones_sb[0:1, 0:128], mr_sb[0:1, :], start=True, stop=True))
            vt_rp = P.inc("tensor", sT)
            P.wait("vector", sT, vt_rp)
            for a in range(8):
                P.emit("vector", lambda e, a_=a, L_=L: e.tensor_mul(
                    out=ln2_fm[:, a_, :], in0=stream_fm[:, a_, L_:L_ + 512], in1=p3_rep[0][:]))
            P.inc("vector", sVS); P.wait("vector", sVS, sVS.n)
            for a in range(8):
                P.emit("vector", lambda e, a_=a: e.tensor_sub(
                    out=ln2_fm[:, a_, :], in0=ln2_fm[:, a_, :], in1=p3_rep[1][:]))
            P.inc("vector", sVS); P.wait("vector", sVS, sVS.n)
            P.emit("vector", lambda e: e.tensor_sub(
                out=d_fm[:, :, 1:512].rearrange("p a t -> p (a t)") if False else d_fm[:].rearrange("p a t -> p a t")[:, :, 1:512],
                in0=ln2_fm[:].rearrange("p a t -> p a t")[:, :, 1:512],
                in1=ln2_fm[:].rearrange("p a t -> p a t")[:, :, 0:511]))
            P.emit("vector", lambda e, s_=sc: e.tensor_sub(
                out=d_fm[:, :, 0:1], in0=ln2_fm[:, :, 0:1], in1=prevf_sb[:, :, s_:s_ + 1]))
            vv_d = P.inc("vector", sV)
            for a in range(8):
                P.emit("vector", lambda e, a_=a: e.scalar_tensor_tensor(
                    out=xr_bf[:, a_, 1:512], in0=d_fm[:, a_, 1:512], scalar=mur_sb[:, a_:a_ + 1],
                    in1=ln2_fm[:, a_, 0:511], op0=ALU.mult, op1=ALU.add))
                P.emit("vector", lambda e, a_=a, s_=sc: e.scalar_tensor_tensor(
                    out=xr_bf[:, a_, 0:1], in0=d_fm[:, a_, 0:1], scalar=mur_sb[:, a_:a_ + 1],
                    in1=prevf_sb[:, a_, s_:s_ + 1], op0=ALU.mult, op1=ALU.add))
                P.emit("vector", lambda e, a_=a: e.scalar_tensor_tensor(
                    out=xk_bf[:, a_, 1:512], in0=d_fm[:, a_, 1:512], scalar=muk_sb[:, a_:a_ + 1],
                    in1=ln2_fm[:, a_, 0:511], op0=ALU.mult, op1=ALU.add))
                P.emit("vector", lambda e, a_=a, s_=sc: e.scalar_tensor_tensor(
                    out=xk_bf[:, a_, 0:1], in0=d_fm[:, a_, 0:1], scalar=muk_sb[:, a_:a_ + 1],
                    in1=prevf_sb[:, a_, s_:s_ + 1], op0=ALU.mult, op1=ALU.add))
            vv_mix = P.inc("vector", sV)
            t_rk = {}
            for m in range(8):
                P.wait("tensor", sG2, w3_ld[0])
                P.wait("tensor", sV, vv_mix)
                if m >= 1:
                    P.wait("tensor", sA, va_rk.get(m - 1, 0))
                for kk in range(8):
                    P.emit("tensor", lambda e, k_=kk, m_=m: e.matmul(
                        p3_r[:], wr_a[:, k_, m_ * 128:(m_ + 1) * 128], xr_bf[:, k_, :], start=(k_ == 0), stop=(k_ == 7)))
                for kk in range(8):
                    P.emit("tensor", lambda e, k_=kk, m_=m: e.matmul(
                        p3_k[:], wk_a[:, k_, m_ * 128:(m_ + 1) * 128], xk_bf[:, k_, :], start=(k_ == 0), stop=(k_ == 7)))
                t_rk[m] = P.inc("tensor", sT)
                P.wait("scalar", sT, t_rk[m])
                P.emit("scalar", lambda e, m_=m: e.activation(
                    out=r_sb[:, m_, :], in_=p3_r[:], func=AF.Sigmoid))
                va_sig = P.inc("scalar", sA)
                P.wait("vector", sT, t_rk[m])
                P.emit("vector", lambda e: e.tensor_relu(out=kr_sb[:], in_=p3_k[:]))
                vv_kr = P.inc("vector", sV)
                P.wait("scalar", sV, vv_kr)
                P.emit("scalar", lambda e, m_=m: e.activation(
                    out=k2_bf[:, m_, :], in_=kr_sb[:], func=AF.Square))
                va_rk[m] = P.inc("scalar", sA)
            for m in range(8):
                P.wait("tensor", sA, va_rk[7])
                if m >= 1:
                    P.wait("tensor", sV, vv_str.get(m - 1, 0))
                for kk in range(8):
                    P.emit("tensor", lambda e, k_=kk, m_=m: e.matmul(
                        p3_v[:], wv_a[:, k_, m_ * 128:(m_ + 1) * 128], k2_bf[:, k_, :], start=(k_ == 0), stop=(k_ == 7)))
                t_v[m] = P.inc("tensor", sT)
                P.wait("vector", sT, t_v[m])
                P.emit("vector", lambda e, m_=m: e.tensor_copy(out=kr_sb[:], in_=r_sb[:, m_, :]))
                P.emit("vector", lambda e: e.tensor_mul(out=t2_s[:], in0=kr_sb[:], in1=p3_v[:]))
                P.inc("vector", sVS); P.wait("vector", sVS, sVS.n)
                P.emit("vector", lambda e, m_=m, L_=L: e.tensor_add(
                    out=stream_fm[:, m_, L_:L_ + 512], in0=stream_fm[:, m_, L_:L_ + 512], in1=t2_s[:]))
                vv_str[m] = P.inc("vector", sV)
            return sV.n, sA.n, sT.n

        va_rk = {}
        t_v = {}
        vv_str = {}
        f_done = []
        for sc in range(NSC):
            if sc > 0:
                for _e in ("sync", "vector", "scalar", "tensor"):
                    P.wait(_e, sV, f_done[-1][0])
                    P.wait(_e, sA, f_done[-1][1])
            f_done.append(ffn_block(sc))
        if dbg_out is not None:
            P.wait("sync", sV, sV.n)
            P.wait("sync", sA, sA.n)
            sDBG.n += 16
            P.emit("sync", lambda e: e.dma_start(out=dbg_out, in_=stream_fm[:]).then_inc(sDBG.h, 16))
        es3.close()
        return


    es2.close()
    for _e in ENGS:
        P.wait(_e, sT, sT.n)
        P.wait(_e, sA, sA.n)
        P.wait(_e, sV, sV.n)
        P.wait(_e, sP, sP.n)

    def attn_phase(dbg_out=None):
        es4 = ExitStack()
        E4 = lambda t: es4.enter_context(t)
        xh_q = E4(nc.sbuf_tensor("a_xhq", [128, 8, TQ], BF16))
        den_sb = E4(nc.sbuf_tensor("a_den", [1, 512], F32))
        invrep = E4(nc.sbuf_tensor("a_invrep", [128, 512], F32))
        t4 = E4(nc.sbuf_tensor("a_t4", [128, 512], F32))
        p4_s = E4(nc.psum_tensor("p4_s", [128, 512], F32))
        p4_rep = [E4(nc.psum_tensor(f"p4_rep{i}", [128, 512], F32)) for i in range(2)]
        p4_o = [E4(nc.psum_tensor(f"p4_o{i}", [128, 512], F32)) for i in range(4)]
        es4a = ExitStack()
        E4a = lambda t: es4a.enter_context(t)
        m4 = E4a(nc.sbuf_tensor("a_m", [1, 512], F32))
        r4 = E4a(nc.sbuf_tensor("a_rstd", [1, 512], F32))
        mr4 = E4a(nc.sbuf_tensor("a_mr", [1, 512], F32))
        sq4 = E4a(nc.sbuf_tensor("a_sq", [128, 8, 512], F32))
        stat4 = E4a(nc.sbuf_tensor("a_stat", [1, 1024], F32))
        ln3_fm = E4a(nc.sbuf_tensor("a_ln3", [128, 8, 512], F32))
        x3c_bf = E4a(nc.sbuf_tensor("a_x3c", [128, 8, 512], BF16))
        x3tm = E4a(nc.sbuf_tensor("a_x3tm", [128, 4, D], BF16))
        p4_tb = E4a(nc.psum_tensor("p4_tb", [128, 128], BF16))

        st_xh = 0
        st_xv = 0
        ev_tb = [0]
        for sc in range(NSC):
            L = sc * 512
            P.wait("scalar", sV, sV.n)
            P.emit("scalar", lambda e, L_=L: e.activation(
                out=sq4[:], in_=stream_fm[:, :, L_:L_ + 512], func=AF.Square))
            va_sq4 = P.inc("scalar", sA)
            P.wait("tensor", sA, va_sq4)
            for kk in range(8):
                P.emit("tensor", lambda e, k_=kk, L_=L: e.matmul(
                    p4_s[0:1, :], ones_sb[0:128, 0:1], stream_fm[:, k_, L_:L_ + 512],
                    start=(k_ == 0), stop=(k_ == 7)))
            vt1 = P.inc("tensor", sT)
            P.wait("scalar", sT, vt1)
            P.emit("scalar", lambda e: e.activation(out=stat4[0:1, 0:512], in_=p4_s[0:1, :], func=AF.Copy))
            va1 = P.inc("scalar", sA)
            P.wait("tensor", sA, va1)
            for kk in range(8):
                P.emit("tensor", lambda e, k_=kk: e.matmul(
                    p4_s[0:1, :], ones_sb[0:128, 0:1], sq4[:, k_, :], start=(k_ == 0), stop=(k_ == 7)))
            vt2 = P.inc("tensor", sT)
            P.wait("scalar", sT, vt2)
            P.emit("scalar", lambda e: e.activation(out=stat4[0:1, 512:1024], in_=p4_s[0:1, :], func=AF.Copy))
            va2 = P.inc("scalar", sA)
            P.wait("vector", sA, va2)
            P.emit("vector", lambda e, L_=L: e.tensor_scalar_mul(
                out=m4[0:1, :], in0=stat4[0:1, 0:512], scalar1=1.0 / D))
            P.inc("vector", sVS); P.wait("vector", sVS, sVS.n)
            P.emit("vector", lambda e, L_=L: e.tensor_mul(
                out=stat4[0:1, 0:512], in0=m4[0:1, :], in1=m4[0:1, :]))
            P.inc("vector", sVS); P.wait("vector", sVS, sVS.n)
            P.emit("vector", lambda e: e.tensor_scalar_mul(
                out=stat4[0:1, 512:1024], in0=stat4[0:1, 512:1024], scalar1=1.0 / D))
            P.inc("vector", sVS); P.wait("vector", sVS, sVS.n)
            P.emit("vector", lambda e: e.tensor_sub(
                out=stat4[0:1, 512:1024], in0=stat4[0:1, 512:1024], in1=stat4[0:1, 0:512]))
            vv_v4 = P.inc("vector", sV)
            P.wait("scalar", sV, vv_v4)
            P.emit("scalar", lambda e, L_=L: e.activation(
                out=r4[0:1, :], in_=stat4[0:1, 512:1024], func=AF.Ln, bias=cst_sb[0:1, 0:1]))
            P.inc("scalar", sAS); P.wait("scalar", sAS, sAS.n)
            P.emit("scalar", lambda e, L_=L: e.activation(
                out=r4[0:1, :], in_=r4[0:1, :], func=AF.Exp, scale=-0.5))
            va_r4 = P.inc("scalar", sA)
            P.wait("vector", sA, va_r4)
            P.emit("vector", lambda e, L_=L: e.tensor_mul(
                out=mr4[0:1, :], in0=m4[0:1, :], in1=r4[0:1, :]))
            vv_mr4 = P.inc("vector", sV)
            P.wait("tensor", sV, vv_mr4)
            P.emit("tensor", lambda e, L_=L: e.matmul(
                p4_rep[0][:], ones_sb[0:1, 0:128], r4[0:1, :], start=True, stop=True))
            P.emit("tensor", lambda e, L_=L: e.matmul(
                p4_rep[1][:], ones_sb[0:1, 0:128], mr4[0:1, :], start=True, stop=True))
            vt_rp = P.inc("tensor", sT)
            P.wait("vector", sT, vt_rp)
            for a in range(8):
                P.emit("vector", lambda e, a_=a, L_=L: e.tensor_mul(
                    out=ln3_fm[:, a_, :], in0=stream_fm[:, a_, L_:L_ + 512], in1=p4_rep[0][:]))
            P.inc("vector", sVS); P.wait("vector", sVS, sVS.n)
            for a in range(8):
                P.emit("vector", lambda e, a_=a: e.tensor_sub(
                    out=ln3_fm[:, a_, :], in0=ln3_fm[:, a_, :], in1=p4_rep[1][:]))
            P.inc("vector", sVS); P.wait("vector", sVS, sVS.n)
            for a in range(8):
                P.emit("vector", lambda e, a_=a, L_=L: e.tensor_add(
                    out=stream_fm[:, a_, L_:L_ + 512], in0=stream_fm[:, a_, L_:L_ + 512],
                    in1=ln3_fm[:, a_, :]))
            vv_res = P.inc("vector", sV)
            P.wait("scalar", sV, vv_res)
            P.emit("scalar", lambda e: e.activation(out=sq4[:], in_=ln3_fm[:], func=AF.Square))
            va_sq5 = P.inc("scalar", sA)
            P.wait("tensor", sA, va_sq5)
            for kk in range(8):
                P.emit("tensor", lambda e, k_=kk: e.matmul(
                    p4_s[0:1, :], ones_sb[0:128, 0:1], sq4[:, k_, :], start=(k_ == 0), stop=(k_ == 7)))
            vt_n = P.inc("tensor", sT)
            P.wait("scalar", sT, vt_n)
            P.emit("scalar", lambda e: e.activation(out=stat4[0:1, 0:512], in_=p4_s[0:1, :], func=AF.Copy))
            va_n = P.inc("scalar", sA)
            P.wait("vector", sA, va_n)
            P.emit("vector", lambda e: e.tensor_scalar_max(out=stat4[0:1, 0:512], in0=stat4[0:1, 0:512], scalar1=1e-8))
            vv_mx = P.inc("vector", sV)
            P.wait("scalar", sV, vv_mx)
            P.emit("scalar", lambda e: e.activation(out=stat4[0:1, 0:512], in_=stat4[0:1, 0:512], func=AF.Ln))
            P.inc("scalar", sAS); P.wait("scalar", sAS, sAS.n)
            P.emit("scalar", lambda e: e.activation(out=stat4[0:1, 0:512], in_=stat4[0:1, 0:512], func=AF.Exp, scale=-0.5))
            va_inv = P.inc("scalar", sA)
            P.wait("tensor", sA, va_inv)
            P.emit("tensor", lambda e: e.matmul(
                p4_rep[0][:], ones_sb[0:1, 0:128], stat4[0:1, 0:512], start=True, stop=True))
            vt_ir = P.inc("tensor", sT)
            P.wait("vector", sT, vt_ir)
            for a in range(8):
                P.emit("vector", lambda e, a_=a, L_=L: e.tensor_mul(
                    out=xh_q[:, a_, L_:L_ + 512], in0=ln3_fm[:, a_, :], in1=p4_rep[0][:]))
            vv_xh = P.inc("vector", sV)
            P.wait("scalar", sV, vv_xh)
            P.emit("scalar", lambda e: e.activation(out=x3c_bf[:], in_=ln3_fm[:], func=AF.Copy))
            va_x3c = P.inc("scalar", sA)
            P.wait("tensor", sA, va_x3c)
            for tt in range(4):
                for kk in range(8):
                    P.wait("tensor", sA, ev_tb[0])
                    P.emit("tensor", lambda e, t_=tt, k_=kk: e.transpose(
                        p4_tb[:], x3c_bf[:, k_, t_ * 128:(t_ + 1) * 128], ident_sb[:]))
                    vtb = P.inc("tensor", sT)
                    P.wait("scalar", sT, vtb)
                    P.emit("scalar", lambda e, t_=tt, k_=kk: e.activation(
                        out=x3tm[:, t_, k_ * 128:(k_ + 1) * 128], in_=p4_tb[:], func=AF.Copy))
                    ev_tb[0] = P.inc("scalar", sA)
            P.wait("sync", sA, sA.n)
            P.wait("sync", sV, vv_xh)
            sHT.n += 32
            P.emit("sync", lambda e, L_=L, s_=sc: e.dma_start(
                out=xh_my[s_], in_=xh_q[:, :, L_:L_ + 512]).then_inc(sHT.h, 16))
            P.emit("sync", lambda e, L_=L: e.dma_start(
                out=xv_my[L_:L_ + 512, :].rearrange("(t p) c -> p t c", p=128),
                in_=x3tm[:]).then_inc(sHT.h, 16))
        st_keys = sHT.n
        es4a.close()
        E_sb = E4(nc.sbuf_tensor("a_E", [128, 64, 512], BF16))
        xh_k = [E4(nc.sbuf_tensor(f"a_xhk{i}", [128, 8, 512], BF16)) for i in range(2)]
        xv_k = [E4(nc.sbuf_tensor(f"a_xvk{i}", [128, D], BF16)) for i in range(2)]
        p4_ox = E4(nc.psum_tensor("p4_ox", [128, 512], F32))
        OB = [p4_o[0], p4_o[1], p4_o[2], p4_o[3], p4_rep[0], p4_rep[1], p4_s, p4_ox]
        P.wait("gpsimd", sHT, st_keys)
        sCC.n += 2
        P.emit("gpsimd", lambda e: e.collective_compute(
            "AllGather", ALU.bypass, replica_groups=GROUPS,
            ins=[xh_my.opt()], outs=[xh_all.opt()]).then_inc(sCC.h, 1))
        P.emit("gpsimd", lambda e: e.collective_compute(
            "AllGather", ALU.bypass, replica_groups=GROUPS,
            ins=[xv_my.opt()], outs=[xv_all.opt()]).then_inc(sCC.h, 1))
        v_agh = sCC.n - 1
        v_agv = sCC.n

        exp_done = {}
        pv_done_t = [0]
        evO = {}
        xh_ld = {}
        den_ev = [0]
        for qc in range(4):
            Q0 = qc * 512
            for kt in range(64):
                k4, ko = kt // 4, (kt % 4) * 128
                r_, j4 = kt // 16, (kt % 16) // 4
                pk = k4 % 2
                shk = sH[pk]
                if kt % 4 == 0:
                    qe = "sync" if pk == 0 else "scalar"
                    if k4 >= 2:
                        P.wait(qe, sT, qk_done[(qc, (k4 - 2) * 4 + 3)])
                    P.wait(qe, sCC, v_agh)
                    shk.n += 16
                    P.emit(qe, lambda e, r2=r_, j2=j4, p=pk, s=shk: e.dma_start(
                        out=xh_k[p][:], in_=xh_all[r2 * 4 + j2]).then_inc(s.h, 16))
                    xh_ld[(qc, k4)] = shk.n
                P.wait("tensor", shk, xh_ld[(qc, k4)])
                if qc > 0 and kt == 0:
                    P.wait("tensor", sV, str_done[qc - 1])
                if qc > 0 and kt == 0:
                    P.wait("scalar", sT, pv_done_t[0])
                pq = kt % 2
                if kt >= 2:
                    P.wait("tensor", sA, exp_done[(qc, kt - 2)])
                for kk in range(8):
                    P.emit("tensor", lambda e, k_=kk, p=pk, pr=pq, q_=Q0, o=ko: e.matmul(
                        p4_rep[pr][:], xh_k[p][:, k_, o:o + 128], xh_q[:, k_, q_:q_ + 512],
                        start=(k_ == 0), stop=(k_ == 7)))
                vt_qk = P.inc("tensor", sT)
                qk_done[(qc, kt)] = vt_qk
                P.wait("scalar", sT, vt_qk)
                P.emit("scalar", lambda e, kt_=kt, p=pq: e.activation(
                    out=E_sb[:, kt_, :], in_=p4_rep[p][:], func=AF.Exp))
                exp_done[(qc, kt)] = P.inc("scalar", sA)
            P.wait("tensor", sA, exp_done[(qc, 63)])
            P.wait("tensor", sA, den_ev[0])
            for kt in range(64):
                P.emit("tensor", lambda e, kt_=kt: e.matmul(
                    p4_s[0:1, :], onesbf_sb[:], E_sb[:, kt_, :], start=(kt_ == 0), stop=(kt_ == 63)))
            vt_den = P.inc("tensor", sT)
            P.wait("scalar", sT, vt_den)
            P.emit("scalar", lambda e: e.activation(out=den_sb[:], in_=p4_s[0:1, :], func=AF.Copy))
            va_den = P.inc("scalar", sA)
            den_ev[0] = va_den
            P.wait("vector", sA, va_den)
            P.emit("vector", lambda e: e.reciprocal(out=den_sb[:], in_=den_sb[:]))
            vv_inv = P.inc("vector", sV)
            P.wait("tensor", sV, vv_inv)
            P.emit("tensor", lambda e: e.matmul(
                p4_rep[0][:], ones_sb[0:1, 0:128], den_sb[:], start=True, stop=True))
            vt_ivr = P.inc("tensor", sT)
            P.wait("scalar", sT, vt_ivr)
            P.emit("scalar", lambda e: e.activation(out=invrep[:], in_=p4_rep[0][:], func=AF.Copy))
            va_ivr = P.inc("scalar", sA)
            for kt in range(64):
                r_, j_ = kt // 16, kt % 16
                pk = kt % 2
                svk = sX[pk]
                qv = "scalar" if pk == 0 else "sync"
                if kt >= 2:
                    P.wait(qv, sT, pv_mm[(qc, kt - 2)])
                if kt < 2:
                    P.wait(qv, sCC, v_agv)
                svk.n += 16
                P.emit(qv, lambda e, r2=r_, j2=j_, p=pk, s=svk: e.dma_start(
                    out=xv_k[p][:],
                    in_=xv_all[r2 * TQ + j2 * 128:r2 * TQ + (j2 + 1) * 128, :]
                ).then_inc(s.h, 16))
                v_v = svk.n
                P.wait("tensor", svk, v_v)
                if kt == 0:
                    P.wait("tensor", sA, exp_done[(qc, 63)])
                    P.wait("tensor", sA, va_ivr)
                    P.wait("tensor", sV, str_done.get(qc - 1, 0))
                for ci in range(8):
                    P.emit("tensor", lambda e, c2=ci, p=pk, kt_=kt: e.matmul(
                        OB[c2][:], xv_k[p][:, c2 * 128:(c2 + 1) * 128], E_sb[:, kt_, :],
                        start=(kt_ == 0), stop=(kt_ == 63)))
                pv_mm[(qc, kt)] = P.inc("tensor", sT)
            pv_done_t[0] = pv_mm[(qc, 63)]
            P.wait("vector", sT, pv_mm[(qc, 63)])
            P.wait("vector", sA, va_ivr)
            for ci in range(8):
                P.emit("vector", lambda e, c2=ci: e.tensor_mul(out=t4[:], in0=OB[c2][:], in1=invrep[:]))
                P.inc("vector", sVS); P.wait("vector", sVS, sVS.n)
                P.emit("vector", lambda e, c2=ci, q_=Q0: e.tensor_add(
                    out=stream_fm[:, c2, q_:q_ + 512],
                    in0=stream_fm[:, c2, q_:q_ + 512], in1=t4[:]))
            str_done[qc] = P.inc("vector", sV)
        if dbg_out is not None:
            P.wait("sync", sV, sV.n)
            P.wait("sync", sA, sA.n)
            sDBG.n += 16
            P.emit("sync", lambda e: e.dma_start(out=dbg_out, in_=stream_fm[:]).then_inc(sDBG.h, 16))
        es4.close()

    qk_done = {}
    pv_mm = {}
    str_done = {}

    DO_ATTN = 2
    if FFN_STOP >= 1:
        ffn_phase(0)
        if FFN_STOP >= 4 and DO_ATTN:
            for _e in ENGS:
                P.wait(_e, sT, sT.n)
                P.wait(_e, sA, sA.n)
                P.wait(_e, sV, sV.n)
                P.wait(_e, sP, sP.n)
            attn_phase()
            if DO_ATTN >= 2:
                for _e in ENGS:
                    P.wait(_e, sT, sT.n)
                    P.wait(_e, sA, sA.n)
                    P.wait(_e, sV, sV.n)
                    P.wait(_e, sP, sP.n)
                ffn_phase(1)
                for _e in ("tensor", "scalar", "sync"):
                    P.wait(_e, sT, sT.n)
                    P.wait(_e, sA, sA.n)
                    P.wait(_e, sV, sV.n)
                es6 = ExitStack()
                p6_t = [es6.enter_context(nc.psum_tensor(f"p6_t{i}", [128, 128], F32)) for i in range(2)]
                o6 = es6.enter_context(nc.sbuf_tensor("o6", [128, 4, D], F32))
                ev6 = {}
                st6 = {}
                q6 = [(sOP[0], "sync"), (sOP[1], "scalar")]
                for ti in range(16):
                    ob = ti % 4
                    sq, qe = q6[ti % 2]
                    if ti >= 4:
                        P.wait("scalar", st6[ti - 4][0], st6[ti - 4][1])
                    for kk in range(8):
                        sl = (ti * 8 + kk) % 2
                        P.wait("tensor", sA, ev6.get(sl, 0))
                        P.emit("tensor", lambda e, t_=ti, k_=kk, s=sl: e.transpose(
                            p6_t[s][:], stream_fm[:, k_, t_ * 128:(t_ + 1) * 128], identf_sb[:]))
                        vt6 = P.inc("tensor", sT)
                        P.wait("scalar", sT, vt6)
                        P.emit("scalar", lambda e, k_=kk, s=sl, o_=ob: e.activation(
                            out=o6[:, o_, k_ * 128:(k_ + 1) * 128], in_=p6_t[s][:], func=AF.Copy))
                        ev6[sl] = P.inc("scalar", sA)
                    P.wait(qe, sA, sA.n)
                    sq.n += 16
                    P.emit(qe, lambda e, t_=ti, o_=ob, s=sq: e.dma_start(
                        out=out_q[t_ * 128:(t_ + 1) * 128, :], in_=o6[:, o_, :]).then_inc(s.h, 16))
                    st6[ti] = (sq, sq.n)
                for sq, _qe in q6:
                    P.wait("sync", sq, sq.n)
    P.run(nc)
    return nc



def _make_selp(r):
    sel = np.zeros((16, 4), np.float32)
    for sc in range(4):
        if r > 0:
            sel[(r - 1) * 4 + sc, sc] = 1.0
        elif sc > 0:
            sel[3 * 4 + (sc - 1), sc] = 1.0
    return sel.astype(ml_dtypes.bfloat16)


def _gidx(r):
    return np.concatenate([np.arange(sc * 2048 + r * 512, sc * 2048 + r * 512 + 512)
                           for sc in range(4)])


def _shard_inputs(inputs, core):
    b, r = core // 4, core % 4
    d0, d1 = r * DSL, (r + 1) * DSL
    def shp(v):
        return np.ascontiguousarray(v.reshape(4, 128).T)
    x = inputs["L0_attn_a_hidden_states"]
    w_in_full = inputs["L0_attn_a_in_proj_w"]
    return dict(
        x_q=np.ascontiguousarray(x[b][_gidx(r)]),
        w_in=np.ascontiguousarray(np.concatenate(
            [w_in_full[:, d0:d1], w_in_full[:, DI + d0:DI + d1]], axis=1)),
        conv_w=np.ascontiguousarray(
            inputs["L0_attn_a_conv_w"][d0:d1].reshape(4, 128, KC).transpose(1, 0, 2)),
        conv_b=shp(inputs["L0_attn_a_conv_b"][d0:d1]),
        xproj_w=np.ascontiguousarray(inputs["L0_attn_a_x_proj_w"][d0:d1]),
        dtw=np.ascontiguousarray(inputs["L0_attn_a_dt_proj_w"][:, d0:d1]),
        dtb=shp(inputs["L0_attn_a_dt_proj_b"][d0:d1]),
        dtlnw=np.ascontiguousarray(inputs["L0_attn_a_dt_ln_w"].reshape(RDT, 1)),
        blnw=np.ascontiguousarray(inputs["L0_attn_a_b_ln_w"].reshape(1, NST)),
        clnw=np.ascontiguousarray(inputs["L0_attn_a_c_ln_w"].reshape(1, NST)),
        alog=np.ascontiguousarray(
            inputs["L0_attn_a_A_log"][d0:d1].reshape(4, 128, NST).transpose(1, 0, 2)),
        dskip=shp(inputs["L0_attn_a_D_skip"][d0:d1]),
        wout=np.ascontiguousarray(inputs["L0_attn_a_out_proj_w"][d0:d1]),
        inlnw=np.ascontiguousarray(inputs["L0_attn_a_input_ln_w"].reshape(1, D)),
        ident_bf=np.eye(128, dtype=ml_dtypes.bfloat16),
        ident_f=np.eye(128, dtype=np.float32),
        gate_w=inputs["L0_attn_a_gate_w"],
        up_w=inputs["L0_attn_a_up_w"],
        down_w=inputs["L0_attn_a_down_w"],
        pmw=np.ascontiguousarray(inputs["L0_attn_a_pre_moe_ln_w"].reshape(1, D)),
        w_r=inputs["L0_ffn_W_r"],
        w_k=inputs["L0_ffn_W_k"],
        w_v=inputs["L0_ffn_W_v"],
        mu_r=np.ascontiguousarray(inputs["L0_ffn_mu_r"].reshape(8, 128).T),
        mu_k=np.ascontiguousarray(inputs["L0_ffn_mu_k"].reshape(8, 128).T),
        selp=_make_selp(r),
    )


_NC_CACHE = [None]


def kernel(**inputs):
    if _NC_CACHE[0] is None:
        _NC_CACHE[0] = build(detect_races=False)
    nc = _NC_CACHE[0]
    in_maps = [_shard_inputs(inputs, c) for c in range(8)]
    res = run_bass_kernel_spmd(nc, in_maps, core_ids=list(range(8)))
    out = np.empty((B, S, D), np.float32)
    for c in range(8):
        b, r = c // 4, c % 4
        out[b][_gidx(r)] = res.results[c]["out_q"]
    return out


# version 17
# speedup vs baseline: 1.0184x; 1.0184x over previous
import numpy as np
import ml_dtypes
from contextlib import ExitStack
import concourse.bass as bass
import concourse.mybir as mybir
from concourse.bass_utils import run_bass_kernel_spmd

F32 = mybir.dt.float32
BF16 = mybir.dt.bfloat16
AF = mybir.ActivationFunctionType
ALU = mybir.AluOpType
AX = mybir.AxisListType

B, S, D = 2, 8192, 1024
DI, NST, RDT, KC, DF = 2048, 16, 64, 4, 4096
DSL = DI // 4
TQ = S // 4
TC = 512
NCH = S // TC
ARG = 1
NAR = NCH // ARG
NSC = 4
NSLOT = 2
ENGS = ("sync", "vector", "scalar", "gpsimd", "tensor")
GROUPS = [[0, 1, 2, 3], [4, 5, 6, 7]]


class Sem:
    def __init__(self, nc, es, name):
        self.h = es.enter_context(nc.semaphore(name))
        self.n = 0


class Prog:
    def __init__(self):
        self.plan = {e: [] for e in ENGS}
    def emit(self, eng, fn):
        self.plan[eng].append(("op", fn))
    def wait(self, eng, sem, val):
        if val > 0:
            self.plan[eng].append(("wait", sem.h, val))
    def inc(self, eng, sem, k=1):
        sem.n += k
        self.plan[eng].append(("inc", sem.h, k))
        return sem.n
    def run(self, nc):
        def runner(name):
            def go(e):
                last = None
                for item in self.plan[name]:
                    if item[0] == "op":
                        last = item[1](e)
                    elif item[0] == "wait":
                        e.wait_ge(item[1], item[2])
                    else:
                        assert last is not None, f"inc with no prior op on {name}"
                        last.then_inc(item[1], item[2])
                        last = None
            return go
        with nc.Block() as block:
            block.sync(runner("sync"))
            block.vector(runner("vector"))
            block.scalar(runner("scalar"))
            block.gpsimd(runner("gpsimd"))
            block.tensor(runner("tensor"))


def build(debug=(), detect_races=True):
    DO_CACHE = True
    DO_PH2 = True
    DO_WT = True
    FFN_STOP = 4

    nc = bass.Bass("TRN2", target_bir_lowering=False, detect_race_conditions=detect_races)
    es = ExitStack()
    E = lambda t: es.enter_context(t)

    x_q = nc.dram_tensor("x_q", [TQ, D], F32, kind="ExternalInput").ap()
    w_in = nc.dram_tensor("w_in", [D, 2 * DSL], F32, kind="ExternalInput").ap()
    conv_w = nc.dram_tensor("conv_w", [128, 4, KC], F32, kind="ExternalInput").ap()
    conv_b = nc.dram_tensor("conv_b", [128, 4], F32, kind="ExternalInput").ap()
    xproj_w = nc.dram_tensor("xproj_w", [DSL, 96], F32, kind="ExternalInput").ap()
    dtw = nc.dram_tensor("dtw", [RDT, DSL], F32, kind="ExternalInput").ap()
    dtb = nc.dram_tensor("dtb", [128, 4], F32, kind="ExternalInput").ap()
    dtlnw = nc.dram_tensor("dtlnw", [RDT, 1], F32, kind="ExternalInput").ap()
    blnw = nc.dram_tensor("blnw", [1, NST], F32, kind="ExternalInput").ap()
    clnw = nc.dram_tensor("clnw", [1, NST], F32, kind="ExternalInput").ap()
    alog = nc.dram_tensor("alog", [128, 4, NST], F32, kind="ExternalInput").ap()
    dskip = nc.dram_tensor("dskip", [128, 4], F32, kind="ExternalInput").ap()
    wout = nc.dram_tensor("wout", [DSL, D], F32, kind="ExternalInput").ap()
    inlnw = nc.dram_tensor("inlnw", [1, D], F32, kind="ExternalInput").ap()
    ident_bf = nc.dram_tensor("ident_bf", [128, 128], BF16, kind="ExternalInput").ap()
    ident_f = nc.dram_tensor("ident_f", [128, 128], F32, kind="ExternalInput").ap()
    gate_w = nc.dram_tensor("gate_w", [D, DF], F32, kind="ExternalInput").ap()
    up_w = nc.dram_tensor("up_w", [D, DF], F32, kind="ExternalInput").ap()
    down_w = nc.dram_tensor("down_w", [DF, D], F32, kind="ExternalInput").ap()
    pmw = nc.dram_tensor("pmw", [1, D], F32, kind="ExternalInput").ap()
    w_r = nc.dram_tensor("w_r", [D, D], F32, kind="ExternalInput").ap()
    w_k = nc.dram_tensor("w_k", [D, D], F32, kind="ExternalInput").ap()
    w_v = nc.dram_tensor("w_v", [D, D], F32, kind="ExternalInput").ap()
    mu_r = nc.dram_tensor("mu_r", [128, 8], F32, kind="ExternalInput").ap()
    mu_k = nc.dram_tensor("mu_k", [128, 8], F32, kind="ExternalInput").ap()
    selp = nc.dram_tensor("selp", [16, 4], BF16, kind="ExternalInput").ap()

    out_q = nc.dram_tensor("out_q", [TQ, D], F32, kind="ExternalOutput").ap()

    hT_my = nc.dram_tensor("hT_my", [D, TQ], BF16).ap()
    hT_all = nc.dram_tensor("hT_all", [4 * D, TQ], BF16).ap()
    xp_part = [nc.dram_tensor(f"xp_part{j}", [96, ARG * TC], F32).ap() for j in range(NAR)]
    xp_red = [nc.dram_tensor(f"xp_red{j}", [96, ARG * TC], F32).ap() for j in range(NAR)]
    op_part = nc.dram_tensor("op_part", [S, D], F32).ap()
    gate_bf = nc.dram_tensor("gate_bf", [D, DF], BF16).ap()
    up_bf = nc.dram_tensor("up_bf", [D, DF], BF16).ap()
    down_bf = nc.dram_tensor("down_bf", [DF, D], BF16).ap()
    wrT_bf = nc.dram_tensor("wrT_bf", [D, D], BF16).ap()
    wkT_bf = nc.dram_tensor("wkT_bf", [D, D], BF16).ap()
    wvT_bf = nc.dram_tensor("wvT_bf", [D, D], BF16).ap()
    bnd_my = [nc.dram_tensor(f"bnd_my{i}", [4, D], BF16).ap() for i in range(2)]
    bnd_all = [nc.dram_tensor(f"bnd_all{i}", [16, D], BF16).ap() for i in range(2)]
    xh_my = nc.dram_tensor("xh_my", [4, 128, 8, 512], BF16).ap()
    xh_all = nc.dram_tensor("xh_all", [16, 128, 8, 512], BF16).ap()
    xv_my = nc.dram_tensor("xv_my", [TQ, D], BF16).ap()
    xv_all = nc.dram_tensor("xv_all", [S, D], BF16).ap()
    rs1_out = nc.dram_tensor("rs1_out", [TQ, D], F32).ap()

    ident_sb = E(nc.sbuf_tensor("ident_sb", [128, 128], BF16))
    ones_sb = E(nc.sbuf_tensor("ones_sb", [128, 128], F32))
    cst_sb = E(nc.sbuf_tensor("cst_sb", [128, 2], F32))
    onesbf_sb = E(nc.sbuf_tensor("onesbf_sb", [128, 1], BF16))
    es1 = ExitStack()
    E1 = lambda t: es1.enter_context(t)
    w_in_bf = E1(nc.sbuf_tensor("w_in_bf", [128, 8, 2 * DSL], BF16))
    xproj_sb = E1(nc.sbuf_tensor("xproj_sb", [128, 4, 96], F32))
    dtw_sb = E1(nc.sbuf_tensor("dtw_sb", [RDT, DSL], F32))
    wout_bf = E1(nc.sbuf_tensor("wout_bf", [128, 4, D], BF16))
    convw_sb = E1(nc.sbuf_tensor("convw_sb", [128, 4, KC], F32))
    convb_sb = E1(nc.sbuf_tensor("convb_sb", [128, 4], F32))
    dtb_sb = E1(nc.sbuf_tensor("dtb_sb", [128, 4], F32))
    dtlnw_sb = E1(nc.sbuf_tensor("dtlnw_sb", [RDT, 1], F32))
    clnw_eff = E1(nc.sbuf_tensor("clnw_eff", [128, NST], F32))
    blnw_rep = E1(nc.sbuf_tensor("blnw_rep", [128, NST], F32))
    A_sb = E1(nc.sbuf_tensor("A_sb", [128, 4, NST], F32))
    dskip_sb = E1(nc.sbuf_tensor("dskip_sb", [128, 4], F32))

    hTin = [E1(nc.sbuf_tensor(f"hTin{i}", [128, 8, TC], BF16)) for i in range(2)]
    xc_sl = [E1(nc.sbuf_tensor(f"xc{i}", [128, 4, TC + 3], F32)) for i in range(NSLOT)]
    sz_sl = [E1(nc.sbuf_tensor(f"sz{i}", [128, 4, TC], BF16)) for i in range(NSLOT)]
    xpev = E1(nc.sbuf_tensor("xpev", [80, TC], F32))
    xpev_c = E1(nc.sbuf_tensor("xpev_c", [16, TC], F32))
    xpd_in = E1(nc.sbuf_tensor("xpd_in", [RDT, TC], F32))
    xpb_in = E1(nc.sbuf_tensor("xpb_in", [NST, TC], F32))
    xpc_in = E1(nc.sbuf_tensor("xpc_in", [NST, TC], F32))
    sel_sb = E1(nc.sbuf_tensor("sel_sb", [128, 128], F32))
    rstdd = E1(nc.sbuf_tensor("rstdd", [1, TC], F32))
    rstdb = E1(nc.sbuf_tensor("rstdb", [1, TC], F32))
    rstdc = E1(nc.sbuf_tensor("rstdc", [1, TC], F32))
    dtrms = E1(nc.sbuf_tensor("dtrms", [RDT, TC], F32))
    dt_sb = E1(nc.sbuf_tensor("dt_sb", [128, 4, TC], F32))
    u_sb = E1(nc.sbuf_tensor("u_sb", [128, TC], F32))
    up_sb = E1(nc.sbuf_tensor("up_sb", [128, 4, TC], F32))
    state = E1(nc.sbuf_tensor("state", [128, 4, NST], F32))
    yred = E1(nc.sbuf_tensor("yred", [128, TC], F32))
    t1_sb = E1(nc.sbuf_tensor("t1_sb", [128, TC], F32))
    cfin = E1(nc.sbuf_tensor("cfin", [128, TC], F32))
    yg_bf = E1(nc.sbuf_tensor("yg_bf", [128, 4, TC], BF16))
    oo_sb = E1(nc.sbuf_tensor("oo_sb", [128, D], F32))

    es0 = ExitStack()
    E0 = lambda t: es0.enter_context(t)
    wstage = E0(nc.sbuf_tensor("wstage", [128, 2 * DSL], F32))
    wln_rep = E0(nc.sbuf_tensor("wln_rep", [128, D], F32))
    xt_sb = [E0(nc.sbuf_tensor(f"xt{i}", [128, D], F32)) for i in range(2)]
    sq_scr = E0(nc.sbuf_tensor("sq_scr", [128, D], F32))
    ln1_t = E0(nc.sbuf_tensor("ln1_t", [128, D], F32))
    h2_bf = E0(nc.sbuf_tensor("h2_bf", [128, D], BF16))
    hTcol = [E0(nc.sbuf_tensor(f"hTcol{i}", [128, 8, 512], BF16)) for i in range(2)]
    st0 = E0(nc.sbuf_tensor("st0", [128, 8], F32))
    wst_bf = [E0(nc.sbuf_tensor(f"wst_bf{i}", [128, D], BF16)) for i in range(2)]
    wst_f = [E0(nc.sbuf_tensor(f"wst_f{i}", [128, D], F32)) for i in range(2)]
    wrTcol = E0(nc.sbuf_tensor("wrTcol", [128, 8, 128], BF16))

    esp = ExitStack()
    p_tr = [esp.enter_context(nc.psum_tensor(f"p_tr{i}", [128, 128], BF16)) for i in range(2)]
    p_wt = [esp.enter_context(nc.psum_tensor(f"p_wt{i}", [128, 128], BF16)) for i in range(2)]

    sW = Sem(nc, es, "sW")
    sWC = Sem(nc, es, "sWC")
    sX = [Sem(nc, es, "sX0"), Sem(nc, es, "sX1")]
    sH = [Sem(nc, es, "sH0"), Sem(nc, es, "sH1")]
    sBC = Sem(nc, es, "sBC")
    sBR = Sem(nc, es, "sBR")
    sSt = Sem(nc, es, "sSt")
    sHT = Sem(nc, es, "sHT")
    sDBG = Sem(nc, es, "sDBG")
    sXP = [Sem(nc, es, "sXP0"), Sem(nc, es, "sXP1")]
    sOP = [Sem(nc, es, "sOP0"), Sem(nc, es, "sOP1")]
    sCC = Sem(nc, es, "sCC")
    sAS = Sem(nc, es, "sAS")
    sVS = Sem(nc, es, "sVS")
    sGB = Sem(nc, es, "sGB")
    sG2 = Sem(nc, es, "sG2")
    sG3 = Sem(nc, es, "sG3")
    sRS = Sem(nc, es, "sRS")
    sRS2 = Sem(nc, es, "sRS2")
    sWL = [Sem(nc, es, "sWL0"), Sem(nc, es, "sWL1")]
    sGBp = [Sem(nc, es, "sGBp0"), Sem(nc, es, "sGBp1")]
    sPC = Sem(nc, es, "sPC")
    sWT = Sem(nc, es, "sWT")
    sWE = Sem(nc, es, "sWE")
    sT = Sem(nc, es, "sT")
    sA = Sem(nc, es, "sA")
    sV = Sem(nc, es, "sV")
    sP = Sem(nc, es, "sP")
    sQGA = [Sem(nc, es, f"sQGA{i}") for i in range(8)]
    sQG3 = Sem(nc, es, "sQG3")

    P = Prog()

    P.emit("vector", lambda e: e.memset(cst_sb[:, 0:1], 1e-5))
    P.emit("vector", lambda e: e.memset(cst_sb[:, 1:2], 1e-6))
    P.inc("vector", sV)
    loads = [
        (convw_sb, conv_w), (convb_sb, conv_b), (dtb_sb, dtb),
        (dtlnw_sb, dtlnw), (A_sb, alog), (dskip_sb, dskip),
        (ident_sb, ident_bf), (xproj_sb, xproj_w.rearrange("(a p) m -> p a m", p=128)),
        (dtw_sb, dtw),
    ]
    for dst, src in loads:
        sW.n += 16
        P.emit("sync", lambda e, d=dst, s=src: e.dma_start(out=d[:], in_=s).then_inc(sW.h, 16))
    for dst, src, n in ((blnw_rep, blnw, NST), (clnw_eff, clnw, NST), (wln_rep, inlnw, D)):
        sW.n += 16
        P.emit("sync", lambda e, d=dst, s=src, nn=n: e.dma_start(
            out=d[:], in_=s.broadcast_to([128, nn])).then_inc(sW.h, 16))
    w_loaded = sW.n

    cast_jobs = []
    for kk in range(8):
        cast_jobs.append((w_in_bf[:, kk, :], w_in[kk * 128:(kk + 1) * 128, :], 2 * DSL))
    for kk in range(4):
        cast_jobs.append((wout_bf[:, kk, :], wout[kk * 128:(kk + 1) * 128, :], D))
    for dst, src, width in cast_jobs:
        P.wait("sync", sP, sP.n)
        sWC.n += 16
        P.emit("sync", lambda e, s=src, w=width: e.dma_start(
            out=wstage[:, 0:w], in_=s).then_inc(sWC.h, 16))
        P.wait("gpsimd", sWC, sWC.n)
        P.emit("gpsimd", lambda e, d=dst, w=width: e.tensor_copy(out=d, in_=wstage[:, 0:w]))
        P.inc("gpsimd", sP)
    P.wait("gpsimd", sW, w_loaded)
    P.emit("gpsimd", lambda e: e.memset(ones_sb[:], 1.0))
    P.emit("gpsimd", lambda e: e.memset(onesbf_sb[:], 1.0))
    P.emit("gpsimd", lambda e: e.memset(sel_sb[:], 1.0))
    P.inc("gpsimd", sP); P.wait("gpsimd", sP, sP.n)
    P.emit("gpsimd", lambda e: e.memset(sel_sb[64:65, :], 0.0))
    w_casts = P.inc("gpsimd", sP)

    cache_jobs = []
    for wsrc, wdst, RT, CC in ((gate_w, gate_bf, 8, DF), (up_w, up_bf, 8, DF),
                               (down_w, down_bf, 32, D)):
        ncc = CC // D
        for rt in range(RT):
            for cc in range(max(1, ncc)):
                c0_, c1_ = cc * D, min(CC, (cc + 1) * D)
                cache_jobs.append((wsrc[rt * 128:(rt + 1) * 128, c0_:c1_],
                                   wdst[rt * 128:(rt + 1) * 128, c0_:c1_], None, None))
    if DO_WT:
        for wsrc, wdst in ((w_r, wrT_bf), (w_k, wkT_bf), (w_v, wvT_bf)):
            for rt in range(8):
                cache_jobs.append((wsrc[rt * 128:(rt + 1) * 128, :], wdst, "T", rt))
    cache_state = dict(jw=0, casts=[0, 0], stores=[0, 0], wt_ev={}, wt_slot=0)

    def emit_cache_jobs(n):
        if not DO_CACHE:
            return
        for _ in range(n):
            jw = cache_state["jw"]
            if jw >= len(cache_jobs):
                return
            cache_state["jw"] = jw + 1
            srcs, dsts, kind, rt = cache_jobs[jw]
            p = jw % 2
            P.wait("scalar", sPC, cache_state["casts"][p])
            sWL[p].n += 16
            P.emit("scalar", lambda e, s=srcs, pp=p, sm=sWL[p]: e.dma_start(
                out=wst_f[pp][:, 0:s.shape[1]], in_=s).then_inc(sm.h, 16))
            P.wait("gpsimd", sWL[p], sWL[p].n)
            P.wait("gpsimd", sGBp[p], cache_state["stores"][p])
            P.emit("gpsimd", lambda e, pp=p, w=srcs.shape[1]: e.tensor_copy(
                out=wst_bf[pp][:, 0:w], in_=wst_f[pp][:, 0:w]))
            cache_state["casts"][p] = P.inc("gpsimd", sPC)
            if kind is None:
                P.wait("sync", sPC, cache_state["casts"][p])
                sGBp[p].n += 16
                sGB.n += 16
                P.emit("sync", lambda e, d=dsts, pp=p, w=srcs.shape[1], sm=sGBp[p]: e.dma_start(
                    out=d, in_=wst_bf[pp][:, 0:w]).then_inc(sm.h, 16))
                P.emit("sync", lambda e: e.nop()) if False else None
                cache_state["stores"][p] = sGBp[p].n
            else:
                P.wait("tensor", sPC, cache_state["casts"][p])
                for mm in range(8):
                    slot = cache_state["wt_slot"]
                    cache_state["wt_slot"] = (slot + 1) % 2
                    P.wait("tensor", sWE, cache_state["wt_ev"].get(slot, 0))
                    P.emit("tensor", lambda e, pp=p, m_=mm, sl=slot: e.transpose(
                        p_wt[sl][:], wst_bf[pp][:, m_ * 128:(m_ + 1) * 128], ident_sb[:]))
                    vt_w = P.inc("tensor", sWT)
                    P.wait("scalar", sWT, vt_w)
                    P.emit("scalar", lambda e, m_=mm, sl=slot: e.activation(
                        out=wrTcol[:, m_, :], in_=p_wt[sl][:], func=AF.Copy))
                    cache_state["wt_ev"][slot] = P.inc("scalar", sWE)
                P.wait("sync", sWE, sWE.n)
                sGBp[p].n += 16
                P.emit("sync", lambda e, d=dsts, r=rt, sm=sGBp[p]: e.dma_start(
                    out=d.rearrange("(m p) c -> p m c", p=128)[:, :, r * 128:(r + 1) * 128],
                    in_=wrTcol[:]).then_inc(sm.h, 16))
                cache_state["stores"][p] = sGBp[p].n
                P.wait("scalar", sGBp[p], sGBp[p].n)

    P.wait("vector", sW, w_loaded)
    P.emit("vector", lambda e: e.tensor_mul(out=clnw_eff[:], in0=clnw_eff[:], in1=blnw_rep[:]))
    P.inc("vector", sV)
    P.wait("scalar", sW, w_loaded)
    P.emit("scalar", lambda e: e.activation(out=A_sb[:], in_=A_sb[:], func=AF.Exp))
    w_a_exp = P.inc("scalar", sA)
    P.wait("vector", sA, w_a_exp)
    P.emit("vector", lambda e: e.tensor_scalar_mul(out=A_sb[:], in0=A_sb[:], scalar1=-1.0))
    w_small_v = P.inc("vector", sV)

    ntile = TQ // 128
    ph0_a_done = {}
    ph0_t_done = {}
    ph0_st = {}
    vv_st_prev = 0
    ev_parity = [0, 0]
    for i in range(ntile):
        buf = xt_sb[i % 2]
        if i >= 2:
            P.wait("sync", sA, ph0_a_done[i - 2])
        sxp = sX[i % 2]
        sxp.n += 16
        P.emit("sync", lambda e, b=buf, i_=i, s=sxp: e.dma_start(
            out=b[:], in_=x_q[i_ * 128:(i_ + 1) * 128, :]).then_inc(s.h, 16))
        v_x = sxp.n
        P.wait("vector", sxp, v_x)
        P.emit("vector", lambda e, b=buf: e.reduce_sum(out=st0[:, 0:1], in_=b[:], axis=AX.X))
        P.inc("vector", sVS); P.wait("vector", sVS, sVS.n)
        P.wait("scalar", sxp, v_x)
        P.wait("scalar", sV, vv_st_prev)
        P.emit("scalar", lambda e, b=buf: e.activation(
            out=sq_scr[:], in_=b[:], func=AF.Square, accum_out=st0[:, 1:2]))
        va_sq = P.inc("scalar", sA)
        P.wait("vector", sA, va_sq)
        P.emit("vector", lambda e: e.tensor_scalar_mul(out=st0[:, 2:3], in0=st0[:, 0:1], scalar1=1.0 / D))
        P.emit("vector", lambda e: e.tensor_scalar_mul(out=st0[:, 3:4], in0=st0[:, 1:2], scalar1=1.0 / D))
        P.inc("vector", sVS); P.wait("vector", sVS, sVS.n)
        P.emit("vector", lambda e: e.tensor_mul(out=st0[:, 4:5], in0=st0[:, 2:3], in1=st0[:, 2:3]))
        P.inc("vector", sVS); P.wait("vector", sVS, sVS.n)
        P.emit("vector", lambda e: e.tensor_sub(out=st0[:, 3:4], in0=st0[:, 3:4], in1=st0[:, 4:5]))
        vv_st = P.inc("vector", sV)
        vv_st_prev = vv_st
        P.wait("scalar", sV, vv_st)
        P.emit("scalar", lambda e: e.activation(out=st0[:, 5:6], in_=st0[:, 3:4], func=AF.Ln, bias=cst_sb[:, 0:1]))
        P.inc("scalar", sAS); P.wait("scalar", sAS, sAS.n)
        P.emit("scalar", lambda e: e.activation(out=st0[:, 5:6], in_=st0[:, 5:6], func=AF.Exp, scale=-0.5))
        va_r = P.inc("scalar", sA)
        P.wait("scalar", sA, va_r)
        P.wait("vector", sA, va_r)
        P.emit("vector", lambda e: e.tensor_mul(out=st0[:, 6:7], in0=st0[:, 2:3], in1=st0[:, 5:6]))
        P.inc("vector", sVS); P.wait("vector", sVS, sVS.n)
        P.emit("vector", lambda e: e.tensor_scalar_mul(out=st0[:, 6:7], in0=st0[:, 6:7], scalar1=-1.0))
        vv_nmr = P.inc("vector", sV)
        P.wait("scalar", sV, vv_nmr)
        P.emit("scalar", lambda e, b=buf: e.activation(
            out=ln1_t[:], in_=b[:], func=AF.Identity, scale=st0[:, 5:6], bias=st0[:, 6:7]))
        P.emit("scalar", lambda e: e.activation(
            out=sq_scr[:], in_=ln1_t[:], func=AF.Square, accum_out=st0[:, 7:8]))
        P.inc("scalar", sAS); P.wait("scalar", sAS, sAS.n)
        P.emit("scalar", lambda e: e.activation(out=st0[:, 7:8], in_=st0[:, 7:8], func=AF.Ln, scale=1.0 / D, bias=cst_sb[:, 1:2]))
        P.inc("scalar", sAS); P.wait("scalar", sAS, sAS.n)
        P.emit("scalar", lambda e: e.activation(out=st0[:, 7:8], in_=st0[:, 7:8], func=AF.Exp, scale=-0.5))
        va_ln = P.inc("scalar", sA)
        P.wait("scalar", sA, va_ln)
        ph0_a_done[i] = va_ln
        P.wait("vector", sA, va_ln)
        if i >= 1:
            P.wait("vector", sT, ph0_t_done[i - 1])
        P.emit("vector", lambda e: e.scalar_tensor_tensor(
            out=h2_bf[:], in0=ln1_t[:], scalar=st0[:, 7:8], in1=wln_rep[:],
            op0=ALU.mult, op1=ALU.mult))
        vv_h2 = P.inc("vector", sV)
        col = hTcol[(i // 4) % 2]
        io = (i % 4) * 128
        P.wait("tensor", sV, vv_h2)
        if i % 4 == 0 and i >= 8:
            P.wait("scalar", sHT, ph0_st[i // 4 - 2])
        for a in range(8):
            pt = p_tr[a % 2]
            P.wait("tensor", sA, ev_parity[a % 2])
            P.emit("tensor", lambda e, a_=a, p=pt: e.transpose(
                p[:], h2_bf[:, a_ * 128:(a_ + 1) * 128], ident_sb[:]))
            vt = P.inc("tensor", sT)
            P.wait("scalar", sT, vt)
            P.emit("scalar", lambda e, a_=a, p=pt, c=col, o=io: e.activation(
                out=c[:, a_, o:o + 128], in_=p[:], func=AF.Copy))
            ev_parity[a % 2] = P.inc("scalar", sA)
        ph0_t_done[i] = sT.n
        if i % 4 == 3:
            P.wait("sync", sA, sA.n)
            sHT.n += 16
            P.emit("sync", lambda e, c=col, i_=i: e.dma_start(
                out=hT_my.rearrange("(a p) l -> p a l", p=128)[:, :, (i_ - 3) * 128:(i_ + 1) * 128],
                in_=c[:]).then_inc(sHT.h, 16))
            ph0_st[i // 4] = sHT.n
        emit_cache_jobs(8)
    emit_cache_jobs(10**9)
    P.wait("gpsimd", sHT, sHT.n)
    sCC.n += 1
    P.emit("gpsimd", lambda e: e.collective_compute(
        "AllGather", ALU.bypass, replica_groups=GROUPS,
        ins=[hT_my.opt()], outs=[hT_all.opt()]).then_inc(sCC.h, 1))
    v_ag1 = sCC.n

    es0.close()
    esp.close()
    esS = ExitStack()
    ES = lambda t: esS.enter_context(t)
    crep_sl = [ES(nc.sbuf_tensor(f"crep{i}", [128, 4, TC], F32)) for i in range(2)]
    dA_b = ES(nc.sbuf_tensor("dA", [128, 4, TC], F32))
    bb_b = ES(nc.sbuf_tensor("bb", [128, 4, TC], F32))
    H_sl = [ES(nc.sbuf_tensor(f"H{i}", [128, 4, TC], F32)) for i in range(2)]
    G_b = ES(nc.sbuf_tensor("G", [128, 4, TC], F32))
    yacc4 = ES(nc.sbuf_tensor("yacc4", [128, 4, TC], F32))
    bsrcB = ES(nc.sbuf_tensor("bsrcB", [128, 4, TC], F32))
    bsrcC = ES(nc.sbuf_tensor("bsrcC", [128, 4, TC], F32))
    rcrep_sb = ES(nc.sbuf_tensor("rcrep", [128, TC], F32))
    p_q = [ES(nc.psum_tensor(f"p_q{i}", [128, TC], F32)) for i in range(8)]

    a_t_done = {}
    a_ev_last = {}
    a_act_done = {}
    a_xpst = {}
    a_xpev_done = {}
    a_xp_dma = {}
    ar_done = {}
    b_done_v = {}
    last_b_done = [0]
    op_dma = {}
    op_ev = {}
    b_prev_dtr = [0]
    op_st_last = [0, 0]
    bsrc_rd = [0]
    scanK_vv = {}
    scanK_h1 = {}
    scanK_vp = {}
    scanK_vp2 = {}
    scanK_vy = {}
    crep_ready = {}
    gmul_last = {}
    bank_ev = {6: 0, 7: 0}
    cc_wait = [v_ag1]

    REPL_BASE = [0, 32, 64, 65]

    def repl_mm(e, srcT, g_, jj_, bank):
        if jj_ < 3:
            b0 = REPL_BASE[jj_]
            return e.matmul(p_q[bank][:], ones_sb[b0:b0 + 1, 0:128],
                            srcT[b0:b0 + 1, g_, :], start=True, stop=True)
        return e.matmul(p_q[bank][:], sel_sb[64:66, 0:128],
                        srcT[64:66, g_, :], start=True, stop=True)

    P.emit("vector", lambda e: e.memset(state[:], 0.0))
    P.inc("vector", sV)

    def a_side(k):
        sc, q = k // 4, k % 4
        lc = sc * TC
        hbuf = hTin[k % 2]
        slot = xc_sl[k % NSLOT]
        zslot = sz_sl[k % NSLOT]
        P.wait("sync", sCC, v_ag1)
        if k >= 2:
            P.wait("sync", sT, a_t_done[k - 2])
        shp_ = sH[k % 2]
        shp_.n += 16
        P.emit("sync", lambda e, b=hbuf, q_=q, l=lc, s=shp_: e.dma_start(
            out=b[:],
            in_=hT_all[q_ * D:(q_ + 1) * D, l:l + TC].rearrange("(kk p) t -> p kk t", p=128)
        ).then_inc(s.h, 16))
        v_h = shp_.n
        P.wait("tensor", shp_, v_h)
        P.wait("tensor", sP, w_casts)
        if k == 0:
            P.wait("tensor", sWT, sWT.n)
            P.wait("tensor", sWE, sWE.n)
            P.wait("tensor", sPC, sPC.n)
            P.wait("tensor", sGBp[0], sGBp[0].n)
            P.wait("tensor", sGBp[1], sGBp[1].n)
        if k >= 1:
            P.wait("tensor", sA, a_ev_last[k - 1])
        if k >= NSLOT:
            P.wait("scalar", sV, b_done_v[k - NSLOT])
        ev_after = []
        for m in range(8):
            pp = p_q[m % 2]
            if m >= 2:
                P.wait("tensor", sA, ev_after[m - 2])
            for kk in range(8):
                P.emit("tensor", lambda e, m_=m, kk_=kk, p=pp, hb=hbuf: e.matmul(
                    p[:], w_in_bf[:, kk_, m_ * 128:(m_ + 1) * 128], hb[:, kk_, :],
                    start=(kk_ == 0), stop=(kk_ == 7)))
            vt = P.inc("tensor", sT)
            P.wait("scalar", sT, vt)
            if m < 4:
                P.emit("scalar", lambda e, m_=m, p=pp, s=slot: e.activation(
                    out=s[:, m_, 3:3 + TC], in_=p[:], func=AF.Copy))
            else:
                P.emit("scalar", lambda e, m_=m, p=pp, zs=zslot: e.activation(
                    out=zs[:, m_ - 4, :], in_=p[:], func=AF.Silu))
            ev_after.append(P.inc("scalar", sA))
        a_t_done[k] = sT.n
        a_ev_last[k] = ev_after[7]
        nslot = xc_sl[(k + 1) % NSLOT]
        P.wait("vector", sA, ev_after[3])
        if k == 0:
            P.emit("vector", lambda e, s=slot: e.memset(s[:, :, 0:3], 0.0))
            P.inc("vector", sVS); P.wait("vector", sVS, sVS.n)
        if k + 1 < NCH:
            P.emit("vector", lambda e, s=slot, ns=nslot: e.tensor_copy(
                out=ns[:, :, 0:3], in_=s[:, :, TC:TC + 3]))
            P.inc("vector", sV)
        silu_done = {}
        for a in range(4):
            fin = yred if a % 2 == 0 else cfin
            if a >= 2:
                P.wait("vector", sA, silu_done[a - 2])
            P.emit("vector", lambda e, a_=a, s=slot: e.tensor_scalar(
                out=t1_sb[:], in0=s[:, a_, 0:TC], scalar1=convw_sb[:, a_, 0:1],
                scalar2=None, op0=ALU.mult))
            P.emit("vector", lambda e, a_=a, s=slot, f=fin: e.scalar_tensor_tensor(
                out=f[:], in0=s[:, a_, 1:1 + TC], scalar=convw_sb[:, a_, 1:2],
                in1=t1_sb[:], op0=ALU.mult, op1=ALU.add))
            P.emit("vector", lambda e, a_=a, s=slot, f=fin: e.scalar_tensor_tensor(
                out=t1_sb[:], in0=s[:, a_, 2:2 + TC], scalar=convw_sb[:, a_, 2:3],
                in1=f[:], op0=ALU.mult, op1=ALU.add))
            P.emit("vector", lambda e, a_=a, s=slot, f=fin: e.scalar_tensor_tensor(
                out=f[:], in0=s[:, a_, 3:3 + TC], scalar=convw_sb[:, a_, 3:4],
                in1=t1_sb[:], op0=ALU.mult, op1=ALU.add))
            vv_c = P.inc("vector", sV)
            P.wait("scalar", sV, vv_c)
            P.emit("scalar", lambda e, a_=a, s=slot, f=fin: e.activation(
                out=s[:, a_, 3:3 + TC], in_=f[:], func=AF.Silu, bias=convb_sb[:, a_:a_ + 1]))
            silu_done[a] = P.inc("scalar", sA)
        a_act_done[k] = sA.n
        P.wait("tensor", sA, a_act_done[k])
        P.wait("tensor", sV, last_b_done[0])
        for pp_sl, c0_, c1_ in ((p_q[2][0:64, :], 0, RDT), (p_q[3][0:16, :], RDT, RDT + NST), (p_q[3][32:48, :], RDT + NST, 96)):
            for kk in range(4):
                P.emit("tensor", lambda e, p=pp_sl, kk_=kk, cc0=c0_, cc1=c1_, s=slot: e.matmul(
                    p, xproj_sb[:, kk_, cc0:cc1], s[:, kk_, 3:3 + TC],
                    start=(kk_ == 0), stop=(kk_ == 3)))
        vt_xp = P.inc("tensor", sT)
        P.wait("scalar", sT, vt_xp)
        if k >= 1:
            P.wait("scalar", sXP[a_xpst[k - 1][0]], a_xpst[k - 1][1])
        P.emit("scalar", lambda e: e.activation(out=xpev[0:RDT, :], in_=p_q[2][0:64, :], func=AF.Copy))
        P.emit("scalar", lambda e: e.activation(out=xpev[RDT:RDT + NST, :], in_=p_q[3][0:16, :], func=AF.Copy))
        P.emit("scalar", lambda e: e.activation(out=xpev_c[:], in_=p_q[3][32:48, :], func=AF.Copy))
        va_xpev = P.inc("scalar", sA)
        a_xpev_done[k] = va_xpev
        P.wait("sync", sA, va_xpev)
        j_ar = k // ARG
        sxp_ = sXP[j_ar % 2]
        if j_ar >= 2 and k % ARG == 0:
            P.wait("sync", sCC, ar_done[j_ar - 2])
        sxp_.n += 32
        P.emit("sync", lambda e, j=j_ar, q_=k % ARG, s=sxp_: e.dma_start(
            out=xp_part[j][0:80, q_ * TC:(q_ + 1) * TC], in_=xpev[:]).then_inc(s.h, 16))
        P.emit("sync", lambda e, j=j_ar, q_=k % ARG, s=sxp_: e.dma_start(
            out=xp_part[j][80:96, q_ * TC:(q_ + 1) * TC], in_=xpev_c[:]).then_inc(s.h, 16))
        a_xpst[k] = (j_ar % 2, sxp_.n)
        if k % ARG == ARG - 1:
            a_xp_dma[j_ar] = sxp_.n

    def all_reduce_xp(j):
        P.wait("gpsimd", sXP[j % 2], a_xp_dma[j])
        P.wait("gpsimd", sCC, cc_wait[0])
        sCC.n += 1
        P.emit("gpsimd", lambda e, j_=j: e.collective_compute(
            "AllReduce", ALU.add, replica_groups=GROUPS,
            ins=[xp_part[j_].opt()], outs=[xp_red[j_].opt()]).then_inc(sCC.h, 1))
        ar_done[j] = sCC.n
        cc_wait[0] = sCC.n

    def b_side(k):
        sc = k // 4
        j_ar, qa = k // ARG, k % ARG
        slot = xc_sl[k % NSLOT]
        zslot = sz_sl[k % NSLOT]
        c0, c1 = qa * TC, (qa + 1) * TC
        if k == 0:
            for _eng in ("scalar", "vector", "gpsimd"):
                P.wait(_eng, sWE, sWE.n)
                P.wait(_eng, sGBp[0], sGBp[0].n)
                P.wait(_eng, sGBp[1], sGBp[1].n)
        P.wait("scalar", sCC, ar_done[j_ar])
        P.wait("scalar", sV, b_prev_dtr[0])
        for dst, r0, r1 in ((xpd_in, 0, RDT), (xpb_in, RDT, RDT + NST), (xpc_in, RDT + NST, 96)):
            sBC.n += 16
            P.emit("scalar", lambda e, d=dst, a=r0, b_=r1, j=j_ar: e.dma_start(
                out=d[:], in_=xp_red[j][a:b_, c0:c1]).then_inc(sBC.h, 16))
        v_ld = sBC.n
        P.wait("scalar", sT, bsrc_rd[0])
        for dst, r0 in ((bsrcB, RDT), (bsrcC, RDT + NST)):
            for jj in range(4):
                b0 = REPL_BASE[jj]
                sBR.n += 16
                P.emit("scalar", lambda e, d=dst, rr=r0, jj_=jj, b0_=b0, j=j_ar: e.dma_start(
                    out=d[b0_:b0_ + 1, :, :],
                    in_=xp_red[j][rr:rr + NST, c0:c1].rearrange(
                        "(g jj) t -> jj g t", jj=4)[jj_:jj_ + 1]).then_inc(sBR.h, 16))
        v_bsrc = sBR.n
        first_stat = [True]
        for src_t, rstd_t, nrm in ((xpd_in, rstdd, RDT), (xpb_in, rstdb, NST), (xpc_in, rstdc, NST)):
            P.wait("scalar", sBC, v_ld)
            P.wait("scalar", sT, sT.n)
            P.emit("scalar", lambda e, s=src_t, n=nrm: e.activation(
                out=dtrms[0:n, :], in_=s[:], func=AF.Square))
            va = P.inc("scalar", sA)
            P.wait("tensor", sA, va)
            if first_stat[0]:
                first_stat[0] = False
                P.wait("tensor", sA, op_ev.get(k - 1, 0))
            P.emit("tensor", lambda e, n=nrm: e.matmul(
                p_q[6][0:1, :], ones_sb[0:n, 0:1], dtrms[0:n, :], start=True, stop=True))
            vt = P.inc("tensor", sT)
            P.wait("scalar", sT, vt)
            P.emit("scalar", lambda e, d=rstd_t, n=nrm: e.activation(
                out=d[:], in_=p_q[6][0:1, :], func=AF.Ln, scale=1.0 / n, bias=cst_sb[0:1, 1:2]))
            P.inc("scalar", sAS); P.wait("scalar", sAS, sAS.n)
            P.emit("scalar", lambda e, d=rstd_t: e.activation(
                out=d[:], in_=d[:], func=AF.Exp, scale=-0.5))
            P.inc("scalar", sA)
        va_rs = sA.n
        P.wait("tensor", sA, va_rs)
        P.wait("tensor", sV, b_done_v.get(k - 1, 0))
        P.emit("tensor", lambda e: e.matmul(p_q[5][0:RDT, :], ones_sb[0:1, 0:RDT], rstdd[:], start=True, stop=True))
        vt2 = P.inc("tensor", sT)
        P.wait("vector", sT, vt2)
        P.emit("vector", lambda e: e.scalar_tensor_tensor(
            out=dtrms[:], in0=xpd_in[:], scalar=dtlnw_sb[:], in1=p_q[5][0:RDT, :],
            op0=ALU.mult, op1=ALU.mult))
        vv_dtr = P.inc("vector", sV)
        b_prev_dtr[0] = vv_dtr
        P.wait("tensor", sV, vv_dtr)
        va_es = {}
        for m in range(4):
            dtb_bank = p_q[7 - (m % 2)]
            if m >= 2:
                P.wait("tensor", sA, va_es[m - 2])
            P.emit("tensor", lambda e, m_=m, p=dtb_bank: e.matmul(
                p[:], dtw_sb[:, m_ * 128:(m_ + 1) * 128], dtrms[:], start=True, stop=True))
            vt3 = P.inc("tensor", sT)
            P.wait("scalar", sT, vt3)
            P.emit("scalar", lambda e, m_=m, p=dtb_bank: e.activation(
                out=u_sb[:], in_=p[:], func=AF.Exp, bias=dtb_sb[:, m_:m_ + 1]))
            va_es[m] = P.inc("scalar", sA)
            P.emit("scalar", lambda e, m_=m: e.activation(
                out=dt_sb[:, m_, :], in_=u_sb[:], func=AF.Ln, bias=1.0))
        va_dt = P.inc("scalar", sA)
        P.wait("tensor", sA, va_dt)
        P.wait("tensor", sV, sV.n)
        P.emit("tensor", lambda e: e.matmul(p_q[5][:], ones_sb[0:1, 0:128], rstdb[:], start=True, stop=True))
        vt4 = P.inc("tensor", sT)
        P.wait("vector", sA, va_dt)
        P.wait("vector", sT, vt4)
        P.wait("vector", sA, a_act_done[k])
        for a in range(4):
            P.emit("vector", lambda e, a_=a, s=slot: e.tensor_mul(
                out=u_sb[:], in0=dt_sb[:, a_, :], in1=s[:, a_, 3:3 + TC]))
            P.emit("vector", lambda e, a_=a: e.tensor_mul(
                out=up_sb[:, a_, :], in0=u_sb[:], in1=p_q[5][:]))
        vv_up = P.inc("vector", sV)
        P.wait("tensor", sV, vv_up)
        P.emit("tensor", lambda e: e.matmul(p_q[5][:], ones_sb[0:1, 0:128], rstdc[:], start=True, stop=True))
        vt_c = P.inc("tensor", sT)
        P.wait("scalar", sT, vt_c)
        P.emit("scalar", lambda e: e.activation(out=rcrep_sb[:], in_=p_q[5][:], func=AF.Copy))
        va_rc = P.inc("scalar", sA)

        vB = {}
        vbmul_last = {}

        def repl_C(g):
            fg = k * 4 + g
            if fg >= 2:
                P.wait("scalar", sP, gmul_last.get(fg - 2, 0))
            for jj in range(4):
                bank = 6 + (jj % 2)
                P.wait("tensor", sA, bank_ev[bank])
                P.emit("tensor", lambda e, g_=g, jj_=jj, b_=bank: repl_mm(e, bsrcC, g_, jj_, b_))
                vt_r = P.inc("tensor", sT)
                P.wait("scalar", sT, vt_r)
                P.emit("scalar", lambda e, g_=g, jj_=jj, b_=bank: e.activation(
                    out=crep_sl[g_ % 2][:, jj_, :], in_=p_q[b_][:], func=AF.Copy))
                bank_ev[bank] = P.inc("scalar", sA)
            crep_ready[fg] = bank_ev[7]

        def repl_B(g):
            if g >= 1:
                P.wait("tensor", sV, vbmul_last[g - 1])
            else:
                P.wait("tensor", sV, b_done_v.get(k - 1, 0))
                P.wait("tensor", sBR, v_bsrc)
                P.wait("tensor", sA, va_rc)
                P.wait("tensor", sA, a_xpev_done[(k // 2) * 2 + 1])
            for jj in range(4):
                P.emit("tensor", lambda e, g_=g, jj_=jj: repl_mm(e, bsrcB, g_, jj_, 2 + jj_))
            vB[g] = P.inc("tensor", sT)

        def emit_reduce(g, a):
            K = k * 16 + g * 4 + a
            P.wait("vector", sP, scanK_vp2[K])
            if g == 0:
                P.emit("vector", lambda e, a_=a: e.tensor_add(
                    out=yacc4[:, a_, :], in0=G_b[:, 0, :], in1=G_b[:, 2, :]))
            else:
                P.emit("vector", lambda e, a_=a: e.tensor_add(
                    out=yacc4[:, a_, :], in0=yacc4[:, a_, :], in1=G_b[:, 0, :]))
                P.emit("vector", lambda e, a_=a: e.tensor_add(
                    out=yacc4[:, a_, :], in0=yacc4[:, a_, :], in1=G_b[:, 2, :]))
            scanK_vy[K] = P.inc("vector", sV)

        P.wait("tensor", sBR, v_bsrc)
        P.wait("tensor", sA, va_dt)
        repl_C(0)
        repl_B(0)
        pend = [None]
        for g in range(4):
            for a in range(4):
                K = k * 16 + g * 4 + a
                dAb = dA_b
                P.wait("scalar", sV, scanK_h1.get(K - 1, 0))
                if K == 0:
                    P.wait("scalar", sV, w_small_v)
                for j in range(2):
                    P.emit("scalar", lambda e, a_=a, g_=g, j_=j, d=dAb: e.activation(
                        out=d[:, j_, :], in_=dt_sb[:, a_, :], func=AF.Exp,
                        scale=A_sb[:, a_, g_ * 4 + j_:g_ * 4 + j_ + 1]))
                P.wait("scalar", sV, scanK_vv.get(K - 1, 0))
                for j in range(2, 4):
                    P.emit("scalar", lambda e, a_=a, g_=g, j_=j, d=dAb: e.activation(
                        out=d[:, j_, :], in_=dt_sb[:, a_, :], func=AF.Exp,
                        scale=A_sb[:, a_, g_ * 4 + j_:g_ * 4 + j_ + 1]))
                va_x = P.inc("scalar", sA)
                P.wait("vector", sT, vB[g])
                for j in range(4):
                    P.emit("vector", lambda e, j_=j, g_=g, a_=a: e.scalar_tensor_tensor(
                        out=bb_b[:, j_, :], in0=p_q[2 + j_][:],
                        scalar=clnw_eff[:, g_ * 4 + j_:g_ * 4 + j_ + 1],
                        in1=up_sb[:, a_, :], op0=ALU.mult, op1=ALU.mult))
                vbm = P.inc("vector", sV)
                if a == 3:
                    vbmul_last[g] = vbm
                Hb = H_sl[K % 2]
                P.wait("vector", sA, va_x)
                P.wait("vector", sP, scanK_vp.get(K - 2, 0))
                for j in range(4):
                    P.emit("vector", lambda e, a_=a, g_=g, j_=j, d=dAb, h=Hb: e.tensor_tensor_scan(
                        out=h[:, j_, :], data0=d[:, j_, :], data1=bb_b[:, j_, :],
                        initial=state[:, a_, g_ * 4 + j_:g_ * 4 + j_ + 1],
                        op0=ALU.mult, op1=ALU.add))
                    if j == 1:
                        scanK_h1[K] = P.inc("vector", sV)
                P.inc("vector", sVS); P.wait("vector", sVS, sVS.n)
                P.emit("vector", lambda e, a_=a, g_=g, h=Hb: e.tensor_copy(
                    out=state[:, a_, g_ * 4:g_ * 4 + 4], in_=h[:, :, TC - 1]))
                scanK_vv[K] = P.inc("vector", sV)
                if pend[0] is not None:
                    emit_reduce(*pend[0])
                pend[0] = (g, a)
                P.wait("gpsimd", sV, scanK_vv[K])
                if a == 0:
                    P.wait("gpsimd", sA, crep_ready[k * 4 + g])
                P.wait("gpsimd", sV, scanK_vy.get(K - 1, 0))
                P.emit("gpsimd", lambda e, g_=g, h=Hb: e.tensor_mul(
                    out=G_b[:], in0=crep_sl[g_ % 2][:], in1=h[:]))
                scanK_vp[K] = P.inc("gpsimd", sP)
                P.emit("gpsimd", lambda e: e.tensor_add(
                    out=G_b[:, 0, :], in0=G_b[:, 0, :], in1=G_b[:, 1, :]))
                P.emit("gpsimd", lambda e: e.tensor_add(
                    out=G_b[:, 2, :], in0=G_b[:, 2, :], in1=G_b[:, 3, :]))
                scanK_vp2[K] = P.inc("gpsimd", sP)
                gmul_last[k * 4 + g] = scanK_vp2[K]
                if a == 3 and g < 3:
                    repl_C(g + 1)
                    repl_B(g + 1)
        emit_reduce(*pend[0])
        bsrc_rd[0] = vB[3]
        P.wait("vector", sA, va_rc)
        for a in range(4):
            P.emit("vector", lambda e, a_=a: e.tensor_mul(
                out=t1_sb[:], in0=yacc4[:, a_, :], in1=rcrep_sb[:]))
            P.emit("vector", lambda e, a_=a, zs=zslot: e.tensor_copy(
                out=yred[:], in_=zs[:, a_, :]))
            P.emit("vector", lambda e, a_=a, s=slot: e.scalar_tensor_tensor(
                out=t1_sb[:], in0=s[:, a_, 3:3 + TC], scalar=dskip_sb[:, a_:a_ + 1],
                in1=t1_sb[:], op0=ALU.mult, op1=ALU.add))
            P.emit("vector", lambda e, a_=a: e.tensor_mul(
                out=yg_bf[:, a_, :], in0=t1_sb[:], in1=yred[:]))
            P.inc("vector", sV)
        b_done_v[k] = sV.n
        last_b_done[0] = b_done_v[k]
        P.wait("tensor", sV, b_done_v[k])
        if k >= 1:
            P.wait("tensor", sA, op_ev.get(k - 1, 0))
        P.wait("tensor", sA, bank_ev[6])
        P.wait("tensor", sA, bank_ev[7])
        op_ev_h = {}
        sc_par = sc % 2
        if k % 4 == 0 and sc >= 2:
            P.wait("sync", sCC, rs_done[sc - 2])
        for tt in range(4):
            P.wait("scalar", sOP[op_st_last[0]], op_st_last[1])
            for h in range(2):
                po = p_q[6 + h]
                if tt >= 1:
                    P.wait("tensor", sA, op_ev_h[(tt - 1, h)])
                for a in range(4):
                    P.emit("tensor", lambda e, tt_=tt, h_=h, a_=a, p=po: e.matmul(
                        p[:], yg_bf[:, a_, tt_ * 128:(tt_ + 1) * 128],
                        wout_bf[:, a_, h_ * 512:(h_ + 1) * 512],
                        start=(a_ == 0), stop=(a_ == 3)))
                vt_o = P.inc("tensor", sT)
                P.wait("scalar", sT, vt_o)
                P.emit("scalar", lambda e, h_=h, p=po: e.activation(
                    out=oo_sb[:, h_ * 512:(h_ + 1) * 512], in_=p[:], func=AF.Copy))
                op_ev_h[(tt, h)] = P.inc("scalar", sA)
            va_o = op_ev_h[(tt, 1)]
            P.wait("sync", sA, va_o)
            sOP[sc_par].n += 16
            P.emit("sync", lambda e, k_=k, tt_=tt, s=sOP[sc_par]: e.dma_start(
                out=op_part[k_ * TC + tt_ * 128:k_ * TC + (tt_ + 1) * 128, :],
                in_=oo_sb[:]).then_inc(s.h, 16))
            op_st_last[0] = sc_par
            op_st_last[1] = sOP[sc_par].n
        op_ev[k] = sA.n
        bank_ev[6] = op_ev[k]
        bank_ev[7] = op_ev[k]
        op_dma[k // 4] = sOP[sc_par].n

    def rs1(sc):
        P.wait("gpsimd", sOP[sc % 2], op_dma[sc])
        P.wait("gpsimd", sCC, cc_wait[0])
        sCC.n += 1
        P.emit("gpsimd", lambda e, s=sc: e.collective_compute(
            "ReduceScatter", ALU.add, replica_groups=GROUPS,
            ins=[op_part[s * 4 * TC:(s + 1) * 4 * TC, :].opt()],
            outs=[rs1_out[s * 512:(s + 1) * 512, :].opt()]).then_inc(sCC.h, 1))
        cc_wait[0] = sCC.n
        return sCC.n

    rs_done = {}
    for jp in range(NCH // 2):
        k0 = 2 * jp
        a_side(k0)
        all_reduce_xp(k0)
        a_side(k0 + 1)
        all_reduce_xp(k0 + 1)
        b_side(k0)
        b_side(k0 + 1)
        if (k0 + 1) % 4 == 3:
            rs_done[(k0 + 1) // 4] = rs1((k0 + 1) // 4)


    esS.close()
    es1.close()
    for _e in ENGS:
        P.wait(_e, sCC, rs_done[NSC - 1])
        P.wait(_e, sT, sT.n)
        P.wait(_e, sA, sA.n)
        P.wait(_e, sV, sV.n)
        P.wait(_e, sP, sP.n)
    va_sgl = {}
    vv_ff = {}
    t_d = {}
    vv_dev = 0
    es2a = ExitStack()
    stream_fm = es2a.enter_context(nc.sbuf_tensor("stream_fm", [128, 8, TQ], F32))
    identf_sb = es2a.enter_context(nc.sbuf_tensor("identf_sb", [128, 128], F32))
    es2 = ExitStack()
    E2 = lambda t: es2.enter_context(t)
    x_tm = E2(nc.sbuf_tensor("x_tm", [128, 4, D], F32))
    mo_tm = E2(nc.sbuf_tensor("mo_tm", [128, 4, D], F32))
    ln1b = E2(nc.sbuf_tensor("ln1b", [128, D], F32))
    sq2 = E2(nc.sbuf_tensor("sq2", [128, D], F32))
    st2 = E2(nc.sbuf_tensor("st2", [128, 8], F32))
    pmw_rep = E2(nc.sbuf_tensor("pmw_rep", [128, D], F32))
    hn_tm = E2(nc.sbuf_tensor("hn_tm", [128, 4, D], BF16))
    hn_fm = E2(nc.sbuf_tensor("hn_fm", [128, 8, TC], BF16))
    ff_sb = E2(nc.sbuf_tensor("ff_sb", [128, 32, TC], BF16))
    gw_t = [E2(nc.sbuf_tensor(f"gw{i}", [128, 8, 512], BF16)) for i in range(2)]
    uw_t = [E2(nc.sbuf_tensor(f"uw{i}", [128, 8, 512], BF16)) for i in range(2)]
    dw_t = [E2(nc.sbuf_tensor(f"dw{i}", [128, TC], BF16)) for i in range(2)]
    sgl = E2(nc.sbuf_tensor("sgl", [128, TC], F32))
    p2_tr = [E2(nc.psum_tensor(f"p2_tr{i}", [128, 128], BF16)) for i in range(2)]
    p2_g = E2(nc.psum_tensor("p2_g", [128, TC], F32))
    p2_u = E2(nc.psum_tensor("p2_u", [128, TC], F32))
    p2_d = [E2(nc.psum_tensor(f"p2_d{i}", [128, TC], F32)) for i in range(4)]

    sW.n += 48
    P.emit("sync", lambda e: e.dma_start(out=identf_sb[:], in_=ident_f).then_inc(sW.h, 16))
    P.emit("sync", lambda e: e.dma_start(out=pmw_rep[:], in_=pmw.broadcast_to([128, D])).then_inc(sW.h, 16))
    P.emit("sync", lambda e: e.dma_start(out=ident_sb[:], in_=ident_bf).then_inc(sW.h, 16))
    v_w2 = sW.n

    def phase2(sc):
        nonlocal vv_dev
        L = sc * 512
        ev_tr = p2_evtr
        va_hn = {}
        for tt in range(4):
            sld = sRS if tt % 2 == 0 else sRS2
            if tt == 0:
                P.wait("sync", sA, p2_trans_prev[0])
            sld.n += 32
            P.emit("sync", lambda e, t_=tt, s=sld: e.dma_start(
                out=x_tm[:, t_, :], in_=x_q[L + t_ * 128:L + (t_ + 1) * 128, :]).then_inc(s.h, 16))
            P.emit("sync", lambda e, t_=tt, s=sld: e.dma_start(
                out=mo_tm[:, t_, :], in_=rs1_out[L + t_ * 128:L + (t_ + 1) * 128, :]).then_inc(s.h, 16))
            v_ld = sld.n
            P.wait("vector", sld, v_ld)
            if tt == 0:
                P.wait("vector", sA, p2_trans_prev[0])
            P.emit("vector", lambda e, t_=tt: e.reduce_sum(out=st2[:, 0:1], in_=x_tm[:, t_, :], axis=AX.X))
            P.inc("vector", sVS); P.wait("vector", sVS, sVS.n)
            P.wait("scalar", sld, v_ld)
            P.wait("scalar", sV, p2_hn_prev[0])
            P.emit("scalar", lambda e, t_=tt: e.activation(
                out=sq2[:], in_=x_tm[:, t_, :], func=AF.Square, accum_out=st2[:, 1:2]))
            va_sq = P.inc("scalar", sA)
            P.wait("vector", sA, va_sq)
            P.emit("vector", lambda e: e.tensor_scalar_mul(out=st2[:, 2:3], in0=st2[:, 0:1], scalar1=1.0 / D))
            P.emit("vector", lambda e: e.tensor_scalar_mul(out=st2[:, 3:4], in0=st2[:, 1:2], scalar1=1.0 / D))
            P.inc("vector", sVS); P.wait("vector", sVS, sVS.n)
            P.emit("vector", lambda e: e.tensor_mul(out=st2[:, 4:5], in0=st2[:, 2:3], in1=st2[:, 2:3]))
            P.inc("vector", sVS); P.wait("vector", sVS, sVS.n)
            P.emit("vector", lambda e: e.tensor_sub(out=st2[:, 3:4], in0=st2[:, 3:4], in1=st2[:, 4:5]))
            vv_st = P.inc("vector", sV)
            P.wait("scalar", sV, vv_st)
            P.emit("scalar", lambda e: e.activation(out=st2[:, 5:6], in_=st2[:, 3:4], func=AF.Ln, bias=cst_sb[:, 0:1]))
            P.inc("scalar", sAS); P.wait("scalar", sAS, sAS.n)
            P.emit("scalar", lambda e: e.activation(out=st2[:, 5:6], in_=st2[:, 5:6], func=AF.Exp, scale=-0.5))
            va_r = P.inc("scalar", sA)
            P.wait("scalar", sA, va_r)
            P.wait("vector", sA, va_r)
            P.emit("vector", lambda e: e.tensor_mul(out=st2[:, 6:7], in0=st2[:, 2:3], in1=st2[:, 5:6]))
            P.inc("vector", sVS); P.wait("vector", sVS, sVS.n)
            P.emit("vector", lambda e: e.tensor_scalar_mul(out=st2[:, 6:7], in0=st2[:, 6:7], scalar1=-1.0))
            vv_nm = P.inc("vector", sV)
            P.wait("scalar", sV, vv_nm)
            P.emit("scalar", lambda e, t_=tt: e.activation(
                out=ln1b[:], in_=x_tm[:, t_, :], func=AF.Identity, scale=st2[:, 5:6], bias=st2[:, 6:7]))
            va_ln1 = P.inc("scalar", sA)
            P.wait("vector", sA, va_ln1)
            P.emit("vector", lambda e, t_=tt: e.tensor_add(out=mo_tm[:, t_, :], in0=ln1b[:], in1=mo_tm[:, t_, :]))
            P.inc("vector", sVS); P.wait("vector", sVS, sVS.n)
            P.emit("vector", lambda e, t_=tt: e.tensor_add(out=x_tm[:, t_, :], in0=x_tm[:, t_, :], in1=mo_tm[:, t_, :]))
            vv_s1 = P.inc("vector", sV)
            P.wait("scalar", sV, vv_s1)
            P.emit("scalar", lambda e, t_=tt: e.activation(
                out=sq2[:], in_=mo_tm[:, t_, :], func=AF.Square, accum_out=st2[:, 7:8]))
            P.inc("scalar", sAS); P.wait("scalar", sAS, sAS.n)
            P.emit("scalar", lambda e: e.activation(out=st2[:, 7:8], in_=st2[:, 7:8], func=AF.Ln, scale=1.0 / D, bias=cst_sb[:, 1:2]))
            P.inc("scalar", sAS); P.wait("scalar", sAS, sAS.n)
            P.emit("scalar", lambda e: e.activation(out=st2[:, 7:8], in_=st2[:, 7:8], func=AF.Exp, scale=-0.5))
            va_r3 = P.inc("scalar", sA)
            P.wait("vector", sA, va_r3)
            P.emit("vector", lambda e, t_=tt: e.scalar_tensor_tensor(
                out=hn_tm[:, t_, :], in0=mo_tm[:, t_, :], scalar=st2[:, 7:8], in1=pmw_rep[:],
                op0=ALU.mult, op1=ALU.mult))
            va_hn[tt] = P.inc("vector", sV)
            p2_hn_prev[0] = va_hn[tt]
        P.wait("tensor", sW, v_w2)
        P.wait("tensor", sV, va_hn[3])
        P.wait("tensor", sT, t_gu.get((sc - 1, 31), 0))
        slot = 0
        first_hn = [True]
        for tt in range(4):
            for kk in range(8):
                P.wait("tensor", sA, ev_tr.get(slot, 0))
                P.emit("tensor", lambda e, t_=tt, k_=kk, sl=slot: e.transpose(
                    p2_tr[sl][:], hn_tm[:, t_, k_ * 128:(k_ + 1) * 128], ident_sb[:]))
                vt = P.inc("tensor", sT)
                P.wait("scalar", sT, vt)
                if first_hn[0]:
                    first_hn[0] = False
                    P.wait("scalar", sT, t_gu.get((sc - 1, 31), 0))
                P.emit("scalar", lambda e, t_=tt, k_=kk, sl=slot: e.activation(
                    out=hn_fm[:, k_, t_ * 128:(t_ + 1) * 128], in_=p2_tr[sl][:], func=AF.Copy))
                ev_tr[slot] = P.inc("scalar", sA)
                slot = (slot + 1) % 2
        slotf = 4
        P.wait("tensor", sV, vv_dev)
        for tt in range(4):
            for kk in range(8):
                P.wait("tensor", sA, ev_tr.get(slotf, 0))
                P.emit("tensor", lambda e, t_=tt, k_=kk, sl=slotf - 4: e.transpose(
                    p2_d[sl][:, 0:128], x_tm[:, t_, k_ * 128:(k_ + 1) * 128], identf_sb[:]))
                vt = P.inc("tensor", sT)
                P.wait("scalar", sT, vt)
                P.emit("scalar", lambda e, t_=tt, k_=kk, sl=slotf - 4: e.activation(
                    out=stream_fm[:, k_, L + t_ * 128:L + (t_ + 1) * 128], in_=p2_d[sl][:, 0:128], func=AF.Copy))
                ev_tr[slotf] = P.inc("scalar", sA)
                slotf = 4 + ((slotf - 4 + 1) % 2)
        va_trans = sA.n
        p2_trans_prev[0] = va_trans
        for f in range(32):
            f4, fo = f // 4, (f % 4) * 128
            pf = (f // 4) % 2
            sgw = sG2 if pf == 0 else sG3
            if f % 4 == 0:
                if f4 >= 2:
                    P.wait("sync", sT, t_gu[(sc, (f4 - 2) * 4 + 3)])
                elif sc >= 1:
                    P.wait("sync", sT, t_gu[(sc - 1, (6 + f4) * 4 + 3)])
                P.wait("sync", sGBp[0], sGBp[0].n)
                P.wait("sync", sGBp[1], sGBp[1].n)
                sgw.n += 32
                P.emit("sync", lambda e, f_=f4, p=pf, s=sgw: e.dma_start(
                    out=gw_t[p][:], in_=gate_bf[:, f_ * 512:(f_ + 1) * 512].rearrange("(kk p) m -> p kk m", p=128)).then_inc(s.h, 16))
                P.emit("sync", lambda e, f_=f4, p=pf, s=sgw: e.dma_start(
                    out=uw_t[p][:], in_=up_bf[:, f_ * 512:(f_ + 1) * 512].rearrange("(kk p) m -> p kk m", p=128)).then_inc(s.h, 16))
                gld[(sc, f4)] = sgw.n
            P.wait("tensor", sgw, gld[(sc, f4)])
            P.wait("tensor", sA, va_trans)
            if f >= 1:
                P.wait("tensor", sV, vv_ff[f - 1])
                P.wait("tensor", sA, va_sgl[f - 1])
            elif sc >= 1:
                P.wait("tensor", sV, vv_ff.get(31, 0))
                P.wait("tensor", sA, va_sgl.get(31, 0))
            for kk in range(8):
                P.emit("tensor", lambda e, k_=kk, p=pf, o=fo: e.matmul(
                    p2_g[:], gw_t[p][:, k_, o:o + 128], hn_fm[:, k_, :], start=(k_ == 0), stop=(k_ == 7)))
            for kk in range(8):
                P.emit("tensor", lambda e, k_=kk, p=pf, o=fo: e.matmul(
                    p2_u[:], uw_t[p][:, k_, o:o + 128], hn_fm[:, k_, :], start=(k_ == 0), stop=(k_ == 7)))
            vt_gu = P.inc("tensor", sT)
            t_gu[(sc, f)] = vt_gu
            P.wait("scalar", sT, vt_gu)
            P.emit("scalar", lambda e: e.activation(out=sgl[:], in_=p2_g[:], func=AF.Silu))
            va_sgl[f] = P.inc("scalar", sA)
            P.wait("vector", sA, va_sgl[f])
            if f == 0:
                P.wait("vector", sT, t_d.get((1, 31), 0))
            P.emit("vector", lambda e, f_=f: e.tensor_mul(out=ff_sb[:, f_, :], in0=sgl[:], in1=p2_u[:]))
            vv_ff[f] = P.inc("vector", sV)
        ev_d = sA.n
        for ch in range(2):
            for f in range(32):
                pf = f % 2
                sgw = sG2 if pf == 0 else sG3
                if not (ch == 0 and f < 2):
                    P.wait("sync", sT, t_d.get((ch, f - 2), t_d.get((ch - 1, 30 + f), sT.n if False else t_gu[31] if f < 2 else t_d[(ch, f - 2)])) ) if False else None
                if ch > 0 or f >= 2:
                    prev = (ch, f - 2) if f >= 2 else (ch - 1, 30 + f)
                    P.wait("sync", sT, t_d[prev])
                elif sc >= 1:
                    P.wait("sync", sT, t_d.get((1, 30 + f), 0))
                sgw.n += 16
                P.emit("sync", lambda e, f_=f, c_=ch, p=pf, s=sgw: e.dma_start(
                    out=dw_t[p][:], in_=down_bf[f_ * 128:(f_ + 1) * 128, c_ * 512:(c_ + 1) * 512]).then_inc(s.h, 16))
                P.wait("tensor", sgw, sgw.n)
                P.wait("tensor", sV, vv_ff[31])
                if f == 0:
                    P.wait("tensor", sV, vv_dev)
                for ct in range(4):
                    P.emit("tensor", lambda e, c2=ct, p=pf, f_=f: e.matmul(
                        p2_d[c2][:], dw_t[p][:, c2 * 128:(c2 + 1) * 128], ff_sb[:, f_, :],
                        start=(f_ == 0), stop=(f_ == 31)))
                t_d[(ch, f)] = P.inc("tensor", sT)
            P.wait("vector", sT, t_d[(ch, 31)])
            for ct in range(4):
                P.emit("vector", lambda e, c_=ch, c2=ct: e.tensor_add(
                    out=stream_fm[:, c_ * 4 + c2, L:L + 512],
                    in0=stream_fm[:, c_ * 4 + c2, L:L + 512], in1=p2_d[c2][:]))
            vv_dev = P.inc("vector", sV)
        return sA.n, sV.n

    p2_evtr = {}
    p2_trans_prev = [0]
    p2_hn_prev = [0]
    gld = {}
    t_gu = {}
    p2_done = []
    for sc in range(NSC):
        p2_done.append(phase2(sc))


    def ffn_phase(which, dbg_out=None):
        W_ = which
        """which in (0, 1): first or second application."""
        es3 = ExitStack()
        E3 = lambda t: es3.enter_context(t)
        m_sb = E3(nc.sbuf_tensor(f"f_m{W_}", [1, TQ], F32))
        r_std = E3(nc.sbuf_tensor(f"f_rstd{W_}", [1, TQ], F32))
        mr_sb = E3(nc.sbuf_tensor(f"f_mr{W_}", [1, 512], F32))
        stat_ev = E3(nc.sbuf_tensor(f"f_stat{W_}", [1, 512], F32))
        ln2_fm = E3(nc.sbuf_tensor(f"f_ln2{W_}", [128, 8, 512], F32))
        d_fm = E3(nc.sbuf_tensor(f"f_d{W_}", [128, 8, 512], F32))
        xr_bf = E3(nc.sbuf_tensor(f"f_xr{W_}", [128, 8, 512], BF16))
        xk_bf = E3(nc.sbuf_tensor(f"f_xk{W_}", [128, 8, 512], BF16))
        r_sb = E3(nc.sbuf_tensor(f"f_r{W_}", [128, 8, 512], BF16))
        k2_bf = E3(nc.sbuf_tensor(f"f_k2{W_}", [128, 8, 512], BF16))
        kr_sb = E3(nc.sbuf_tensor(f"f_kr{W_}", [128, 512], F32))
        mur_sb = E3(nc.sbuf_tensor(f"f_mur{W_}", [128, 8], F32))
        muk_sb = E3(nc.sbuf_tensor(f"f_muk{W_}", [128, 8], F32))
        sel_sb = E3(nc.sbuf_tensor(f"f_sel{W_}", [16, 4], BF16))
        bndrow = E3(nc.sbuf_tensor(f"f_bndrow{W_}", [8, 128], BF16))
        bcol_fm = E3(nc.sbuf_tensor(f"f_bcol{W_}", [128, 8], BF16))
        rep_ev = E3(nc.sbuf_tensor(f"f_repev{W_}", [128, 2], F32))
        bnd_sb2 = E3(nc.sbuf_tensor(f"f_bnd2{W_}", [16, D], BF16))
        prev_sb = E3(nc.sbuf_tensor(f"f_prev{W_}", [128, D], BF16))
        prevf_sb = E3(nc.sbuf_tensor(f"f_prevf{W_}", [128, 8, 4], F32))
        t2_s = E3(nc.sbuf_tensor(f"f_t2{W_}", [128, 512], F32))
        wr_a = E3(nc.sbuf_tensor(f"f_wr{W_}", [128, 8, D], BF16))
        wk_a = E3(nc.sbuf_tensor(f"f_wk{W_}", [128, 8, D], BF16))
        wv_a = E3(nc.sbuf_tensor(f"f_wv{W_}", [128, 8, D], BF16))
        p3_s = E3(nc.psum_tensor(f"p3_s{W_}", [1, 512], F32))
        p3_rep = [E3(nc.psum_tensor(f"p3_rep{W_}_{i}", [128, 512], F32)) for i in range(2)]
        p3_r = E3(nc.psum_tensor(f"p3_r{W_}", [128, 512], F32))
        p3_k = E3(nc.psum_tensor(f"p3_k{W_}", [128, 512], F32))
        p3_v = E3(nc.psum_tensor(f"p3_v{W_}", [128, 512], F32))
        p3_t = E3(nc.psum_tensor(f"p3_t{W_}", [128, 128], BF16))

        sW.n += 48
        P.emit("sync", lambda e: e.dma_start(out=mur_sb[:], in_=mu_r).then_inc(sW.h, 16))
        P.emit("sync", lambda e: e.dma_start(out=muk_sb[:], in_=mu_k).then_inc(sW.h, 16))
        P.emit("sync", lambda e: e.dma_start(out=sel_sb[:], in_=selp).then_inc(sW.h, 16))
        v_l3 = sW.n
        P.wait("sync", sGBp[0], sGBp[0].n)
        P.wait("sync", sGBp[1], sGBp[1].n)
        sG2.n += 48
        for wsrc, wdst in ((wrT_bf, wr_a), (wkT_bf, wk_a), (wvT_bf, wv_a)):
            P.emit("sync", lambda e, s=wsrc, d=wdst: e.dma_start(
                out=d[:], in_=s.rearrange("(kk p) c -> p kk c", p=128)).then_inc(sG2.h, 16))
        w3_ld = [sG2.n]
        P.wait("vector", sW, v_l3)
        P.emit("vector", lambda e: e.tensor_scalar(
            out=mur_sb[:], in0=mur_sb[:], scalar1=1.0, scalar2=None, op0=ALU.mult))
        P.inc("vector", sVS); P.wait("vector", sVS, sVS.n)

        bnd_st = 0
        bcol_ev = [0]
        for sc in range(NSC):
            L = sc * 512
            P.wait("scalar", sV, sV.n)
            P.emit("scalar", lambda e, L_=L: e.activation(
                out=d_fm[:], in_=stream_fm[:, :, L_:L_ + 512], func=AF.Square))
            va_sq3 = P.inc("scalar", sA)
            P.wait("tensor", sA, va_sq3)
            P.wait("tensor", sW, v_l3)
            for kk in range(8):
                P.emit("tensor", lambda e, k_=kk, L_=L: e.matmul(
                    p3_s[:], ones_sb[0:128, 0:1], stream_fm[:, k_, L_:L_ + 512],
                    start=(k_ == 0), stop=(k_ == 7)))
            vt_s1 = P.inc("tensor", sT)
            P.wait("scalar", sT, vt_s1)
            P.emit("scalar", lambda e, L_=L: e.activation(out=m_sb[0:1, L_:L_ + 512], in_=p3_s[:], func=AF.Copy))
            va_s1 = P.inc("scalar", sA)
            P.wait("tensor", sA, va_s1)
            for kk in range(8):
                P.emit("tensor", lambda e, k_=kk: e.matmul(
                    p3_s[:], ones_sb[0:128, 0:1], d_fm[:, k_, :],
                    start=(k_ == 0), stop=(k_ == 7)))
            vt_s2 = P.inc("tensor", sT)
            P.wait("scalar", sT, vt_s2)
            P.emit("scalar", lambda e: e.activation(out=stat_ev[0:1, :], in_=p3_s[:], func=AF.Copy))
            va_s2 = P.inc("scalar", sA)
            P.wait("vector", sA, va_s2)
            P.emit("vector", lambda e, L_=L: e.tensor_scalar_mul(
                out=m_sb[0:1, L_:L_ + 512], in0=m_sb[0:1, L_:L_ + 512], scalar1=1.0 / D))
            P.inc("vector", sVS); P.wait("vector", sVS, sVS.n)
            P.emit("vector", lambda e, L_=L: e.tensor_mul(
                out=mr_sb[0:1, :], in0=m_sb[0:1, L_:L_ + 512], in1=m_sb[0:1, L_:L_ + 512]))
            P.inc("vector", sVS); P.wait("vector", sVS, sVS.n)
            P.emit("vector", lambda e: e.tensor_scalar_mul(
                out=stat_ev[0:1, :], in0=stat_ev[0:1, :], scalar1=1.0 / D))
            P.inc("vector", sVS); P.wait("vector", sVS, sVS.n)
            P.emit("vector", lambda e: e.tensor_sub(
                out=stat_ev[0:1, :], in0=stat_ev[0:1, :], in1=mr_sb[0:1, :]))
            vv_var = P.inc("vector", sV)
            P.wait("scalar", sV, vv_var)
            P.emit("scalar", lambda e, L_=L: e.activation(
                out=r_std[0:1, L_:L_ + 512], in_=stat_ev[0:1, :], func=AF.Ln, bias=cst_sb[0:1, 0:1]))
            P.inc("scalar", sAS); P.wait("scalar", sAS, sAS.n)
            P.emit("scalar", lambda e, L_=L: e.activation(
                out=r_std[0:1, L_:L_ + 512], in_=r_std[0:1, L_:L_ + 512], func=AF.Exp, scale=-0.5))
            va_rstd = P.inc("scalar", sA)
            P.wait("tensor", sA, va_rstd)
            P.emit("tensor", lambda e, L_=L: e.matmul(
                p3_rep[0][:, 0:1], ones_sb[0:1, 0:128], r_std[0:1, L_ + 511:L_ + 512], start=True, stop=True))
            P.emit("tensor", lambda e, L_=L: e.matmul(
                p3_rep[1][:, 0:1], ones_sb[0:1, 0:128], m_sb[0:1, L_ + 511:L_ + 512], start=True, stop=True))
            vt_rep = P.inc("tensor", sT)
            P.wait("scalar", sT, vt_rep)
            P.emit("scalar", lambda e: e.activation(out=rep_ev[:, 0:1], in_=p3_rep[0][:, 0:1], func=AF.Copy))
            P.emit("scalar", lambda e: e.activation(out=rep_ev[:, 1:2], in_=p3_rep[1][:, 0:1], func=AF.Copy))
            va_repev = P.inc("scalar", sA)
            P.wait("vector", sA, va_repev)
            P.emit("vector", lambda e: e.tensor_mul(out=rep_ev[:, 1:2], in0=rep_ev[:, 1:2], in1=rep_ev[:, 0:1]))
            P.inc("vector", sVS); P.wait("vector", sVS, sVS.n)
            P.emit("vector", lambda e, L_=L: e.scalar_tensor_tensor(
                out=bcol_fm[:], in0=stream_fm[:, :, L_ + 511], scalar=rep_ev[:, 0:1],
                in1=rep_ev[:, 1:2].broadcast_to([128, 8]), op0=ALU.mult, op1=ALU.subtract))
            vv_bcol = P.inc("vector", sV)
            P.wait("tensor", sV, vv_bcol)
            P.wait("tensor", sA, bcol_ev[0])
            P.emit("tensor", lambda e: e.transpose(p3_t[0:8, 0:128], bcol_fm[:], ident_sb[:]))
            vt_b = P.inc("tensor", sT)
            P.wait("scalar", sT, vt_b)
            P.emit("scalar", lambda e: e.activation(out=bndrow[:], in_=p3_t[0:8, 0:128], func=AF.Copy))
            va_bnd = P.inc("scalar", sA)
            bcol_ev[0] = va_bnd
            P.wait("sync", sA, va_bnd)
            sDBG.n += 16
            P.emit("sync", lambda e, s=sc, w=which: e.dma_start(
                out=bnd_my[w][s, :].rearrange("(a c) -> a c", a=8), in_=bndrow[:]).then_inc(sDBG.h, 16))
        bnd_st = sDBG.n
        P.wait("gpsimd", sDBG, bnd_st)
        sCC.n += 1
        P.emit("gpsimd", lambda e, w=which: e.collective_compute(
            "AllGather", ALU.bypass, replica_groups=GROUPS,
            ins=[bnd_my[w].opt()], outs=[bnd_all[w].opt()]).then_inc(sCC.h, 1))
        v_agb = sCC.n
        P.wait("gpsimd", sCC, v_agb)
        P.wait("sync", sCC, v_agb)
        sW.n += 16
        P.emit("sync", lambda e, w=which: e.dma_start(out=bnd_sb2[:], in_=bnd_all[w]).then_inc(sW.h, 16))
        v_bl = sW.n
        P.wait("tensor", sW, v_bl)
        for hh in range(2):
            P.emit("tensor", lambda e, h_=hh: e.matmul(
                p3_rep[h_][0:4, :], sel_sb[:], bnd_sb2[:, h_ * 512:(h_ + 1) * 512], start=True, stop=True))
        vt_sel = P.inc("tensor", sT)
        P.wait("scalar", sT, vt_sel)
        P.emit("scalar", lambda e: e.activation(out=prev_sb[0:4, 0:512], in_=p3_rep[0][0:4, :], func=AF.Copy))
        P.emit("scalar", lambda e: e.activation(out=prev_sb[0:4, 512:1024], in_=p3_rep[1][0:4, :], func=AF.Copy))
        va_prev = P.inc("scalar", sA)
        P.wait("tensor", sA, va_prev)
        ev_pt = 0
        for kk in range(8):
            P.wait("tensor", sA, ev_pt)
            P.emit("tensor", lambda e, k_=kk: e.transpose(
                p3_t[:], prev_sb[:, k_ * 128:(k_ + 1) * 128], ident_sb[:]))
            vt_p = P.inc("tensor", sT)
            P.wait("scalar", sT, vt_p)
            P.emit("scalar", lambda e, k_=kk: e.activation(
                out=prevf_sb[:, k_, :], in_=p3_t[:, 0:4], func=AF.Copy))
            ev_pt = P.inc("scalar", sA)
        va_prevf = sA.n

        def ffn_block(sc):
            L = sc * 512
            P.wait("vector", sA, va_prevf)
            P.emit("vector", lambda e, L_=L: e.tensor_mul(
                out=mr_sb[0:1, :], in0=m_sb[0:1, L_:L_ + 512], in1=r_std[0:1, L_:L_ + 512]))
            vv_mr = P.inc("vector", sV)
            P.wait("tensor", sV, vv_mr)
            P.emit("tensor", lambda e, L_=L: e.matmul(
                p3_rep[0][:], ones_sb[0:1, 0:128], r_std[0:1, L_:L_ + 512], start=True, stop=True))
            P.emit("tensor", lambda e: e.matmul(
                p3_rep[1][:], ones_sb[0:1, 0:128], mr_sb[0:1, :], start=True, stop=True))
            vt_rp = P.inc("tensor", sT)
            P.wait("vector", sT, vt_rp)
            for a in range(8):
                P.emit("vector", lambda e, a_=a, L_=L: e.tensor_mul(
                    out=ln2_fm[:, a_, :], in0=stream_fm[:, a_, L_:L_ + 512], in1=p3_rep[0][:]))
            P.inc("vector", sVS); P.wait("vector", sVS, sVS.n)
            for a in range(8):
                P.emit("vector", lambda e, a_=a: e.tensor_sub(
                    out=ln2_fm[:, a_, :], in0=ln2_fm[:, a_, :], in1=p3_rep[1][:]))
            P.inc("vector", sVS); P.wait("vector", sVS, sVS.n)
            P.emit("vector", lambda e: e.tensor_sub(
                out=d_fm[:, :, 1:512].rearrange("p a t -> p (a t)") if False else d_fm[:].rearrange("p a t -> p a t")[:, :, 1:512],
                in0=ln2_fm[:].rearrange("p a t -> p a t")[:, :, 1:512],
                in1=ln2_fm[:].rearrange("p a t -> p a t")[:, :, 0:511]))
            P.emit("vector", lambda e, s_=sc: e.tensor_sub(
                out=d_fm[:, :, 0:1], in0=ln2_fm[:, :, 0:1], in1=prevf_sb[:, :, s_:s_ + 1]))
            vv_d = P.inc("vector", sV)
            P.wait("vector", sT, t_rk_prev[0])
            for a in range(8):
                P.emit("vector", lambda e, a_=a: e.scalar_tensor_tensor(
                    out=xr_bf[:, a_, 1:512], in0=d_fm[:, a_, 1:512], scalar=mur_sb[:, a_:a_ + 1],
                    in1=ln2_fm[:, a_, 0:511], op0=ALU.mult, op1=ALU.add))
                P.emit("vector", lambda e, a_=a, s_=sc: e.scalar_tensor_tensor(
                    out=xr_bf[:, a_, 0:1], in0=d_fm[:, a_, 0:1], scalar=mur_sb[:, a_:a_ + 1],
                    in1=prevf_sb[:, a_, s_:s_ + 1], op0=ALU.mult, op1=ALU.add))
                P.emit("vector", lambda e, a_=a: e.scalar_tensor_tensor(
                    out=xk_bf[:, a_, 1:512], in0=d_fm[:, a_, 1:512], scalar=muk_sb[:, a_:a_ + 1],
                    in1=ln2_fm[:, a_, 0:511], op0=ALU.mult, op1=ALU.add))
                P.emit("vector", lambda e, a_=a, s_=sc: e.scalar_tensor_tensor(
                    out=xk_bf[:, a_, 0:1], in0=d_fm[:, a_, 0:1], scalar=muk_sb[:, a_:a_ + 1],
                    in1=prevf_sb[:, a_, s_:s_ + 1], op0=ALU.mult, op1=ALU.add))
            vv_mix = P.inc("vector", sV)
            t_rk = {}
            for m in range(8):
                P.wait("tensor", sG2, w3_ld[0])
                P.wait("tensor", sV, vv_mix)
                if m >= 1:
                    P.wait("tensor", sA, va_rk.get(m - 1, 0))
                else:
                    P.wait("tensor", sA, va_rk.get(7, 0))
                for kk in range(8):
                    P.emit("tensor", lambda e, k_=kk, m_=m: e.matmul(
                        p3_r[:], wr_a[:, k_, m_ * 128:(m_ + 1) * 128], xr_bf[:, k_, :], start=(k_ == 0), stop=(k_ == 7)))
                for kk in range(8):
                    P.emit("tensor", lambda e, k_=kk, m_=m: e.matmul(
                        p3_k[:], wk_a[:, k_, m_ * 128:(m_ + 1) * 128], xk_bf[:, k_, :], start=(k_ == 0), stop=(k_ == 7)))
                t_rk[m] = P.inc("tensor", sT)
                P.wait("scalar", sT, t_rk[m])
                if m == 0:
                    P.wait("scalar", sV, vv_str.get(7, 0))
                    P.wait("scalar", sT, t_v.get(7, 0))
                P.emit("scalar", lambda e, m_=m: e.activation(
                    out=r_sb[:, m_, :], in_=p3_r[:], func=AF.Sigmoid))
                va_sig = P.inc("scalar", sA)
                P.wait("vector", sT, t_rk[m])
                P.emit("vector", lambda e: e.tensor_relu(out=kr_sb[:], in_=p3_k[:]))
                vv_kr = P.inc("vector", sV)
                P.wait("scalar", sV, vv_kr)
                P.emit("scalar", lambda e, m_=m: e.activation(
                    out=k2_bf[:, m_, :], in_=kr_sb[:], func=AF.Square))
                va_rk[m] = P.inc("scalar", sA)
            t_rk_prev[0] = t_rk[7]
            for m in range(8):
                P.wait("tensor", sA, va_rk[7])
                if m >= 1:
                    P.wait("tensor", sV, vv_str.get(m - 1, 0))
                else:
                    P.wait("tensor", sV, vv_str.get(7, 0))
                for kk in range(8):
                    P.emit("tensor", lambda e, k_=kk, m_=m: e.matmul(
                        p3_v[:], wv_a[:, k_, m_ * 128:(m_ + 1) * 128], k2_bf[:, k_, :], start=(k_ == 0), stop=(k_ == 7)))
                t_v[m] = P.inc("tensor", sT)
                P.wait("vector", sT, t_v[m])
                P.emit("vector", lambda e, m_=m: e.tensor_copy(out=kr_sb[:], in_=r_sb[:, m_, :]))
                P.emit("vector", lambda e: e.tensor_mul(out=t2_s[:], in0=kr_sb[:], in1=p3_v[:]))
                P.inc("vector", sVS); P.wait("vector", sVS, sVS.n)
                P.emit("vector", lambda e, m_=m, L_=L: e.tensor_add(
                    out=stream_fm[:, m_, L_:L_ + 512], in0=stream_fm[:, m_, L_:L_ + 512], in1=t2_s[:]))
                vv_str[m] = P.inc("vector", sV)
            return sV.n, sA.n, sT.n

        va_rk = {}
        t_v = {}
        vv_str = {}
        t_rk_prev = [0]
        f_done = []
        for sc in range(NSC):
            f_done.append(ffn_block(sc))
        if dbg_out is not None:
            P.wait("sync", sV, sV.n)
            P.wait("sync", sA, sA.n)
            sDBG.n += 16
            P.emit("sync", lambda e: e.dma_start(out=dbg_out, in_=stream_fm[:]).then_inc(sDBG.h, 16))
        es3.close()
        return


    es2.close()
    for _e in ENGS:
        P.wait(_e, sT, sT.n)
        P.wait(_e, sA, sA.n)
        P.wait(_e, sV, sV.n)
        P.wait(_e, sP, sP.n)

    def attn_phase(dbg_out=None):
        es4 = ExitStack()
        E4 = lambda t: es4.enter_context(t)
        xh_q = E4(nc.sbuf_tensor("a_xhq", [128, 8, TQ], BF16))
        den_sb = E4(nc.sbuf_tensor("a_den", [1, 512], F32))
        invrep = E4(nc.sbuf_tensor("a_invrep", [128, 512], F32))
        t4 = E4(nc.sbuf_tensor("a_t4", [128, 512], F32))
        p4_s = E4(nc.psum_tensor("p4_s", [128, 512], F32))
        p4_rep = [E4(nc.psum_tensor(f"p4_rep{i}", [128, 512], F32)) for i in range(2)]
        p4_o = [E4(nc.psum_tensor(f"p4_o{i}", [128, 512], F32)) for i in range(4)]
        es4a = ExitStack()
        E4a = lambda t: es4a.enter_context(t)
        m4 = E4a(nc.sbuf_tensor("a_m", [1, 512], F32))
        r4 = E4a(nc.sbuf_tensor("a_rstd", [1, 512], F32))
        mr4 = E4a(nc.sbuf_tensor("a_mr", [1, 512], F32))
        sq4 = E4a(nc.sbuf_tensor("a_sq", [128, 8, 512], F32))
        stat4 = E4a(nc.sbuf_tensor("a_stat", [1, 1024], F32))
        ln3_fm = E4a(nc.sbuf_tensor("a_ln3", [128, 8, 512], F32))
        x3c_bf = E4a(nc.sbuf_tensor("a_x3c", [128, 8, 512], BF16))
        x3tm = E4a(nc.sbuf_tensor("a_x3tm", [128, 4, D], BF16))
        p4_tb = E4a(nc.psum_tensor("p4_tb", [128, 128], BF16))

        st_xh = 0
        st_xv = 0
        ev_tb = [0]
        for sc in range(NSC):
            L = sc * 512
            P.wait("scalar", sV, sV.n)
            P.emit("scalar", lambda e, L_=L: e.activation(
                out=sq4[:], in_=stream_fm[:, :, L_:L_ + 512], func=AF.Square))
            va_sq4 = P.inc("scalar", sA)
            P.wait("tensor", sA, va_sq4)
            for kk in range(8):
                P.emit("tensor", lambda e, k_=kk, L_=L: e.matmul(
                    p4_s[0:1, :], ones_sb[0:128, 0:1], stream_fm[:, k_, L_:L_ + 512],
                    start=(k_ == 0), stop=(k_ == 7)))
            vt1 = P.inc("tensor", sT)
            P.wait("scalar", sT, vt1)
            P.emit("scalar", lambda e: e.activation(out=stat4[0:1, 0:512], in_=p4_s[0:1, :], func=AF.Copy))
            va1 = P.inc("scalar", sA)
            P.wait("tensor", sA, va1)
            for kk in range(8):
                P.emit("tensor", lambda e, k_=kk: e.matmul(
                    p4_s[0:1, :], ones_sb[0:128, 0:1], sq4[:, k_, :], start=(k_ == 0), stop=(k_ == 7)))
            vt2 = P.inc("tensor", sT)
            P.wait("scalar", sT, vt2)
            P.emit("scalar", lambda e: e.activation(out=stat4[0:1, 512:1024], in_=p4_s[0:1, :], func=AF.Copy))
            va2 = P.inc("scalar", sA)
            P.wait("vector", sA, va2)
            P.emit("vector", lambda e, L_=L: e.tensor_scalar_mul(
                out=m4[0:1, :], in0=stat4[0:1, 0:512], scalar1=1.0 / D))
            P.inc("vector", sVS); P.wait("vector", sVS, sVS.n)
            P.emit("vector", lambda e, L_=L: e.tensor_mul(
                out=stat4[0:1, 0:512], in0=m4[0:1, :], in1=m4[0:1, :]))
            P.inc("vector", sVS); P.wait("vector", sVS, sVS.n)
            P.emit("vector", lambda e: e.tensor_scalar_mul(
                out=stat4[0:1, 512:1024], in0=stat4[0:1, 512:1024], scalar1=1.0 / D))
            P.inc("vector", sVS); P.wait("vector", sVS, sVS.n)
            P.emit("vector", lambda e: e.tensor_sub(
                out=stat4[0:1, 512:1024], in0=stat4[0:1, 512:1024], in1=stat4[0:1, 0:512]))
            vv_v4 = P.inc("vector", sV)
            P.wait("scalar", sV, vv_v4)
            P.emit("scalar", lambda e, L_=L: e.activation(
                out=r4[0:1, :], in_=stat4[0:1, 512:1024], func=AF.Ln, bias=cst_sb[0:1, 0:1]))
            P.inc("scalar", sAS); P.wait("scalar", sAS, sAS.n)
            P.emit("scalar", lambda e, L_=L: e.activation(
                out=r4[0:1, :], in_=r4[0:1, :], func=AF.Exp, scale=-0.5))
            va_r4 = P.inc("scalar", sA)
            P.wait("vector", sA, va_r4)
            P.emit("vector", lambda e, L_=L: e.tensor_mul(
                out=mr4[0:1, :], in0=m4[0:1, :], in1=r4[0:1, :]))
            vv_mr4 = P.inc("vector", sV)
            P.wait("tensor", sV, vv_mr4)
            P.emit("tensor", lambda e, L_=L: e.matmul(
                p4_rep[0][:], ones_sb[0:1, 0:128], r4[0:1, :], start=True, stop=True))
            P.emit("tensor", lambda e, L_=L: e.matmul(
                p4_rep[1][:], ones_sb[0:1, 0:128], mr4[0:1, :], start=True, stop=True))
            vt_rp = P.inc("tensor", sT)
            P.wait("vector", sT, vt_rp)
            for a in range(8):
                P.emit("vector", lambda e, a_=a, L_=L: e.tensor_mul(
                    out=ln3_fm[:, a_, :], in0=stream_fm[:, a_, L_:L_ + 512], in1=p4_rep[0][:]))
            P.inc("vector", sVS); P.wait("vector", sVS, sVS.n)
            for a in range(8):
                P.emit("vector", lambda e, a_=a: e.tensor_sub(
                    out=ln3_fm[:, a_, :], in0=ln3_fm[:, a_, :], in1=p4_rep[1][:]))
            P.inc("vector", sVS); P.wait("vector", sVS, sVS.n)
            for a in range(8):
                P.emit("vector", lambda e, a_=a, L_=L: e.tensor_add(
                    out=stream_fm[:, a_, L_:L_ + 512], in0=stream_fm[:, a_, L_:L_ + 512],
                    in1=ln3_fm[:, a_, :]))
            vv_res = P.inc("vector", sV)
            P.wait("scalar", sV, vv_res)
            P.emit("scalar", lambda e: e.activation(out=sq4[:], in_=ln3_fm[:], func=AF.Square))
            va_sq5 = P.inc("scalar", sA)
            P.wait("tensor", sA, va_sq5)
            for kk in range(8):
                P.emit("tensor", lambda e, k_=kk: e.matmul(
                    p4_s[0:1, :], ones_sb[0:128, 0:1], sq4[:, k_, :], start=(k_ == 0), stop=(k_ == 7)))
            vt_n = P.inc("tensor", sT)
            P.wait("scalar", sT, vt_n)
            P.emit("scalar", lambda e: e.activation(out=stat4[0:1, 0:512], in_=p4_s[0:1, :], func=AF.Copy))
            va_n = P.inc("scalar", sA)
            P.wait("vector", sA, va_n)
            P.emit("vector", lambda e: e.tensor_scalar_max(out=stat4[0:1, 0:512], in0=stat4[0:1, 0:512], scalar1=1e-8))
            vv_mx = P.inc("vector", sV)
            P.wait("scalar", sV, vv_mx)
            P.emit("scalar", lambda e: e.activation(out=stat4[0:1, 0:512], in_=stat4[0:1, 0:512], func=AF.Ln))
            P.inc("scalar", sAS); P.wait("scalar", sAS, sAS.n)
            P.emit("scalar", lambda e: e.activation(out=stat4[0:1, 0:512], in_=stat4[0:1, 0:512], func=AF.Exp, scale=-0.5))
            va_inv = P.inc("scalar", sA)
            P.wait("tensor", sA, va_inv)
            P.emit("tensor", lambda e: e.matmul(
                p4_rep[0][:], ones_sb[0:1, 0:128], stat4[0:1, 0:512], start=True, stop=True))
            vt_ir = P.inc("tensor", sT)
            P.wait("vector", sT, vt_ir)
            for a in range(8):
                P.emit("vector", lambda e, a_=a, L_=L: e.tensor_mul(
                    out=xh_q[:, a_, L_:L_ + 512], in0=ln3_fm[:, a_, :], in1=p4_rep[0][:]))
            vv_xh = P.inc("vector", sV)
            P.wait("scalar", sV, vv_xh)
            P.emit("scalar", lambda e: e.activation(out=x3c_bf[:], in_=ln3_fm[:], func=AF.Copy))
            va_x3c = P.inc("scalar", sA)
            P.wait("tensor", sA, va_x3c)
            for tt in range(4):
                for kk in range(8):
                    P.wait("tensor", sA, ev_tb[0])
                    P.emit("tensor", lambda e, t_=tt, k_=kk: e.transpose(
                        p4_tb[:], x3c_bf[:, k_, t_ * 128:(t_ + 1) * 128], ident_sb[:]))
                    vtb = P.inc("tensor", sT)
                    P.wait("scalar", sT, vtb)
                    P.emit("scalar", lambda e, t_=tt, k_=kk: e.activation(
                        out=x3tm[:, t_, k_ * 128:(k_ + 1) * 128], in_=p4_tb[:], func=AF.Copy))
                    ev_tb[0] = P.inc("scalar", sA)
            P.wait("sync", sA, sA.n)
            P.wait("sync", sV, vv_xh)
            sHT.n += 32
            P.emit("sync", lambda e, L_=L, s_=sc: e.dma_start(
                out=xh_my[s_], in_=xh_q[:, :, L_:L_ + 512]).then_inc(sHT.h, 16))
            P.emit("sync", lambda e, L_=L: e.dma_start(
                out=xv_my[L_:L_ + 512, :].rearrange("(t p) c -> p t c", p=128),
                in_=x3tm[:]).then_inc(sHT.h, 16))
        st_keys = sHT.n
        es4a.close()
        E_sb = E4(nc.sbuf_tensor("a_E", [128, 64, 512], BF16))
        xh_k = [E4(nc.sbuf_tensor(f"a_xhk{i}", [128, 8, 512], BF16)) for i in range(2)]
        xv_k = [E4(nc.sbuf_tensor(f"a_xvk{i}", [128, D], BF16)) for i in range(2)]
        p4_ox = E4(nc.psum_tensor("p4_ox", [128, 512], F32))
        OB = [p4_o[0], p4_o[1], p4_o[2], p4_o[3], p4_rep[0], p4_rep[1], p4_s, p4_ox]
        P.wait("gpsimd", sHT, st_keys)
        sCC.n += 2
        P.emit("gpsimd", lambda e: e.collective_compute(
            "AllGather", ALU.bypass, replica_groups=GROUPS,
            ins=[xh_my.opt()], outs=[xh_all.opt()]).then_inc(sCC.h, 1))
        P.emit("gpsimd", lambda e: e.collective_compute(
            "AllGather", ALU.bypass, replica_groups=GROUPS,
            ins=[xv_my.opt()], outs=[xv_all.opt()]).then_inc(sCC.h, 1))
        v_agh = sCC.n - 1
        v_agv = sCC.n

        exp_done = {}
        pv_done_t = [0]
        evO = {}
        xh_ld = {}
        den_ev = [0]
        for qc in range(4):
            Q0 = qc * 512
            for kt in range(64):
                k4, ko = kt // 4, (kt % 4) * 128
                r_, j4 = kt // 16, (kt % 16) // 4
                pk = k4 % 2
                shk = sH[pk]
                if kt % 4 == 0:
                    qe = "sync"
                    if k4 >= 2:
                        P.wait(qe, sT, qk_done[(qc, (k4 - 2) * 4 + 3)])
                    P.wait(qe, sCC, v_agh)
                    shk.n += 16
                    P.emit(qe, lambda e, r2=r_, j2=j4, p=pk, s=shk: e.dma_start(
                        out=xh_k[p][:], in_=xh_all[r2 * 4 + j2]).then_inc(s.h, 16))
                    xh_ld[(qc, k4)] = shk.n
                P.wait("tensor", shk, xh_ld[(qc, k4)])
                if qc > 0 and kt == 0:
                    P.wait("tensor", sV, str_done[qc - 1])
                if qc > 0 and kt == 0:
                    P.wait("scalar", sT, pv_done_t[0])
                pq = kt % 2
                if kt >= 2:
                    P.wait("tensor", sA, exp_done[(qc, kt - 2)])
                for kk in range(8):
                    P.emit("tensor", lambda e, k_=kk, p=pk, pr=pq, q_=Q0, o=ko: e.matmul(
                        p4_rep[pr][:], xh_k[p][:, k_, o:o + 128], xh_q[:, k_, q_:q_ + 512],
                        start=(k_ == 0), stop=(k_ == 7)))
                vt_qk = P.inc("tensor", sT)
                qk_done[(qc, kt)] = vt_qk
                P.wait("scalar", sT, vt_qk)
                P.emit("scalar", lambda e, kt_=kt, p=pq: e.activation(
                    out=E_sb[:, kt_, :], in_=p4_rep[p][:], func=AF.Exp))
                exp_done[(qc, kt)] = P.inc("scalar", sA)
            P.wait("tensor", sA, exp_done[(qc, 63)])
            P.wait("tensor", sA, den_ev[0])
            for kt in range(64):
                P.emit("tensor", lambda e, kt_=kt: e.matmul(
                    p4_s[0:1, :], onesbf_sb[:], E_sb[:, kt_, :], start=(kt_ == 0), stop=(kt_ == 63)))
            vt_den = P.inc("tensor", sT)
            P.wait("scalar", sT, vt_den)
            P.emit("scalar", lambda e: e.activation(out=den_sb[:], in_=p4_s[0:1, :], func=AF.Copy))
            va_den = P.inc("scalar", sA)
            den_ev[0] = va_den
            P.wait("vector", sA, va_den)
            P.emit("vector", lambda e: e.reciprocal(out=den_sb[:], in_=den_sb[:]))
            vv_inv = P.inc("vector", sV)
            P.wait("tensor", sV, vv_inv)
            P.emit("tensor", lambda e: e.matmul(
                p4_rep[0][:], ones_sb[0:1, 0:128], den_sb[:], start=True, stop=True))
            vt_ivr = P.inc("tensor", sT)
            P.wait("scalar", sT, vt_ivr)
            P.emit("scalar", lambda e: e.activation(out=invrep[:], in_=p4_rep[0][:], func=AF.Copy))
            va_ivr = P.inc("scalar", sA)
            for kt in range(64):
                r_, j_ = kt // 16, kt % 16
                pk = kt % 2
                svk = sX[pk]
                if kt >= 2:
                    P.wait("scalar", sT, pv_mm[(qc, kt - 2)])
                if kt < 2:
                    P.wait("scalar", sCC, v_agv)
                svk.n += 16
                P.emit("scalar", lambda e, r2=r_, j2=j_, p=pk, s=svk: e.dma_start(
                    out=xv_k[p][:],
                    in_=xv_all[r2 * TQ + j2 * 128:r2 * TQ + (j2 + 1) * 128, :]
                ).then_inc(s.h, 16))
                v_v = svk.n
                P.wait("tensor", svk, v_v)
                if kt == 0:
                    P.wait("tensor", sA, exp_done[(qc, 63)])
                    P.wait("tensor", sA, va_ivr)
                    P.wait("tensor", sV, str_done.get(qc - 1, 0))
                for ci in range(8):
                    P.emit("tensor", lambda e, c2=ci, p=pk, kt_=kt: e.matmul(
                        OB[c2][:], xv_k[p][:, c2 * 128:(c2 + 1) * 128], E_sb[:, kt_, :],
                        start=(kt_ == 0), stop=(kt_ == 63)))
                pv_mm[(qc, kt)] = P.inc("tensor", sT)
            pv_done_t[0] = pv_mm[(qc, 63)]
            P.wait("vector", sT, pv_mm[(qc, 63)])
            P.wait("vector", sA, va_ivr)
            for ci in range(8):
                P.emit("vector", lambda e, c2=ci: e.tensor_mul(out=t4[:], in0=OB[c2][:], in1=invrep[:]))
                P.inc("vector", sVS); P.wait("vector", sVS, sVS.n)
                P.emit("vector", lambda e, c2=ci, q_=Q0: e.tensor_add(
                    out=stream_fm[:, c2, q_:q_ + 512],
                    in0=stream_fm[:, c2, q_:q_ + 512], in1=t4[:]))
            str_done[qc] = P.inc("vector", sV)
        if dbg_out is not None:
            P.wait("sync", sV, sV.n)
            P.wait("sync", sA, sA.n)
            sDBG.n += 16
            P.emit("sync", lambda e: e.dma_start(out=dbg_out, in_=stream_fm[:]).then_inc(sDBG.h, 16))
        es4.close()

    qk_done = {}
    pv_mm = {}
    str_done = {}

    DO_ATTN = 2
    if FFN_STOP >= 1:
        ffn_phase(0)
        if FFN_STOP >= 4 and DO_ATTN:
            for _e in ENGS:
                P.wait(_e, sT, sT.n)
                P.wait(_e, sA, sA.n)
                P.wait(_e, sV, sV.n)
                P.wait(_e, sP, sP.n)
            attn_phase()
            if DO_ATTN >= 2:
                for _e in ENGS:
                    P.wait(_e, sT, sT.n)
                    P.wait(_e, sA, sA.n)
                    P.wait(_e, sV, sV.n)
                    P.wait(_e, sP, sP.n)
                ffn_phase(1)
                for _e in ("tensor", "scalar", "sync"):
                    P.wait(_e, sT, sT.n)
                    P.wait(_e, sA, sA.n)
                    P.wait(_e, sV, sV.n)
                es6 = ExitStack()
                p6_t = [es6.enter_context(nc.psum_tensor(f"p6_t{i}", [128, 128], F32)) for i in range(2)]
                o6 = es6.enter_context(nc.sbuf_tensor("o6", [128, 4, D], F32))
                ev6 = {}
                st6 = {}
                q6 = [(sOP[0], "sync"), (sOP[1], "scalar")]
                for ti in range(16):
                    ob = ti % 4
                    sq, qe = q6[ti % 2]
                    if ti >= 4:
                        P.wait("scalar", st6[ti - 4][0], st6[ti - 4][1])
                    for kk in range(8):
                        sl = (ti * 8 + kk) % 2
                        P.wait("tensor", sA, ev6.get(sl, 0))
                        P.emit("tensor", lambda e, t_=ti, k_=kk, s=sl: e.transpose(
                            p6_t[s][:], stream_fm[:, k_, t_ * 128:(t_ + 1) * 128], identf_sb[:]))
                        vt6 = P.inc("tensor", sT)
                        P.wait("scalar", sT, vt6)
                        P.emit("scalar", lambda e, k_=kk, s=sl, o_=ob: e.activation(
                            out=o6[:, o_, k_ * 128:(k_ + 1) * 128], in_=p6_t[s][:], func=AF.Copy))
                        ev6[sl] = P.inc("scalar", sA)
                    P.wait(qe, sA, sA.n)
                    sq.n += 16
                    P.emit(qe, lambda e, t_=ti, o_=ob, s=sq: e.dma_start(
                        out=out_q[t_ * 128:(t_ + 1) * 128, :], in_=o6[:, o_, :]).then_inc(s.h, 16))
                    st6[ti] = (sq, sq.n)
                for sq, _qe in q6:
                    P.wait("sync", sq, sq.n)
    P.run(nc)
    return nc



def _make_selp(r):
    sel = np.zeros((16, 4), np.float32)
    for sc in range(4):
        if r > 0:
            sel[(r - 1) * 4 + sc, sc] = 1.0
        elif sc > 0:
            sel[3 * 4 + (sc - 1), sc] = 1.0
    return sel.astype(ml_dtypes.bfloat16)


def _gidx(r):
    return np.concatenate([np.arange(sc * 2048 + r * 512, sc * 2048 + r * 512 + 512)
                           for sc in range(4)])


def _shard_inputs(inputs, core):
    b, r = core // 4, core % 4
    d0, d1 = r * DSL, (r + 1) * DSL
    def shp(v):
        return np.ascontiguousarray(v.reshape(4, 128).T)
    x = inputs["L0_attn_a_hidden_states"]
    w_in_full = inputs["L0_attn_a_in_proj_w"]
    return dict(
        x_q=np.ascontiguousarray(x[b][_gidx(r)]),
        w_in=np.ascontiguousarray(np.concatenate(
            [w_in_full[:, d0:d1], w_in_full[:, DI + d0:DI + d1]], axis=1)),
        conv_w=np.ascontiguousarray(
            inputs["L0_attn_a_conv_w"][d0:d1].reshape(4, 128, KC).transpose(1, 0, 2)),
        conv_b=shp(inputs["L0_attn_a_conv_b"][d0:d1]),
        xproj_w=np.ascontiguousarray(inputs["L0_attn_a_x_proj_w"][d0:d1]),
        dtw=np.ascontiguousarray(inputs["L0_attn_a_dt_proj_w"][:, d0:d1]),
        dtb=shp(inputs["L0_attn_a_dt_proj_b"][d0:d1]),
        dtlnw=np.ascontiguousarray(inputs["L0_attn_a_dt_ln_w"].reshape(RDT, 1)),
        blnw=np.ascontiguousarray(inputs["L0_attn_a_b_ln_w"].reshape(1, NST)),
        clnw=np.ascontiguousarray(inputs["L0_attn_a_c_ln_w"].reshape(1, NST)),
        alog=np.ascontiguousarray(
            inputs["L0_attn_a_A_log"][d0:d1].reshape(4, 128, NST).transpose(1, 0, 2)),
        dskip=shp(inputs["L0_attn_a_D_skip"][d0:d1]),
        wout=np.ascontiguousarray(inputs["L0_attn_a_out_proj_w"][d0:d1]),
        inlnw=np.ascontiguousarray(inputs["L0_attn_a_input_ln_w"].reshape(1, D)),
        ident_bf=np.eye(128, dtype=ml_dtypes.bfloat16),
        ident_f=np.eye(128, dtype=np.float32),
        gate_w=inputs["L0_attn_a_gate_w"],
        up_w=inputs["L0_attn_a_up_w"],
        down_w=inputs["L0_attn_a_down_w"],
        pmw=np.ascontiguousarray(inputs["L0_attn_a_pre_moe_ln_w"].reshape(1, D)),
        w_r=inputs["L0_ffn_W_r"],
        w_k=inputs["L0_ffn_W_k"],
        w_v=inputs["L0_ffn_W_v"],
        mu_r=np.ascontiguousarray(inputs["L0_ffn_mu_r"].reshape(8, 128).T),
        mu_k=np.ascontiguousarray(inputs["L0_ffn_mu_k"].reshape(8, 128).T),
        selp=_make_selp(r),
    )


_NC_CACHE = [None]


def kernel(**inputs):
    if _NC_CACHE[0] is None:
        _NC_CACHE[0] = build(detect_races=False)
    nc = _NC_CACHE[0]
    in_maps = [_shard_inputs(inputs, c) for c in range(8)]
    res = run_bass_kernel_spmd(nc, in_maps, core_ids=list(range(8)))
    out = np.empty((B, S, D), np.float32)
    for c in range(8):
        b, r = c // 4, c % 4
        out[b][_gidx(r)] = res.results[c]["out_q"]
    return out


# version 19
# speedup vs baseline: 1.0289x; 1.0103x over previous
import numpy as np
import ml_dtypes
from contextlib import ExitStack
import concourse.bass as bass
import concourse.mybir as mybir
from concourse.bass_utils import run_bass_kernel_spmd

F32 = mybir.dt.float32
BF16 = mybir.dt.bfloat16
AF = mybir.ActivationFunctionType
ALU = mybir.AluOpType
AX = mybir.AxisListType

B, S, D = 2, 8192, 1024
DI, NST, RDT, KC, DF = 2048, 16, 64, 4, 4096
DSL = DI // 4
TQ = S // 4
TC = 512
NCH = S // TC
ARG = 1
NAR = NCH // ARG
NSC = 4
NSLOT = 2
ENGS = ("sync", "vector", "scalar", "gpsimd", "tensor")
GROUPS = [[0, 1, 2, 3], [4, 5, 6, 7]]


class Sem:
    def __init__(self, nc, es, name):
        self.h = es.enter_context(nc.semaphore(name))
        self.n = 0


class Prog:
    def __init__(self):
        self.plan = {e: [] for e in ENGS}
    def emit(self, eng, fn):
        self.plan[eng].append(("op", fn))
    def wait(self, eng, sem, val):
        if val > 0:
            self.plan[eng].append(("wait", sem.h, val))
    def inc(self, eng, sem, k=1):
        sem.n += k
        self.plan[eng].append(("inc", sem.h, k))
        return sem.n
    def run(self, nc):
        def runner(name):
            def go(e):
                last = None
                for item in self.plan[name]:
                    if item[0] == "op":
                        last = item[1](e)
                    elif item[0] == "wait":
                        e.wait_ge(item[1], item[2])
                    else:
                        assert last is not None, f"inc with no prior op on {name}"
                        last.then_inc(item[1], item[2])
                        last = None
            return go
        with nc.Block() as block:
            block.sync(runner("sync"))
            block.vector(runner("vector"))
            block.scalar(runner("scalar"))
            block.gpsimd(runner("gpsimd"))
            block.tensor(runner("tensor"))


def build(debug=(), detect_races=True):
    DO_CACHE = True
    DO_PH2 = True
    DO_WT = True
    FFN_STOP = 4

    nc = bass.Bass("TRN2", target_bir_lowering=False, detect_race_conditions=detect_races)
    es = ExitStack()
    E = lambda t: es.enter_context(t)

    x_q = nc.dram_tensor("x_q", [TQ, D], F32, kind="ExternalInput").ap()
    w_in = nc.dram_tensor("w_in", [D, 2 * DSL], F32, kind="ExternalInput").ap()
    conv_w = nc.dram_tensor("conv_w", [128, 4, KC], F32, kind="ExternalInput").ap()
    conv_b = nc.dram_tensor("conv_b", [128, 4], F32, kind="ExternalInput").ap()
    xproj_w = nc.dram_tensor("xproj_w", [DSL, 96], F32, kind="ExternalInput").ap()
    dtw = nc.dram_tensor("dtw", [RDT, DSL], F32, kind="ExternalInput").ap()
    dtb = nc.dram_tensor("dtb", [128, 4], F32, kind="ExternalInput").ap()
    dtlnw = nc.dram_tensor("dtlnw", [RDT, 1], F32, kind="ExternalInput").ap()
    blnw = nc.dram_tensor("blnw", [1, NST], F32, kind="ExternalInput").ap()
    clnw = nc.dram_tensor("clnw", [1, NST], F32, kind="ExternalInput").ap()
    alog = nc.dram_tensor("alog", [128, 4, NST], F32, kind="ExternalInput").ap()
    dskip = nc.dram_tensor("dskip", [128, 4], F32, kind="ExternalInput").ap()
    wout = nc.dram_tensor("wout", [DSL, D], F32, kind="ExternalInput").ap()
    inlnw = nc.dram_tensor("inlnw", [1, D], F32, kind="ExternalInput").ap()
    ident_bf = nc.dram_tensor("ident_bf", [128, 128], BF16, kind="ExternalInput").ap()
    ident_f = nc.dram_tensor("ident_f", [128, 128], F32, kind="ExternalInput").ap()
    gate_w = nc.dram_tensor("gate_w", [D, DF], F32, kind="ExternalInput").ap()
    up_w = nc.dram_tensor("up_w", [D, DF], F32, kind="ExternalInput").ap()
    down_w = nc.dram_tensor("down_w", [DF, D], F32, kind="ExternalInput").ap()
    pmw = nc.dram_tensor("pmw", [1, D], F32, kind="ExternalInput").ap()
    w_r = nc.dram_tensor("w_r", [D, D], F32, kind="ExternalInput").ap()
    w_k = nc.dram_tensor("w_k", [D, D], F32, kind="ExternalInput").ap()
    w_v = nc.dram_tensor("w_v", [D, D], F32, kind="ExternalInput").ap()
    mu_r = nc.dram_tensor("mu_r", [128, 8], F32, kind="ExternalInput").ap()
    mu_k = nc.dram_tensor("mu_k", [128, 8], F32, kind="ExternalInput").ap()
    selp = nc.dram_tensor("selp", [16, 4], BF16, kind="ExternalInput").ap()
    ind3_d = nc.dram_tensor("ind3_d", [128, 4], F32, kind="ExternalInput").ap()
    ind2_d = nc.dram_tensor("ind2_d", [3, 384], F32, kind="ExternalInput").ap()
    cst3_d = nc.dram_tensor("cst3_d", [128, 1], F32, kind="ExternalInput").ap()

    out_q = nc.dram_tensor("out_q", [TQ, D], F32, kind="ExternalOutput").ap()

    hT_my = nc.dram_tensor("hT_my", [D, TQ], BF16).ap()
    hT_all = nc.dram_tensor("hT_all", [4 * D, TQ], BF16).ap()
    xp_part = [nc.dram_tensor(f"xp_part{j}", [96, ARG * TC], F32).ap() for j in range(NAR)]
    xp_red = [nc.dram_tensor(f"xp_red{j}", [96, ARG * TC], F32).ap() for j in range(NAR)]
    op_part = nc.dram_tensor("op_part", [S, D], F32).ap()
    gate_bf = nc.dram_tensor("gate_bf", [D, DF], BF16).ap()
    up_bf = nc.dram_tensor("up_bf", [D, DF], BF16).ap()
    down_bf = nc.dram_tensor("down_bf", [DF, D], BF16).ap()
    wrT_bf = nc.dram_tensor("wrT_bf", [D, D], BF16).ap()
    wkT_bf = nc.dram_tensor("wkT_bf", [D, D], BF16).ap()
    wvT_bf = nc.dram_tensor("wvT_bf", [D, D], BF16).ap()
    bnd_my = [nc.dram_tensor(f"bnd_my{i}", [4, D], BF16).ap() for i in range(2)]
    bnd_all = [nc.dram_tensor(f"bnd_all{i}", [16, D], BF16).ap() for i in range(2)]
    xh_my = nc.dram_tensor("xh_my", [4, 128, 8, 512], BF16).ap()
    xh_all = nc.dram_tensor("xh_all", [16, 128, 8, 512], BF16).ap()
    xv_my = nc.dram_tensor("xv_my", [TQ, D], BF16).ap()
    xv_all = nc.dram_tensor("xv_all", [S, D], BF16).ap()
    rs1_out = nc.dram_tensor("rs1_out", [TQ, D], F32).ap()

    ident_sb = E(nc.sbuf_tensor("ident_sb", [128, 128], BF16))
    ones_sb = E(nc.sbuf_tensor("ones_sb", [128, 128], F32))
    cst_sb = E(nc.sbuf_tensor("cst_sb", [128, 2], F32))
    onesbf_sb = E(nc.sbuf_tensor("onesbf_sb", [128, 1], BF16))
    es1 = ExitStack()
    E1 = lambda t: es1.enter_context(t)
    w_in_bf = E1(nc.sbuf_tensor("w_in_bf", [128, 8, 2 * DSL], BF16))
    xproj_sb = E1(nc.sbuf_tensor("xproj_sb", [128, 4, 96], F32))
    dtw_sb = E1(nc.sbuf_tensor("dtw_sb", [RDT, DSL], F32))
    wout_bf = E1(nc.sbuf_tensor("wout_bf", [128, 4, D], BF16))
    convw_sb = E1(nc.sbuf_tensor("convw_sb", [128, 4, KC], F32))
    convb_sb = E1(nc.sbuf_tensor("convb_sb", [128, 4], F32))
    dtb_sb = E1(nc.sbuf_tensor("dtb_sb", [128, 4], F32))
    dtlnw_sb = E1(nc.sbuf_tensor("dtlnw_sb", [RDT, 1], F32))
    clnw_eff = E1(nc.sbuf_tensor("clnw_eff", [128, NST], F32))
    blnw_rep = E1(nc.sbuf_tensor("blnw_rep", [128, NST], F32))
    A_sb = E1(nc.sbuf_tensor("A_sb", [128, 4, NST], F32))
    dskip_sb = E1(nc.sbuf_tensor("dskip_sb", [128, 4], F32))

    hTin = [E1(nc.sbuf_tensor(f"hTin{i}", [128, 8, TC], BF16)) for i in range(2)]
    xc_sl = [E1(nc.sbuf_tensor(f"xc{i}", [128, 4, TC + 3], F32)) for i in range(NSLOT)]
    sz_sl = [E1(nc.sbuf_tensor(f"sz{i}", [128, 4, TC], BF16)) for i in range(NSLOT)]
    xpev = E1(nc.sbuf_tensor("xpev", [80, TC], F32))
    xpev_c = E1(nc.sbuf_tensor("xpev_c", [16, TC], F32))
    sel_sb = E1(nc.sbuf_tensor("sel_sb", [128, 128], F32))
    xp_in = E1(nc.sbuf_tensor("xp_in", [128, TC], F32))
    sq96 = E1(nc.sbuf_tensor("sq96", [128, TC], F32))
    rstd3 = E1(nc.sbuf_tensor("rstd3", [3, TC], F32))
    ind3 = E1(nc.sbuf_tensor("ind3", [128, 4], F32))
    ind2 = E1(nc.sbuf_tensor("ind2", [3, 384], F32))
    cst3 = E1(nc.sbuf_tensor("cst3", [128, 1], F32))
    dtrms = E1(nc.sbuf_tensor("dtrms", [RDT, TC], F32))
    dt_sb = E1(nc.sbuf_tensor("dt_sb", [128, 4, TC], F32))
    u_sb = E1(nc.sbuf_tensor("u_sb", [128, TC], F32))
    up_sb = E1(nc.sbuf_tensor("up_sb", [128, 4, TC], F32))
    state = E1(nc.sbuf_tensor("state", [128, 4, NST], F32))
    yred = E1(nc.sbuf_tensor("yred", [128, TC], F32))
    t1_sb = E1(nc.sbuf_tensor("t1_sb", [128, TC], F32))
    cfin = E1(nc.sbuf_tensor("cfin", [128, TC], F32))
    yg_bf = E1(nc.sbuf_tensor("yg_bf", [128, 4, TC], BF16))
    oo_sb = E1(nc.sbuf_tensor("oo_sb", [128, D], F32))

    es0 = ExitStack()
    E0 = lambda t: es0.enter_context(t)
    wstage = E0(nc.sbuf_tensor("wstage", [128, 2 * DSL], F32))
    wln_rep = E0(nc.sbuf_tensor("wln_rep", [128, D], F32))
    xt_sb = [E0(nc.sbuf_tensor(f"xt{i}", [128, D], F32)) for i in range(2)]
    sq_scr = E0(nc.sbuf_tensor("sq_scr", [128, D], F32))
    ln1_t = E0(nc.sbuf_tensor("ln1_t", [128, D], F32))
    h2_bf = E0(nc.sbuf_tensor("h2_bf", [128, D], BF16))
    hTcol = [E0(nc.sbuf_tensor(f"hTcol{i}", [128, 8, 512], BF16)) for i in range(2)]
    st0 = E0(nc.sbuf_tensor("st0", [128, 8], F32))
    wst_bf = [E0(nc.sbuf_tensor(f"wst_bf{i}", [128, D], BF16)) for i in range(2)]
    wst_f = [E0(nc.sbuf_tensor(f"wst_f{i}", [128, D], F32)) for i in range(2)]
    wrTcol = E0(nc.sbuf_tensor("wrTcol", [128, 8, 128], BF16))

    esp = ExitStack()
    p_tr = [esp.enter_context(nc.psum_tensor(f"p_tr{i}", [128, 128], BF16)) for i in range(2)]
    p_wt = [esp.enter_context(nc.psum_tensor(f"p_wt{i}", [128, 128], BF16)) for i in range(2)]

    sW = Sem(nc, es, "sW")
    sWC = Sem(nc, es, "sWC")
    sX = [Sem(nc, es, "sX0"), Sem(nc, es, "sX1")]
    sH = [Sem(nc, es, "sH0"), Sem(nc, es, "sH1")]
    sBC = Sem(nc, es, "sBC")
    sBR = Sem(nc, es, "sBR")
    sSt = Sem(nc, es, "sSt")
    sHT = Sem(nc, es, "sHT")
    sDBG = Sem(nc, es, "sDBG")
    sXP = [Sem(nc, es, "sXP0"), Sem(nc, es, "sXP1")]
    sOP = [Sem(nc, es, "sOP0"), Sem(nc, es, "sOP1")]
    sCC = Sem(nc, es, "sCC")
    sAS = Sem(nc, es, "sAS")
    sVS = Sem(nc, es, "sVS")
    sGB = Sem(nc, es, "sGB")
    sG2 = Sem(nc, es, "sG2")
    sG3 = Sem(nc, es, "sG3")
    sRS = Sem(nc, es, "sRS")
    sRS2 = Sem(nc, es, "sRS2")
    sWL = [Sem(nc, es, "sWL0"), Sem(nc, es, "sWL1")]
    sGBp = [Sem(nc, es, "sGBp0"), Sem(nc, es, "sGBp1")]
    sPC = Sem(nc, es, "sPC")
    sWT = Sem(nc, es, "sWT")
    sWE = Sem(nc, es, "sWE")
    sT = Sem(nc, es, "sT")
    sA = Sem(nc, es, "sA")
    sV = Sem(nc, es, "sV")
    sP = Sem(nc, es, "sP")
    sQGA = [Sem(nc, es, f"sQGA{i}") for i in range(8)]
    sQG3 = Sem(nc, es, "sQG3")

    P = Prog()

    P.emit("vector", lambda e: e.memset(cst_sb[:, 0:1], 1e-5))
    P.emit("vector", lambda e: e.memset(cst_sb[:, 1:2], 1e-6))
    P.inc("vector", sV)
    loads = [
        (convw_sb, conv_w), (convb_sb, conv_b), (dtb_sb, dtb),
        (dtlnw_sb, dtlnw), (A_sb, alog), (dskip_sb, dskip),
        (ident_sb, ident_bf), (xproj_sb, xproj_w.rearrange("(a p) m -> p a m", p=128)),
        (dtw_sb, dtw), (ind3, ind3_d), (ind2, ind2_d), (cst3, cst3_d),
    ]
    for dst, src in loads:
        sW.n += 16
        P.emit("sync", lambda e, d=dst, s=src: e.dma_start(out=d[:], in_=s).then_inc(sW.h, 16))
    for dst, src, n in ((blnw_rep, blnw, NST), (clnw_eff, clnw, NST), (wln_rep, inlnw, D)):
        sW.n += 16
        P.emit("sync", lambda e, d=dst, s=src, nn=n: e.dma_start(
            out=d[:], in_=s.broadcast_to([128, nn])).then_inc(sW.h, 16))
    w_loaded = sW.n

    cast_jobs = []
    for kk in range(8):
        cast_jobs.append((w_in_bf[:, kk, :], w_in[kk * 128:(kk + 1) * 128, :], 2 * DSL))
    for kk in range(4):
        cast_jobs.append((wout_bf[:, kk, :], wout[kk * 128:(kk + 1) * 128, :], D))
    for dst, src, width in cast_jobs:
        P.wait("sync", sP, sP.n)
        sWC.n += 16
        P.emit("sync", lambda e, s=src, w=width: e.dma_start(
            out=wstage[:, 0:w], in_=s).then_inc(sWC.h, 16))
        P.wait("gpsimd", sWC, sWC.n)
        P.emit("gpsimd", lambda e, d=dst, w=width: e.tensor_copy(out=d, in_=wstage[:, 0:w]))
        P.inc("gpsimd", sP)
    P.wait("gpsimd", sW, w_loaded)
    P.emit("gpsimd", lambda e: e.memset(ones_sb[:], 1.0))
    P.emit("gpsimd", lambda e: e.memset(onesbf_sb[:], 1.0))
    P.emit("gpsimd", lambda e: e.memset(sel_sb[:], 1.0))
    P.emit("gpsimd", lambda e: e.memset(xp_in[:], 0.0))
    P.inc("gpsimd", sP); P.wait("gpsimd", sP, sP.n)
    P.emit("gpsimd", lambda e: e.memset(sel_sb[64:65, :], 0.0))
    w_casts = P.inc("gpsimd", sP)

    cache_jobs = []
    for wsrc, wdst, RT, CC in ((gate_w, gate_bf, 8, DF), (up_w, up_bf, 8, DF),
                               (down_w, down_bf, 32, D)):
        ncc = CC // D
        for rt in range(RT):
            for cc in range(max(1, ncc)):
                c0_, c1_ = cc * D, min(CC, (cc + 1) * D)
                cache_jobs.append((wsrc[rt * 128:(rt + 1) * 128, c0_:c1_],
                                   wdst[rt * 128:(rt + 1) * 128, c0_:c1_], None, None))
    if DO_WT:
        for wsrc, wdst in ((w_r, wrT_bf), (w_k, wkT_bf), (w_v, wvT_bf)):
            for rt in range(8):
                cache_jobs.append((wsrc[rt * 128:(rt + 1) * 128, :], wdst, "T", rt))
    cache_state = dict(jw=0, casts=[0, 0], stores=[0, 0], wt_ev={}, wt_slot=0)

    def emit_cache_jobs(n):
        if not DO_CACHE:
            return
        for _ in range(n):
            jw = cache_state["jw"]
            if jw >= len(cache_jobs):
                return
            cache_state["jw"] = jw + 1
            srcs, dsts, kind, rt = cache_jobs[jw]
            p = jw % 2
            P.wait("scalar", sPC, cache_state["casts"][p])
            sWL[p].n += 16
            P.emit("scalar", lambda e, s=srcs, pp=p, sm=sWL[p]: e.dma_start(
                out=wst_f[pp][:, 0:s.shape[1]], in_=s).then_inc(sm.h, 16))
            P.wait("gpsimd", sWL[p], sWL[p].n)
            P.wait("gpsimd", sGBp[p], cache_state["stores"][p])
            P.emit("gpsimd", lambda e, pp=p, w=srcs.shape[1]: e.tensor_copy(
                out=wst_bf[pp][:, 0:w], in_=wst_f[pp][:, 0:w]))
            cache_state["casts"][p] = P.inc("gpsimd", sPC)
            if kind is None:
                P.wait("sync", sPC, cache_state["casts"][p])
                sGBp[p].n += 16
                sGB.n += 16
                P.emit("sync", lambda e, d=dsts, pp=p, w=srcs.shape[1], sm=sGBp[p]: e.dma_start(
                    out=d, in_=wst_bf[pp][:, 0:w]).then_inc(sm.h, 16))
                P.emit("sync", lambda e: e.nop()) if False else None
                cache_state["stores"][p] = sGBp[p].n
            else:
                P.wait("tensor", sPC, cache_state["casts"][p])
                for mm in range(8):
                    slot = cache_state["wt_slot"]
                    cache_state["wt_slot"] = (slot + 1) % 2
                    P.wait("tensor", sWE, cache_state["wt_ev"].get(slot, 0))
                    P.emit("tensor", lambda e, pp=p, m_=mm, sl=slot: e.transpose(
                        p_wt[sl][:], wst_bf[pp][:, m_ * 128:(m_ + 1) * 128], ident_sb[:]))
                    vt_w = P.inc("tensor", sWT)
                    P.wait("scalar", sWT, vt_w)
                    P.emit("scalar", lambda e, m_=mm, sl=slot: e.activation(
                        out=wrTcol[:, m_, :], in_=p_wt[sl][:], func=AF.Copy))
                    cache_state["wt_ev"][slot] = P.inc("scalar", sWE)
                P.wait("sync", sWE, sWE.n)
                sGBp[p].n += 16
                P.emit("sync", lambda e, d=dsts, r=rt, sm=sGBp[p]: e.dma_start(
                    out=d.rearrange("(m p) c -> p m c", p=128)[:, :, r * 128:(r + 1) * 128],
                    in_=wrTcol[:]).then_inc(sm.h, 16))
                cache_state["stores"][p] = sGBp[p].n
                P.wait("scalar", sGBp[p], sGBp[p].n)

    P.wait("vector", sW, w_loaded)
    P.emit("vector", lambda e: e.tensor_mul(out=clnw_eff[:], in0=clnw_eff[:], in1=blnw_rep[:]))
    P.inc("vector", sV)
    P.wait("scalar", sW, w_loaded)
    P.emit("scalar", lambda e: e.activation(out=A_sb[:], in_=A_sb[:], func=AF.Exp))
    w_a_exp = P.inc("scalar", sA)
    P.wait("vector", sA, w_a_exp)
    P.emit("vector", lambda e: e.tensor_scalar_mul(out=A_sb[:], in0=A_sb[:], scalar1=-1.0))
    w_small_v = P.inc("vector", sV)

    ntile = TQ // 128
    ph0_a_done = {}
    ph0_t_done = {}
    ph0_st = {}
    vv_st_prev = 0
    ev_parity = [0, 0]
    for i in range(ntile):
        buf = xt_sb[i % 2]
        if i >= 2:
            P.wait("sync", sA, ph0_a_done[i - 2])
        sxp = sX[i % 2]
        sxp.n += 16
        P.emit("sync", lambda e, b=buf, i_=i, s=sxp: e.dma_start(
            out=b[:], in_=x_q[i_ * 128:(i_ + 1) * 128, :]).then_inc(s.h, 16))
        v_x = sxp.n
        P.wait("vector", sxp, v_x)
        P.emit("vector", lambda e, b=buf: e.reduce_sum(out=st0[:, 0:1], in_=b[:], axis=AX.X))
        P.inc("vector", sVS); P.wait("vector", sVS, sVS.n)
        P.wait("scalar", sxp, v_x)
        P.wait("scalar", sV, vv_st_prev)
        P.emit("scalar", lambda e, b=buf: e.activation(
            out=sq_scr[:], in_=b[:], func=AF.Square, accum_out=st0[:, 1:2]))
        va_sq = P.inc("scalar", sA)
        P.wait("vector", sA, va_sq)
        P.emit("vector", lambda e: e.tensor_scalar_mul(out=st0[:, 2:3], in0=st0[:, 0:1], scalar1=1.0 / D))
        P.emit("vector", lambda e: e.tensor_scalar_mul(out=st0[:, 3:4], in0=st0[:, 1:2], scalar1=1.0 / D))
        P.inc("vector", sVS); P.wait("vector", sVS, sVS.n)
        P.emit("vector", lambda e: e.tensor_mul(out=st0[:, 4:5], in0=st0[:, 2:3], in1=st0[:, 2:3]))
        P.inc("vector", sVS); P.wait("vector", sVS, sVS.n)
        P.emit("vector", lambda e: e.tensor_sub(out=st0[:, 3:4], in0=st0[:, 3:4], in1=st0[:, 4:5]))
        vv_st = P.inc("vector", sV)
        vv_st_prev = vv_st
        P.wait("scalar", sV, vv_st)
        P.emit("scalar", lambda e: e.activation(out=st0[:, 5:6], in_=st0[:, 3:4], func=AF.Ln, bias=cst_sb[:, 0:1]))
        P.inc("scalar", sAS); P.wait("scalar", sAS, sAS.n)
        P.emit("scalar", lambda e: e.activation(out=st0[:, 5:6], in_=st0[:, 5:6], func=AF.Exp, scale=-0.5))
        va_r = P.inc("scalar", sA)
        P.wait("scalar", sA, va_r)
        P.wait("vector", sA, va_r)
        P.emit("vector", lambda e: e.tensor_mul(out=st0[:, 6:7], in0=st0[:, 2:3], in1=st0[:, 5:6]))
        P.inc("vector", sVS); P.wait("vector", sVS, sVS.n)
        P.emit("vector", lambda e: e.tensor_scalar_mul(out=st0[:, 6:7], in0=st0[:, 6:7], scalar1=-1.0))
        vv_nmr = P.inc("vector", sV)
        P.wait("scalar", sV, vv_nmr)
        P.emit("scalar", lambda e, b=buf: e.activation(
            out=ln1_t[:], in_=b[:], func=AF.Identity, scale=st0[:, 5:6], bias=st0[:, 6:7]))
        P.emit("scalar", lambda e: e.activation(
            out=sq_scr[:], in_=ln1_t[:], func=AF.Square, accum_out=st0[:, 7:8]))
        P.inc("scalar", sAS); P.wait("scalar", sAS, sAS.n)
        P.emit("scalar", lambda e: e.activation(out=st0[:, 7:8], in_=st0[:, 7:8], func=AF.Ln, scale=1.0 / D, bias=cst_sb[:, 1:2]))
        P.inc("scalar", sAS); P.wait("scalar", sAS, sAS.n)
        P.emit("scalar", lambda e: e.activation(out=st0[:, 7:8], in_=st0[:, 7:8], func=AF.Exp, scale=-0.5))
        va_ln = P.inc("scalar", sA)
        P.wait("scalar", sA, va_ln)
        ph0_a_done[i] = va_ln
        P.wait("vector", sA, va_ln)
        if i >= 1:
            P.wait("vector", sT, ph0_t_done[i - 1])
        P.emit("vector", lambda e: e.scalar_tensor_tensor(
            out=h2_bf[:], in0=ln1_t[:], scalar=st0[:, 7:8], in1=wln_rep[:],
            op0=ALU.mult, op1=ALU.mult))
        vv_h2 = P.inc("vector", sV)
        col = hTcol[(i // 4) % 2]
        io = (i % 4) * 128
        P.wait("tensor", sV, vv_h2)
        if i % 4 == 0 and i >= 8:
            P.wait("scalar", sHT, ph0_st[i // 4 - 2])
        for a in range(8):
            pt = p_tr[a % 2]
            P.wait("tensor", sA, ev_parity[a % 2])
            P.emit("tensor", lambda e, a_=a, p=pt: e.transpose(
                p[:], h2_bf[:, a_ * 128:(a_ + 1) * 128], ident_sb[:]))
            vt = P.inc("tensor", sT)
            P.wait("scalar", sT, vt)
            P.emit("scalar", lambda e, a_=a, p=pt, c=col, o=io: e.activation(
                out=c[:, a_, o:o + 128], in_=p[:], func=AF.Copy))
            ev_parity[a % 2] = P.inc("scalar", sA)
        ph0_t_done[i] = sT.n
        if i % 4 == 3:
            P.wait("sync", sA, sA.n)
            sHT.n += 16
            P.emit("sync", lambda e, c=col, i_=i: e.dma_start(
                out=hT_my.rearrange("(a p) l -> p a l", p=128)[:, :, (i_ - 3) * 128:(i_ + 1) * 128],
                in_=c[:]).then_inc(sHT.h, 16))
            ph0_st[i // 4] = sHT.n
        emit_cache_jobs(8)
    emit_cache_jobs(10**9)
    P.wait("gpsimd", sHT, sHT.n)
    sCC.n += 1
    P.emit("gpsimd", lambda e: e.collective_compute(
        "AllGather", ALU.bypass, replica_groups=GROUPS,
        ins=[hT_my.opt()], outs=[hT_all.opt()]).then_inc(sCC.h, 1))
    v_ag1 = sCC.n

    es0.close()
    esp.close()
    esS = ExitStack()
    ES = lambda t: esS.enter_context(t)
    crep_sl = [ES(nc.sbuf_tensor(f"crep{i}", [128, 4, TC], F32)) for i in range(2)]
    dA_b = ES(nc.sbuf_tensor("dA", [128, 4, TC], F32))
    bb_b = ES(nc.sbuf_tensor("bb", [128, 4, TC], F32))
    H_sl = [ES(nc.sbuf_tensor(f"H{i}", [128, 4, TC], F32)) for i in range(2)]
    G_b = ES(nc.sbuf_tensor("G", [128, 4, TC], F32))
    yacc4 = ES(nc.sbuf_tensor("yacc4", [128, 4, TC], F32))
    bsrcB = ES(nc.sbuf_tensor("bsrcB", [128, 4, TC], F32))
    bsrcC = ES(nc.sbuf_tensor("bsrcC", [128, 4, TC], F32))
    rcrep_sb = ES(nc.sbuf_tensor("rcrep", [128, TC], F32))
    p_q = [ES(nc.psum_tensor(f"p_q{i}", [128, TC], F32)) for i in range(8)]

    a_t_done = {}
    a_ev_last = {}
    a_act_done = {}
    a_xpst = {}
    a_xpev_done = {}
    a_xp_dma = {}
    ar_done = {}
    b_done_v = {}
    last_b_done = [0]
    op_dma = {}
    op_ev = {}
    b_prev_dtr = [0]
    stats_mm_prev = [0]
    op_st_last = [0, 0]
    bsrc_rd = [0]
    scanK_vv = {}
    scanK_h1 = {}
    scanK_vp = {}
    scanK_vp2 = {}
    scanK_vy = {}
    crep_ready = {}
    gmul_last = {}
    bank_ev = {6: 0, 7: 0}
    cc_wait = [v_ag1]

    REPL_BASE = [0, 32, 64, 65]

    def repl_mm(e, srcT, g_, jj_, bank):
        if jj_ < 3:
            b0 = REPL_BASE[jj_]
            return e.matmul(p_q[bank][:], ones_sb[b0:b0 + 1, 0:128],
                            srcT[b0:b0 + 1, g_, :], start=True, stop=True)
        return e.matmul(p_q[bank][:], sel_sb[64:66, 0:128],
                        srcT[64:66, g_, :], start=True, stop=True)

    P.emit("vector", lambda e: e.memset(state[:], 0.0))
    P.inc("vector", sV)

    def a_side(k):
        sc, q = k // 4, k % 4
        lc = sc * TC
        hbuf = hTin[k % 2]
        slot = xc_sl[k % NSLOT]
        zslot = sz_sl[k % NSLOT]
        P.wait("sync", sCC, v_ag1)
        if k >= 2:
            P.wait("sync", sT, a_t_done[k - 2])
        shp_ = sH[k % 2]
        shp_.n += 16
        P.emit("sync", lambda e, b=hbuf, q_=q, l=lc, s=shp_: e.dma_start(
            out=b[:],
            in_=hT_all[q_ * D:(q_ + 1) * D, l:l + TC].rearrange("(kk p) t -> p kk t", p=128)
        ).then_inc(s.h, 16))
        v_h = shp_.n
        P.wait("tensor", shp_, v_h)
        P.wait("tensor", sP, w_casts)
        if k == 0:
            P.wait("tensor", sWT, sWT.n)
            P.wait("tensor", sWE, sWE.n)
            P.wait("tensor", sPC, sPC.n)
            P.wait("tensor", sGBp[0], sGBp[0].n)
            P.wait("tensor", sGBp[1], sGBp[1].n)
        if k >= 1:
            P.wait("tensor", sA, a_ev_last[k - 1])
        if k >= NSLOT:
            P.wait("scalar", sV, b_done_v[k - NSLOT])
        ev_after = []
        for m in range(8):
            pp = p_q[m % 2]
            if m >= 2:
                P.wait("tensor", sA, ev_after[m - 2])
            for kk in range(8):
                P.emit("tensor", lambda e, m_=m, kk_=kk, p=pp, hb=hbuf: e.matmul(
                    p[:], w_in_bf[:, kk_, m_ * 128:(m_ + 1) * 128], hb[:, kk_, :],
                    start=(kk_ == 0), stop=(kk_ == 7)))
            vt = P.inc("tensor", sT)
            P.wait("scalar", sT, vt)
            if m < 4:
                P.emit("scalar", lambda e, m_=m, p=pp, s=slot: e.activation(
                    out=s[:, m_, 3:3 + TC], in_=p[:], func=AF.Copy))
            else:
                P.emit("scalar", lambda e, m_=m, p=pp, zs=zslot: e.activation(
                    out=zs[:, m_ - 4, :], in_=p[:], func=AF.Silu))
            ev_after.append(P.inc("scalar", sA))
        a_t_done[k] = sT.n
        a_ev_last[k] = ev_after[7]
        nslot = xc_sl[(k + 1) % NSLOT]
        P.wait("vector", sA, ev_after[3])
        if k == 0:
            P.emit("vector", lambda e, s=slot: e.memset(s[:, :, 0:3], 0.0))
            P.inc("vector", sVS); P.wait("vector", sVS, sVS.n)
        if k + 1 < NCH:
            P.emit("vector", lambda e, s=slot, ns=nslot: e.tensor_copy(
                out=ns[:, :, 0:3], in_=s[:, :, TC:TC + 3]))
            P.inc("vector", sV)
        silu_done = {}
        for a in range(4):
            fin = yred if a % 2 == 0 else cfin
            if a >= 2:
                P.wait("vector", sA, silu_done[a - 2])
            P.emit("vector", lambda e, a_=a, s=slot: e.tensor_scalar(
                out=t1_sb[:], in0=s[:, a_, 0:TC], scalar1=convw_sb[:, a_, 0:1],
                scalar2=None, op0=ALU.mult))
            P.emit("vector", lambda e, a_=a, s=slot, f=fin: e.scalar_tensor_tensor(
                out=f[:], in0=s[:, a_, 1:1 + TC], scalar=convw_sb[:, a_, 1:2],
                in1=t1_sb[:], op0=ALU.mult, op1=ALU.add))
            P.emit("vector", lambda e, a_=a, s=slot, f=fin: e.scalar_tensor_tensor(
                out=t1_sb[:], in0=s[:, a_, 2:2 + TC], scalar=convw_sb[:, a_, 2:3],
                in1=f[:], op0=ALU.mult, op1=ALU.add))
            P.emit("vector", lambda e, a_=a, s=slot, f=fin: e.scalar_tensor_tensor(
                out=f[:], in0=s[:, a_, 3:3 + TC], scalar=convw_sb[:, a_, 3:4],
                in1=t1_sb[:], op0=ALU.mult, op1=ALU.add))
            vv_c = P.inc("vector", sV)
            P.wait("scalar", sV, vv_c)
            P.emit("scalar", lambda e, a_=a, s=slot, f=fin: e.activation(
                out=s[:, a_, 3:3 + TC], in_=f[:], func=AF.Silu, bias=convb_sb[:, a_:a_ + 1]))
            silu_done[a] = P.inc("scalar", sA)
        a_act_done[k] = sA.n
        P.wait("tensor", sA, a_act_done[k])
        P.wait("tensor", sV, last_b_done[0])
        for pp_sl, c0_, c1_ in ((p_q[2][0:64, :], 0, RDT), (p_q[3][0:16, :], RDT, RDT + NST), (p_q[3][32:48, :], RDT + NST, 96)):
            for kk in range(4):
                P.emit("tensor", lambda e, p=pp_sl, kk_=kk, cc0=c0_, cc1=c1_, s=slot: e.matmul(
                    p, xproj_sb[:, kk_, cc0:cc1], s[:, kk_, 3:3 + TC],
                    start=(kk_ == 0), stop=(kk_ == 3)))
        vt_xp = P.inc("tensor", sT)
        P.wait("scalar", sT, vt_xp)
        if k >= 1:
            P.wait("scalar", sXP[a_xpst[k - 1][0]], a_xpst[k - 1][1])
        P.emit("scalar", lambda e: e.activation(out=xpev[0:RDT, :], in_=p_q[2][0:64, :], func=AF.Copy))
        P.emit("scalar", lambda e: e.activation(out=xpev[RDT:RDT + NST, :], in_=p_q[3][0:16, :], func=AF.Copy))
        P.emit("scalar", lambda e: e.activation(out=xpev_c[:], in_=p_q[3][32:48, :], func=AF.Copy))
        va_xpev = P.inc("scalar", sA)
        a_xpev_done[k] = va_xpev
        P.wait("sync", sA, va_xpev)
        j_ar = k // ARG
        sxp_ = sXP[j_ar % 2]
        if j_ar >= 2 and k % ARG == 0:
            P.wait("sync", sCC, ar_done[j_ar - 2])
        sxp_.n += 32
        P.emit("sync", lambda e, j=j_ar, q_=k % ARG, s=sxp_: e.dma_start(
            out=xp_part[j][0:80, q_ * TC:(q_ + 1) * TC], in_=xpev[:]).then_inc(s.h, 16))
        P.emit("sync", lambda e, j=j_ar, q_=k % ARG, s=sxp_: e.dma_start(
            out=xp_part[j][80:96, q_ * TC:(q_ + 1) * TC], in_=xpev_c[:]).then_inc(s.h, 16))
        a_xpst[k] = (j_ar % 2, sxp_.n)
        if k % ARG == ARG - 1:
            a_xp_dma[j_ar] = sxp_.n

    def all_reduce_xp(j):
        P.wait("gpsimd", sXP[j % 2], a_xp_dma[j])
        P.wait("gpsimd", sCC, cc_wait[0])
        sCC.n += 1
        P.emit("gpsimd", lambda e, j_=j: e.collective_compute(
            "AllReduce", ALU.add, replica_groups=GROUPS,
            ins=[xp_part[j_].opt()], outs=[xp_red[j_].opt()]).then_inc(sCC.h, 1))
        ar_done[j] = sCC.n
        cc_wait[0] = sCC.n

    def b_side(k):
        sc = k // 4
        j_ar, qa = k // ARG, k % ARG
        slot = xc_sl[k % NSLOT]
        zslot = sz_sl[k % NSLOT]
        c0, c1 = qa * TC, (qa + 1) * TC
        if k == 0:
            for _eng in ("scalar", "vector", "gpsimd"):
                P.wait(_eng, sWE, sWE.n)
                P.wait(_eng, sGBp[0], sGBp[0].n)
                P.wait(_eng, sGBp[1], sGBp[1].n)
        P.wait("scalar", sCC, ar_done[j_ar])
        P.wait("scalar", sV, b_prev_dtr[0])
        if k == 0:
            P.wait("scalar", sP, w_casts)
        for p0, p1, r0, r1 in ((0, 64, 0, RDT), (64, 80, RDT, RDT + NST),
                               (80, 96, RDT + NST, 96)):
            sBC.n += 16
            P.emit("scalar", lambda e, a0=p0, a1=p1, a=r0, b_=r1, j=j_ar: e.dma_start(
                out=xp_in[a0:a1, :], in_=xp_red[j][a:b_, c0:c1]).then_inc(sBC.h, 16))
        v_ld = sBC.n
        P.wait("scalar", sT, bsrc_rd[0])
        for dst, r0 in ((bsrcB, RDT), (bsrcC, RDT + NST)):
            for jj in range(4):
                b0 = REPL_BASE[jj]
                sBR.n += 16
                P.emit("scalar", lambda e, d=dst, rr=r0, jj_=jj, b0_=b0, j=j_ar: e.dma_start(
                    out=d[b0_:b0_ + 1, :, :],
                    in_=xp_red[j][rr:rr + NST, c0:c1].rearrange(
                        "(g jj) t -> jj g t", jj=4)[jj_:jj_ + 1]).then_inc(sBR.h, 16))
        v_bsrc = sBR.n
        P.wait("scalar", sBC, v_ld)
        P.wait("scalar", sT, stats_mm_prev[0])
        P.emit("scalar", lambda e: e.activation(out=sq96[:], in_=xp_in[:], func=AF.Square))
        va = P.inc("scalar", sA)
        P.wait("tensor", sA, va)
        P.wait("tensor", sA, op_ev.get(k - 1, 0))
        P.emit("tensor", lambda e: e.matmul(
            p_q[6][0:3, :], ind3[:, 0:3], sq96[:], start=True, stop=True))
        vt = P.inc("tensor", sT)
        stats_mm_prev[0] = vt
        P.wait("scalar", sT, vt)
        P.emit("scalar", lambda e: e.activation(
            out=rstd3[:], in_=p_q[6][0:3, :], func=AF.Ln, scale=cst3[0:3, 0:1],
            bias=cst_sb[0:3, 1:2]))
        P.inc("scalar", sAS); P.wait("scalar", sAS, sAS.n)
        P.emit("scalar", lambda e: e.activation(
            out=rstd3[:], in_=rstd3[:], func=AF.Exp, scale=-0.5))
        va_rs = P.inc("scalar", sA)
        P.wait("tensor", sA, va_rs)
        P.wait("tensor", sV, b_done_v.get(k - 1, 0))
        P.emit("tensor", lambda e: e.matmul(p_q[5][0:RDT, :], ind2[0:3, 0:RDT], rstd3[:], start=True, stop=True))
        vt2 = P.inc("tensor", sT)
        P.wait("vector", sT, vt2)
        P.emit("vector", lambda e: e.scalar_tensor_tensor(
            out=dtrms[:], in0=xp_in[0:64, :], scalar=dtlnw_sb[:], in1=p_q[5][0:RDT, :],
            op0=ALU.mult, op1=ALU.mult))
        vv_dtr = P.inc("vector", sV)
        b_prev_dtr[0] = vv_dtr
        P.wait("tensor", sV, vv_dtr)
        va_es = {}
        for m in range(4):
            dtb_bank = p_q[7 - (m % 2)]
            if m >= 2:
                P.wait("tensor", sA, va_es[m - 2])
            P.emit("tensor", lambda e, m_=m, p=dtb_bank: e.matmul(
                p[:], dtw_sb[:, m_ * 128:(m_ + 1) * 128], dtrms[:], start=True, stop=True))
            vt3 = P.inc("tensor", sT)
            P.wait("scalar", sT, vt3)
            P.emit("scalar", lambda e, m_=m, p=dtb_bank: e.activation(
                out=u_sb[:], in_=p[:], func=AF.Exp, bias=dtb_sb[:, m_:m_ + 1]))
            va_es[m] = P.inc("scalar", sA)
            P.emit("scalar", lambda e, m_=m: e.activation(
                out=dt_sb[:, m_, :], in_=u_sb[:], func=AF.Ln, bias=1.0))
        va_dt = P.inc("scalar", sA)
        P.wait("tensor", sA, va_dt)
        P.wait("tensor", sV, sV.n)
        P.emit("tensor", lambda e: e.matmul(p_q[5][:], ind2[0:3, 128:256], rstd3[:], start=True, stop=True))
        vt4 = P.inc("tensor", sT)
        P.wait("vector", sA, va_dt)
        P.wait("vector", sT, vt4)
        P.wait("vector", sA, a_act_done[k])
        for a in range(4):
            P.emit("vector", lambda e, a_=a, s=slot: e.tensor_mul(
                out=u_sb[:], in0=dt_sb[:, a_, :], in1=s[:, a_, 3:3 + TC]))
            P.emit("vector", lambda e, a_=a: e.tensor_mul(
                out=up_sb[:, a_, :], in0=u_sb[:], in1=p_q[5][:]))
        vv_up = P.inc("vector", sV)
        P.wait("tensor", sV, vv_up)
        P.emit("tensor", lambda e: e.matmul(p_q[5][:], ind2[0:3, 256:384], rstd3[:], start=True, stop=True))
        vt_c = P.inc("tensor", sT)
        P.wait("scalar", sT, vt_c)
        P.emit("scalar", lambda e: e.activation(out=rcrep_sb[:], in_=p_q[5][:], func=AF.Copy))
        va_rc = P.inc("scalar", sA)

        vB = {}
        vbmul_last = {}

        def repl_C(g):
            fg = k * 4 + g
            if fg >= 2:
                P.wait("scalar", sP, gmul_last.get(fg - 2, 0))
            for jj in range(4):
                bank = 6 + (jj % 2)
                P.wait("tensor", sA, bank_ev[bank])
                P.emit("tensor", lambda e, g_=g, jj_=jj, b_=bank: repl_mm(e, bsrcC, g_, jj_, b_))
                vt_r = P.inc("tensor", sT)
                P.wait("scalar", sT, vt_r)
                P.emit("scalar", lambda e, g_=g, jj_=jj, b_=bank: e.activation(
                    out=crep_sl[g_ % 2][:, jj_, :], in_=p_q[b_][:], func=AF.Copy))
                bank_ev[bank] = P.inc("scalar", sA)
            crep_ready[fg] = bank_ev[7]

        def repl_B(g):
            if g >= 1:
                P.wait("tensor", sV, vbmul_last[g - 1])
            else:
                P.wait("tensor", sV, b_done_v.get(k - 1, 0))
                P.wait("tensor", sBR, v_bsrc)
                P.wait("tensor", sA, va_rc)
                P.wait("tensor", sA, a_xpev_done[(k // 2) * 2 + 1])
            for jj in range(4):
                P.emit("tensor", lambda e, g_=g, jj_=jj: repl_mm(e, bsrcB, g_, jj_, 2 + jj_))
            vB[g] = P.inc("tensor", sT)

        def emit_reduce(g, a):
            K = k * 16 + g * 4 + a
            P.wait("vector", sP, scanK_vp2[K])
            if g == 0:
                P.emit("vector", lambda e, a_=a: e.tensor_add(
                    out=yacc4[:, a_, :], in0=G_b[:, 0, :], in1=G_b[:, 2, :]))
            else:
                P.emit("vector", lambda e, a_=a: e.tensor_add(
                    out=yacc4[:, a_, :], in0=yacc4[:, a_, :], in1=G_b[:, 0, :]))
                P.emit("vector", lambda e, a_=a: e.tensor_add(
                    out=yacc4[:, a_, :], in0=yacc4[:, a_, :], in1=G_b[:, 2, :]))
            scanK_vy[K] = P.inc("vector", sV)

        P.wait("tensor", sBR, v_bsrc)
        P.wait("tensor", sA, va_dt)
        repl_C(0)
        repl_B(0)
        pend = [None]
        for g in range(4):
            for a in range(4):
                K = k * 16 + g * 4 + a
                dAb = dA_b
                P.wait("scalar", sV, scanK_h1.get(K - 1, 0))
                if K == 0:
                    P.wait("scalar", sV, w_small_v)
                for j in range(2):
                    P.emit("scalar", lambda e, a_=a, g_=g, j_=j, d=dAb: e.activation(
                        out=d[:, j_, :], in_=dt_sb[:, a_, :], func=AF.Exp,
                        scale=A_sb[:, a_, g_ * 4 + j_:g_ * 4 + j_ + 1]))
                P.wait("scalar", sV, scanK_vv.get(K - 1, 0))
                for j in range(2, 4):
                    P.emit("scalar", lambda e, a_=a, g_=g, j_=j, d=dAb: e.activation(
                        out=d[:, j_, :], in_=dt_sb[:, a_, :], func=AF.Exp,
                        scale=A_sb[:, a_, g_ * 4 + j_:g_ * 4 + j_ + 1]))
                va_x = P.inc("scalar", sA)
                P.wait("vector", sT, vB[g])
                for j in range(4):
                    P.emit("vector", lambda e, j_=j, g_=g, a_=a: e.scalar_tensor_tensor(
                        out=bb_b[:, j_, :], in0=p_q[2 + j_][:],
                        scalar=clnw_eff[:, g_ * 4 + j_:g_ * 4 + j_ + 1],
                        in1=up_sb[:, a_, :], op0=ALU.mult, op1=ALU.mult))
                vbm = P.inc("vector", sV)
                if a == 3:
                    vbmul_last[g] = vbm
                Hb = H_sl[K % 2]
                P.wait("vector", sA, va_x)
                P.wait("vector", sP, scanK_vp.get(K - 2, 0))
                for j in range(4):
                    P.emit("vector", lambda e, a_=a, g_=g, j_=j, d=dAb, h=Hb: e.tensor_tensor_scan(
                        out=h[:, j_, :], data0=d[:, j_, :], data1=bb_b[:, j_, :],
                        initial=state[:, a_, g_ * 4 + j_:g_ * 4 + j_ + 1],
                        op0=ALU.mult, op1=ALU.add))
                    if j == 1:
                        scanK_h1[K] = P.inc("vector", sV)
                P.inc("vector", sVS); P.wait("vector", sVS, sVS.n)
                P.emit("vector", lambda e, a_=a, g_=g, h=Hb: e.tensor_copy(
                    out=state[:, a_, g_ * 4:g_ * 4 + 4], in_=h[:, :, TC - 1]))
                scanK_vv[K] = P.inc("vector", sV)
                if pend[0] is not None:
                    emit_reduce(*pend[0])
                pend[0] = (g, a)
                P.wait("gpsimd", sV, scanK_vv[K])
                if a == 0:
                    P.wait("gpsimd", sA, crep_ready[k * 4 + g])
                P.wait("gpsimd", sV, scanK_vy.get(K - 1, 0))
                P.emit("gpsimd", lambda e, g_=g, h=Hb: e.tensor_mul(
                    out=G_b[:], in0=crep_sl[g_ % 2][:], in1=h[:]))
                scanK_vp[K] = P.inc("gpsimd", sP)
                P.emit("gpsimd", lambda e: e.tensor_add(
                    out=G_b[:, 0, :], in0=G_b[:, 0, :], in1=G_b[:, 1, :]))
                P.emit("gpsimd", lambda e: e.tensor_add(
                    out=G_b[:, 2, :], in0=G_b[:, 2, :], in1=G_b[:, 3, :]))
                scanK_vp2[K] = P.inc("gpsimd", sP)
                gmul_last[k * 4 + g] = scanK_vp2[K]
                if a == 3 and g < 3:
                    repl_C(g + 1)
                    repl_B(g + 1)
        emit_reduce(*pend[0])
        bsrc_rd[0] = vB[3]
        P.wait("vector", sA, va_rc)
        for a in range(4):
            P.emit("vector", lambda e, a_=a: e.tensor_mul(
                out=t1_sb[:], in0=yacc4[:, a_, :], in1=rcrep_sb[:]))
            P.emit("vector", lambda e, a_=a, zs=zslot: e.tensor_copy(
                out=yred[:], in_=zs[:, a_, :]))
            P.emit("vector", lambda e, a_=a, s=slot: e.scalar_tensor_tensor(
                out=t1_sb[:], in0=s[:, a_, 3:3 + TC], scalar=dskip_sb[:, a_:a_ + 1],
                in1=t1_sb[:], op0=ALU.mult, op1=ALU.add))
            P.emit("vector", lambda e, a_=a: e.tensor_mul(
                out=yg_bf[:, a_, :], in0=t1_sb[:], in1=yred[:]))
            P.inc("vector", sV)
        b_done_v[k] = sV.n
        last_b_done[0] = b_done_v[k]
        P.wait("tensor", sV, b_done_v[k])
        if k >= 1:
            P.wait("tensor", sA, op_ev.get(k - 1, 0))
        P.wait("tensor", sA, bank_ev[6])
        P.wait("tensor", sA, bank_ev[7])
        op_ev_h = {}
        sc_par = sc % 2
        if k % 4 == 0 and sc >= 2:
            P.wait("sync", sCC, rs_done[sc - 2])
        for tt in range(4):
            P.wait("scalar", sOP[op_st_last[0]], op_st_last[1])
            for h in range(2):
                po = p_q[6 + h]
                if tt >= 1:
                    P.wait("tensor", sA, op_ev_h[(tt - 1, h)])
                for a in range(4):
                    P.emit("tensor", lambda e, tt_=tt, h_=h, a_=a, p=po: e.matmul(
                        p[:], yg_bf[:, a_, tt_ * 128:(tt_ + 1) * 128],
                        wout_bf[:, a_, h_ * 512:(h_ + 1) * 512],
                        start=(a_ == 0), stop=(a_ == 3)))
                vt_o = P.inc("tensor", sT)
                P.wait("scalar", sT, vt_o)
                P.emit("scalar", lambda e, h_=h, p=po: e.activation(
                    out=oo_sb[:, h_ * 512:(h_ + 1) * 512], in_=p[:], func=AF.Copy))
                op_ev_h[(tt, h)] = P.inc("scalar", sA)
            va_o = op_ev_h[(tt, 1)]
            P.wait("sync", sA, va_o)
            sOP[sc_par].n += 16
            P.emit("sync", lambda e, k_=k, tt_=tt, s=sOP[sc_par]: e.dma_start(
                out=op_part[k_ * TC + tt_ * 128:k_ * TC + (tt_ + 1) * 128, :],
                in_=oo_sb[:]).then_inc(s.h, 16))
            op_st_last[0] = sc_par
            op_st_last[1] = sOP[sc_par].n
        op_ev[k] = sA.n
        bank_ev[6] = op_ev[k]
        bank_ev[7] = op_ev[k]
        op_dma[k // 4] = sOP[sc_par].n

    def rs1(sc):
        P.wait("gpsimd", sOP[sc % 2], op_dma[sc])
        P.wait("gpsimd", sCC, cc_wait[0])
        sCC.n += 1
        P.emit("gpsimd", lambda e, s=sc: e.collective_compute(
            "ReduceScatter", ALU.add, replica_groups=GROUPS,
            ins=[op_part[s * 4 * TC:(s + 1) * 4 * TC, :].opt()],
            outs=[rs1_out[s * 512:(s + 1) * 512, :].opt()]).then_inc(sCC.h, 1))
        cc_wait[0] = sCC.n
        return sCC.n

    rs_done = {}
    for jp in range(NCH // 2):
        k0 = 2 * jp
        a_side(k0)
        all_reduce_xp(k0)
        a_side(k0 + 1)
        all_reduce_xp(k0 + 1)
        b_side(k0)
        b_side(k0 + 1)
        if (k0 + 1) % 4 == 3:
            rs_done[(k0 + 1) // 4] = rs1((k0 + 1) // 4)


    esS.close()
    es1.close()
    for _e in ENGS:
        P.wait(_e, sCC, rs_done[NSC - 1])
        P.wait(_e, sT, sT.n)
        P.wait(_e, sA, sA.n)
        P.wait(_e, sV, sV.n)
        P.wait(_e, sP, sP.n)
    va_sgl = {}
    vv_ff = {}
    t_d = {}
    vv_dev = 0
    es2a = ExitStack()
    stream_fm = es2a.enter_context(nc.sbuf_tensor("stream_fm", [128, 8, TQ], F32))
    identf_sb = es2a.enter_context(nc.sbuf_tensor("identf_sb", [128, 128], F32))
    es2 = ExitStack()
    E2 = lambda t: es2.enter_context(t)
    x_tm = E2(nc.sbuf_tensor("x_tm", [128, 4, D], F32))
    mo_tm = E2(nc.sbuf_tensor("mo_tm", [128, 4, D], F32))
    ln1b = E2(nc.sbuf_tensor("ln1b", [128, D], F32))
    sq2 = E2(nc.sbuf_tensor("sq2", [128, D], F32))
    st2 = E2(nc.sbuf_tensor("st2", [128, 8], F32))
    pmw_rep = E2(nc.sbuf_tensor("pmw_rep", [128, D], F32))
    hn_tm = E2(nc.sbuf_tensor("hn_tm", [128, 4, D], BF16))
    hn_fm = E2(nc.sbuf_tensor("hn_fm", [128, 8, TC], BF16))
    ff_sb = E2(nc.sbuf_tensor("ff_sb", [128, 32, TC], BF16))
    gw_t = [E2(nc.sbuf_tensor(f"gw{i}", [128, 8, 512], BF16)) for i in range(2)]
    uw_t = [E2(nc.sbuf_tensor(f"uw{i}", [128, 8, 512], BF16)) for i in range(2)]
    dw_t = [E2(nc.sbuf_tensor(f"dw{i}", [128, TC], BF16)) for i in range(2)]
    sgl = E2(nc.sbuf_tensor("sgl", [128, TC], F32))
    p2_tr = [E2(nc.psum_tensor(f"p2_tr{i}", [128, 128], BF16)) for i in range(2)]
    p2_g = E2(nc.psum_tensor("p2_g", [128, TC], F32))
    p2_u = E2(nc.psum_tensor("p2_u", [128, TC], F32))
    p2_d = [E2(nc.psum_tensor(f"p2_d{i}", [128, TC], F32)) for i in range(4)]

    sW.n += 48
    P.emit("sync", lambda e: e.dma_start(out=identf_sb[:], in_=ident_f).then_inc(sW.h, 16))
    P.emit("sync", lambda e: e.dma_start(out=pmw_rep[:], in_=pmw.broadcast_to([128, D])).then_inc(sW.h, 16))
    P.emit("sync", lambda e: e.dma_start(out=ident_sb[:], in_=ident_bf).then_inc(sW.h, 16))
    v_w2 = sW.n

    def phase2(sc):
        nonlocal vv_dev
        L = sc * 512
        ev_tr = p2_evtr
        va_hn = {}
        for tt in range(4):
            sld = sRS if tt % 2 == 0 else sRS2
            if tt == 0:
                P.wait("sync", sA, p2_trans_prev[0])
            sld.n += 32
            P.emit("sync", lambda e, t_=tt, s=sld: e.dma_start(
                out=x_tm[:, t_, :], in_=x_q[L + t_ * 128:L + (t_ + 1) * 128, :]).then_inc(s.h, 16))
            P.emit("sync", lambda e, t_=tt, s=sld: e.dma_start(
                out=mo_tm[:, t_, :], in_=rs1_out[L + t_ * 128:L + (t_ + 1) * 128, :]).then_inc(s.h, 16))
            v_ld = sld.n
            P.wait("vector", sld, v_ld)
            if tt == 0:
                P.wait("vector", sA, p2_trans_prev[0])
            P.emit("vector", lambda e, t_=tt: e.reduce_sum(out=st2[:, 0:1], in_=x_tm[:, t_, :], axis=AX.X))
            P.inc("vector", sVS); P.wait("vector", sVS, sVS.n)
            P.wait("scalar", sld, v_ld)
            P.wait("scalar", sV, p2_hn_prev[0])
            P.emit("scalar", lambda e, t_=tt: e.activation(
                out=sq2[:], in_=x_tm[:, t_, :], func=AF.Square, accum_out=st2[:, 1:2]))
            va_sq = P.inc("scalar", sA)
            P.wait("vector", sA, va_sq)
            P.emit("vector", lambda e: e.tensor_scalar_mul(out=st2[:, 2:3], in0=st2[:, 0:1], scalar1=1.0 / D))
            P.emit("vector", lambda e: e.tensor_scalar_mul(out=st2[:, 3:4], in0=st2[:, 1:2], scalar1=1.0 / D))
            P.inc("vector", sVS); P.wait("vector", sVS, sVS.n)
            P.emit("vector", lambda e: e.tensor_mul(out=st2[:, 4:5], in0=st2[:, 2:3], in1=st2[:, 2:3]))
            P.inc("vector", sVS); P.wait("vector", sVS, sVS.n)
            P.emit("vector", lambda e: e.tensor_sub(out=st2[:, 3:4], in0=st2[:, 3:4], in1=st2[:, 4:5]))
            vv_st = P.inc("vector", sV)
            P.wait("scalar", sV, vv_st)
            P.emit("scalar", lambda e: e.activation(out=st2[:, 5:6], in_=st2[:, 3:4], func=AF.Ln, bias=cst_sb[:, 0:1]))
            P.inc("scalar", sAS); P.wait("scalar", sAS, sAS.n)
            P.emit("scalar", lambda e: e.activation(out=st2[:, 5:6], in_=st2[:, 5:6], func=AF.Exp, scale=-0.5))
            va_r = P.inc("scalar", sA)
            P.wait("scalar", sA, va_r)
            P.wait("vector", sA, va_r)
            P.emit("vector", lambda e: e.tensor_mul(out=st2[:, 6:7], in0=st2[:, 2:3], in1=st2[:, 5:6]))
            P.inc("vector", sVS); P.wait("vector", sVS, sVS.n)
            P.emit("vector", lambda e: e.tensor_scalar_mul(out=st2[:, 6:7], in0=st2[:, 6:7], scalar1=-1.0))
            vv_nm = P.inc("vector", sV)
            P.wait("scalar", sV, vv_nm)
            P.emit("scalar", lambda e, t_=tt: e.activation(
                out=ln1b[:], in_=x_tm[:, t_, :], func=AF.Identity, scale=st2[:, 5:6], bias=st2[:, 6:7]))
            va_ln1 = P.inc("scalar", sA)
            P.wait("vector", sA, va_ln1)
            P.emit("vector", lambda e, t_=tt: e.tensor_add(out=mo_tm[:, t_, :], in0=ln1b[:], in1=mo_tm[:, t_, :]))
            P.inc("vector", sVS); P.wait("vector", sVS, sVS.n)
            P.emit("vector", lambda e, t_=tt: e.tensor_add(out=x_tm[:, t_, :], in0=x_tm[:, t_, :], in1=mo_tm[:, t_, :]))
            vv_s1 = P.inc("vector", sV)
            P.wait("scalar", sV, vv_s1)
            P.emit("scalar", lambda e, t_=tt: e.activation(
                out=sq2[:], in_=mo_tm[:, t_, :], func=AF.Square, accum_out=st2[:, 7:8]))
            P.inc("scalar", sAS); P.wait("scalar", sAS, sAS.n)
            P.emit("scalar", lambda e: e.activation(out=st2[:, 7:8], in_=st2[:, 7:8], func=AF.Ln, scale=1.0 / D, bias=cst_sb[:, 1:2]))
            P.inc("scalar", sAS); P.wait("scalar", sAS, sAS.n)
            P.emit("scalar", lambda e: e.activation(out=st2[:, 7:8], in_=st2[:, 7:8], func=AF.Exp, scale=-0.5))
            va_r3 = P.inc("scalar", sA)
            P.wait("vector", sA, va_r3)
            P.emit("vector", lambda e, t_=tt: e.scalar_tensor_tensor(
                out=hn_tm[:, t_, :], in0=mo_tm[:, t_, :], scalar=st2[:, 7:8], in1=pmw_rep[:],
                op0=ALU.mult, op1=ALU.mult))
            va_hn[tt] = P.inc("vector", sV)
            p2_hn_prev[0] = va_hn[tt]
        P.wait("tensor", sW, v_w2)
        P.wait("tensor", sV, va_hn[3])
        P.wait("tensor", sT, t_gu.get((sc - 1, 31), 0))
        slot = 0
        first_hn = [True]
        for tt in range(4):
            for kk in range(8):
                P.wait("tensor", sA, ev_tr.get(slot, 0))
                P.emit("tensor", lambda e, t_=tt, k_=kk, sl=slot: e.transpose(
                    p2_tr[sl][:], hn_tm[:, t_, k_ * 128:(k_ + 1) * 128], ident_sb[:]))
                vt = P.inc("tensor", sT)
                P.wait("scalar", sT, vt)
                if first_hn[0]:
                    first_hn[0] = False
                    P.wait("scalar", sT, t_gu.get((sc - 1, 31), 0))
                P.emit("scalar", lambda e, t_=tt, k_=kk, sl=slot: e.activation(
                    out=hn_fm[:, k_, t_ * 128:(t_ + 1) * 128], in_=p2_tr[sl][:], func=AF.Copy))
                ev_tr[slot] = P.inc("scalar", sA)
                slot = (slot + 1) % 2
        slotf = 4
        P.wait("tensor", sV, vv_dev)
        for tt in range(4):
            for kk in range(8):
                P.wait("tensor", sA, ev_tr.get(slotf, 0))
                P.emit("tensor", lambda e, t_=tt, k_=kk, sl=slotf - 4: e.transpose(
                    p2_d[sl][:, 0:128], x_tm[:, t_, k_ * 128:(k_ + 1) * 128], identf_sb[:]))
                vt = P.inc("tensor", sT)
                P.wait("scalar", sT, vt)
                P.emit("scalar", lambda e, t_=tt, k_=kk, sl=slotf - 4: e.activation(
                    out=stream_fm[:, k_, L + t_ * 128:L + (t_ + 1) * 128], in_=p2_d[sl][:, 0:128], func=AF.Copy))
                ev_tr[slotf] = P.inc("scalar", sA)
                slotf = 4 + ((slotf - 4 + 1) % 2)
        va_trans = sA.n
        p2_trans_prev[0] = va_trans
        for f in range(32):
            f4, fo = f // 4, (f % 4) * 128
            pf = (f // 4) % 2
            sgw = sG2 if pf == 0 else sG3
            if f % 4 == 0:
                if f4 >= 2:
                    P.wait("sync", sT, t_gu[(sc, (f4 - 2) * 4 + 3)])
                elif sc >= 1:
                    P.wait("sync", sT, t_gu[(sc - 1, (6 + f4) * 4 + 3)])
                P.wait("sync", sGBp[0], sGBp[0].n)
                P.wait("sync", sGBp[1], sGBp[1].n)
                sgw.n += 32
                P.emit("sync", lambda e, f_=f4, p=pf, s=sgw: e.dma_start(
                    out=gw_t[p][:], in_=gate_bf[:, f_ * 512:(f_ + 1) * 512].rearrange("(kk p) m -> p kk m", p=128)).then_inc(s.h, 16))
                P.emit("sync", lambda e, f_=f4, p=pf, s=sgw: e.dma_start(
                    out=uw_t[p][:], in_=up_bf[:, f_ * 512:(f_ + 1) * 512].rearrange("(kk p) m -> p kk m", p=128)).then_inc(s.h, 16))
                gld[(sc, f4)] = sgw.n
            P.wait("tensor", sgw, gld[(sc, f4)])
            P.wait("tensor", sA, va_trans)
            if f >= 1:
                P.wait("tensor", sV, vv_ff[f - 1])
                P.wait("tensor", sA, va_sgl[f - 1])
            elif sc >= 1:
                P.wait("tensor", sV, vv_ff.get(31, 0))
                P.wait("tensor", sA, va_sgl.get(31, 0))
            for kk in range(8):
                P.emit("tensor", lambda e, k_=kk, p=pf, o=fo: e.matmul(
                    p2_g[:], gw_t[p][:, k_, o:o + 128], hn_fm[:, k_, :], start=(k_ == 0), stop=(k_ == 7)))
            for kk in range(8):
                P.emit("tensor", lambda e, k_=kk, p=pf, o=fo: e.matmul(
                    p2_u[:], uw_t[p][:, k_, o:o + 128], hn_fm[:, k_, :], start=(k_ == 0), stop=(k_ == 7)))
            vt_gu = P.inc("tensor", sT)
            t_gu[(sc, f)] = vt_gu
            P.wait("scalar", sT, vt_gu)
            P.emit("scalar", lambda e: e.activation(out=sgl[:], in_=p2_g[:], func=AF.Silu))
            va_sgl[f] = P.inc("scalar", sA)
            P.wait("vector", sA, va_sgl[f])
            if f == 0:
                P.wait("vector", sT, t_d.get((1, 31), 0))
            P.emit("vector", lambda e, f_=f: e.tensor_mul(out=ff_sb[:, f_, :], in0=sgl[:], in1=p2_u[:]))
            vv_ff[f] = P.inc("vector", sV)
        ev_d = sA.n
        for ch in range(2):
            for f in range(32):
                pf = f % 2
                sgw = sG2 if pf == 0 else sG3
                if not (ch == 0 and f < 2):
                    P.wait("sync", sT, t_d.get((ch, f - 2), t_d.get((ch - 1, 30 + f), sT.n if False else t_gu[31] if f < 2 else t_d[(ch, f - 2)])) ) if False else None
                if ch > 0 or f >= 2:
                    prev = (ch, f - 2) if f >= 2 else (ch - 1, 30 + f)
                    P.wait("sync", sT, t_d[prev])
                elif sc >= 1:
                    P.wait("sync", sT, t_d.get((1, 30 + f), 0))
                sgw.n += 16
                P.emit("sync", lambda e, f_=f, c_=ch, p=pf, s=sgw: e.dma_start(
                    out=dw_t[p][:], in_=down_bf[f_ * 128:(f_ + 1) * 128, c_ * 512:(c_ + 1) * 512]).then_inc(s.h, 16))
                P.wait("tensor", sgw, sgw.n)
                P.wait("tensor", sV, vv_ff[31])
                if f == 0:
                    P.wait("tensor", sV, vv_dev)
                for ct in range(4):
                    P.emit("tensor", lambda e, c2=ct, p=pf, f_=f: e.matmul(
                        p2_d[c2][:], dw_t[p][:, c2 * 128:(c2 + 1) * 128], ff_sb[:, f_, :],
                        start=(f_ == 0), stop=(f_ == 31)))
                t_d[(ch, f)] = P.inc("tensor", sT)
            P.wait("vector", sT, t_d[(ch, 31)])
            for ct in range(4):
                P.emit("vector", lambda e, c_=ch, c2=ct: e.tensor_add(
                    out=stream_fm[:, c_ * 4 + c2, L:L + 512],
                    in0=stream_fm[:, c_ * 4 + c2, L:L + 512], in1=p2_d[c2][:]))
            vv_dev = P.inc("vector", sV)
        return sA.n, sV.n

    p2_evtr = {}
    p2_trans_prev = [0]
    p2_hn_prev = [0]
    gld = {}
    t_gu = {}
    p2_done = []
    for sc in range(NSC):
        p2_done.append(phase2(sc))


    def ffn_phase(which, dbg_out=None):
        W_ = which
        """which in (0, 1): first or second application."""
        es3 = ExitStack()
        E3 = lambda t: es3.enter_context(t)
        m_sb = E3(nc.sbuf_tensor(f"f_m{W_}", [1, TQ], F32))
        r_std = E3(nc.sbuf_tensor(f"f_rstd{W_}", [1, TQ], F32))
        mr_sb = E3(nc.sbuf_tensor(f"f_mr{W_}", [1, 512], F32))
        stat_ev = E3(nc.sbuf_tensor(f"f_stat{W_}", [1, 512], F32))
        ln2_fm = E3(nc.sbuf_tensor(f"f_ln2{W_}", [128, 8, 512], F32))
        d_fm = E3(nc.sbuf_tensor(f"f_d{W_}", [128, 8, 512], F32))
        xr_bf = E3(nc.sbuf_tensor(f"f_xr{W_}", [128, 8, 512], BF16))
        xk_bf = E3(nc.sbuf_tensor(f"f_xk{W_}", [128, 8, 512], BF16))
        r_sb = E3(nc.sbuf_tensor(f"f_r{W_}", [128, 8, 512], BF16))
        k2_bf = E3(nc.sbuf_tensor(f"f_k2{W_}", [128, 8, 512], BF16))
        kr_sb = E3(nc.sbuf_tensor(f"f_kr{W_}", [128, 512], F32))
        mur_sb = E3(nc.sbuf_tensor(f"f_mur{W_}", [128, 8], F32))
        muk_sb = E3(nc.sbuf_tensor(f"f_muk{W_}", [128, 8], F32))
        sel_sb = E3(nc.sbuf_tensor(f"f_sel{W_}", [16, 4], BF16))
        bndrow = E3(nc.sbuf_tensor(f"f_bndrow{W_}", [8, 128], BF16))
        bcol_fm = E3(nc.sbuf_tensor(f"f_bcol{W_}", [128, 8], BF16))
        rep_ev = E3(nc.sbuf_tensor(f"f_repev{W_}", [128, 2], F32))
        bnd_sb2 = E3(nc.sbuf_tensor(f"f_bnd2{W_}", [16, D], BF16))
        prev_sb = E3(nc.sbuf_tensor(f"f_prev{W_}", [128, D], BF16))
        prevf_sb = E3(nc.sbuf_tensor(f"f_prevf{W_}", [128, 8, 4], F32))
        t2_s = E3(nc.sbuf_tensor(f"f_t2{W_}", [128, 512], F32))
        wr_a = E3(nc.sbuf_tensor(f"f_wr{W_}", [128, 8, D], BF16))
        wk_a = E3(nc.sbuf_tensor(f"f_wk{W_}", [128, 8, D], BF16))
        wv_a = E3(nc.sbuf_tensor(f"f_wv{W_}", [128, 8, D], BF16))
        p3_s = E3(nc.psum_tensor(f"p3_s{W_}", [1, 512], F32))
        p3_rep = [E3(nc.psum_tensor(f"p3_rep{W_}_{i}", [128, 512], F32)) for i in range(2)]
        p3_r = E3(nc.psum_tensor(f"p3_r{W_}", [128, 512], F32))
        p3_k = E3(nc.psum_tensor(f"p3_k{W_}", [128, 512], F32))
        p3_v = E3(nc.psum_tensor(f"p3_v{W_}", [128, 512], F32))
        p3_t = E3(nc.psum_tensor(f"p3_t{W_}", [128, 128], BF16))

        sW.n += 48
        P.emit("sync", lambda e: e.dma_start(out=mur_sb[:], in_=mu_r).then_inc(sW.h, 16))
        P.emit("sync", lambda e: e.dma_start(out=muk_sb[:], in_=mu_k).then_inc(sW.h, 16))
        P.emit("sync", lambda e: e.dma_start(out=sel_sb[:], in_=selp).then_inc(sW.h, 16))
        v_l3 = sW.n
        P.wait("sync", sGBp[0], sGBp[0].n)
        P.wait("sync", sGBp[1], sGBp[1].n)
        sG2.n += 48
        for wsrc, wdst in ((wrT_bf, wr_a), (wkT_bf, wk_a), (wvT_bf, wv_a)):
            P.emit("sync", lambda e, s=wsrc, d=wdst: e.dma_start(
                out=d[:], in_=s.rearrange("(kk p) c -> p kk c", p=128)).then_inc(sG2.h, 16))
        w3_ld = [sG2.n]
        P.wait("vector", sW, v_l3)
        P.emit("vector", lambda e: e.tensor_scalar(
            out=mur_sb[:], in0=mur_sb[:], scalar1=1.0, scalar2=None, op0=ALU.mult))
        P.inc("vector", sVS); P.wait("vector", sVS, sVS.n)

        bnd_st = 0
        bcol_ev = [0]
        for sc in range(NSC):
            L = sc * 512
            P.wait("scalar", sV, sV.n)
            P.emit("scalar", lambda e, L_=L: e.activation(
                out=d_fm[:], in_=stream_fm[:, :, L_:L_ + 512], func=AF.Square))
            va_sq3 = P.inc("scalar", sA)
            P.wait("tensor", sA, va_sq3)
            P.wait("tensor", sW, v_l3)
            for kk in range(8):
                P.emit("tensor", lambda e, k_=kk, L_=L: e.matmul(
                    p3_s[:], ones_sb[0:128, 0:1], stream_fm[:, k_, L_:L_ + 512],
                    start=(k_ == 0), stop=(k_ == 7)))
            vt_s1 = P.inc("tensor", sT)
            P.wait("scalar", sT, vt_s1)
            P.emit("scalar", lambda e, L_=L: e.activation(out=m_sb[0:1, L_:L_ + 512], in_=p3_s[:], func=AF.Copy))
            va_s1 = P.inc("scalar", sA)
            P.wait("tensor", sA, va_s1)
            for kk in range(8):
                P.emit("tensor", lambda e, k_=kk: e.matmul(
                    p3_s[:], ones_sb[0:128, 0:1], d_fm[:, k_, :],
                    start=(k_ == 0), stop=(k_ == 7)))
            vt_s2 = P.inc("tensor", sT)
            P.wait("scalar", sT, vt_s2)
            P.emit("scalar", lambda e: e.activation(out=stat_ev[0:1, :], in_=p3_s[:], func=AF.Copy))
            va_s2 = P.inc("scalar", sA)
            P.wait("vector", sA, va_s2)
            P.emit("vector", lambda e, L_=L: e.tensor_scalar_mul(
                out=m_sb[0:1, L_:L_ + 512], in0=m_sb[0:1, L_:L_ + 512], scalar1=1.0 / D))
            P.inc("vector", sVS); P.wait("vector", sVS, sVS.n)
            P.emit("vector", lambda e, L_=L: e.tensor_mul(
                out=mr_sb[0:1, :], in0=m_sb[0:1, L_:L_ + 512], in1=m_sb[0:1, L_:L_ + 512]))
            P.inc("vector", sVS); P.wait("vector", sVS, sVS.n)
            P.emit("vector", lambda e: e.tensor_scalar_mul(
                out=stat_ev[0:1, :], in0=stat_ev[0:1, :], scalar1=1.0 / D))
            P.inc("vector", sVS); P.wait("vector", sVS, sVS.n)
            P.emit("vector", lambda e: e.tensor_sub(
                out=stat_ev[0:1, :], in0=stat_ev[0:1, :], in1=mr_sb[0:1, :]))
            vv_var = P.inc("vector", sV)
            P.wait("scalar", sV, vv_var)
            P.emit("scalar", lambda e, L_=L: e.activation(
                out=r_std[0:1, L_:L_ + 512], in_=stat_ev[0:1, :], func=AF.Ln, bias=cst_sb[0:1, 0:1]))
            P.inc("scalar", sAS); P.wait("scalar", sAS, sAS.n)
            P.emit("scalar", lambda e, L_=L: e.activation(
                out=r_std[0:1, L_:L_ + 512], in_=r_std[0:1, L_:L_ + 512], func=AF.Exp, scale=-0.5))
            va_rstd = P.inc("scalar", sA)
            P.wait("tensor", sA, va_rstd)
            P.emit("tensor", lambda e, L_=L: e.matmul(
                p3_rep[0][:, 0:1], ones_sb[0:1, 0:128], r_std[0:1, L_ + 511:L_ + 512], start=True, stop=True))
            P.emit("tensor", lambda e, L_=L: e.matmul(
                p3_rep[1][:, 0:1], ones_sb[0:1, 0:128], m_sb[0:1, L_ + 511:L_ + 512], start=True, stop=True))
            vt_rep = P.inc("tensor", sT)
            P.wait("scalar", sT, vt_rep)
            P.emit("scalar", lambda e: e.activation(out=rep_ev[:, 0:1], in_=p3_rep[0][:, 0:1], func=AF.Copy))
            P.emit("scalar", lambda e: e.activation(out=rep_ev[:, 1:2], in_=p3_rep[1][:, 0:1], func=AF.Copy))
            va_repev = P.inc("scalar", sA)
            P.wait("vector", sA, va_repev)
            P.emit("vector", lambda e: e.tensor_mul(out=rep_ev[:, 1:2], in0=rep_ev[:, 1:2], in1=rep_ev[:, 0:1]))
            P.inc("vector", sVS); P.wait("vector", sVS, sVS.n)
            P.emit("vector", lambda e, L_=L: e.scalar_tensor_tensor(
                out=bcol_fm[:], in0=stream_fm[:, :, L_ + 511], scalar=rep_ev[:, 0:1],
                in1=rep_ev[:, 1:2].broadcast_to([128, 8]), op0=ALU.mult, op1=ALU.subtract))
            vv_bcol = P.inc("vector", sV)
            P.wait("tensor", sV, vv_bcol)
            P.wait("tensor", sA, bcol_ev[0])
            P.emit("tensor", lambda e: e.transpose(p3_t[0:8, 0:128], bcol_fm[:], ident_sb[:]))
            vt_b = P.inc("tensor", sT)
            P.wait("scalar", sT, vt_b)
            P.emit("scalar", lambda e: e.activation(out=bndrow[:], in_=p3_t[0:8, 0:128], func=AF.Copy))
            va_bnd = P.inc("scalar", sA)
            bcol_ev[0] = va_bnd
            P.wait("sync", sA, va_bnd)
            sDBG.n += 16
            P.emit("sync", lambda e, s=sc, w=which: e.dma_start(
                out=bnd_my[w][s, :].rearrange("(a c) -> a c", a=8), in_=bndrow[:]).then_inc(sDBG.h, 16))
        bnd_st = sDBG.n
        P.wait("gpsimd", sDBG, bnd_st)
        sCC.n += 1
        P.emit("gpsimd", lambda e, w=which: e.collective_compute(
            "AllGather", ALU.bypass, replica_groups=GROUPS,
            ins=[bnd_my[w].opt()], outs=[bnd_all[w].opt()]).then_inc(sCC.h, 1))
        v_agb = sCC.n
        P.wait("gpsimd", sCC, v_agb)
        P.wait("sync", sCC, v_agb)
        sW.n += 16
        P.emit("sync", lambda e, w=which: e.dma_start(out=bnd_sb2[:], in_=bnd_all[w]).then_inc(sW.h, 16))
        v_bl = sW.n
        P.wait("tensor", sW, v_bl)
        for hh in range(2):
            P.emit("tensor", lambda e, h_=hh: e.matmul(
                p3_rep[h_][0:4, :], sel_sb[:], bnd_sb2[:, h_ * 512:(h_ + 1) * 512], start=True, stop=True))
        vt_sel = P.inc("tensor", sT)
        P.wait("scalar", sT, vt_sel)
        P.emit("scalar", lambda e: e.activation(out=prev_sb[0:4, 0:512], in_=p3_rep[0][0:4, :], func=AF.Copy))
        P.emit("scalar", lambda e: e.activation(out=prev_sb[0:4, 512:1024], in_=p3_rep[1][0:4, :], func=AF.Copy))
        va_prev = P.inc("scalar", sA)
        P.wait("tensor", sA, va_prev)
        ev_pt = 0
        for kk in range(8):
            P.wait("tensor", sA, ev_pt)
            P.emit("tensor", lambda e, k_=kk: e.transpose(
                p3_t[:], prev_sb[:, k_ * 128:(k_ + 1) * 128], ident_sb[:]))
            vt_p = P.inc("tensor", sT)
            P.wait("scalar", sT, vt_p)
            P.emit("scalar", lambda e, k_=kk: e.activation(
                out=prevf_sb[:, k_, :], in_=p3_t[:, 0:4], func=AF.Copy))
            ev_pt = P.inc("scalar", sA)
        va_prevf = sA.n

        def ffn_block(sc):
            L = sc * 512
            P.wait("vector", sA, va_prevf)
            P.emit("vector", lambda e, L_=L: e.tensor_mul(
                out=mr_sb[0:1, :], in0=m_sb[0:1, L_:L_ + 512], in1=r_std[0:1, L_:L_ + 512]))
            vv_mr = P.inc("vector", sV)
            P.wait("tensor", sV, vv_mr)
            P.emit("tensor", lambda e, L_=L: e.matmul(
                p3_rep[0][:], ones_sb[0:1, 0:128], r_std[0:1, L_:L_ + 512], start=True, stop=True))
            P.emit("tensor", lambda e: e.matmul(
                p3_rep[1][:], ones_sb[0:1, 0:128], mr_sb[0:1, :], start=True, stop=True))
            vt_rp = P.inc("tensor", sT)
            P.wait("vector", sT, vt_rp)
            for a in range(8):
                P.emit("vector", lambda e, a_=a, L_=L: e.tensor_mul(
                    out=ln2_fm[:, a_, :], in0=stream_fm[:, a_, L_:L_ + 512], in1=p3_rep[0][:]))
            P.inc("vector", sVS); P.wait("vector", sVS, sVS.n)
            for a in range(8):
                P.emit("vector", lambda e, a_=a: e.tensor_sub(
                    out=ln2_fm[:, a_, :], in0=ln2_fm[:, a_, :], in1=p3_rep[1][:]))
            P.inc("vector", sVS); P.wait("vector", sVS, sVS.n)
            P.emit("vector", lambda e: e.tensor_sub(
                out=d_fm[:, :, 1:512].rearrange("p a t -> p (a t)") if False else d_fm[:].rearrange("p a t -> p a t")[:, :, 1:512],
                in0=ln2_fm[:].rearrange("p a t -> p a t")[:, :, 1:512],
                in1=ln2_fm[:].rearrange("p a t -> p a t")[:, :, 0:511]))
            P.emit("vector", lambda e, s_=sc: e.tensor_sub(
                out=d_fm[:, :, 0:1], in0=ln2_fm[:, :, 0:1], in1=prevf_sb[:, :, s_:s_ + 1]))
            vv_d = P.inc("vector", sV)
            for a in range(8):
                P.emit("vector", lambda e, a_=a: e.scalar_tensor_tensor(
                    out=xr_bf[:, a_, 1:512], in0=d_fm[:, a_, 1:512], scalar=mur_sb[:, a_:a_ + 1],
                    in1=ln2_fm[:, a_, 0:511], op0=ALU.mult, op1=ALU.add))
                P.emit("vector", lambda e, a_=a, s_=sc: e.scalar_tensor_tensor(
                    out=xr_bf[:, a_, 0:1], in0=d_fm[:, a_, 0:1], scalar=mur_sb[:, a_:a_ + 1],
                    in1=prevf_sb[:, a_, s_:s_ + 1], op0=ALU.mult, op1=ALU.add))
                P.emit("vector", lambda e, a_=a: e.scalar_tensor_tensor(
                    out=xk_bf[:, a_, 1:512], in0=d_fm[:, a_, 1:512], scalar=muk_sb[:, a_:a_ + 1],
                    in1=ln2_fm[:, a_, 0:511], op0=ALU.mult, op1=ALU.add))
                P.emit("vector", lambda e, a_=a, s_=sc: e.scalar_tensor_tensor(
                    out=xk_bf[:, a_, 0:1], in0=d_fm[:, a_, 0:1], scalar=muk_sb[:, a_:a_ + 1],
                    in1=prevf_sb[:, a_, s_:s_ + 1], op0=ALU.mult, op1=ALU.add))
            vv_mix = P.inc("vector", sV)
            t_rk = {}
            for m in range(8):
                P.wait("tensor", sG2, w3_ld[0])
                P.wait("tensor", sV, vv_mix)
                if m >= 1:
                    P.wait("tensor", sA, va_rk.get(m - 1, 0))
                for kk in range(8):
                    P.emit("tensor", lambda e, k_=kk, m_=m: e.matmul(
                        p3_r[:], wr_a[:, k_, m_ * 128:(m_ + 1) * 128], xr_bf[:, k_, :], start=(k_ == 0), stop=(k_ == 7)))
                for kk in range(8):
                    P.emit("tensor", lambda e, k_=kk, m_=m: e.matmul(
                        p3_k[:], wk_a[:, k_, m_ * 128:(m_ + 1) * 128], xk_bf[:, k_, :], start=(k_ == 0), stop=(k_ == 7)))
                t_rk[m] = P.inc("tensor", sT)
                P.wait("scalar", sT, t_rk[m])
                P.emit("scalar", lambda e, m_=m: e.activation(
                    out=r_sb[:, m_, :], in_=p3_r[:], func=AF.Sigmoid))
                va_sig = P.inc("scalar", sA)
                P.wait("vector", sT, t_rk[m])
                P.emit("vector", lambda e: e.tensor_relu(out=kr_sb[:], in_=p3_k[:]))
                vv_kr = P.inc("vector", sV)
                P.wait("scalar", sV, vv_kr)
                P.emit("scalar", lambda e, m_=m: e.activation(
                    out=k2_bf[:, m_, :], in_=kr_sb[:], func=AF.Square))
                va_rk[m] = P.inc("scalar", sA)
            for m in range(8):
                P.wait("tensor", sA, va_rk[7])
                if m >= 1:
                    P.wait("tensor", sV, vv_str.get(m - 1, 0))
                for kk in range(8):
                    P.emit("tensor", lambda e, k_=kk, m_=m: e.matmul(
                        p3_v[:], wv_a[:, k_, m_ * 128:(m_ + 1) * 128], k2_bf[:, k_, :], start=(k_ == 0), stop=(k_ == 7)))
                t_v[m] = P.inc("tensor", sT)
                P.wait("vector", sT, t_v[m])
                P.emit("vector", lambda e, m_=m: e.tensor_copy(out=kr_sb[:], in_=r_sb[:, m_, :]))
                P.emit("vector", lambda e: e.tensor_mul(out=t2_s[:], in0=kr_sb[:], in1=p3_v[:]))
                P.inc("vector", sVS); P.wait("vector", sVS, sVS.n)
                P.emit("vector", lambda e, m_=m, L_=L: e.tensor_add(
                    out=stream_fm[:, m_, L_:L_ + 512], in0=stream_fm[:, m_, L_:L_ + 512], in1=t2_s[:]))
                vv_str[m] = P.inc("vector", sV)
            return sV.n, sA.n, sT.n

        va_rk = {}
        t_v = {}
        vv_str = {}
        f_done = []
        for sc in range(NSC):
            if sc > 0:
                for _e in ("sync", "vector", "scalar", "tensor"):
                    P.wait(_e, sV, f_done[-1][0])
                    P.wait(_e, sA, f_done[-1][1])
            f_done.append(ffn_block(sc))
        if dbg_out is not None:
            P.wait("sync", sV, sV.n)
            P.wait("sync", sA, sA.n)
            sDBG.n += 16
            P.emit("sync", lambda e: e.dma_start(out=dbg_out, in_=stream_fm[:]).then_inc(sDBG.h, 16))
        es3.close()
        return


    es2.close()
    for _e in ENGS:
        P.wait(_e, sT, sT.n)
        P.wait(_e, sA, sA.n)
        P.wait(_e, sV, sV.n)
        P.wait(_e, sP, sP.n)

    def attn_phase(dbg_out=None):
        es4 = ExitStack()
        E4 = lambda t: es4.enter_context(t)
        xh_q = E4(nc.sbuf_tensor("a_xhq", [128, 8, TQ], BF16))
        den_sb = E4(nc.sbuf_tensor("a_den", [1, 512], F32))
        invrep = E4(nc.sbuf_tensor("a_invrep", [128, 512], F32))
        t4 = E4(nc.sbuf_tensor("a_t4", [128, 512], F32))
        p4_s = E4(nc.psum_tensor("p4_s", [128, 512], F32))
        p4_rep = [E4(nc.psum_tensor(f"p4_rep{i}", [128, 512], F32)) for i in range(2)]
        p4_o = [E4(nc.psum_tensor(f"p4_o{i}", [128, 512], F32)) for i in range(4)]
        es4a = ExitStack()
        E4a = lambda t: es4a.enter_context(t)
        m4 = E4a(nc.sbuf_tensor("a_m", [1, 512], F32))
        r4 = E4a(nc.sbuf_tensor("a_rstd", [1, 512], F32))
        mr4 = E4a(nc.sbuf_tensor("a_mr", [1, 512], F32))
        sq4 = E4a(nc.sbuf_tensor("a_sq", [128, 8, 512], F32))
        stat4 = E4a(nc.sbuf_tensor("a_stat", [1, 1024], F32))
        ln3_fm = E4a(nc.sbuf_tensor("a_ln3", [128, 8, 512], F32))
        x3c_bf = E4a(nc.sbuf_tensor("a_x3c", [128, 8, 512], BF16))
        x3tm = E4a(nc.sbuf_tensor("a_x3tm", [128, 4, D], BF16))
        p4_tb = E4a(nc.psum_tensor("p4_tb", [128, 128], BF16))

        st_xh = 0
        st_xv = 0
        ev_tb = [0]
        for sc in range(NSC):
            L = sc * 512
            P.wait("scalar", sV, sV.n)
            P.emit("scalar", lambda e, L_=L: e.activation(
                out=sq4[:], in_=stream_fm[:, :, L_:L_ + 512], func=AF.Square))
            va_sq4 = P.inc("scalar", sA)
            P.wait("tensor", sA, va_sq4)
            for kk in range(8):
                P.emit("tensor", lambda e, k_=kk, L_=L: e.matmul(
                    p4_s[0:1, :], ones_sb[0:128, 0:1], stream_fm[:, k_, L_:L_ + 512],
                    start=(k_ == 0), stop=(k_ == 7)))
            vt1 = P.inc("tensor", sT)
            P.wait("scalar", sT, vt1)
            P.emit("scalar", lambda e: e.activation(out=stat4[0:1, 0:512], in_=p4_s[0:1, :], func=AF.Copy))
            va1 = P.inc("scalar", sA)
            P.wait("tensor", sA, va1)
            for kk in range(8):
                P.emit("tensor", lambda e, k_=kk: e.matmul(
                    p4_s[0:1, :], ones_sb[0:128, 0:1], sq4[:, k_, :], start=(k_ == 0), stop=(k_ == 7)))
            vt2 = P.inc("tensor", sT)
            P.wait("scalar", sT, vt2)
            P.emit("scalar", lambda e: e.activation(out=stat4[0:1, 512:1024], in_=p4_s[0:1, :], func=AF.Copy))
            va2 = P.inc("scalar", sA)
            P.wait("vector", sA, va2)
            P.emit("vector", lambda e, L_=L: e.tensor_scalar_mul(
                out=m4[0:1, :], in0=stat4[0:1, 0:512], scalar1=1.0 / D))
            P.inc("vector", sVS); P.wait("vector", sVS, sVS.n)
            P.emit("vector", lambda e, L_=L: e.tensor_mul(
                out=stat4[0:1, 0:512], in0=m4[0:1, :], in1=m4[0:1, :]))
            P.inc("vector", sVS); P.wait("vector", sVS, sVS.n)
            P.emit("vector", lambda e: e.tensor_scalar_mul(
                out=stat4[0:1, 512:1024], in0=stat4[0:1, 512:1024], scalar1=1.0 / D))
            P.inc("vector", sVS); P.wait("vector", sVS, sVS.n)
            P.emit("vector", lambda e: e.tensor_sub(
                out=stat4[0:1, 512:1024], in0=stat4[0:1, 512:1024], in1=stat4[0:1, 0:512]))
            vv_v4 = P.inc("vector", sV)
            P.wait("scalar", sV, vv_v4)
            P.emit("scalar", lambda e, L_=L: e.activation(
                out=r4[0:1, :], in_=stat4[0:1, 512:1024], func=AF.Ln, bias=cst_sb[0:1, 0:1]))
            P.inc("scalar", sAS); P.wait("scalar", sAS, sAS.n)
            P.emit("scalar", lambda e, L_=L: e.activation(
                out=r4[0:1, :], in_=r4[0:1, :], func=AF.Exp, scale=-0.5))
            va_r4 = P.inc("scalar", sA)
            P.wait("vector", sA, va_r4)
            P.emit("vector", lambda e, L_=L: e.tensor_mul(
                out=mr4[0:1, :], in0=m4[0:1, :], in1=r4[0:1, :]))
            vv_mr4 = P.inc("vector", sV)
            P.wait("tensor", sV, vv_mr4)
            P.emit("tensor", lambda e, L_=L: e.matmul(
                p4_rep[0][:], ones_sb[0:1, 0:128], r4[0:1, :], start=True, stop=True))
            P.emit("tensor", lambda e, L_=L: e.matmul(
                p4_rep[1][:], ones_sb[0:1, 0:128], mr4[0:1, :], start=True, stop=True))
            vt_rp = P.inc("tensor", sT)
            P.wait("vector", sT, vt_rp)
            for a in range(8):
                P.emit("vector", lambda e, a_=a, L_=L: e.tensor_mul(
                    out=ln3_fm[:, a_, :], in0=stream_fm[:, a_, L_:L_ + 512], in1=p4_rep[0][:]))
            P.inc("vector", sVS); P.wait("vector", sVS, sVS.n)
            for a in range(8):
                P.emit("vector", lambda e, a_=a: e.tensor_sub(
                    out=ln3_fm[:, a_, :], in0=ln3_fm[:, a_, :], in1=p4_rep[1][:]))
            P.inc("vector", sVS); P.wait("vector", sVS, sVS.n)
            for a in range(8):
                P.emit("vector", lambda e, a_=a, L_=L: e.tensor_add(
                    out=stream_fm[:, a_, L_:L_ + 512], in0=stream_fm[:, a_, L_:L_ + 512],
                    in1=ln3_fm[:, a_, :]))
            vv_res = P.inc("vector", sV)
            P.wait("scalar", sV, vv_res)
            P.emit("scalar", lambda e: e.activation(out=sq4[:], in_=ln3_fm[:], func=AF.Square))
            va_sq5 = P.inc("scalar", sA)
            P.wait("tensor", sA, va_sq5)
            for kk in range(8):
                P.emit("tensor", lambda e, k_=kk: e.matmul(
                    p4_s[0:1, :], ones_sb[0:128, 0:1], sq4[:, k_, :], start=(k_ == 0), stop=(k_ == 7)))
            vt_n = P.inc("tensor", sT)
            P.wait("scalar", sT, vt_n)
            P.emit("scalar", lambda e: e.activation(out=stat4[0:1, 0:512], in_=p4_s[0:1, :], func=AF.Copy))
            va_n = P.inc("scalar", sA)
            P.wait("vector", sA, va_n)
            P.emit("vector", lambda e: e.tensor_scalar_max(out=stat4[0:1, 0:512], in0=stat4[0:1, 0:512], scalar1=1e-8))
            vv_mx = P.inc("vector", sV)
            P.wait("scalar", sV, vv_mx)
            P.emit("scalar", lambda e: e.activation(out=stat4[0:1, 0:512], in_=stat4[0:1, 0:512], func=AF.Ln))
            P.inc("scalar", sAS); P.wait("scalar", sAS, sAS.n)
            P.emit("scalar", lambda e: e.activation(out=stat4[0:1, 0:512], in_=stat4[0:1, 0:512], func=AF.Exp, scale=-0.5))
            va_inv = P.inc("scalar", sA)
            P.wait("tensor", sA, va_inv)
            P.emit("tensor", lambda e: e.matmul(
                p4_rep[0][:], ones_sb[0:1, 0:128], stat4[0:1, 0:512], start=True, stop=True))
            vt_ir = P.inc("tensor", sT)
            P.wait("vector", sT, vt_ir)
            for a in range(8):
                P.emit("vector", lambda e, a_=a, L_=L: e.tensor_mul(
                    out=xh_q[:, a_, L_:L_ + 512], in0=ln3_fm[:, a_, :], in1=p4_rep[0][:]))
            vv_xh = P.inc("vector", sV)
            P.wait("scalar", sV, vv_xh)
            P.emit("scalar", lambda e: e.activation(out=x3c_bf[:], in_=ln3_fm[:], func=AF.Copy))
            va_x3c = P.inc("scalar", sA)
            P.wait("tensor", sA, va_x3c)
            for tt in range(4):
                for kk in range(8):
                    P.wait("tensor", sA, ev_tb[0])
                    P.emit("tensor", lambda e, t_=tt, k_=kk: e.transpose(
                        p4_tb[:], x3c_bf[:, k_, t_ * 128:(t_ + 1) * 128], ident_sb[:]))
                    vtb = P.inc("tensor", sT)
                    P.wait("scalar", sT, vtb)
                    P.emit("scalar", lambda e, t_=tt, k_=kk: e.activation(
                        out=x3tm[:, t_, k_ * 128:(k_ + 1) * 128], in_=p4_tb[:], func=AF.Copy))
                    ev_tb[0] = P.inc("scalar", sA)
            P.wait("sync", sA, sA.n)
            P.wait("sync", sV, vv_xh)
            sHT.n += 32
            P.emit("sync", lambda e, L_=L, s_=sc: e.dma_start(
                out=xh_my[s_], in_=xh_q[:, :, L_:L_ + 512]).then_inc(sHT.h, 16))
            P.emit("sync", lambda e, L_=L: e.dma_start(
                out=xv_my[L_:L_ + 512, :].rearrange("(t p) c -> p t c", p=128),
                in_=x3tm[:]).then_inc(sHT.h, 16))
        st_keys = sHT.n
        es4a.close()
        E_sb = E4(nc.sbuf_tensor("a_E", [128, 64, 512], BF16))
        xh_k = [E4(nc.sbuf_tensor(f"a_xhk{i}", [128, 8, 512], BF16)) for i in range(2)]
        xv_k = [E4(nc.sbuf_tensor(f"a_xvk{i}", [128, D], BF16)) for i in range(2)]
        p4_ox = E4(nc.psum_tensor("p4_ox", [128, 512], F32))
        OB = [p4_o[0], p4_o[1], p4_o[2], p4_o[3], p4_rep[0], p4_rep[1], p4_s, p4_ox]
        P.wait("gpsimd", sHT, st_keys)
        sCC.n += 2
        P.emit("gpsimd", lambda e: e.collective_compute(
            "AllGather", ALU.bypass, replica_groups=GROUPS,
            ins=[xh_my.opt()], outs=[xh_all.opt()]).then_inc(sCC.h, 1))
        P.emit("gpsimd", lambda e: e.collective_compute(
            "AllGather", ALU.bypass, replica_groups=GROUPS,
            ins=[xv_my.opt()], outs=[xv_all.opt()]).then_inc(sCC.h, 1))
        v_agh = sCC.n - 1
        v_agv = sCC.n

        exp_done = {}
        pv_done_t = [0]
        evO = {}
        xh_ld = {}
        den_ev = [0]
        for qc in range(4):
            Q0 = qc * 512
            for kt in range(64):
                k4, ko = kt // 4, (kt % 4) * 128
                r_, j4 = kt // 16, (kt % 16) // 4
                pk = k4 % 2
                shk = sH[pk]
                if kt % 4 == 0:
                    qe = "sync"
                    if k4 >= 2:
                        P.wait(qe, sT, qk_done[(qc, (k4 - 2) * 4 + 3)])
                    P.wait(qe, sCC, v_agh)
                    shk.n += 16
                    P.emit(qe, lambda e, r2=r_, j2=j4, p=pk, s=shk: e.dma_start(
                        out=xh_k[p][:], in_=xh_all[r2 * 4 + j2]).then_inc(s.h, 16))
                    xh_ld[(qc, k4)] = shk.n
                P.wait("tensor", shk, xh_ld[(qc, k4)])
                if qc > 0 and kt == 0:
                    P.wait("tensor", sV, str_done[qc - 1])
                if qc > 0 and kt == 0:
                    P.wait("scalar", sT, pv_done_t[0])
                pq = kt % 2
                if kt >= 2:
                    P.wait("tensor", sA, exp_done[(qc, kt - 2)])
                for kk in range(8):
                    P.emit("tensor", lambda e, k_=kk, p=pk, pr=pq, q_=Q0, o=ko: e.matmul(
                        p4_rep[pr][:], xh_k[p][:, k_, o:o + 128], xh_q[:, k_, q_:q_ + 512],
                        start=(k_ == 0), stop=(k_ == 7)))
                vt_qk = P.inc("tensor", sT)
                qk_done[(qc, kt)] = vt_qk
                P.wait("scalar", sT, vt_qk)
                P.emit("scalar", lambda e, kt_=kt, p=pq: e.activation(
                    out=E_sb[:, kt_, :], in_=p4_rep[p][:], func=AF.Exp))
                exp_done[(qc, kt)] = P.inc("scalar", sA)
            P.wait("tensor", sA, exp_done[(qc, 63)])
            P.wait("tensor", sA, den_ev[0])
            for kt in range(64):
                P.emit("tensor", lambda e, kt_=kt: e.matmul(
                    p4_s[0:1, :], onesbf_sb[:], E_sb[:, kt_, :], start=(kt_ == 0), stop=(kt_ == 63)))
            vt_den = P.inc("tensor", sT)
            P.wait("scalar", sT, vt_den)
            P.emit("scalar", lambda e: e.activation(out=den_sb[:], in_=p4_s[0:1, :], func=AF.Copy))
            va_den = P.inc("scalar", sA)
            den_ev[0] = va_den
            P.wait("vector", sA, va_den)
            P.emit("vector", lambda e: e.reciprocal(out=den_sb[:], in_=den_sb[:]))
            vv_inv = P.inc("vector", sV)
            P.wait("tensor", sV, vv_inv)
            P.emit("tensor", lambda e: e.matmul(
                p4_rep[0][:], ones_sb[0:1, 0:128], den_sb[:], start=True, stop=True))
            vt_ivr = P.inc("tensor", sT)
            P.wait("scalar", sT, vt_ivr)
            P.emit("scalar", lambda e: e.activation(out=invrep[:], in_=p4_rep[0][:], func=AF.Copy))
            va_ivr = P.inc("scalar", sA)
            for kt in range(64):
                r_, j_ = kt // 16, kt % 16
                pk = kt % 2
                svk = sX[pk]
                if kt >= 2:
                    P.wait("scalar", sT, pv_mm[(qc, kt - 2)])
                if kt < 2:
                    P.wait("scalar", sCC, v_agv)
                svk.n += 16
                P.emit("scalar", lambda e, r2=r_, j2=j_, p=pk, s=svk: e.dma_start(
                    out=xv_k[p][:],
                    in_=xv_all[r2 * TQ + j2 * 128:r2 * TQ + (j2 + 1) * 128, :]
                ).then_inc(s.h, 16))
                v_v = svk.n
                P.wait("tensor", svk, v_v)
                if kt == 0:
                    P.wait("tensor", sA, exp_done[(qc, 63)])
                    P.wait("tensor", sA, va_ivr)
                    P.wait("tensor", sV, str_done.get(qc - 1, 0))
                for ci in range(8):
                    P.emit("tensor", lambda e, c2=ci, p=pk, kt_=kt: e.matmul(
                        OB[c2][:], xv_k[p][:, c2 * 128:(c2 + 1) * 128], E_sb[:, kt_, :],
                        start=(kt_ == 0), stop=(kt_ == 63)))
                pv_mm[(qc, kt)] = P.inc("tensor", sT)
            pv_done_t[0] = pv_mm[(qc, 63)]
            P.wait("vector", sT, pv_mm[(qc, 63)])
            P.wait("vector", sA, va_ivr)
            for ci in range(8):
                P.emit("vector", lambda e, c2=ci: e.tensor_mul(out=t4[:], in0=OB[c2][:], in1=invrep[:]))
                P.inc("vector", sVS); P.wait("vector", sVS, sVS.n)
                P.emit("vector", lambda e, c2=ci, q_=Q0: e.tensor_add(
                    out=stream_fm[:, c2, q_:q_ + 512],
                    in0=stream_fm[:, c2, q_:q_ + 512], in1=t4[:]))
            str_done[qc] = P.inc("vector", sV)
        if dbg_out is not None:
            P.wait("sync", sV, sV.n)
            P.wait("sync", sA, sA.n)
            sDBG.n += 16
            P.emit("sync", lambda e: e.dma_start(out=dbg_out, in_=stream_fm[:]).then_inc(sDBG.h, 16))
        es4.close()

    qk_done = {}
    pv_mm = {}
    str_done = {}

    DO_ATTN = 2
    if FFN_STOP >= 1:
        ffn_phase(0)
        if FFN_STOP >= 4 and DO_ATTN:
            for _e in ENGS:
                P.wait(_e, sT, sT.n)
                P.wait(_e, sA, sA.n)
                P.wait(_e, sV, sV.n)
                P.wait(_e, sP, sP.n)
            attn_phase()
            if DO_ATTN >= 2:
                for _e in ENGS:
                    P.wait(_e, sT, sT.n)
                    P.wait(_e, sA, sA.n)
                    P.wait(_e, sV, sV.n)
                    P.wait(_e, sP, sP.n)
                ffn_phase(1)
                for _e in ("tensor", "scalar", "sync"):
                    P.wait(_e, sT, sT.n)
                    P.wait(_e, sA, sA.n)
                    P.wait(_e, sV, sV.n)
                es6 = ExitStack()
                p6_t = [es6.enter_context(nc.psum_tensor(f"p6_t{i}", [128, 128], F32)) for i in range(2)]
                o6 = es6.enter_context(nc.sbuf_tensor("o6", [128, 4, D], F32))
                ev6 = {}
                st6 = {}
                q6 = [(sOP[0], "sync"), (sOP[1], "scalar")]
                for ti in range(16):
                    ob = ti % 4
                    sq, qe = q6[ti % 2]
                    if ti >= 4:
                        P.wait("scalar", st6[ti - 4][0], st6[ti - 4][1])
                    for kk in range(8):
                        sl = (ti * 8 + kk) % 2
                        P.wait("tensor", sA, ev6.get(sl, 0))
                        P.emit("tensor", lambda e, t_=ti, k_=kk, s=sl: e.transpose(
                            p6_t[s][:], stream_fm[:, k_, t_ * 128:(t_ + 1) * 128], identf_sb[:]))
                        vt6 = P.inc("tensor", sT)
                        P.wait("scalar", sT, vt6)
                        P.emit("scalar", lambda e, k_=kk, s=sl, o_=ob: e.activation(
                            out=o6[:, o_, k_ * 128:(k_ + 1) * 128], in_=p6_t[s][:], func=AF.Copy))
                        ev6[sl] = P.inc("scalar", sA)
                    P.wait(qe, sA, sA.n)
                    sq.n += 16
                    P.emit(qe, lambda e, t_=ti, o_=ob, s=sq: e.dma_start(
                        out=out_q[t_ * 128:(t_ + 1) * 128, :], in_=o6[:, o_, :]).then_inc(s.h, 16))
                    st6[ti] = (sq, sq.n)
                for sq, _qe in q6:
                    P.wait("sync", sq, sq.n)
    P.run(nc)
    return nc



def _make_selp(r):
    sel = np.zeros((16, 4), np.float32)
    for sc in range(4):
        if r > 0:
            sel[(r - 1) * 4 + sc, sc] = 1.0
        elif sc > 0:
            sel[3 * 4 + (sc - 1), sc] = 1.0
    return sel.astype(ml_dtypes.bfloat16)


def _gidx(r):
    return np.concatenate([np.arange(sc * 2048 + r * 512, sc * 2048 + r * 512 + 512)
                           for sc in range(4)])


def _mk_ind3():
    a = np.zeros((128, 4), np.float32)
    a[0:64, 0] = 1.0
    a[64:80, 1] = 1.0
    a[80:96, 2] = 1.0
    return a


def _mk_ind2():
    a = np.zeros((3, 384), np.float32)
    for r_ in range(3):
        a[r_, r_ * 128:(r_ + 1) * 128] = 1.0
    return a


def _mk_cst3():
    a = np.zeros((128, 1), np.float32)
    a[0, 0] = 1.0 / RDT
    a[1, 0] = 1.0 / NST
    a[2, 0] = 1.0 / NST
    return a


_IND3 = _mk_ind3()
_IND2 = _mk_ind2()
_CST3 = _mk_cst3()


def _shard_inputs(inputs, core):
    b, r = core // 4, core % 4
    d0, d1 = r * DSL, (r + 1) * DSL
    def shp(v):
        return np.ascontiguousarray(v.reshape(4, 128).T)
    x = inputs["L0_attn_a_hidden_states"]
    w_in_full = inputs["L0_attn_a_in_proj_w"]
    return dict(
        x_q=np.ascontiguousarray(x[b][_gidx(r)]),
        w_in=np.ascontiguousarray(np.concatenate(
            [w_in_full[:, d0:d1], w_in_full[:, DI + d0:DI + d1]], axis=1)),
        conv_w=np.ascontiguousarray(
            inputs["L0_attn_a_conv_w"][d0:d1].reshape(4, 128, KC).transpose(1, 0, 2)),
        conv_b=shp(inputs["L0_attn_a_conv_b"][d0:d1]),
        xproj_w=np.ascontiguousarray(inputs["L0_attn_a_x_proj_w"][d0:d1]),
        dtw=np.ascontiguousarray(inputs["L0_attn_a_dt_proj_w"][:, d0:d1]),
        dtb=shp(inputs["L0_attn_a_dt_proj_b"][d0:d1]),
        dtlnw=np.ascontiguousarray(inputs["L0_attn_a_dt_ln_w"].reshape(RDT, 1)),
        blnw=np.ascontiguousarray(inputs["L0_attn_a_b_ln_w"].reshape(1, NST)),
        clnw=np.ascontiguousarray(inputs["L0_attn_a_c_ln_w"].reshape(1, NST)),
        alog=np.ascontiguousarray(
            inputs["L0_attn_a_A_log"][d0:d1].reshape(4, 128, NST).transpose(1, 0, 2)),
        dskip=shp(inputs["L0_attn_a_D_skip"][d0:d1]),
        wout=np.ascontiguousarray(inputs["L0_attn_a_out_proj_w"][d0:d1]),
        inlnw=np.ascontiguousarray(inputs["L0_attn_a_input_ln_w"].reshape(1, D)),
        ident_bf=np.eye(128, dtype=ml_dtypes.bfloat16),
        ident_f=np.eye(128, dtype=np.float32),
        gate_w=inputs["L0_attn_a_gate_w"],
        up_w=inputs["L0_attn_a_up_w"],
        down_w=inputs["L0_attn_a_down_w"],
        pmw=np.ascontiguousarray(inputs["L0_attn_a_pre_moe_ln_w"].reshape(1, D)),
        w_r=inputs["L0_ffn_W_r"],
        w_k=inputs["L0_ffn_W_k"],
        w_v=inputs["L0_ffn_W_v"],
        mu_r=np.ascontiguousarray(inputs["L0_ffn_mu_r"].reshape(8, 128).T),
        mu_k=np.ascontiguousarray(inputs["L0_ffn_mu_k"].reshape(8, 128).T),
        selp=_make_selp(r),
        ind3_d=_IND3, ind2_d=_IND2, cst3_d=_CST3,
    )


_NC_CACHE = [None]


def kernel(**inputs):
    if _NC_CACHE[0] is None:
        _NC_CACHE[0] = build(detect_races=False)
    nc = _NC_CACHE[0]
    in_maps = [_shard_inputs(inputs, c) for c in range(8)]
    res = run_bass_kernel_spmd(nc, in_maps, core_ids=list(range(8)))
    out = np.empty((B, S, D), np.float32)
    for c in range(8):
        b, r = c // 4, c % 4
        out[b][_gidx(r)] = res.results[c]["out_q"]
    return out


# version 20 (final)
# speedup vs baseline: 1.0405x; 1.0112x over previous
import numpy as np
import ml_dtypes
from contextlib import ExitStack
import concourse.bass as bass
import concourse.mybir as mybir
from concourse.bass_utils import run_bass_kernel_spmd

F32 = mybir.dt.float32
BF16 = mybir.dt.bfloat16
AF = mybir.ActivationFunctionType
ALU = mybir.AluOpType
AX = mybir.AxisListType

B, S, D = 2, 8192, 1024
DI, NST, RDT, KC, DF = 2048, 16, 64, 4, 4096
DSL = DI // 4
TQ = S // 4
TC = 512
NCH = S // TC
ARG = 1
NAR = NCH // ARG
NSC = 4
NSLOT = 2
ENGS = ("sync", "vector", "scalar", "gpsimd", "tensor")
GROUPS = [[0, 1, 2, 3], [4, 5, 6, 7]]


class Sem:
    def __init__(self, nc, es, name):
        self.h = es.enter_context(nc.semaphore(name))
        self.n = 0


class Prog:
    def __init__(self):
        self.plan = {e: [] for e in ENGS}
    def emit(self, eng, fn):
        self.plan[eng].append(("op", fn))
    def wait(self, eng, sem, val):
        if val > 0:
            self.plan[eng].append(("wait", sem.h, val))
    def inc(self, eng, sem, k=1):
        sem.n += k
        self.plan[eng].append(("inc", sem.h, k))
        return sem.n
    def run(self, nc):
        def runner(name):
            def go(e):
                last = None
                for item in self.plan[name]:
                    if item[0] == "op":
                        last = item[1](e)
                    elif item[0] == "wait":
                        e.wait_ge(item[1], item[2])
                    else:
                        assert last is not None, f"inc with no prior op on {name}"
                        last.then_inc(item[1], item[2])
                        last = None
            return go
        with nc.Block() as block:
            block.sync(runner("sync"))
            block.vector(runner("vector"))
            block.scalar(runner("scalar"))
            block.gpsimd(runner("gpsimd"))
            block.tensor(runner("tensor"))


def build(debug=(), detect_races=True):
    DO_CACHE = True
    DO_PH2 = True
    DO_WT = True
    FFN_STOP = 4

    nc = bass.Bass("TRN2", target_bir_lowering=False, detect_race_conditions=detect_races)
    es = ExitStack()
    E = lambda t: es.enter_context(t)

    x_q = nc.dram_tensor("x_q", [TQ, D], F32, kind="ExternalInput").ap()
    w_in = nc.dram_tensor("w_in", [D, 2 * DSL], F32, kind="ExternalInput").ap()
    conv_w = nc.dram_tensor("conv_w", [128, 4, KC], F32, kind="ExternalInput").ap()
    conv_b = nc.dram_tensor("conv_b", [128, 4], F32, kind="ExternalInput").ap()
    xproj_w = nc.dram_tensor("xproj_w", [DSL, 96], F32, kind="ExternalInput").ap()
    dtw = nc.dram_tensor("dtw", [RDT, DSL], F32, kind="ExternalInput").ap()
    dtb = nc.dram_tensor("dtb", [128, 4], F32, kind="ExternalInput").ap()
    dtlnw = nc.dram_tensor("dtlnw", [RDT, 1], F32, kind="ExternalInput").ap()
    blnw = nc.dram_tensor("blnw", [1, NST], F32, kind="ExternalInput").ap()
    clnw = nc.dram_tensor("clnw", [1, NST], F32, kind="ExternalInput").ap()
    alog = nc.dram_tensor("alog", [128, 4, NST], F32, kind="ExternalInput").ap()
    dskip = nc.dram_tensor("dskip", [128, 4], F32, kind="ExternalInput").ap()
    wout = nc.dram_tensor("wout", [DSL, D], F32, kind="ExternalInput").ap()
    inlnw = nc.dram_tensor("inlnw", [1, D], F32, kind="ExternalInput").ap()
    ident_bf = nc.dram_tensor("ident_bf", [128, 128], BF16, kind="ExternalInput").ap()
    ident_f = nc.dram_tensor("ident_f", [128, 128], F32, kind="ExternalInput").ap()
    gate_w = nc.dram_tensor("gate_w", [D, DF], F32, kind="ExternalInput").ap()
    up_w = nc.dram_tensor("up_w", [D, DF], F32, kind="ExternalInput").ap()
    down_w = nc.dram_tensor("down_w", [DF, D], F32, kind="ExternalInput").ap()
    pmw = nc.dram_tensor("pmw", [1, D], F32, kind="ExternalInput").ap()
    w_r = nc.dram_tensor("w_r", [D, D], F32, kind="ExternalInput").ap()
    w_k = nc.dram_tensor("w_k", [D, D], F32, kind="ExternalInput").ap()
    w_v = nc.dram_tensor("w_v", [D, D], F32, kind="ExternalInput").ap()
    mu_r = nc.dram_tensor("mu_r", [128, 8], F32, kind="ExternalInput").ap()
    mu_k = nc.dram_tensor("mu_k", [128, 8], F32, kind="ExternalInput").ap()
    selp = nc.dram_tensor("selp", [16, 4], BF16, kind="ExternalInput").ap()
    ind3_d = nc.dram_tensor("ind3_d", [128, 4], F32, kind="ExternalInput").ap()
    ind2_d = nc.dram_tensor("ind2_d", [3, 384], F32, kind="ExternalInput").ap()
    cst3_d = nc.dram_tensor("cst3_d", [128, 1], F32, kind="ExternalInput").ap()

    out_q = nc.dram_tensor("out_q", [TQ, D], F32, kind="ExternalOutput").ap()

    hT_my = nc.dram_tensor("hT_my", [D, TQ], BF16).ap()
    hT_all = nc.dram_tensor("hT_all", [4 * D, TQ], BF16).ap()
    xp_part = [nc.dram_tensor(f"xp_part{j}", [96, ARG * TC], F32).ap() for j in range(NAR)]
    xp_red = [nc.dram_tensor(f"xp_red{j}", [96, ARG * TC], F32).ap() for j in range(NAR)]
    op_part = nc.dram_tensor("op_part", [S, D], F32).ap()
    gate_bf = nc.dram_tensor("gate_bf", [D, DF], BF16).ap()
    up_bf = nc.dram_tensor("up_bf", [D, DF], BF16).ap()
    down_bf = nc.dram_tensor("down_bf", [DF, D], BF16).ap()
    wrT_bf = nc.dram_tensor("wrT_bf", [D, D], BF16).ap()
    wkT_bf = nc.dram_tensor("wkT_bf", [D, D], BF16).ap()
    wvT_bf = nc.dram_tensor("wvT_bf", [D, D], BF16).ap()
    bnd_my = [nc.dram_tensor(f"bnd_my{i}", [4, D], BF16).ap() for i in range(2)]
    bnd_all = [nc.dram_tensor(f"bnd_all{i}", [16, D], BF16).ap() for i in range(2)]
    xh_my = nc.dram_tensor("xh_my", [4, 128, 8, 512], BF16).ap()
    xh_all = nc.dram_tensor("xh_all", [16, 128, 8, 512], BF16).ap()
    xv_my = nc.dram_tensor("xv_my", [TQ, D], BF16).ap()
    xv_all = nc.dram_tensor("xv_all", [S, D], BF16).ap()
    rs1_out = nc.dram_tensor("rs1_out", [TQ, D], F32).ap()

    ident_sb = E(nc.sbuf_tensor("ident_sb", [128, 128], BF16))
    ones_sb = E(nc.sbuf_tensor("ones_sb", [128, 128], F32))
    cst_sb = E(nc.sbuf_tensor("cst_sb", [128, 2], F32))
    onesbf_sb = E(nc.sbuf_tensor("onesbf_sb", [128, 1], BF16))
    es1 = ExitStack()
    E1 = lambda t: es1.enter_context(t)
    w_in_bf = E1(nc.sbuf_tensor("w_in_bf", [128, 8, 2 * DSL], BF16))
    xproj_sb = E1(nc.sbuf_tensor("xproj_sb", [128, 4, 96], F32))
    dtw_sb = E1(nc.sbuf_tensor("dtw_sb", [RDT, DSL], F32))
    wout_bf = E1(nc.sbuf_tensor("wout_bf", [128, 4, D], BF16))
    convw_sb = E1(nc.sbuf_tensor("convw_sb", [128, 4, KC], F32))
    convb_sb = E1(nc.sbuf_tensor("convb_sb", [128, 4], F32))
    dtb_sb = E1(nc.sbuf_tensor("dtb_sb", [128, 4], F32))
    dtlnw_sb = E1(nc.sbuf_tensor("dtlnw_sb", [RDT, 1], F32))
    clnw_eff = E1(nc.sbuf_tensor("clnw_eff", [128, NST], F32))
    blnw_rep = E1(nc.sbuf_tensor("blnw_rep", [128, NST], F32))
    A_sb = E1(nc.sbuf_tensor("A_sb", [128, 4, NST], F32))
    dskip_sb = E1(nc.sbuf_tensor("dskip_sb", [128, 4], F32))

    hTin = [E1(nc.sbuf_tensor(f"hTin{i}", [128, 8, TC], BF16)) for i in range(2)]
    xc_sl = [E1(nc.sbuf_tensor(f"xc{i}", [128, 4, TC + 3], F32)) for i in range(NSLOT)]
    sz_sl = [E1(nc.sbuf_tensor(f"sz{i}", [128, 4, TC], BF16)) for i in range(NSLOT)]
    xpev = E1(nc.sbuf_tensor("xpev", [80, TC], F32))
    xpev_c = E1(nc.sbuf_tensor("xpev_c", [16, TC], F32))
    sel_sb = E1(nc.sbuf_tensor("sel_sb", [128, 128], F32))
    xp_in = E1(nc.sbuf_tensor("xp_in", [128, TC], F32))
    sq96 = E1(nc.sbuf_tensor("sq96", [128, TC], F32))
    rstd3 = E1(nc.sbuf_tensor("rstd3", [3, TC], F32))
    ind3 = E1(nc.sbuf_tensor("ind3", [128, 4], F32))
    ind2 = E1(nc.sbuf_tensor("ind2", [3, 384], F32))
    cst3 = E1(nc.sbuf_tensor("cst3", [128, 1], F32))
    dtrms = E1(nc.sbuf_tensor("dtrms", [RDT, TC], F32))
    dt_sb = E1(nc.sbuf_tensor("dt_sb", [128, 4, TC], F32))
    u_sb = E1(nc.sbuf_tensor("u_sb", [128, TC], F32))
    up_sb = E1(nc.sbuf_tensor("up_sb", [128, 4, TC], F32))
    state = E1(nc.sbuf_tensor("state", [128, 4, NST], F32))
    yred = E1(nc.sbuf_tensor("yred", [128, TC], F32))
    t1_sb = E1(nc.sbuf_tensor("t1_sb", [128, TC], F32))
    cfin = E1(nc.sbuf_tensor("cfin", [128, TC], F32))
    yg_bf = E1(nc.sbuf_tensor("yg_bf", [128, 4, TC], BF16))
    oo_sb = E1(nc.sbuf_tensor("oo_sb", [128, D], F32))

    es0 = ExitStack()
    E0 = lambda t: es0.enter_context(t)
    wstage = E0(nc.sbuf_tensor("wstage", [128, 2 * DSL], F32))
    wln_rep = E0(nc.sbuf_tensor("wln_rep", [128, D], F32))
    xt_sb = [E0(nc.sbuf_tensor(f"xt{i}", [128, D], F32)) for i in range(2)]
    sq_scr = E0(nc.sbuf_tensor("sq_scr", [128, D], F32))
    ln1_t = E0(nc.sbuf_tensor("ln1_t", [128, D], F32))
    h2_bf = E0(nc.sbuf_tensor("h2_bf", [128, D], BF16))
    hTcol = [E0(nc.sbuf_tensor(f"hTcol{i}", [128, 8, 512], BF16)) for i in range(2)]
    st0 = E0(nc.sbuf_tensor("st0", [128, 8], F32))
    wst_bf = [E0(nc.sbuf_tensor(f"wst_bf{i}", [128, D], BF16)) for i in range(2)]
    wst_f = [E0(nc.sbuf_tensor(f"wst_f{i}", [128, D], F32)) for i in range(2)]
    wrTcol = E0(nc.sbuf_tensor("wrTcol", [128, 8, 128], BF16))

    esp = ExitStack()
    p_tr = [esp.enter_context(nc.psum_tensor(f"p_tr{i}", [128, 128], BF16)) for i in range(2)]
    p_wt = [esp.enter_context(nc.psum_tensor(f"p_wt{i}", [128, 128], BF16)) for i in range(2)]

    sW = Sem(nc, es, "sW")
    sWC = Sem(nc, es, "sWC")
    sX = [Sem(nc, es, "sX0"), Sem(nc, es, "sX1")]
    sH = [Sem(nc, es, "sH0"), Sem(nc, es, "sH1")]
    sBC = Sem(nc, es, "sBC")
    sBR = Sem(nc, es, "sBR")
    sSt = Sem(nc, es, "sSt")
    sHT = Sem(nc, es, "sHT")
    sDBG = Sem(nc, es, "sDBG")
    sXP = [Sem(nc, es, "sXP0"), Sem(nc, es, "sXP1")]
    sOP = [Sem(nc, es, "sOP0"), Sem(nc, es, "sOP1")]
    sCC = Sem(nc, es, "sCC")
    sAS = Sem(nc, es, "sAS")
    sVS = Sem(nc, es, "sVS")
    sGB = Sem(nc, es, "sGB")
    sG2 = Sem(nc, es, "sG2")
    sG3 = Sem(nc, es, "sG3")
    sRS = Sem(nc, es, "sRS")
    sRS2 = Sem(nc, es, "sRS2")
    sWL = [Sem(nc, es, "sWL0"), Sem(nc, es, "sWL1")]
    sGBp = [Sem(nc, es, "sGBp0"), Sem(nc, es, "sGBp1")]
    sPC = Sem(nc, es, "sPC")
    sWT = Sem(nc, es, "sWT")
    sWE = Sem(nc, es, "sWE")
    sT = Sem(nc, es, "sT")
    sA = Sem(nc, es, "sA")
    sV = Sem(nc, es, "sV")
    sP = Sem(nc, es, "sP")
    sQGA = [Sem(nc, es, f"sQGA{i}") for i in range(8)]
    sQG3 = Sem(nc, es, "sQG3")

    P = Prog()

    P.emit("vector", lambda e: e.memset(cst_sb[:, 0:1], 1e-5))
    P.emit("vector", lambda e: e.memset(cst_sb[:, 1:2], 1e-6))
    P.inc("vector", sV)
    loads = [
        (convw_sb, conv_w), (convb_sb, conv_b), (dtb_sb, dtb),
        (dtlnw_sb, dtlnw), (A_sb, alog), (dskip_sb, dskip),
        (ident_sb, ident_bf), (xproj_sb, xproj_w.rearrange("(a p) m -> p a m", p=128)),
        (dtw_sb, dtw), (ind3, ind3_d), (ind2, ind2_d), (cst3, cst3_d),
    ]
    for dst, src in loads:
        sW.n += 16
        P.emit("sync", lambda e, d=dst, s=src: e.dma_start(out=d[:], in_=s).then_inc(sW.h, 16))
    for dst, src, n in ((blnw_rep, blnw, NST), (clnw_eff, clnw, NST), (wln_rep, inlnw, D)):
        sW.n += 16
        P.emit("sync", lambda e, d=dst, s=src, nn=n: e.dma_start(
            out=d[:], in_=s.broadcast_to([128, nn])).then_inc(sW.h, 16))
    w_loaded = sW.n

    cast_jobs = []
    for kk in range(8):
        cast_jobs.append((w_in_bf[:, kk, :], w_in[kk * 128:(kk + 1) * 128, :], 2 * DSL))
    for kk in range(4):
        cast_jobs.append((wout_bf[:, kk, :], wout[kk * 128:(kk + 1) * 128, :], D))
    for dst, src, width in cast_jobs:
        P.wait("sync", sP, sP.n)
        sWC.n += 16
        P.emit("sync", lambda e, s=src, w=width: e.dma_start(
            out=wstage[:, 0:w], in_=s).then_inc(sWC.h, 16))
        P.wait("gpsimd", sWC, sWC.n)
        P.emit("gpsimd", lambda e, d=dst, w=width: e.tensor_copy(out=d, in_=wstage[:, 0:w]))
        P.inc("gpsimd", sP)
    P.wait("gpsimd", sW, w_loaded)
    P.emit("gpsimd", lambda e: e.memset(ones_sb[:], 1.0))
    P.emit("gpsimd", lambda e: e.memset(onesbf_sb[:], 1.0))
    P.emit("gpsimd", lambda e: e.memset(sel_sb[:], 1.0))
    P.emit("gpsimd", lambda e: e.memset(xp_in[:], 0.0))
    P.inc("gpsimd", sP); P.wait("gpsimd", sP, sP.n)
    P.emit("gpsimd", lambda e: e.memset(sel_sb[64:65, :], 0.0))
    w_casts = P.inc("gpsimd", sP)

    cache_jobs = []
    for wsrc, wdst, RT, CC in ((gate_w, gate_bf, 8, DF), (up_w, up_bf, 8, DF),
                               (down_w, down_bf, 32, D)):
        ncc = CC // D
        for rt in range(RT):
            for cc in range(max(1, ncc)):
                c0_, c1_ = cc * D, min(CC, (cc + 1) * D)
                cache_jobs.append((wsrc[rt * 128:(rt + 1) * 128, c0_:c1_],
                                   wdst[rt * 128:(rt + 1) * 128, c0_:c1_], None, None))
    if DO_WT:
        for wsrc, wdst in ((w_r, wrT_bf), (w_k, wkT_bf), (w_v, wvT_bf)):
            for rt in range(8):
                cache_jobs.append((wsrc[rt * 128:(rt + 1) * 128, :], wdst, "T", rt))
    cache_state = dict(jw=0, casts=[0, 0], stores=[0, 0], wt_ev={}, wt_slot=0)

    def emit_cache_jobs(n):
        if not DO_CACHE:
            return
        for _ in range(n):
            jw = cache_state["jw"]
            if jw >= len(cache_jobs):
                return
            cache_state["jw"] = jw + 1
            srcs, dsts, kind, rt = cache_jobs[jw]
            p = jw % 2
            P.wait("scalar", sPC, cache_state["casts"][p])
            sWL[p].n += 16
            P.emit("scalar", lambda e, s=srcs, pp=p, sm=sWL[p]: e.dma_start(
                out=wst_f[pp][:, 0:s.shape[1]], in_=s).then_inc(sm.h, 16))
            P.wait("gpsimd", sWL[p], sWL[p].n)
            P.wait("gpsimd", sGBp[p], cache_state["stores"][p])
            P.emit("gpsimd", lambda e, pp=p, w=srcs.shape[1]: e.tensor_copy(
                out=wst_bf[pp][:, 0:w], in_=wst_f[pp][:, 0:w]))
            cache_state["casts"][p] = P.inc("gpsimd", sPC)
            if kind is None:
                P.wait("sync", sPC, cache_state["casts"][p])
                sGBp[p].n += 16
                sGB.n += 16
                P.emit("sync", lambda e, d=dsts, pp=p, w=srcs.shape[1], sm=sGBp[p]: e.dma_start(
                    out=d, in_=wst_bf[pp][:, 0:w]).then_inc(sm.h, 16))
                P.emit("sync", lambda e: e.nop()) if False else None
                cache_state["stores"][p] = sGBp[p].n
            else:
                P.wait("tensor", sPC, cache_state["casts"][p])
                for mm in range(8):
                    slot = cache_state["wt_slot"]
                    cache_state["wt_slot"] = (slot + 1) % 2
                    P.wait("tensor", sWE, cache_state["wt_ev"].get(slot, 0))
                    P.emit("tensor", lambda e, pp=p, m_=mm, sl=slot: e.transpose(
                        p_wt[sl][:], wst_bf[pp][:, m_ * 128:(m_ + 1) * 128], ident_sb[:]))
                    vt_w = P.inc("tensor", sWT)
                    P.wait("scalar", sWT, vt_w)
                    P.emit("scalar", lambda e, m_=mm, sl=slot: e.activation(
                        out=wrTcol[:, m_, :], in_=p_wt[sl][:], func=AF.Copy))
                    cache_state["wt_ev"][slot] = P.inc("scalar", sWE)
                P.wait("sync", sWE, sWE.n)
                sGBp[p].n += 16
                P.emit("sync", lambda e, d=dsts, r=rt, sm=sGBp[p]: e.dma_start(
                    out=d.rearrange("(m p) c -> p m c", p=128)[:, :, r * 128:(r + 1) * 128],
                    in_=wrTcol[:]).then_inc(sm.h, 16))
                cache_state["stores"][p] = sGBp[p].n
                P.wait("scalar", sGBp[p], sGBp[p].n)

    P.wait("vector", sW, w_loaded)
    P.emit("vector", lambda e: e.tensor_mul(out=clnw_eff[:], in0=clnw_eff[:], in1=blnw_rep[:]))
    P.inc("vector", sV)
    P.wait("scalar", sW, w_loaded)
    P.emit("scalar", lambda e: e.activation(out=A_sb[:], in_=A_sb[:], func=AF.Exp))
    w_a_exp = P.inc("scalar", sA)
    P.wait("vector", sA, w_a_exp)
    P.emit("vector", lambda e: e.tensor_scalar_mul(out=A_sb[:], in0=A_sb[:], scalar1=-1.0))
    w_small_v = P.inc("vector", sV)

    ntile = TQ // 128
    ph0_a_done = {}
    ph0_t_done = {}
    ph0_st = {}
    vv_st_prev = 0
    ev_parity = [0, 0]
    for i in range(ntile):
        buf = xt_sb[i % 2]
        if i >= 2:
            P.wait("sync", sA, ph0_a_done[i - 2])
        sxp = sX[i % 2]
        sxp.n += 16
        P.emit("sync", lambda e, b=buf, i_=i, s=sxp: e.dma_start(
            out=b[:], in_=x_q[i_ * 128:(i_ + 1) * 128, :]).then_inc(s.h, 16))
        v_x = sxp.n
        P.wait("vector", sxp, v_x)
        P.emit("vector", lambda e, b=buf: e.reduce_sum(out=st0[:, 0:1], in_=b[:], axis=AX.X))
        P.inc("vector", sVS); P.wait("vector", sVS, sVS.n)
        P.wait("scalar", sxp, v_x)
        P.wait("scalar", sV, vv_st_prev)
        P.emit("scalar", lambda e, b=buf: e.activation(
            out=sq_scr[:], in_=b[:], func=AF.Square, accum_out=st0[:, 1:2]))
        va_sq = P.inc("scalar", sA)
        P.wait("vector", sA, va_sq)
        P.emit("vector", lambda e: e.tensor_scalar_mul(out=st0[:, 2:3], in0=st0[:, 0:1], scalar1=1.0 / D))
        P.emit("vector", lambda e: e.tensor_scalar_mul(out=st0[:, 3:4], in0=st0[:, 1:2], scalar1=1.0 / D))
        P.inc("vector", sVS); P.wait("vector", sVS, sVS.n)
        P.emit("vector", lambda e: e.tensor_mul(out=st0[:, 4:5], in0=st0[:, 2:3], in1=st0[:, 2:3]))
        P.inc("vector", sVS); P.wait("vector", sVS, sVS.n)
        P.emit("vector", lambda e: e.tensor_sub(out=st0[:, 3:4], in0=st0[:, 3:4], in1=st0[:, 4:5]))
        vv_st = P.inc("vector", sV)
        vv_st_prev = vv_st
        P.wait("scalar", sV, vv_st)
        P.emit("scalar", lambda e: e.activation(out=st0[:, 5:6], in_=st0[:, 3:4], func=AF.Ln, bias=cst_sb[:, 0:1]))
        P.inc("scalar", sAS); P.wait("scalar", sAS, sAS.n)
        P.emit("scalar", lambda e: e.activation(out=st0[:, 5:6], in_=st0[:, 5:6], func=AF.Exp, scale=-0.5))
        va_r = P.inc("scalar", sA)
        P.wait("scalar", sA, va_r)
        P.wait("vector", sA, va_r)
        P.emit("vector", lambda e: e.tensor_mul(out=st0[:, 6:7], in0=st0[:, 2:3], in1=st0[:, 5:6]))
        P.inc("vector", sVS); P.wait("vector", sVS, sVS.n)
        P.emit("vector", lambda e: e.tensor_scalar_mul(out=st0[:, 6:7], in0=st0[:, 6:7], scalar1=-1.0))
        vv_nmr = P.inc("vector", sV)
        P.wait("scalar", sV, vv_nmr)
        P.emit("scalar", lambda e, b=buf: e.activation(
            out=ln1_t[:], in_=b[:], func=AF.Identity, scale=st0[:, 5:6], bias=st0[:, 6:7]))
        P.emit("scalar", lambda e: e.activation(
            out=sq_scr[:], in_=ln1_t[:], func=AF.Square, accum_out=st0[:, 7:8]))
        P.inc("scalar", sAS); P.wait("scalar", sAS, sAS.n)
        P.emit("scalar", lambda e: e.activation(out=st0[:, 7:8], in_=st0[:, 7:8], func=AF.Ln, scale=1.0 / D, bias=cst_sb[:, 1:2]))
        P.inc("scalar", sAS); P.wait("scalar", sAS, sAS.n)
        P.emit("scalar", lambda e: e.activation(out=st0[:, 7:8], in_=st0[:, 7:8], func=AF.Exp, scale=-0.5))
        va_ln = P.inc("scalar", sA)
        P.wait("scalar", sA, va_ln)
        ph0_a_done[i] = va_ln
        P.wait("vector", sA, va_ln)
        if i >= 1:
            P.wait("vector", sT, ph0_t_done[i - 1])
        P.emit("vector", lambda e: e.scalar_tensor_tensor(
            out=h2_bf[:], in0=ln1_t[:], scalar=st0[:, 7:8], in1=wln_rep[:],
            op0=ALU.mult, op1=ALU.mult))
        vv_h2 = P.inc("vector", sV)
        col = hTcol[(i // 4) % 2]
        io = (i % 4) * 128
        P.wait("tensor", sV, vv_h2)
        if i % 4 == 0 and i >= 8:
            P.wait("scalar", sHT, ph0_st[i // 4 - 2])
        for a in range(8):
            pt = p_tr[a % 2]
            P.wait("tensor", sA, ev_parity[a % 2])
            P.emit("tensor", lambda e, a_=a, p=pt: e.transpose(
                p[:], h2_bf[:, a_ * 128:(a_ + 1) * 128], ident_sb[:]))
            vt = P.inc("tensor", sT)
            P.wait("scalar", sT, vt)
            P.emit("scalar", lambda e, a_=a, p=pt, c=col, o=io: e.activation(
                out=c[:, a_, o:o + 128], in_=p[:], func=AF.Copy))
            ev_parity[a % 2] = P.inc("scalar", sA)
        ph0_t_done[i] = sT.n
        if i % 4 == 3:
            P.wait("sync", sA, sA.n)
            sHT.n += 16
            P.emit("sync", lambda e, c=col, i_=i: e.dma_start(
                out=hT_my.rearrange("(a p) l -> p a l", p=128)[:, :, (i_ - 3) * 128:(i_ + 1) * 128],
                in_=c[:]).then_inc(sHT.h, 16))
            ph0_st[i // 4] = sHT.n
        emit_cache_jobs(8)
    emit_cache_jobs(10**9)
    P.wait("gpsimd", sHT, sHT.n)
    sCC.n += 1
    P.emit("gpsimd", lambda e: e.collective_compute(
        "AllGather", ALU.bypass, replica_groups=GROUPS,
        ins=[hT_my.opt()], outs=[hT_all.opt()]).then_inc(sCC.h, 1))
    v_ag1 = sCC.n

    es0.close()
    esp.close()
    esS = ExitStack()
    ES = lambda t: esS.enter_context(t)
    crep_sl = [ES(nc.sbuf_tensor(f"crep{i}", [128, 4, TC], F32)) for i in range(2)]
    dA_b = ES(nc.sbuf_tensor("dA", [128, 4, TC], F32))
    bb_b = ES(nc.sbuf_tensor("bb", [128, 4, TC], F32))
    H_sl = [ES(nc.sbuf_tensor(f"H{i}", [128, 4, TC], F32)) for i in range(2)]
    G_b = ES(nc.sbuf_tensor("G", [128, 4, TC], F32))
    yacc4 = ES(nc.sbuf_tensor("yacc4", [128, 4, TC], F32))
    bsrcB = ES(nc.sbuf_tensor("bsrcB", [128, 4, TC], F32))
    bsrcC = ES(nc.sbuf_tensor("bsrcC", [128, 4, TC], F32))
    rcrep_sb = ES(nc.sbuf_tensor("rcrep", [128, TC], F32))
    p_q = [ES(nc.psum_tensor(f"p_q{i}", [128, TC], F32)) for i in range(8)]

    a_t_done = {}
    a_ev_last = {}
    a_act_done = {}
    a_xpst = {}
    a_xpev_done = {}
    a_xp_dma = {}
    ar_done = {}
    b_done_v = {}
    last_b_done = [0]
    op_dma = {}
    op_ev = {}
    b_prev_dtr = [0]
    stats_mm_prev = [0]
    op_st_last = [0, 0]
    bsrc_rd = [0]
    scanK_vv = {}
    scanK_h1 = {}
    scanK_vp = {}
    scanK_vp2 = {}
    scanK_vy = {}
    crep_ready = {}
    gmul_last = {}
    bank_ev = {6: 0, 7: 0}
    cc_wait = [v_ag1]

    REPL_BASE = [0, 32, 64, 65]

    def repl_mm(e, srcT, g_, jj_, bank):
        if jj_ < 3:
            b0 = REPL_BASE[jj_]
            return e.matmul(p_q[bank][:], ones_sb[b0:b0 + 1, 0:128],
                            srcT[b0:b0 + 1, g_, :], start=True, stop=True)
        return e.matmul(p_q[bank][:], sel_sb[64:66, 0:128],
                        srcT[64:66, g_, :], start=True, stop=True)

    P.emit("vector", lambda e: e.memset(state[:], 0.0))
    P.inc("vector", sV)

    def a_side(k):
        sc, q = k // 4, k % 4
        lc = sc * TC
        hbuf = hTin[k % 2]
        slot = xc_sl[k % NSLOT]
        zslot = sz_sl[k % NSLOT]
        P.wait("sync", sCC, v_ag1)
        if k >= 2:
            P.wait("sync", sT, a_t_done[k - 2])
        shp_ = sH[k % 2]
        shp_.n += 16
        P.emit("sync", lambda e, b=hbuf, q_=q, l=lc, s=shp_: e.dma_start(
            out=b[:],
            in_=hT_all[q_ * D:(q_ + 1) * D, l:l + TC].rearrange("(kk p) t -> p kk t", p=128)
        ).then_inc(s.h, 16))
        v_h = shp_.n
        P.wait("tensor", shp_, v_h)
        P.wait("tensor", sP, w_casts)
        if k == 0:
            P.wait("tensor", sWT, sWT.n)
            P.wait("tensor", sWE, sWE.n)
            P.wait("tensor", sPC, sPC.n)
            P.wait("tensor", sGBp[0], sGBp[0].n)
            P.wait("tensor", sGBp[1], sGBp[1].n)
        if k >= 1:
            P.wait("tensor", sA, a_ev_last[k - 1])
        if k >= NSLOT:
            P.wait("scalar", sV, b_done_v[k - NSLOT])
        ev_after = []
        for m in range(8):
            pp = p_q[m % 2]
            if m >= 2:
                P.wait("tensor", sA, ev_after[m - 2])
            for kk in range(8):
                P.emit("tensor", lambda e, m_=m, kk_=kk, p=pp, hb=hbuf: e.matmul(
                    p[:], w_in_bf[:, kk_, m_ * 128:(m_ + 1) * 128], hb[:, kk_, :],
                    start=(kk_ == 0), stop=(kk_ == 7)))
            vt = P.inc("tensor", sT)
            P.wait("scalar", sT, vt)
            if m < 4:
                P.emit("scalar", lambda e, m_=m, p=pp, s=slot: e.activation(
                    out=s[:, m_, 3:3 + TC], in_=p[:], func=AF.Copy))
            else:
                P.emit("scalar", lambda e, m_=m, p=pp, zs=zslot: e.activation(
                    out=zs[:, m_ - 4, :], in_=p[:], func=AF.Silu))
            ev_after.append(P.inc("scalar", sA))
        a_t_done[k] = sT.n
        a_ev_last[k] = ev_after[7]
        nslot = xc_sl[(k + 1) % NSLOT]
        P.wait("vector", sA, ev_after[3])
        if k == 0:
            P.emit("vector", lambda e, s=slot: e.memset(s[:, :, 0:3], 0.0))
            P.inc("vector", sVS); P.wait("vector", sVS, sVS.n)
        if k + 1 < NCH:
            P.emit("vector", lambda e, s=slot, ns=nslot: e.tensor_copy(
                out=ns[:, :, 0:3], in_=s[:, :, TC:TC + 3]))
            P.inc("vector", sV)
        silu_done = {}
        for a in range(4):
            fin = yred if a % 2 == 0 else cfin
            if a >= 2:
                P.wait("vector", sA, silu_done[a - 2])
            P.emit("vector", lambda e, a_=a, s=slot: e.tensor_scalar(
                out=t1_sb[:], in0=s[:, a_, 0:TC], scalar1=convw_sb[:, a_, 0:1],
                scalar2=None, op0=ALU.mult))
            P.emit("vector", lambda e, a_=a, s=slot, f=fin: e.scalar_tensor_tensor(
                out=f[:], in0=s[:, a_, 1:1 + TC], scalar=convw_sb[:, a_, 1:2],
                in1=t1_sb[:], op0=ALU.mult, op1=ALU.add))
            P.emit("vector", lambda e, a_=a, s=slot, f=fin: e.scalar_tensor_tensor(
                out=t1_sb[:], in0=s[:, a_, 2:2 + TC], scalar=convw_sb[:, a_, 2:3],
                in1=f[:], op0=ALU.mult, op1=ALU.add))
            P.emit("vector", lambda e, a_=a, s=slot, f=fin: e.scalar_tensor_tensor(
                out=f[:], in0=s[:, a_, 3:3 + TC], scalar=convw_sb[:, a_, 3:4],
                in1=t1_sb[:], op0=ALU.mult, op1=ALU.add))
            vv_c = P.inc("vector", sV)
            P.wait("scalar", sV, vv_c)
            P.emit("scalar", lambda e, a_=a, s=slot, f=fin: e.activation(
                out=s[:, a_, 3:3 + TC], in_=f[:], func=AF.Silu, bias=convb_sb[:, a_:a_ + 1]))
            silu_done[a] = P.inc("scalar", sA)
        a_act_done[k] = sA.n
        P.wait("tensor", sA, a_act_done[k])
        P.wait("tensor", sV, last_b_done[0])
        for pp_sl, c0_, c1_ in ((p_q[2][0:64, :], 0, RDT), (p_q[3][0:16, :], RDT, RDT + NST), (p_q[3][32:48, :], RDT + NST, 96)):
            for kk in range(4):
                P.emit("tensor", lambda e, p=pp_sl, kk_=kk, cc0=c0_, cc1=c1_, s=slot: e.matmul(
                    p, xproj_sb[:, kk_, cc0:cc1], s[:, kk_, 3:3 + TC],
                    start=(kk_ == 0), stop=(kk_ == 3)))
        vt_xp = P.inc("tensor", sT)
        P.wait("scalar", sT, vt_xp)
        if k >= 1:
            P.wait("scalar", sXP[a_xpst[k - 1][0]], a_xpst[k - 1][1])
        P.emit("scalar", lambda e: e.activation(out=xpev[0:RDT, :], in_=p_q[2][0:64, :], func=AF.Copy))
        P.emit("scalar", lambda e: e.activation(out=xpev[RDT:RDT + NST, :], in_=p_q[3][0:16, :], func=AF.Copy))
        P.emit("scalar", lambda e: e.activation(out=xpev_c[:], in_=p_q[3][32:48, :], func=AF.Copy))
        va_xpev = P.inc("scalar", sA)
        a_xpev_done[k] = va_xpev
        P.wait("sync", sA, va_xpev)
        j_ar = k // ARG
        sxp_ = sXP[j_ar % 2]
        if j_ar >= 2 and k % ARG == 0:
            P.wait("sync", sCC, ar_done[j_ar - 2])
        sxp_.n += 32
        P.emit("sync", lambda e, j=j_ar, q_=k % ARG, s=sxp_: e.dma_start(
            out=xp_part[j][0:80, q_ * TC:(q_ + 1) * TC], in_=xpev[:]).then_inc(s.h, 16))
        P.emit("sync", lambda e, j=j_ar, q_=k % ARG, s=sxp_: e.dma_start(
            out=xp_part[j][80:96, q_ * TC:(q_ + 1) * TC], in_=xpev_c[:]).then_inc(s.h, 16))
        a_xpst[k] = (j_ar % 2, sxp_.n)
        if k % ARG == ARG - 1:
            a_xp_dma[j_ar] = sxp_.n

    def all_reduce_xp(j):
        P.wait("gpsimd", sXP[j % 2], a_xp_dma[j])
        P.wait("gpsimd", sCC, cc_wait[0])
        sCC.n += 1
        P.emit("gpsimd", lambda e, j_=j: e.collective_compute(
            "AllReduce", ALU.add, replica_groups=GROUPS,
            ins=[xp_part[j_].opt()], outs=[xp_red[j_].opt()]).then_inc(sCC.h, 1))
        ar_done[j] = sCC.n
        cc_wait[0] = sCC.n

    def b_side(k):
        sc = k // 4
        j_ar, qa = k // ARG, k % ARG
        slot = xc_sl[k % NSLOT]
        zslot = sz_sl[k % NSLOT]
        c0, c1 = qa * TC, (qa + 1) * TC
        if k == 0:
            for _eng in ("scalar", "vector", "gpsimd"):
                P.wait(_eng, sWE, sWE.n)
                P.wait(_eng, sGBp[0], sGBp[0].n)
                P.wait(_eng, sGBp[1], sGBp[1].n)
        P.wait("scalar", sCC, ar_done[j_ar])
        P.wait("scalar", sV, b_prev_dtr[0])
        if k == 0:
            P.wait("scalar", sP, w_casts)
        for p0, p1, r0, r1 in ((0, 64, 0, RDT), (64, 80, RDT, RDT + NST),
                               (80, 96, RDT + NST, 96)):
            sBC.n += 16
            P.emit("scalar", lambda e, a0=p0, a1=p1, a=r0, b_=r1, j=j_ar: e.dma_start(
                out=xp_in[a0:a1, :], in_=xp_red[j][a:b_, c0:c1]).then_inc(sBC.h, 16))
        v_ld = sBC.n
        P.wait("scalar", sT, bsrc_rd[0])
        for dst, r0 in ((bsrcB, RDT), (bsrcC, RDT + NST)):
            for jj in range(4):
                b0 = REPL_BASE[jj]
                sBR.n += 16
                P.emit("scalar", lambda e, d=dst, rr=r0, jj_=jj, b0_=b0, j=j_ar: e.dma_start(
                    out=d[b0_:b0_ + 1, :, :],
                    in_=xp_red[j][rr:rr + NST, c0:c1].rearrange(
                        "(g jj) t -> jj g t", jj=4)[jj_:jj_ + 1]).then_inc(sBR.h, 16))
        v_bsrc = sBR.n
        P.wait("scalar", sBC, v_ld)
        P.wait("scalar", sT, stats_mm_prev[0])
        P.emit("scalar", lambda e: e.activation(out=sq96[:], in_=xp_in[:], func=AF.Square))
        va = P.inc("scalar", sA)
        P.wait("tensor", sA, va)
        P.wait("tensor", sA, op_ev.get(k - 1, 0))
        P.emit("tensor", lambda e: e.matmul(
            p_q[6][0:3, :], ind3[:, 0:3], sq96[:], start=True, stop=True))
        vt = P.inc("tensor", sT)
        stats_mm_prev[0] = vt
        P.wait("scalar", sT, vt)
        P.emit("scalar", lambda e: e.activation(
            out=rstd3[:], in_=p_q[6][0:3, :], func=AF.Ln, scale=cst3[0:3, 0:1],
            bias=cst_sb[0:3, 1:2]))
        P.inc("scalar", sAS); P.wait("scalar", sAS, sAS.n)
        P.emit("scalar", lambda e: e.activation(
            out=rstd3[:], in_=rstd3[:], func=AF.Exp, scale=-0.5))
        va_rs = P.inc("scalar", sA)
        P.wait("tensor", sA, va_rs)
        P.wait("tensor", sV, b_done_v.get(k - 1, 0))
        P.emit("tensor", lambda e: e.matmul(p_q[5][0:RDT, :], ind2[0:3, 0:RDT], rstd3[:], start=True, stop=True))
        vt2 = P.inc("tensor", sT)
        P.wait("vector", sT, vt2)
        P.emit("vector", lambda e: e.scalar_tensor_tensor(
            out=dtrms[:], in0=xp_in[0:64, :], scalar=dtlnw_sb[:], in1=p_q[5][0:RDT, :],
            op0=ALU.mult, op1=ALU.mult))
        vv_dtr = P.inc("vector", sV)
        b_prev_dtr[0] = vv_dtr
        P.wait("tensor", sV, vv_dtr)
        va_es = {}
        for m in range(4):
            dtb_bank = p_q[7 - (m % 2)]
            if m >= 2:
                P.wait("tensor", sA, va_es[m - 2])
            P.emit("tensor", lambda e, m_=m, p=dtb_bank: e.matmul(
                p[:], dtw_sb[:, m_ * 128:(m_ + 1) * 128], dtrms[:], start=True, stop=True))
            vt3 = P.inc("tensor", sT)
            P.wait("scalar", sT, vt3)
            P.emit("scalar", lambda e, m_=m, p=dtb_bank: e.activation(
                out=u_sb[:], in_=p[:], func=AF.Exp, bias=dtb_sb[:, m_:m_ + 1]))
            va_es[m] = P.inc("scalar", sA)
            P.emit("scalar", lambda e, m_=m: e.activation(
                out=dt_sb[:, m_, :], in_=u_sb[:], func=AF.Ln, bias=1.0))
        va_dt = P.inc("scalar", sA)
        P.wait("tensor", sA, va_rs)
        P.emit("tensor", lambda e: e.matmul(p_q[4][:], ind2[0:3, 256:384], rstd3[:], start=True, stop=True))
        vt_c = P.inc("tensor", sT)
        P.wait("scalar", sT, vt_c)
        P.wait("scalar", sV, b_done_v.get(k - 1, 0))
        P.emit("scalar", lambda e: e.activation(out=rcrep_sb[:], in_=p_q[4][:], func=AF.Copy))
        va_rc = P.inc("scalar", sA)
        P.wait("tensor", sA, va_dt)
        P.wait("tensor", sV, sV.n)
        P.emit("tensor", lambda e: e.matmul(p_q[5][:], ind2[0:3, 128:256], rstd3[:], start=True, stop=True))
        vt4 = P.inc("tensor", sT)
        P.wait("vector", sA, va_dt)
        P.wait("vector", sT, vt4)
        P.wait("vector", sA, a_act_done[k])
        for a in range(4):
            P.emit("vector", lambda e, a_=a, s=slot: e.tensor_mul(
                out=u_sb[:], in0=dt_sb[:, a_, :], in1=s[:, a_, 3:3 + TC]))
            P.emit("vector", lambda e, a_=a: e.tensor_mul(
                out=up_sb[:, a_, :], in0=u_sb[:], in1=p_q[5][:]))
        vv_up = P.inc("vector", sV)

        vB = {}
        vbmul_last = {}

        def repl_C(g):
            fg = k * 4 + g
            if fg >= 2:
                P.wait("scalar", sP, gmul_last.get(fg - 2, 0))
            for jj in range(4):
                bank = 6 + (jj % 2)
                P.wait("tensor", sA, bank_ev[bank])
                P.emit("tensor", lambda e, g_=g, jj_=jj, b_=bank: repl_mm(e, bsrcC, g_, jj_, b_))
                vt_r = P.inc("tensor", sT)
                P.wait("scalar", sT, vt_r)
                P.emit("scalar", lambda e, g_=g, jj_=jj, b_=bank: e.activation(
                    out=crep_sl[g_ % 2][:, jj_, :], in_=p_q[b_][:], func=AF.Copy))
                bank_ev[bank] = P.inc("scalar", sA)
            crep_ready[fg] = bank_ev[7]

        def repl_B(g):
            if g >= 1:
                P.wait("tensor", sV, vbmul_last[g - 1])
            else:
                P.wait("tensor", sV, b_done_v.get(k - 1, 0))
                P.wait("tensor", sBR, v_bsrc)
                P.wait("tensor", sA, va_rc)
                P.wait("tensor", sA, a_xpev_done[(k // 2) * 2 + 1])
            for jj in range(4):
                P.emit("tensor", lambda e, g_=g, jj_=jj: repl_mm(e, bsrcB, g_, jj_, 2 + jj_))
            vB[g] = P.inc("tensor", sT)

        def emit_reduce(g, a):
            K = k * 16 + g * 4 + a
            P.wait("vector", sP, scanK_vp2[K])
            if g == 0:
                P.emit("vector", lambda e, a_=a: e.tensor_add(
                    out=yacc4[:, a_, :], in0=G_b[:, 0, :], in1=G_b[:, 2, :]))
            else:
                P.emit("vector", lambda e, a_=a: e.tensor_add(
                    out=yacc4[:, a_, :], in0=yacc4[:, a_, :], in1=G_b[:, 0, :]))
                P.emit("vector", lambda e, a_=a: e.tensor_add(
                    out=yacc4[:, a_, :], in0=yacc4[:, a_, :], in1=G_b[:, 2, :]))
            scanK_vy[K] = P.inc("vector", sV)

        P.wait("tensor", sBR, v_bsrc)
        P.wait("tensor", sA, va_dt)
        repl_C(0)
        repl_B(0)
        pend = [None]
        for g in range(4):
            for a in range(4):
                K = k * 16 + g * 4 + a
                dAb = dA_b
                P.wait("scalar", sV, scanK_h1.get(K - 1, 0))
                if K == 0:
                    P.wait("scalar", sV, w_small_v)
                for j in range(2):
                    P.emit("scalar", lambda e, a_=a, g_=g, j_=j, d=dAb: e.activation(
                        out=d[:, j_, :], in_=dt_sb[:, a_, :], func=AF.Exp,
                        scale=A_sb[:, a_, g_ * 4 + j_:g_ * 4 + j_ + 1]))
                P.wait("scalar", sV, scanK_vv.get(K - 1, 0))
                for j in range(2, 4):
                    P.emit("scalar", lambda e, a_=a, g_=g, j_=j, d=dAb: e.activation(
                        out=d[:, j_, :], in_=dt_sb[:, a_, :], func=AF.Exp,
                        scale=A_sb[:, a_, g_ * 4 + j_:g_ * 4 + j_ + 1]))
                va_x = P.inc("scalar", sA)
                P.wait("vector", sT, vB[g])
                for j in range(4):
                    P.emit("vector", lambda e, j_=j, g_=g, a_=a: e.scalar_tensor_tensor(
                        out=bb_b[:, j_, :], in0=p_q[2 + j_][:],
                        scalar=clnw_eff[:, g_ * 4 + j_:g_ * 4 + j_ + 1],
                        in1=up_sb[:, a_, :], op0=ALU.mult, op1=ALU.mult))
                vbm = P.inc("vector", sV)
                if a == 3:
                    vbmul_last[g] = vbm
                Hb = H_sl[K % 2]
                P.wait("vector", sA, va_x)
                P.wait("vector", sP, scanK_vp.get(K - 2, 0))
                for j in range(4):
                    P.emit("vector", lambda e, a_=a, g_=g, j_=j, d=dAb, h=Hb: e.tensor_tensor_scan(
                        out=h[:, j_, :], data0=d[:, j_, :], data1=bb_b[:, j_, :],
                        initial=state[:, a_, g_ * 4 + j_:g_ * 4 + j_ + 1],
                        op0=ALU.mult, op1=ALU.add))
                    if j == 1:
                        scanK_h1[K] = P.inc("vector", sV)
                P.inc("vector", sVS); P.wait("vector", sVS, sVS.n)
                P.emit("vector", lambda e, a_=a, g_=g, h=Hb: e.tensor_copy(
                    out=state[:, a_, g_ * 4:g_ * 4 + 4], in_=h[:, :, TC - 1]))
                scanK_vv[K] = P.inc("vector", sV)
                if pend[0] is not None:
                    emit_reduce(*pend[0])
                pend[0] = (g, a)
                P.wait("gpsimd", sV, scanK_vv[K])
                if a == 0:
                    P.wait("gpsimd", sA, crep_ready[k * 4 + g])
                P.wait("gpsimd", sV, scanK_vy.get(K - 1, 0))
                P.emit("gpsimd", lambda e, g_=g, h=Hb: e.tensor_mul(
                    out=G_b[:], in0=crep_sl[g_ % 2][:], in1=h[:]))
                scanK_vp[K] = P.inc("gpsimd", sP)
                P.emit("gpsimd", lambda e: e.tensor_add(
                    out=G_b[:, 0, :], in0=G_b[:, 0, :], in1=G_b[:, 1, :]))
                P.emit("gpsimd", lambda e: e.tensor_add(
                    out=G_b[:, 2, :], in0=G_b[:, 2, :], in1=G_b[:, 3, :]))
                scanK_vp2[K] = P.inc("gpsimd", sP)
                gmul_last[k * 4 + g] = scanK_vp2[K]
                if a == 3 and g < 3:
                    repl_C(g + 1)
                    repl_B(g + 1)
        emit_reduce(*pend[0])
        bsrc_rd[0] = vB[3]
        P.wait("vector", sA, va_rc)
        for a in range(4):
            P.emit("vector", lambda e, a_=a: e.tensor_mul(
                out=t1_sb[:], in0=yacc4[:, a_, :], in1=rcrep_sb[:]))
            P.emit("vector", lambda e, a_=a, zs=zslot: e.tensor_copy(
                out=yred[:], in_=zs[:, a_, :]))
            P.emit("vector", lambda e, a_=a, s=slot: e.scalar_tensor_tensor(
                out=t1_sb[:], in0=s[:, a_, 3:3 + TC], scalar=dskip_sb[:, a_:a_ + 1],
                in1=t1_sb[:], op0=ALU.mult, op1=ALU.add))
            P.emit("vector", lambda e, a_=a: e.tensor_mul(
                out=yg_bf[:, a_, :], in0=t1_sb[:], in1=yred[:]))
            P.inc("vector", sV)
        b_done_v[k] = sV.n
        last_b_done[0] = b_done_v[k]
        P.wait("tensor", sV, b_done_v[k])
        if k >= 1:
            P.wait("tensor", sA, op_ev.get(k - 1, 0))
        P.wait("tensor", sA, bank_ev[6])
        P.wait("tensor", sA, bank_ev[7])
        op_ev_h = {}
        sc_par = sc % 2
        if k % 4 == 0 and sc >= 2:
            P.wait("sync", sCC, rs_done[sc - 2])
        for tt in range(4):
            P.wait("scalar", sOP[op_st_last[0]], op_st_last[1])
            for h in range(2):
                po = p_q[6 + h]
                if tt >= 1:
                    P.wait("tensor", sA, op_ev_h[(tt - 1, h)])
                for a in range(4):
                    P.emit("tensor", lambda e, tt_=tt, h_=h, a_=a, p=po: e.matmul(
                        p[:], yg_bf[:, a_, tt_ * 128:(tt_ + 1) * 128],
                        wout_bf[:, a_, h_ * 512:(h_ + 1) * 512],
                        start=(a_ == 0), stop=(a_ == 3)))
                vt_o = P.inc("tensor", sT)
                P.wait("scalar", sT, vt_o)
                P.emit("scalar", lambda e, h_=h, p=po: e.activation(
                    out=oo_sb[:, h_ * 512:(h_ + 1) * 512], in_=p[:], func=AF.Copy))
                op_ev_h[(tt, h)] = P.inc("scalar", sA)
            va_o = op_ev_h[(tt, 1)]
            P.wait("sync", sA, va_o)
            sOP[sc_par].n += 16
            P.emit("sync", lambda e, k_=k, tt_=tt, s=sOP[sc_par]: e.dma_start(
                out=op_part[k_ * TC + tt_ * 128:k_ * TC + (tt_ + 1) * 128, :],
                in_=oo_sb[:]).then_inc(s.h, 16))
            op_st_last[0] = sc_par
            op_st_last[1] = sOP[sc_par].n
        op_ev[k] = sA.n
        bank_ev[6] = op_ev[k]
        bank_ev[7] = op_ev[k]
        op_dma[k // 4] = sOP[sc_par].n

    def rs1(sc):
        P.wait("gpsimd", sOP[sc % 2], op_dma[sc])
        P.wait("gpsimd", sCC, cc_wait[0])
        sCC.n += 1
        P.emit("gpsimd", lambda e, s=sc: e.collective_compute(
            "ReduceScatter", ALU.add, replica_groups=GROUPS,
            ins=[op_part[s * 4 * TC:(s + 1) * 4 * TC, :].opt()],
            outs=[rs1_out[s * 512:(s + 1) * 512, :].opt()]).then_inc(sCC.h, 1))
        cc_wait[0] = sCC.n
        return sCC.n

    rs_done = {}
    for jp in range(NCH // 2):
        k0 = 2 * jp
        a_side(k0)
        all_reduce_xp(k0)
        a_side(k0 + 1)
        all_reduce_xp(k0 + 1)
        b_side(k0)
        b_side(k0 + 1)
        if (k0 + 1) % 4 == 3:
            rs_done[(k0 + 1) // 4] = rs1((k0 + 1) // 4)


    esS.close()
    es1.close()
    for _e in ENGS:
        P.wait(_e, sCC, rs_done[NSC - 1])
        P.wait(_e, sT, sT.n)
        P.wait(_e, sA, sA.n)
        P.wait(_e, sV, sV.n)
        P.wait(_e, sP, sP.n)
    va_sgl = {}
    vv_ff = {}
    t_d = {}
    vv_dev = 0
    es2a = ExitStack()
    stream_fm = es2a.enter_context(nc.sbuf_tensor("stream_fm", [128, 8, TQ], F32))
    identf_sb = es2a.enter_context(nc.sbuf_tensor("identf_sb", [128, 128], F32))
    es2 = ExitStack()
    E2 = lambda t: es2.enter_context(t)
    x_tm = E2(nc.sbuf_tensor("x_tm", [128, 4, D], F32))
    mo_tm = E2(nc.sbuf_tensor("mo_tm", [128, 4, D], F32))
    ln1b = E2(nc.sbuf_tensor("ln1b", [128, D], F32))
    sq2 = E2(nc.sbuf_tensor("sq2", [128, D], F32))
    st2 = E2(nc.sbuf_tensor("st2", [128, 8], F32))
    pmw_rep = E2(nc.sbuf_tensor("pmw_rep", [128, D], F32))
    hn_tm = E2(nc.sbuf_tensor("hn_tm", [128, 4, D], BF16))
    hn_fm = E2(nc.sbuf_tensor("hn_fm", [128, 8, TC], BF16))
    ff_sb = E2(nc.sbuf_tensor("ff_sb", [128, 32, TC], BF16))
    gw_t = [E2(nc.sbuf_tensor(f"gw{i}", [128, 8, 512], BF16)) for i in range(2)]
    uw_t = [E2(nc.sbuf_tensor(f"uw{i}", [128, 8, 512], BF16)) for i in range(2)]
    dw_t = [E2(nc.sbuf_tensor(f"dw{i}", [128, TC], BF16)) for i in range(2)]
    sgl = E2(nc.sbuf_tensor("sgl", [128, TC], F32))
    p2_tr = [E2(nc.psum_tensor(f"p2_tr{i}", [128, 128], BF16)) for i in range(2)]
    p2_g = E2(nc.psum_tensor("p2_g", [128, TC], F32))
    p2_u = E2(nc.psum_tensor("p2_u", [128, TC], F32))
    p2_d = [E2(nc.psum_tensor(f"p2_d{i}", [128, TC], F32)) for i in range(4)]

    sW.n += 48
    P.emit("sync", lambda e: e.dma_start(out=identf_sb[:], in_=ident_f).then_inc(sW.h, 16))
    P.emit("sync", lambda e: e.dma_start(out=pmw_rep[:], in_=pmw.broadcast_to([128, D])).then_inc(sW.h, 16))
    P.emit("sync", lambda e: e.dma_start(out=ident_sb[:], in_=ident_bf).then_inc(sW.h, 16))
    v_w2 = sW.n

    def phase2(sc):
        nonlocal vv_dev
        L = sc * 512
        ev_tr = p2_evtr
        va_hn = {}
        for tt in range(4):
            sld = sRS if tt % 2 == 0 else sRS2
            if tt == 0:
                P.wait("sync", sA, p2_trans_prev[0])
            sld.n += 32
            P.emit("sync", lambda e, t_=tt, s=sld: e.dma_start(
                out=x_tm[:, t_, :], in_=x_q[L + t_ * 128:L + (t_ + 1) * 128, :]).then_inc(s.h, 16))
            P.emit("sync", lambda e, t_=tt, s=sld: e.dma_start(
                out=mo_tm[:, t_, :], in_=rs1_out[L + t_ * 128:L + (t_ + 1) * 128, :]).then_inc(s.h, 16))
            v_ld = sld.n
            P.wait("vector", sld, v_ld)
            if tt == 0:
                P.wait("vector", sA, p2_trans_prev[0])
            P.emit("vector", lambda e, t_=tt: e.reduce_sum(out=st2[:, 0:1], in_=x_tm[:, t_, :], axis=AX.X))
            P.inc("vector", sVS); P.wait("vector", sVS, sVS.n)
            P.wait("scalar", sld, v_ld)
            P.wait("scalar", sV, p2_hn_prev[0])
            P.emit("scalar", lambda e, t_=tt: e.activation(
                out=sq2[:], in_=x_tm[:, t_, :], func=AF.Square, accum_out=st2[:, 1:2]))
            va_sq = P.inc("scalar", sA)
            P.wait("vector", sA, va_sq)
            P.emit("vector", lambda e: e.tensor_scalar_mul(out=st2[:, 2:3], in0=st2[:, 0:1], scalar1=1.0 / D))
            P.emit("vector", lambda e: e.tensor_scalar_mul(out=st2[:, 3:4], in0=st2[:, 1:2], scalar1=1.0 / D))
            P.inc("vector", sVS); P.wait("vector", sVS, sVS.n)
            P.emit("vector", lambda e: e.tensor_mul(out=st2[:, 4:5], in0=st2[:, 2:3], in1=st2[:, 2:3]))
            P.inc("vector", sVS); P.wait("vector", sVS, sVS.n)
            P.emit("vector", lambda e: e.tensor_sub(out=st2[:, 3:4], in0=st2[:, 3:4], in1=st2[:, 4:5]))
            vv_st = P.inc("vector", sV)
            P.wait("scalar", sV, vv_st)
            P.emit("scalar", lambda e: e.activation(out=st2[:, 5:6], in_=st2[:, 3:4], func=AF.Ln, bias=cst_sb[:, 0:1]))
            P.inc("scalar", sAS); P.wait("scalar", sAS, sAS.n)
            P.emit("scalar", lambda e: e.activation(out=st2[:, 5:6], in_=st2[:, 5:6], func=AF.Exp, scale=-0.5))
            va_r = P.inc("scalar", sA)
            P.wait("scalar", sA, va_r)
            P.wait("vector", sA, va_r)
            P.emit("vector", lambda e: e.tensor_mul(out=st2[:, 6:7], in0=st2[:, 2:3], in1=st2[:, 5:6]))
            P.inc("vector", sVS); P.wait("vector", sVS, sVS.n)
            P.emit("vector", lambda e: e.tensor_scalar_mul(out=st2[:, 6:7], in0=st2[:, 6:7], scalar1=-1.0))
            vv_nm = P.inc("vector", sV)
            P.wait("scalar", sV, vv_nm)
            P.emit("scalar", lambda e, t_=tt: e.activation(
                out=ln1b[:], in_=x_tm[:, t_, :], func=AF.Identity, scale=st2[:, 5:6], bias=st2[:, 6:7]))
            va_ln1 = P.inc("scalar", sA)
            P.wait("vector", sA, va_ln1)
            P.emit("vector", lambda e, t_=tt: e.tensor_add(out=mo_tm[:, t_, :], in0=ln1b[:], in1=mo_tm[:, t_, :]))
            P.inc("vector", sVS); P.wait("vector", sVS, sVS.n)
            P.emit("vector", lambda e, t_=tt: e.tensor_add(out=x_tm[:, t_, :], in0=x_tm[:, t_, :], in1=mo_tm[:, t_, :]))
            vv_s1 = P.inc("vector", sV)
            P.wait("scalar", sV, vv_s1)
            P.emit("scalar", lambda e, t_=tt: e.activation(
                out=sq2[:], in_=mo_tm[:, t_, :], func=AF.Square, accum_out=st2[:, 7:8]))
            P.inc("scalar", sAS); P.wait("scalar", sAS, sAS.n)
            P.emit("scalar", lambda e: e.activation(out=st2[:, 7:8], in_=st2[:, 7:8], func=AF.Ln, scale=1.0 / D, bias=cst_sb[:, 1:2]))
            P.inc("scalar", sAS); P.wait("scalar", sAS, sAS.n)
            P.emit("scalar", lambda e: e.activation(out=st2[:, 7:8], in_=st2[:, 7:8], func=AF.Exp, scale=-0.5))
            va_r3 = P.inc("scalar", sA)
            P.wait("vector", sA, va_r3)
            P.emit("vector", lambda e, t_=tt: e.scalar_tensor_tensor(
                out=hn_tm[:, t_, :], in0=mo_tm[:, t_, :], scalar=st2[:, 7:8], in1=pmw_rep[:],
                op0=ALU.mult, op1=ALU.mult))
            va_hn[tt] = P.inc("vector", sV)
            p2_hn_prev[0] = va_hn[tt]
        P.wait("tensor", sW, v_w2)
        P.wait("tensor", sV, va_hn[3])
        P.wait("tensor", sT, t_gu.get((sc - 1, 31), 0))
        slot = 0
        first_hn = [True]
        for tt in range(4):
            for kk in range(8):
                P.wait("tensor", sA, ev_tr.get(slot, 0))
                P.emit("tensor", lambda e, t_=tt, k_=kk, sl=slot: e.transpose(
                    p2_tr[sl][:], hn_tm[:, t_, k_ * 128:(k_ + 1) * 128], ident_sb[:]))
                vt = P.inc("tensor", sT)
                P.wait("scalar", sT, vt)
                if first_hn[0]:
                    first_hn[0] = False
                    P.wait("scalar", sT, t_gu.get((sc - 1, 31), 0))
                P.emit("scalar", lambda e, t_=tt, k_=kk, sl=slot: e.activation(
                    out=hn_fm[:, k_, t_ * 128:(t_ + 1) * 128], in_=p2_tr[sl][:], func=AF.Copy))
                ev_tr[slot] = P.inc("scalar", sA)
                slot = (slot + 1) % 2
        slotf = 4
        P.wait("tensor", sV, vv_dev)
        for tt in range(4):
            for kk in range(8):
                P.wait("tensor", sA, ev_tr.get(slotf, 0))
                P.emit("tensor", lambda e, t_=tt, k_=kk, sl=slotf - 4: e.transpose(
                    p2_d[sl][:, 0:128], x_tm[:, t_, k_ * 128:(k_ + 1) * 128], identf_sb[:]))
                vt = P.inc("tensor", sT)
                P.wait("scalar", sT, vt)
                P.emit("scalar", lambda e, t_=tt, k_=kk, sl=slotf - 4: e.activation(
                    out=stream_fm[:, k_, L + t_ * 128:L + (t_ + 1) * 128], in_=p2_d[sl][:, 0:128], func=AF.Copy))
                ev_tr[slotf] = P.inc("scalar", sA)
                slotf = 4 + ((slotf - 4 + 1) % 2)
        va_trans = sA.n
        p2_trans_prev[0] = va_trans
        for f in range(32):
            f4, fo = f // 4, (f % 4) * 128
            pf = (f // 4) % 2
            sgw = sG2 if pf == 0 else sG3
            if f % 4 == 0:
                if f4 >= 2:
                    P.wait("sync", sT, t_gu[(sc, (f4 - 2) * 4 + 3)])
                elif sc >= 1:
                    P.wait("sync", sT, t_gu[(sc - 1, (6 + f4) * 4 + 3)])
                P.wait("sync", sGBp[0], sGBp[0].n)
                P.wait("sync", sGBp[1], sGBp[1].n)
                sgw.n += 32
                P.emit("sync", lambda e, f_=f4, p=pf, s=sgw: e.dma_start(
                    out=gw_t[p][:], in_=gate_bf[:, f_ * 512:(f_ + 1) * 512].rearrange("(kk p) m -> p kk m", p=128)).then_inc(s.h, 16))
                P.emit("sync", lambda e, f_=f4, p=pf, s=sgw: e.dma_start(
                    out=uw_t[p][:], in_=up_bf[:, f_ * 512:(f_ + 1) * 512].rearrange("(kk p) m -> p kk m", p=128)).then_inc(s.h, 16))
                gld[(sc, f4)] = sgw.n
            P.wait("tensor", sgw, gld[(sc, f4)])
            P.wait("tensor", sA, va_trans)
            if f >= 1:
                P.wait("tensor", sV, vv_ff[f - 1])
                P.wait("tensor", sA, va_sgl[f - 1])
            elif sc >= 1:
                P.wait("tensor", sV, vv_ff.get(31, 0))
                P.wait("tensor", sA, va_sgl.get(31, 0))
            for kk in range(8):
                P.emit("tensor", lambda e, k_=kk, p=pf, o=fo: e.matmul(
                    p2_g[:], gw_t[p][:, k_, o:o + 128], hn_fm[:, k_, :], start=(k_ == 0), stop=(k_ == 7)))
            for kk in range(8):
                P.emit("tensor", lambda e, k_=kk, p=pf, o=fo: e.matmul(
                    p2_u[:], uw_t[p][:, k_, o:o + 128], hn_fm[:, k_, :], start=(k_ == 0), stop=(k_ == 7)))
            vt_gu = P.inc("tensor", sT)
            t_gu[(sc, f)] = vt_gu
            P.wait("scalar", sT, vt_gu)
            P.emit("scalar", lambda e: e.activation(out=sgl[:], in_=p2_g[:], func=AF.Silu))
            va_sgl[f] = P.inc("scalar", sA)
            P.wait("vector", sA, va_sgl[f])
            if f == 0:
                P.wait("vector", sT, t_d.get((1, 31), 0))
            P.emit("vector", lambda e, f_=f: e.tensor_mul(out=ff_sb[:, f_, :], in0=sgl[:], in1=p2_u[:]))
            vv_ff[f] = P.inc("vector", sV)
        ev_d = sA.n
        for ch in range(2):
            for f in range(32):
                pf = f % 2
                sgw = sG2 if pf == 0 else sG3
                if not (ch == 0 and f < 2):
                    P.wait("sync", sT, t_d.get((ch, f - 2), t_d.get((ch - 1, 30 + f), sT.n if False else t_gu[31] if f < 2 else t_d[(ch, f - 2)])) ) if False else None
                if ch > 0 or f >= 2:
                    prev = (ch, f - 2) if f >= 2 else (ch - 1, 30 + f)
                    P.wait("sync", sT, t_d[prev])
                elif sc >= 1:
                    P.wait("sync", sT, t_d.get((1, 30 + f), 0))
                sgw.n += 16
                P.emit("sync", lambda e, f_=f, c_=ch, p=pf, s=sgw: e.dma_start(
                    out=dw_t[p][:], in_=down_bf[f_ * 128:(f_ + 1) * 128, c_ * 512:(c_ + 1) * 512]).then_inc(s.h, 16))
                P.wait("tensor", sgw, sgw.n)
                P.wait("tensor", sV, vv_ff[31])
                if f == 0:
                    P.wait("tensor", sV, vv_dev)
                for ct in range(4):
                    P.emit("tensor", lambda e, c2=ct, p=pf, f_=f: e.matmul(
                        p2_d[c2][:], dw_t[p][:, c2 * 128:(c2 + 1) * 128], ff_sb[:, f_, :],
                        start=(f_ == 0), stop=(f_ == 31)))
                t_d[(ch, f)] = P.inc("tensor", sT)
            P.wait("vector", sT, t_d[(ch, 31)])
            for ct in range(4):
                P.emit("vector", lambda e, c_=ch, c2=ct: e.tensor_add(
                    out=stream_fm[:, c_ * 4 + c2, L:L + 512],
                    in0=stream_fm[:, c_ * 4 + c2, L:L + 512], in1=p2_d[c2][:]))
            vv_dev = P.inc("vector", sV)
        return sA.n, sV.n

    p2_evtr = {}
    p2_trans_prev = [0]
    p2_hn_prev = [0]
    gld = {}
    t_gu = {}
    p2_done = []
    for sc in range(NSC):
        p2_done.append(phase2(sc))


    def ffn_phase(which, dbg_out=None):
        W_ = which
        """which in (0, 1): first or second application."""
        es3 = ExitStack()
        E3 = lambda t: es3.enter_context(t)
        m_sb = E3(nc.sbuf_tensor(f"f_m{W_}", [1, TQ], F32))
        r_std = E3(nc.sbuf_tensor(f"f_rstd{W_}", [1, TQ], F32))
        mr_sb = E3(nc.sbuf_tensor(f"f_mr{W_}", [1, 512], F32))
        stat_ev = E3(nc.sbuf_tensor(f"f_stat{W_}", [1, 512], F32))
        ln2_fm = E3(nc.sbuf_tensor(f"f_ln2{W_}", [128, 8, 512], F32))
        d_fm = E3(nc.sbuf_tensor(f"f_d{W_}", [128, 8, 512], F32))
        xr_bf = E3(nc.sbuf_tensor(f"f_xr{W_}", [128, 8, 512], BF16))
        xk_bf = E3(nc.sbuf_tensor(f"f_xk{W_}", [128, 8, 512], BF16))
        r_sb = E3(nc.sbuf_tensor(f"f_r{W_}", [128, 8, 512], BF16))
        k2_bf = E3(nc.sbuf_tensor(f"f_k2{W_}", [128, 8, 512], BF16))
        kr_sb = E3(nc.sbuf_tensor(f"f_kr{W_}", [128, 512], F32))
        mur_sb = E3(nc.sbuf_tensor(f"f_mur{W_}", [128, 8], F32))
        muk_sb = E3(nc.sbuf_tensor(f"f_muk{W_}", [128, 8], F32))
        sel_sb = E3(nc.sbuf_tensor(f"f_sel{W_}", [16, 4], BF16))
        bndrow = E3(nc.sbuf_tensor(f"f_bndrow{W_}", [8, 128], BF16))
        bcol_fm = E3(nc.sbuf_tensor(f"f_bcol{W_}", [128, 8], BF16))
        rep_ev = E3(nc.sbuf_tensor(f"f_repev{W_}", [128, 2], F32))
        bnd_sb2 = E3(nc.sbuf_tensor(f"f_bnd2{W_}", [16, D], BF16))
        prev_sb = E3(nc.sbuf_tensor(f"f_prev{W_}", [128, D], BF16))
        prevf_sb = E3(nc.sbuf_tensor(f"f_prevf{W_}", [128, 8, 4], F32))
        t2_s = E3(nc.sbuf_tensor(f"f_t2{W_}", [128, 512], F32))
        wr_a = E3(nc.sbuf_tensor(f"f_wr{W_}", [128, 8, D], BF16))
        wk_a = E3(nc.sbuf_tensor(f"f_wk{W_}", [128, 8, D], BF16))
        wv_a = E3(nc.sbuf_tensor(f"f_wv{W_}", [128, 8, D], BF16))
        p3_s = E3(nc.psum_tensor(f"p3_s{W_}", [1, 512], F32))
        p3_rep = [E3(nc.psum_tensor(f"p3_rep{W_}_{i}", [128, 512], F32)) for i in range(2)]
        p3_r = E3(nc.psum_tensor(f"p3_r{W_}", [128, 512], F32))
        p3_k = E3(nc.psum_tensor(f"p3_k{W_}", [128, 512], F32))
        p3_v = E3(nc.psum_tensor(f"p3_v{W_}", [128, 512], F32))
        p3_t = E3(nc.psum_tensor(f"p3_t{W_}", [128, 128], BF16))

        sW.n += 48
        P.emit("sync", lambda e: e.dma_start(out=mur_sb[:], in_=mu_r).then_inc(sW.h, 16))
        P.emit("sync", lambda e: e.dma_start(out=muk_sb[:], in_=mu_k).then_inc(sW.h, 16))
        P.emit("sync", lambda e: e.dma_start(out=sel_sb[:], in_=selp).then_inc(sW.h, 16))
        v_l3 = sW.n
        P.wait("sync", sGBp[0], sGBp[0].n)
        P.wait("sync", sGBp[1], sGBp[1].n)
        sG2.n += 48
        for wsrc, wdst in ((wrT_bf, wr_a), (wkT_bf, wk_a), (wvT_bf, wv_a)):
            P.emit("sync", lambda e, s=wsrc, d=wdst: e.dma_start(
                out=d[:], in_=s.rearrange("(kk p) c -> p kk c", p=128)).then_inc(sG2.h, 16))
        w3_ld = [sG2.n]
        P.wait("vector", sW, v_l3)
        P.emit("vector", lambda e: e.tensor_scalar(
            out=mur_sb[:], in0=mur_sb[:], scalar1=1.0, scalar2=None, op0=ALU.mult))
        P.inc("vector", sVS); P.wait("vector", sVS, sVS.n)

        bnd_st = 0
        bcol_ev = [0]
        for sc in range(NSC):
            L = sc * 512
            P.wait("scalar", sV, sV.n)
            P.emit("scalar", lambda e, L_=L: e.activation(
                out=d_fm[:], in_=stream_fm[:, :, L_:L_ + 512], func=AF.Square))
            va_sq3 = P.inc("scalar", sA)
            P.wait("tensor", sA, va_sq3)
            P.wait("tensor", sW, v_l3)
            for kk in range(8):
                P.emit("tensor", lambda e, k_=kk, L_=L: e.matmul(
                    p3_s[:], ones_sb[0:128, 0:1], stream_fm[:, k_, L_:L_ + 512],
                    start=(k_ == 0), stop=(k_ == 7)))
            vt_s1 = P.inc("tensor", sT)
            P.wait("scalar", sT, vt_s1)
            P.emit("scalar", lambda e, L_=L: e.activation(out=m_sb[0:1, L_:L_ + 512], in_=p3_s[:], func=AF.Copy))
            va_s1 = P.inc("scalar", sA)
            P.wait("tensor", sA, va_s1)
            for kk in range(8):
                P.emit("tensor", lambda e, k_=kk: e.matmul(
                    p3_s[:], ones_sb[0:128, 0:1], d_fm[:, k_, :],
                    start=(k_ == 0), stop=(k_ == 7)))
            vt_s2 = P.inc("tensor", sT)
            P.wait("scalar", sT, vt_s2)
            P.emit("scalar", lambda e: e.activation(out=stat_ev[0:1, :], in_=p3_s[:], func=AF.Copy))
            va_s2 = P.inc("scalar", sA)
            P.wait("vector", sA, va_s2)
            P.emit("vector", lambda e, L_=L: e.tensor_scalar_mul(
                out=m_sb[0:1, L_:L_ + 512], in0=m_sb[0:1, L_:L_ + 512], scalar1=1.0 / D))
            P.inc("vector", sVS); P.wait("vector", sVS, sVS.n)
            P.emit("vector", lambda e, L_=L: e.tensor_mul(
                out=mr_sb[0:1, :], in0=m_sb[0:1, L_:L_ + 512], in1=m_sb[0:1, L_:L_ + 512]))
            P.inc("vector", sVS); P.wait("vector", sVS, sVS.n)
            P.emit("vector", lambda e: e.tensor_scalar_mul(
                out=stat_ev[0:1, :], in0=stat_ev[0:1, :], scalar1=1.0 / D))
            P.inc("vector", sVS); P.wait("vector", sVS, sVS.n)
            P.emit("vector", lambda e: e.tensor_sub(
                out=stat_ev[0:1, :], in0=stat_ev[0:1, :], in1=mr_sb[0:1, :]))
            vv_var = P.inc("vector", sV)
            P.wait("scalar", sV, vv_var)
            P.emit("scalar", lambda e, L_=L: e.activation(
                out=r_std[0:1, L_:L_ + 512], in_=stat_ev[0:1, :], func=AF.Ln, bias=cst_sb[0:1, 0:1]))
            P.inc("scalar", sAS); P.wait("scalar", sAS, sAS.n)
            P.emit("scalar", lambda e, L_=L: e.activation(
                out=r_std[0:1, L_:L_ + 512], in_=r_std[0:1, L_:L_ + 512], func=AF.Exp, scale=-0.5))
            va_rstd = P.inc("scalar", sA)
            P.wait("tensor", sA, va_rstd)
            P.emit("tensor", lambda e, L_=L: e.matmul(
                p3_rep[0][:, 0:1], ones_sb[0:1, 0:128], r_std[0:1, L_ + 511:L_ + 512], start=True, stop=True))
            P.emit("tensor", lambda e, L_=L: e.matmul(
                p3_rep[1][:, 0:1], ones_sb[0:1, 0:128], m_sb[0:1, L_ + 511:L_ + 512], start=True, stop=True))
            vt_rep = P.inc("tensor", sT)
            P.wait("scalar", sT, vt_rep)
            P.emit("scalar", lambda e: e.activation(out=rep_ev[:, 0:1], in_=p3_rep[0][:, 0:1], func=AF.Copy))
            P.emit("scalar", lambda e: e.activation(out=rep_ev[:, 1:2], in_=p3_rep[1][:, 0:1], func=AF.Copy))
            va_repev = P.inc("scalar", sA)
            P.wait("vector", sA, va_repev)
            P.emit("vector", lambda e: e.tensor_mul(out=rep_ev[:, 1:2], in0=rep_ev[:, 1:2], in1=rep_ev[:, 0:1]))
            P.inc("vector", sVS); P.wait("vector", sVS, sVS.n)
            P.emit("vector", lambda e, L_=L: e.scalar_tensor_tensor(
                out=bcol_fm[:], in0=stream_fm[:, :, L_ + 511], scalar=rep_ev[:, 0:1],
                in1=rep_ev[:, 1:2].broadcast_to([128, 8]), op0=ALU.mult, op1=ALU.subtract))
            vv_bcol = P.inc("vector", sV)
            P.wait("tensor", sV, vv_bcol)
            P.wait("tensor", sA, bcol_ev[0])
            P.emit("tensor", lambda e: e.transpose(p3_t[0:8, 0:128], bcol_fm[:], ident_sb[:]))
            vt_b = P.inc("tensor", sT)
            P.wait("scalar", sT, vt_b)
            P.emit("scalar", lambda e: e.activation(out=bndrow[:], in_=p3_t[0:8, 0:128], func=AF.Copy))
            va_bnd = P.inc("scalar", sA)
            bcol_ev[0] = va_bnd
            P.wait("sync", sA, va_bnd)
            sDBG.n += 16
            P.emit("sync", lambda e, s=sc, w=which: e.dma_start(
                out=bnd_my[w][s, :].rearrange("(a c) -> a c", a=8), in_=bndrow[:]).then_inc(sDBG.h, 16))
        bnd_st = sDBG.n
        P.wait("gpsimd", sDBG, bnd_st)
        sCC.n += 1
        P.emit("gpsimd", lambda e, w=which: e.collective_compute(
            "AllGather", ALU.bypass, replica_groups=GROUPS,
            ins=[bnd_my[w].opt()], outs=[bnd_all[w].opt()]).then_inc(sCC.h, 1))
        v_agb = sCC.n
        P.wait("gpsimd", sCC, v_agb)
        P.wait("sync", sCC, v_agb)
        sW.n += 16
        P.emit("sync", lambda e, w=which: e.dma_start(out=bnd_sb2[:], in_=bnd_all[w]).then_inc(sW.h, 16))
        v_bl = sW.n
        P.wait("tensor", sW, v_bl)
        for hh in range(2):
            P.emit("tensor", lambda e, h_=hh: e.matmul(
                p3_rep[h_][0:4, :], sel_sb[:], bnd_sb2[:, h_ * 512:(h_ + 1) * 512], start=True, stop=True))
        vt_sel = P.inc("tensor", sT)
        P.wait("scalar", sT, vt_sel)
        P.emit("scalar", lambda e: e.activation(out=prev_sb[0:4, 0:512], in_=p3_rep[0][0:4, :], func=AF.Copy))
        P.emit("scalar", lambda e: e.activation(out=prev_sb[0:4, 512:1024], in_=p3_rep[1][0:4, :], func=AF.Copy))
        va_prev = P.inc("scalar", sA)
        P.wait("tensor", sA, va_prev)
        ev_pt = 0
        for kk in range(8):
            P.wait("tensor", sA, ev_pt)
            P.emit("tensor", lambda e, k_=kk: e.transpose(
                p3_t[:], prev_sb[:, k_ * 128:(k_ + 1) * 128], ident_sb[:]))
            vt_p = P.inc("tensor", sT)
            P.wait("scalar", sT, vt_p)
            P.emit("scalar", lambda e, k_=kk: e.activation(
                out=prevf_sb[:, k_, :], in_=p3_t[:, 0:4], func=AF.Copy))
            ev_pt = P.inc("scalar", sA)
        va_prevf = sA.n

        def ffn_block(sc):
            L = sc * 512
            P.wait("vector", sA, va_prevf)
            P.emit("vector", lambda e, L_=L: e.tensor_mul(
                out=mr_sb[0:1, :], in0=m_sb[0:1, L_:L_ + 512], in1=r_std[0:1, L_:L_ + 512]))
            vv_mr = P.inc("vector", sV)
            P.wait("tensor", sV, vv_mr)
            P.emit("tensor", lambda e, L_=L: e.matmul(
                p3_rep[0][:], ones_sb[0:1, 0:128], r_std[0:1, L_:L_ + 512], start=True, stop=True))
            P.emit("tensor", lambda e: e.matmul(
                p3_rep[1][:], ones_sb[0:1, 0:128], mr_sb[0:1, :], start=True, stop=True))
            vt_rp = P.inc("tensor", sT)
            P.wait("vector", sT, vt_rp)
            for a in range(8):
                P.emit("vector", lambda e, a_=a, L_=L: e.tensor_mul(
                    out=ln2_fm[:, a_, :], in0=stream_fm[:, a_, L_:L_ + 512], in1=p3_rep[0][:]))
            P.inc("vector", sVS); P.wait("vector", sVS, sVS.n)
            for a in range(8):
                P.emit("vector", lambda e, a_=a: e.tensor_sub(
                    out=ln2_fm[:, a_, :], in0=ln2_fm[:, a_, :], in1=p3_rep[1][:]))
            P.inc("vector", sVS); P.wait("vector", sVS, sVS.n)
            P.emit("vector", lambda e: e.tensor_sub(
                out=d_fm[:, :, 1:512].rearrange("p a t -> p (a t)") if False else d_fm[:].rearrange("p a t -> p a t")[:, :, 1:512],
                in0=ln2_fm[:].rearrange("p a t -> p a t")[:, :, 1:512],
                in1=ln2_fm[:].rearrange("p a t -> p a t")[:, :, 0:511]))
            P.emit("vector", lambda e, s_=sc: e.tensor_sub(
                out=d_fm[:, :, 0:1], in0=ln2_fm[:, :, 0:1], in1=prevf_sb[:, :, s_:s_ + 1]))
            vv_d = P.inc("vector", sV)
            for a in range(8):
                P.emit("vector", lambda e, a_=a: e.scalar_tensor_tensor(
                    out=xr_bf[:, a_, 1:512], in0=d_fm[:, a_, 1:512], scalar=mur_sb[:, a_:a_ + 1],
                    in1=ln2_fm[:, a_, 0:511], op0=ALU.mult, op1=ALU.add))
                P.emit("vector", lambda e, a_=a, s_=sc: e.scalar_tensor_tensor(
                    out=xr_bf[:, a_, 0:1], in0=d_fm[:, a_, 0:1], scalar=mur_sb[:, a_:a_ + 1],
                    in1=prevf_sb[:, a_, s_:s_ + 1], op0=ALU.mult, op1=ALU.add))
                P.emit("vector", lambda e, a_=a: e.scalar_tensor_tensor(
                    out=xk_bf[:, a_, 1:512], in0=d_fm[:, a_, 1:512], scalar=muk_sb[:, a_:a_ + 1],
                    in1=ln2_fm[:, a_, 0:511], op0=ALU.mult, op1=ALU.add))
                P.emit("vector", lambda e, a_=a, s_=sc: e.scalar_tensor_tensor(
                    out=xk_bf[:, a_, 0:1], in0=d_fm[:, a_, 0:1], scalar=muk_sb[:, a_:a_ + 1],
                    in1=prevf_sb[:, a_, s_:s_ + 1], op0=ALU.mult, op1=ALU.add))
            vv_mix = P.inc("vector", sV)
            t_rk = {}
            for m in range(8):
                P.wait("tensor", sG2, w3_ld[0])
                P.wait("tensor", sV, vv_mix)
                if m >= 1:
                    P.wait("tensor", sA, va_rk.get(m - 1, 0))
                for kk in range(8):
                    P.emit("tensor", lambda e, k_=kk, m_=m: e.matmul(
                        p3_r[:], wr_a[:, k_, m_ * 128:(m_ + 1) * 128], xr_bf[:, k_, :], start=(k_ == 0), stop=(k_ == 7)))
                for kk in range(8):
                    P.emit("tensor", lambda e, k_=kk, m_=m: e.matmul(
                        p3_k[:], wk_a[:, k_, m_ * 128:(m_ + 1) * 128], xk_bf[:, k_, :], start=(k_ == 0), stop=(k_ == 7)))
                t_rk[m] = P.inc("tensor", sT)
                P.wait("scalar", sT, t_rk[m])
                P.emit("scalar", lambda e, m_=m: e.activation(
                    out=r_sb[:, m_, :], in_=p3_r[:], func=AF.Sigmoid))
                va_sig = P.inc("scalar", sA)
                P.wait("vector", sT, t_rk[m])
                P.emit("vector", lambda e: e.tensor_relu(out=kr_sb[:], in_=p3_k[:]))
                vv_kr = P.inc("vector", sV)
                P.wait("scalar", sV, vv_kr)
                P.emit("scalar", lambda e, m_=m: e.activation(
                    out=k2_bf[:, m_, :], in_=kr_sb[:], func=AF.Square))
                va_rk[m] = P.inc("scalar", sA)
            for m in range(8):
                P.wait("tensor", sA, va_rk[7])
                if m >= 1:
                    P.wait("tensor", sV, vv_str.get(m - 1, 0))
                for kk in range(8):
                    P.emit("tensor", lambda e, k_=kk, m_=m: e.matmul(
                        p3_v[:], wv_a[:, k_, m_ * 128:(m_ + 1) * 128], k2_bf[:, k_, :], start=(k_ == 0), stop=(k_ == 7)))
                t_v[m] = P.inc("tensor", sT)
                P.wait("vector", sT, t_v[m])
                P.emit("vector", lambda e, m_=m: e.tensor_copy(out=kr_sb[:], in_=r_sb[:, m_, :]))
                P.emit("vector", lambda e: e.tensor_mul(out=t2_s[:], in0=kr_sb[:], in1=p3_v[:]))
                P.inc("vector", sVS); P.wait("vector", sVS, sVS.n)
                P.emit("vector", lambda e, m_=m, L_=L: e.tensor_add(
                    out=stream_fm[:, m_, L_:L_ + 512], in0=stream_fm[:, m_, L_:L_ + 512], in1=t2_s[:]))
                vv_str[m] = P.inc("vector", sV)
            return sV.n, sA.n, sT.n

        va_rk = {}
        t_v = {}
        vv_str = {}
        f_done = []
        for sc in range(NSC):
            if sc > 0:
                for _e in ("sync", "vector", "scalar", "tensor"):
                    P.wait(_e, sV, f_done[-1][0])
                    P.wait(_e, sA, f_done[-1][1])
            f_done.append(ffn_block(sc))
        if dbg_out is not None:
            P.wait("sync", sV, sV.n)
            P.wait("sync", sA, sA.n)
            sDBG.n += 16
            P.emit("sync", lambda e: e.dma_start(out=dbg_out, in_=stream_fm[:]).then_inc(sDBG.h, 16))
        es3.close()
        return


    es2.close()
    for _e in ENGS:
        P.wait(_e, sT, sT.n)
        P.wait(_e, sA, sA.n)
        P.wait(_e, sV, sV.n)
        P.wait(_e, sP, sP.n)

    def attn_phase(dbg_out=None):
        es4 = ExitStack()
        E4 = lambda t: es4.enter_context(t)
        xh_q = E4(nc.sbuf_tensor("a_xhq", [128, 8, TQ], BF16))
        den_sb = E4(nc.sbuf_tensor("a_den", [1, 512], F32))
        invrep = E4(nc.sbuf_tensor("a_invrep", [128, 512], F32))
        t4 = E4(nc.sbuf_tensor("a_t4", [128, 512], F32))
        p4_s = E4(nc.psum_tensor("p4_s", [128, 512], F32))
        p4_rep = [E4(nc.psum_tensor(f"p4_rep{i}", [128, 512], F32)) for i in range(2)]
        p4_o = [E4(nc.psum_tensor(f"p4_o{i}", [128, 512], F32)) for i in range(4)]
        es4a = ExitStack()
        E4a = lambda t: es4a.enter_context(t)
        m4 = E4a(nc.sbuf_tensor("a_m", [1, 512], F32))
        r4 = E4a(nc.sbuf_tensor("a_rstd", [1, 512], F32))
        mr4 = E4a(nc.sbuf_tensor("a_mr", [1, 512], F32))
        sq4 = E4a(nc.sbuf_tensor("a_sq", [128, 8, 512], F32))
        stat4 = E4a(nc.sbuf_tensor("a_stat", [1, 1024], F32))
        ln3_fm = E4a(nc.sbuf_tensor("a_ln3", [128, 8, 512], F32))
        x3c_bf = E4a(nc.sbuf_tensor("a_x3c", [128, 8, 512], BF16))
        x3tm = E4a(nc.sbuf_tensor("a_x3tm", [128, 4, D], BF16))
        p4_tb = E4a(nc.psum_tensor("p4_tb", [128, 128], BF16))

        st_xh = 0
        st_xv = 0
        ev_tb = [0]
        for sc in range(NSC):
            L = sc * 512
            P.wait("scalar", sV, sV.n)
            P.emit("scalar", lambda e, L_=L: e.activation(
                out=sq4[:], in_=stream_fm[:, :, L_:L_ + 512], func=AF.Square))
            va_sq4 = P.inc("scalar", sA)
            P.wait("tensor", sA, va_sq4)
            for kk in range(8):
                P.emit("tensor", lambda e, k_=kk, L_=L: e.matmul(
                    p4_s[0:1, :], ones_sb[0:128, 0:1], stream_fm[:, k_, L_:L_ + 512],
                    start=(k_ == 0), stop=(k_ == 7)))
            vt1 = P.inc("tensor", sT)
            P.wait("scalar", sT, vt1)
            P.emit("scalar", lambda e: e.activation(out=stat4[0:1, 0:512], in_=p4_s[0:1, :], func=AF.Copy))
            va1 = P.inc("scalar", sA)
            P.wait("tensor", sA, va1)
            for kk in range(8):
                P.emit("tensor", lambda e, k_=kk: e.matmul(
                    p4_s[0:1, :], ones_sb[0:128, 0:1], sq4[:, k_, :], start=(k_ == 0), stop=(k_ == 7)))
            vt2 = P.inc("tensor", sT)
            P.wait("scalar", sT, vt2)
            P.emit("scalar", lambda e: e.activation(out=stat4[0:1, 512:1024], in_=p4_s[0:1, :], func=AF.Copy))
            va2 = P.inc("scalar", sA)
            P.wait("vector", sA, va2)
            P.emit("vector", lambda e, L_=L: e.tensor_scalar_mul(
                out=m4[0:1, :], in0=stat4[0:1, 0:512], scalar1=1.0 / D))
            P.inc("vector", sVS); P.wait("vector", sVS, sVS.n)
            P.emit("vector", lambda e, L_=L: e.tensor_mul(
                out=stat4[0:1, 0:512], in0=m4[0:1, :], in1=m4[0:1, :]))
            P.inc("vector", sVS); P.wait("vector", sVS, sVS.n)
            P.emit("vector", lambda e: e.tensor_scalar_mul(
                out=stat4[0:1, 512:1024], in0=stat4[0:1, 512:1024], scalar1=1.0 / D))
            P.inc("vector", sVS); P.wait("vector", sVS, sVS.n)
            P.emit("vector", lambda e: e.tensor_sub(
                out=stat4[0:1, 512:1024], in0=stat4[0:1, 512:1024], in1=stat4[0:1, 0:512]))
            vv_v4 = P.inc("vector", sV)
            P.wait("scalar", sV, vv_v4)
            P.emit("scalar", lambda e, L_=L: e.activation(
                out=r4[0:1, :], in_=stat4[0:1, 512:1024], func=AF.Ln, bias=cst_sb[0:1, 0:1]))
            P.inc("scalar", sAS); P.wait("scalar", sAS, sAS.n)
            P.emit("scalar", lambda e, L_=L: e.activation(
                out=r4[0:1, :], in_=r4[0:1, :], func=AF.Exp, scale=-0.5))
            va_r4 = P.inc("scalar", sA)
            P.wait("vector", sA, va_r4)
            P.emit("vector", lambda e, L_=L: e.tensor_mul(
                out=mr4[0:1, :], in0=m4[0:1, :], in1=r4[0:1, :]))
            vv_mr4 = P.inc("vector", sV)
            P.wait("tensor", sV, vv_mr4)
            P.emit("tensor", lambda e, L_=L: e.matmul(
                p4_rep[0][:], ones_sb[0:1, 0:128], r4[0:1, :], start=True, stop=True))
            P.emit("tensor", lambda e, L_=L: e.matmul(
                p4_rep[1][:], ones_sb[0:1, 0:128], mr4[0:1, :], start=True, stop=True))
            vt_rp = P.inc("tensor", sT)
            P.wait("vector", sT, vt_rp)
            for a in range(8):
                P.emit("vector", lambda e, a_=a, L_=L: e.tensor_mul(
                    out=ln3_fm[:, a_, :], in0=stream_fm[:, a_, L_:L_ + 512], in1=p4_rep[0][:]))
            P.inc("vector", sVS); P.wait("vector", sVS, sVS.n)
            for a in range(8):
                P.emit("vector", lambda e, a_=a: e.tensor_sub(
                    out=ln3_fm[:, a_, :], in0=ln3_fm[:, a_, :], in1=p4_rep[1][:]))
            P.inc("vector", sVS); P.wait("vector", sVS, sVS.n)
            for a in range(8):
                P.emit("vector", lambda e, a_=a, L_=L: e.tensor_add(
                    out=stream_fm[:, a_, L_:L_ + 512], in0=stream_fm[:, a_, L_:L_ + 512],
                    in1=ln3_fm[:, a_, :]))
            vv_res = P.inc("vector", sV)
            P.wait("scalar", sV, vv_res)
            P.emit("scalar", lambda e: e.activation(out=sq4[:], in_=ln3_fm[:], func=AF.Square))
            va_sq5 = P.inc("scalar", sA)
            P.wait("tensor", sA, va_sq5)
            for kk in range(8):
                P.emit("tensor", lambda e, k_=kk: e.matmul(
                    p4_s[0:1, :], ones_sb[0:128, 0:1], sq4[:, k_, :], start=(k_ == 0), stop=(k_ == 7)))
            vt_n = P.inc("tensor", sT)
            P.wait("scalar", sT, vt_n)
            P.emit("scalar", lambda e: e.activation(out=stat4[0:1, 0:512], in_=p4_s[0:1, :], func=AF.Copy))
            va_n = P.inc("scalar", sA)
            P.wait("vector", sA, va_n)
            P.emit("vector", lambda e: e.tensor_scalar_max(out=stat4[0:1, 0:512], in0=stat4[0:1, 0:512], scalar1=1e-8))
            vv_mx = P.inc("vector", sV)
            P.wait("scalar", sV, vv_mx)
            P.emit("scalar", lambda e: e.activation(out=stat4[0:1, 0:512], in_=stat4[0:1, 0:512], func=AF.Ln))
            P.inc("scalar", sAS); P.wait("scalar", sAS, sAS.n)
            P.emit("scalar", lambda e: e.activation(out=stat4[0:1, 0:512], in_=stat4[0:1, 0:512], func=AF.Exp, scale=-0.5))
            va_inv = P.inc("scalar", sA)
            P.wait("tensor", sA, va_inv)
            P.emit("tensor", lambda e: e.matmul(
                p4_rep[0][:], ones_sb[0:1, 0:128], stat4[0:1, 0:512], start=True, stop=True))
            vt_ir = P.inc("tensor", sT)
            P.wait("vector", sT, vt_ir)
            for a in range(8):
                P.emit("vector", lambda e, a_=a, L_=L: e.tensor_mul(
                    out=xh_q[:, a_, L_:L_ + 512], in0=ln3_fm[:, a_, :], in1=p4_rep[0][:]))
            vv_xh = P.inc("vector", sV)
            P.wait("scalar", sV, vv_xh)
            P.emit("scalar", lambda e: e.activation(out=x3c_bf[:], in_=ln3_fm[:], func=AF.Copy))
            va_x3c = P.inc("scalar", sA)
            P.wait("tensor", sA, va_x3c)
            for tt in range(4):
                for kk in range(8):
                    P.wait("tensor", sA, ev_tb[0])
                    P.emit("tensor", lambda e, t_=tt, k_=kk: e.transpose(
                        p4_tb[:], x3c_bf[:, k_, t_ * 128:(t_ + 1) * 128], ident_sb[:]))
                    vtb = P.inc("tensor", sT)
                    P.wait("scalar", sT, vtb)
                    P.emit("scalar", lambda e, t_=tt, k_=kk: e.activation(
                        out=x3tm[:, t_, k_ * 128:(k_ + 1) * 128], in_=p4_tb[:], func=AF.Copy))
                    ev_tb[0] = P.inc("scalar", sA)
            P.wait("sync", sA, sA.n)
            P.wait("sync", sV, vv_xh)
            sHT.n += 32
            P.emit("sync", lambda e, L_=L, s_=sc: e.dma_start(
                out=xh_my[s_], in_=xh_q[:, :, L_:L_ + 512]).then_inc(sHT.h, 16))
            P.emit("sync", lambda e, L_=L: e.dma_start(
                out=xv_my[L_:L_ + 512, :].rearrange("(t p) c -> p t c", p=128),
                in_=x3tm[:]).then_inc(sHT.h, 16))
        st_keys = sHT.n
        es4a.close()
        E_sb = E4(nc.sbuf_tensor("a_E", [128, 64, 512], BF16))
        xh_k = [E4(nc.sbuf_tensor(f"a_xhk{i}", [128, 8, 512], BF16)) for i in range(2)]
        xv_k = [E4(nc.sbuf_tensor(f"a_xvk{i}", [128, D], BF16)) for i in range(2)]
        p4_ox = E4(nc.psum_tensor("p4_ox", [128, 512], F32))
        OB = [p4_o[0], p4_o[1], p4_o[2], p4_o[3], p4_rep[0], p4_rep[1], p4_s, p4_ox]
        P.wait("gpsimd", sHT, st_keys)
        sCC.n += 2
        P.emit("gpsimd", lambda e: e.collective_compute(
            "AllGather", ALU.bypass, replica_groups=GROUPS,
            ins=[xh_my.opt()], outs=[xh_all.opt()]).then_inc(sCC.h, 1))
        P.emit("gpsimd", lambda e: e.collective_compute(
            "AllGather", ALU.bypass, replica_groups=GROUPS,
            ins=[xv_my.opt()], outs=[xv_all.opt()]).then_inc(sCC.h, 1))
        v_agh = sCC.n - 1
        v_agv = sCC.n

        exp_done = {}
        pv_done_t = [0]
        evO = {}
        xh_ld = {}
        den_ev = [0]
        for qc in range(4):
            Q0 = qc * 512
            for kt in range(64):
                k4, ko = kt // 4, (kt % 4) * 128
                r_, j4 = kt // 16, (kt % 16) // 4
                pk = k4 % 2
                shk = sH[pk]
                if kt % 4 == 0:
                    qe = "sync"
                    if k4 >= 2:
                        P.wait(qe, sT, qk_done[(qc, (k4 - 2) * 4 + 3)])
                    P.wait(qe, sCC, v_agh)
                    shk.n += 16
                    P.emit(qe, lambda e, r2=r_, j2=j4, p=pk, s=shk: e.dma_start(
                        out=xh_k[p][:], in_=xh_all[r2 * 4 + j2]).then_inc(s.h, 16))
                    xh_ld[(qc, k4)] = shk.n
                P.wait("tensor", shk, xh_ld[(qc, k4)])
                if qc > 0 and kt == 0:
                    P.wait("tensor", sV, str_done[qc - 1])
                if qc > 0 and kt == 0:
                    P.wait("scalar", sT, pv_done_t[0])
                pq = kt % 2
                if kt >= 2:
                    P.wait("tensor", sA, exp_done[(qc, kt - 2)])
                for kk in range(8):
                    P.emit("tensor", lambda e, k_=kk, p=pk, pr=pq, q_=Q0, o=ko: e.matmul(
                        p4_rep[pr][:], xh_k[p][:, k_, o:o + 128], xh_q[:, k_, q_:q_ + 512],
                        start=(k_ == 0), stop=(k_ == 7)))
                vt_qk = P.inc("tensor", sT)
                qk_done[(qc, kt)] = vt_qk
                P.wait("scalar", sT, vt_qk)
                P.emit("scalar", lambda e, kt_=kt, p=pq: e.activation(
                    out=E_sb[:, kt_, :], in_=p4_rep[p][:], func=AF.Exp))
                exp_done[(qc, kt)] = P.inc("scalar", sA)
            P.wait("tensor", sA, exp_done[(qc, 63)])
            P.wait("tensor", sA, den_ev[0])
            for kt in range(64):
                P.emit("tensor", lambda e, kt_=kt: e.matmul(
                    p4_s[0:1, :], onesbf_sb[:], E_sb[:, kt_, :], start=(kt_ == 0), stop=(kt_ == 63)))
            vt_den = P.inc("tensor", sT)
            P.wait("scalar", sT, vt_den)
            P.emit("scalar", lambda e: e.activation(out=den_sb[:], in_=p4_s[0:1, :], func=AF.Copy))
            va_den = P.inc("scalar", sA)
            den_ev[0] = va_den
            P.wait("vector", sA, va_den)
            P.emit("vector", lambda e: e.reciprocal(out=den_sb[:], in_=den_sb[:]))
            vv_inv = P.inc("vector", sV)
            P.wait("tensor", sV, vv_inv)
            P.emit("tensor", lambda e: e.matmul(
                p4_rep[0][:], ones_sb[0:1, 0:128], den_sb[:], start=True, stop=True))
            vt_ivr = P.inc("tensor", sT)
            P.wait("scalar", sT, vt_ivr)
            P.emit("scalar", lambda e: e.activation(out=invrep[:], in_=p4_rep[0][:], func=AF.Copy))
            va_ivr = P.inc("scalar", sA)
            for kt in range(64):
                r_, j_ = kt // 16, kt % 16
                pk = kt % 2
                svk = sX[pk]
                if kt >= 2:
                    P.wait("scalar", sT, pv_mm[(qc, kt - 2)])
                if kt < 2:
                    P.wait("scalar", sCC, v_agv)
                svk.n += 16
                P.emit("scalar", lambda e, r2=r_, j2=j_, p=pk, s=svk: e.dma_start(
                    out=xv_k[p][:],
                    in_=xv_all[r2 * TQ + j2 * 128:r2 * TQ + (j2 + 1) * 128, :]
                ).then_inc(s.h, 16))
                v_v = svk.n
                P.wait("tensor", svk, v_v)
                if kt == 0:
                    P.wait("tensor", sA, exp_done[(qc, 63)])
                    P.wait("tensor", sA, va_ivr)
                    P.wait("tensor", sV, str_done.get(qc - 1, 0))
                for ci in range(8):
                    P.emit("tensor", lambda e, c2=ci, p=pk, kt_=kt: e.matmul(
                        OB[c2][:], xv_k[p][:, c2 * 128:(c2 + 1) * 128], E_sb[:, kt_, :],
                        start=(kt_ == 0), stop=(kt_ == 63)))
                pv_mm[(qc, kt)] = P.inc("tensor", sT)
            pv_done_t[0] = pv_mm[(qc, 63)]
            P.wait("vector", sT, pv_mm[(qc, 63)])
            P.wait("vector", sA, va_ivr)
            for ci in range(8):
                P.emit("vector", lambda e, c2=ci: e.tensor_mul(out=t4[:], in0=OB[c2][:], in1=invrep[:]))
                P.inc("vector", sVS); P.wait("vector", sVS, sVS.n)
                P.emit("vector", lambda e, c2=ci, q_=Q0: e.tensor_add(
                    out=stream_fm[:, c2, q_:q_ + 512],
                    in0=stream_fm[:, c2, q_:q_ + 512], in1=t4[:]))
            str_done[qc] = P.inc("vector", sV)
        if dbg_out is not None:
            P.wait("sync", sV, sV.n)
            P.wait("sync", sA, sA.n)
            sDBG.n += 16
            P.emit("sync", lambda e: e.dma_start(out=dbg_out, in_=stream_fm[:]).then_inc(sDBG.h, 16))
        es4.close()

    qk_done = {}
    pv_mm = {}
    str_done = {}

    DO_ATTN = 2
    if FFN_STOP >= 1:
        ffn_phase(0)
        if FFN_STOP >= 4 and DO_ATTN:
            for _e in ENGS:
                P.wait(_e, sT, sT.n)
                P.wait(_e, sA, sA.n)
                P.wait(_e, sV, sV.n)
                P.wait(_e, sP, sP.n)
            attn_phase()
            if DO_ATTN >= 2:
                for _e in ENGS:
                    P.wait(_e, sT, sT.n)
                    P.wait(_e, sA, sA.n)
                    P.wait(_e, sV, sV.n)
                    P.wait(_e, sP, sP.n)
                ffn_phase(1)
                for _e in ("tensor", "scalar", "sync"):
                    P.wait(_e, sT, sT.n)
                    P.wait(_e, sA, sA.n)
                    P.wait(_e, sV, sV.n)
                es6 = ExitStack()
                p6_t = [es6.enter_context(nc.psum_tensor(f"p6_t{i}", [128, 128], F32)) for i in range(2)]
                o6 = es6.enter_context(nc.sbuf_tensor("o6", [128, 4, D], F32))
                ev6 = {}
                st6 = {}
                q6 = [(sOP[0], "sync"), (sOP[1], "scalar")]
                for ti in range(16):
                    ob = ti % 4
                    sq, qe = q6[ti % 2]
                    if ti >= 4:
                        P.wait("scalar", st6[ti - 4][0], st6[ti - 4][1])
                    for kk in range(8):
                        sl = (ti * 8 + kk) % 2
                        P.wait("tensor", sA, ev6.get(sl, 0))
                        P.emit("tensor", lambda e, t_=ti, k_=kk, s=sl: e.transpose(
                            p6_t[s][:], stream_fm[:, k_, t_ * 128:(t_ + 1) * 128], identf_sb[:]))
                        vt6 = P.inc("tensor", sT)
                        P.wait("scalar", sT, vt6)
                        P.emit("scalar", lambda e, k_=kk, s=sl, o_=ob: e.activation(
                            out=o6[:, o_, k_ * 128:(k_ + 1) * 128], in_=p6_t[s][:], func=AF.Copy))
                        ev6[sl] = P.inc("scalar", sA)
                    P.wait(qe, sA, sA.n)
                    sq.n += 16
                    P.emit(qe, lambda e, t_=ti, o_=ob, s=sq: e.dma_start(
                        out=out_q[t_ * 128:(t_ + 1) * 128, :], in_=o6[:, o_, :]).then_inc(s.h, 16))
                    st6[ti] = (sq, sq.n)
                for sq, _qe in q6:
                    P.wait("sync", sq, sq.n)
    P.run(nc)
    return nc



def _make_selp(r):
    sel = np.zeros((16, 4), np.float32)
    for sc in range(4):
        if r > 0:
            sel[(r - 1) * 4 + sc, sc] = 1.0
        elif sc > 0:
            sel[3 * 4 + (sc - 1), sc] = 1.0
    return sel.astype(ml_dtypes.bfloat16)


def _gidx(r):
    return np.concatenate([np.arange(sc * 2048 + r * 512, sc * 2048 + r * 512 + 512)
                           for sc in range(4)])


def _mk_ind3():
    a = np.zeros((128, 4), np.float32)
    a[0:64, 0] = 1.0
    a[64:80, 1] = 1.0
    a[80:96, 2] = 1.0
    return a


def _mk_ind2():
    a = np.zeros((3, 384), np.float32)
    for r_ in range(3):
        a[r_, r_ * 128:(r_ + 1) * 128] = 1.0
    return a


def _mk_cst3():
    a = np.zeros((128, 1), np.float32)
    a[0, 0] = 1.0 / RDT
    a[1, 0] = 1.0 / NST
    a[2, 0] = 1.0 / NST
    return a


_IND3 = _mk_ind3()
_IND2 = _mk_ind2()
_CST3 = _mk_cst3()


def _shard_inputs(inputs, core):
    b, r = core // 4, core % 4
    d0, d1 = r * DSL, (r + 1) * DSL
    def shp(v):
        return np.ascontiguousarray(v.reshape(4, 128).T)
    x = inputs["L0_attn_a_hidden_states"]
    w_in_full = inputs["L0_attn_a_in_proj_w"]
    return dict(
        x_q=np.ascontiguousarray(x[b][_gidx(r)]),
        w_in=np.ascontiguousarray(np.concatenate(
            [w_in_full[:, d0:d1], w_in_full[:, DI + d0:DI + d1]], axis=1)),
        conv_w=np.ascontiguousarray(
            inputs["L0_attn_a_conv_w"][d0:d1].reshape(4, 128, KC).transpose(1, 0, 2)),
        conv_b=shp(inputs["L0_attn_a_conv_b"][d0:d1]),
        xproj_w=np.ascontiguousarray(inputs["L0_attn_a_x_proj_w"][d0:d1]),
        dtw=np.ascontiguousarray(inputs["L0_attn_a_dt_proj_w"][:, d0:d1]),
        dtb=shp(inputs["L0_attn_a_dt_proj_b"][d0:d1]),
        dtlnw=np.ascontiguousarray(inputs["L0_attn_a_dt_ln_w"].reshape(RDT, 1)),
        blnw=np.ascontiguousarray(inputs["L0_attn_a_b_ln_w"].reshape(1, NST)),
        clnw=np.ascontiguousarray(inputs["L0_attn_a_c_ln_w"].reshape(1, NST)),
        alog=np.ascontiguousarray(
            inputs["L0_attn_a_A_log"][d0:d1].reshape(4, 128, NST).transpose(1, 0, 2)),
        dskip=shp(inputs["L0_attn_a_D_skip"][d0:d1]),
        wout=np.ascontiguousarray(inputs["L0_attn_a_out_proj_w"][d0:d1]),
        inlnw=np.ascontiguousarray(inputs["L0_attn_a_input_ln_w"].reshape(1, D)),
        ident_bf=np.eye(128, dtype=ml_dtypes.bfloat16),
        ident_f=np.eye(128, dtype=np.float32),
        gate_w=inputs["L0_attn_a_gate_w"],
        up_w=inputs["L0_attn_a_up_w"],
        down_w=inputs["L0_attn_a_down_w"],
        pmw=np.ascontiguousarray(inputs["L0_attn_a_pre_moe_ln_w"].reshape(1, D)),
        w_r=inputs["L0_ffn_W_r"],
        w_k=inputs["L0_ffn_W_k"],
        w_v=inputs["L0_ffn_W_v"],
        mu_r=np.ascontiguousarray(inputs["L0_ffn_mu_r"].reshape(8, 128).T),
        mu_k=np.ascontiguousarray(inputs["L0_ffn_mu_k"].reshape(8, 128).T),
        selp=_make_selp(r),
        ind3_d=_IND3, ind2_d=_IND2, cst3_d=_CST3,
    )


_NC_CACHE = [None]


def kernel(**inputs):
    if _NC_CACHE[0] is None:
        _NC_CACHE[0] = build(detect_races=False)
    nc = _NC_CACHE[0]
    in_maps = [_shard_inputs(inputs, c) for c in range(8)]
    res = run_bass_kernel_spmd(nc, in_maps, core_ids=list(range(8)))
    out = np.empty((B, S, D), np.float32)
    for c in range(8):
        b, r = c // 4, c % 4
        out[b][_gidx(r)] = res.results[c]["out_q"]
    return out
